# version 09
# baseline (speedup 1.0000x reference)
import numpy as np
import ml_dtypes
import concourse.bass as bass
import concourse.mybir as mybir
from concourse.bass_utils import run_bass_kernel_spmd

BF = ml_dtypes.bfloat16

"""Support code for the mk_seq_preln_d1 deltanet megakernel (TRN2, raw bass).

Layout conventions:
- TM (token-major): [128 tokens on partitions x features on free]
- FM (feature-major): [128 features-within-chunk on partitions x (fc, t) on free]
  global feature index d = fc*128 + p ; head h = d // 64 = 2*fc + p//64
- scan all-heads tiles: [128 x 512] fp32/bf16 ; head h=2j+r at partition block
  r=h%2 (rows 64r..64r+63), free block j=h//2 (cols 64j..64j+63)
- weights in HBM scratch, bf16, transposed to [in_feat x out_feat]
"""
import numpy as np
import ml_dtypes
import concourse.bass as bass
import concourse.mybir as mybir

F32 = mybir.dt.float32
BF16 = mybir.dt.bfloat16
I32 = mybir.dt.int32
AF = mybir.ActivationFunctionType
ALU = mybir.AluOpType
AX = mybir.AxisListType

D = 1024
H = 16
HD = 64
INNER = 4096
CTRL = 2 * H + D
QKV = 3 * D
C = 64


class Cfg:
    def __init__(self, S=4096, T=256):
        assert S % T == 0 and T % 128 == 0
        self.S = S
        self.T = T
        self.NCH = S // T
        self.NTT = T // 128
        self.NSC = T // C


ENGINES = ("sync", "tensor", "vector", "scalar", "gpsimd")


class Op:
    __slots__ = ("idx", "eng", "fn", "reads", "writes", "dma", "tag",
                 "need_inc", "tick", "waits", "_deps", "dkey", "phase")

    def __init__(self, idx, eng, fn, reads, writes, dma, tag):
        self.idx = idx
        self.eng = eng
        self.fn = fn
        self.reads = reads
        self.writes = writes
        self.dma = dma
        self.tag = tag
        self.need_inc = False
        self.tick = 0
        self.waits = []
        self.dkey = None


class Prog:
    def __init__(self):
        self.ops = []
        self.phase = "init"
        self.last_writer = {}
        self.readers = {}

    def op(self, eng, reads=(), writes=(), dma=False, tag=""):
        def deco(fn):
            o = Op(len(self.ops), eng, fn, expand_couples(reads), expand_couples(writes), dma, tag)
            o.phase = self.phase
            if dma:
                o.dkey = "D_" + (o.writes[0] if o.writes else (o.reads[0] if o.reads else "misc"))
            deps = []
            for r in o.reads:
                w = self.last_writer.get(r)
                if w is not None:
                    deps.append(w)
            for w_ in o.writes:
                lw = self.last_writer.get(w_)
                if lw is not None:
                    deps.append(lw)
                deps.extend(self.readers.get(w_, ()))
            for d in deps:
                if d.eng != o.eng:
                    d.need_inc = True
            o._deps = [d for d in deps if d.eng != o.eng]
            for r in o.reads:
                self.readers.setdefault(r, []).append(o)
            for w_ in o.writes:
                self.last_writer[w_] = o
                self.readers[w_] = []
            self.ops.append(o)
            return o
        return deco

    def finalize(self, nc, block):
        for o in self.ops:
            if o.dma:
                o.need_inc = True
        tick = {}
        for o in self.ops:
            key = o.dkey if o.dkey else o.eng
            if o.need_inc:
                tick[key] = tick.get(key, 0) + (16 if o.dma else 1)
            o.tick = tick.get(key, 0)
        water = {}
        for o in self.ops:
            want = {}
            for d in o._deps:
                assert d.need_inc and d.tick > 0
                key = d.dkey if d.dkey else d.eng
                want[key] = max(want.get(key, 0), d.tick)
            for k2, v in want.items():
                wk = (o.eng, k2)
                if v > water.get(wk, 0):
                    o.waits.append((k2, v))
                    water[wk] = v
        n_inc = sum(1 for o in self.ops if o.need_inc)
        n_wait = sum(len(o.waits) for o in self.ops)
        by_eng = {e: sum(1 for o in self.ops if o.eng == e) for e in ENGINES}
        print(f"[prog] ops={len(self.ops)} incs={n_inc} waits={n_wait} by_eng={by_eng}")
        keys = set(e for e in ENGINES)
        for o in self.ops:
            if o.dkey:
                keys.add(o.dkey)
        sems = {k: nc.alloc_semaphore(f"sem_{k}"[:28]) for k in sorted(keys)}
        self.sems = sems

        def run_engine(ename):
            def body(eng):
                for o in self.ops:
                    if o.eng != ename:
                        continue
                    for (pe, v) in o.waits:
                        eng.wait_ge(sems[pe], v)
                    with nc.named_scope(o.phase):
                        ins = o.fn(eng)
                    if o.need_inc:
                        assert ins is not None, f"op {o.idx} {o.tag} returned no instruction"
                        key = o.dkey if o.dkey else ename
                        ins.then_inc(sems[key], 16 if o.dma else 1)
            return body

        for ename in ENGINES:
            getattr(block, ename)(run_engine(ename))


def make_consts():
    c = {}
    c["ident64"] = np.vstack([np.eye(64), np.eye(64)]).astype(ml_dtypes.bfloat16)
    c["ident128"] = np.eye(128, dtype=ml_dtypes.bfloat16)
    c["ones64"] = np.ones((1, 64), dtype=ml_dtypes.bfloat16)
    c["ones128row"] = np.ones((1, 128), dtype=ml_dtypes.bfloat16)
    c["onescol128"] = np.ones((128, 1), dtype=np.float32)
    i_idx = np.arange(64)
    negstriu = -(i_idx[:, None] < i_idx[None, :]).astype(np.float32)
    triuincl = (i_idx[:, None] <= i_idx[None, :]).astype(np.float32)
    def tile_all(m):
        out = np.zeros((128, 512), np.float32)
        for r in range(2):
            for j in range(8):
                out[64*r:64*r+64, 64*j:64*j+64] = m
        return out
    c["negstriu"] = tile_all(negstriu).astype(ml_dtypes.bfloat16)
    c["triuincl"] = tile_all(triuincl).astype(ml_dtypes.bfloat16)
    hm = np.zeros((128, 128), np.float32)
    for fc in range(8):
        for p in range(128):
            hm[p, fc*16 + 2*fc + p//64] = 1.0
    c["headmaskC"] = hm.astype(ml_dtypes.bfloat16)
    m2 = np.zeros((16, 1024), np.float32)
    for h in range(16):
        fc, r = h // 2, h % 2
        m2[h, fc*128 + 64*r: fc*128 + 64*r + 64] = 1.0
    mp = np.zeros((64, 1024), np.float32)
    mp[0:16] = m2
    mp[32:48] = -m2
    c["mask2pack"] = mp.astype(ml_dtypes.bfloat16)
    c["mask2_f32"] = m2.astype(np.float32)
    m2s = np.zeros((16, 128), np.float32)
    for h in range(16):
        m2s[h, 64*(h % 2): 64*(h % 2)+64] = 1.0
    c["mask2s_f32"] = m2s.astype(np.float32)
    c["mask2s_bf"] = m2s.astype(ml_dtypes.bfloat16)
    hm8 = np.zeros((16, 512), np.float32)
    for h in range(16):
        j = h // 2
        hm8[h, 64*j:64*j+64] = 1.0
    c["headmask8_f32"] = hm8
    c["headmask8_bf"] = hm8.astype(ml_dtypes.bfloat16)
    hmj = np.zeros((16, 8), np.float32)
    for h in range(16):
        hmj[h, h//2] = 1.0
    c["headmaskj"] = hmj
    return c


COUPLES = {
    "y": ("hid", "hidFM", "qkvFM", "convq", "convk", "Vfm"),
    "outgate": ("hid", "hidFM"),
    "istage": ("Qn", "Qh"),
    "yqFM": ("Qn", "Qh", "Kn", "Khm"),
    "h2": ("ab", "r_siga", "r_logb", "r_beta", "r_L", "r_negL", "r_eL", "r_invq"),
    "prep": ("convq",),
    "prepbf": ("convq",),
    "prepstage": ("convk",),
    "prepsums": ("convk",),
    "normw_f32": ("qkvFM",),
    "wbuf_outA": ("wbuf_qkvA", "wbuf_g0"),
    "wbuf_outB": ("wbuf_qkvB", "wbuf_g1"),
    "wbuf_g0": ("wbuf_qkvA",),
    "wbuf_g1": ("wbuf_qkvB",),
    "wbuf_u0": ("wbuf_ctrlA", "wbuf_ctrlB"),
    "wbuf_u1": ("readsFM",),
    "wbuf_dA": ("readsFM",),
    "wbuf_dB": ("normedFM",),
}


def expand_couples(res):
    out = list(res)
    for r in res:
        out.extend(COUPLES.get(r, ()))
    return tuple(dict.fromkeys(out))


"""Main megakernel builder. See kb.py for conventions.

Phases per token chunk (T tokens, NTT=T/128 token tiles, NSC=T/64 scan chunks):
  F: x load, LN1, hidden->FM, qkv mm, ctrl mm, conv mm, l2norm, scaled variants
  S: per scan chunk: WY delta rule (all 16 heads batched in [128x512] tiles)
  E: RMS head-norm, out proj (token-major out), gate, residual -> x1
  FF: LN2, act_quant, gate/up mm, silu*up, act_quant, down mm, residual -> x2, store
"""


def ap_custom(t_ap, offset, dims):
    return bass.AP(tensor=t_ap.tensor, offset=offset, ap=[list(d) for d in dims])


class Builder:
    def __init__(self, nc, cfg: Cfg, taps=None):
        self.nc = nc
        self.cfg = cfg
        self.p = Prog()
        self.taps = taps or []
        self.tap_outs = {}
        self.consts_np = make_consts()
        self._alloc()

    def _alloc(self):
        nc, cfg = self.nc, self.cfg
        T = cfg.T
        self.dram_in = {}
        for name, shape in [
            ("x", [cfg.S, D]), ("qkv_w", [QKV, D]), ("ctrl_w", [CTRL, D]),
            ("ctrl_b", [CTRL]), ("convq_w", [D, 4]), ("convk_w", [D, 4]),
            ("convv_w", [D, 4]), ("norm_w_t", [16, D]), ("out_w", [D, D]),
            ("out_b", [D]), ("w_gate", [INNER, D]), ("w_up", [INNER, D]),
            ("w_down", [D, INNER]),
        ]:
            self.dram_in[name] = nc.dram_tensor(name, shape, F32, kind="ExternalInput").ap()
        self.dram_out = nc.dram_tensor("out", [cfg.S, D], F32, kind="ExternalOutput").ap()
        self.scr = {}
        for name, shape in [("qkvT", [D, QKV]), ("ctrlT", [D, CTRL]),
                            ("outT", [D, D]), ("wgT", [D, INNER]),
                            ("wuT", [D, INNER]), ("wdT", [INNER, D])]:
            self.scr[name] = nc.dram_tensor("scr_" + name, shape, BF16, kind="Internal").ap()
        self.cst = {}
        for k, v in self.consts_np.items():
            self.cst[k] = nc.inline_tensor(np.ascontiguousarray(v), "c_" + k).ap()
        self.cst["onesT_bf"] = nc.inline_tensor(np.ones((1, T), BF), "c_onesT").ap()
        self.cst["ones64_f32"] = nc.inline_tensor(np.ones((1, 64), np.float32), "c_ones64f").ap()
        self.cst["ones128row_f32"] = nc.inline_tensor(np.ones((1, 128), np.float32), "c_o128rf").ap()
        self.cst["onescol128_f32"] = nc.inline_tensor(np.ones((128, 1), np.float32), "c_oc128f").ap()
        self.cst["onesf_16x64"] = nc.inline_tensor(np.ones((16, 64), np.float32), "c_of1664").ap()
        self.cst["biasc"] = nc.inline_tensor(
            np.tile(np.array([[1e-5, 1e-6, 0.01, 0.0]], np.float32), (128, 1)), "c_biasc").ap()

        A = nc.alloc_sbuf_tensor
        s = {}
        for nm, shape, dt in [
            ("ident64", [128, 64], BF16), ("ident128", [128, 128], BF16),
            ("negstriu", [128, 512], BF16), ("triuincl", [128, 512], BF16),
            ("headmaskC", [128, 128], BF16), ("mask2pack", [64, 1024], BF16),
            ("mask2s_f32", [16, 128], F32),
            ("mask2s_bf", [16, 128], BF16), ("headmask8_f32", [16, 512], F32),
            ("headmask8_bf", [16, 512], BF16), ("headmaskj", [16, 8], F32),
            ("onesT_bf", [1, T], BF16), ("ones64_bf", [1, 64], BF16),
            ("ones64_f32", [1, 64], F32), ("ones128row_bf", [1, 128], BF16),
            ("ones128row_f32", [1, 128], F32), ("onescol128_f32", [128, 1], F32),
            ("onesf_16x64", [16, 64], F32), ("biasc", [128, 4], F32),
            ("diags", [128, 96 * 128], BF16),
            ("ctrl_b_bf", [1, CTRL], BF16), ("out_b_bf", [1, D], BF16),
            ("mask2w", [16, 1024], BF16),
            ("cw", [128, 96], F32),
            ("xbuf0", [128, 2 * 1024], F32), ("xbuf1", [128, 2 * 1024], F32),
            ("stats", [128, 32], F32), ("stat1", [1, 64], F32),
            ("hid", [128, 2 * 1024], BF16), ("hidFM", [128, 8 * T], BF16),
            ("qkvFM", [128, 24 * (T + 3)], BF16),
            ("convq", [128, 8 * T], F32), ("convk", [128, 8 * T], F32),
            ("Vfm", [128, 8 * T], BF16), ("Qn", [128, 8 * T], BF16),
            ("Qh", [128, 8 * T], BF16), ("Kn", [128, 8 * T], BF16),
            ("Khm", [128, 8 * T], BF16),
            ("ab", [64, T], F32),
            ("r_siga", [16, T], F32), ("r_logb", [16, T], F32),
            ("r_beta", [16, T], F32), ("r_L", [16, T], F32),
            ("r_negL", [16, T], F32), ("r_eL", [16, T], F32),
            ("r_invq", [16, T], F32), ("r_invk", [16, T], F32),
            ("r_t", [16, T], F32),
            ("rb_qn", [16, T], BF16), ("rb_qh", [16, T], BF16),
            ("rb_kn", [16, T], BF16), ("rb_khm", [16, T], BF16),
            ("lhsE", [16, 128], F32), ("lhsB", [16, 128], BF16),
            ("E", [128, 512], F32), ("bpartU", [128, 512], F32),
            ("bmask_t", [128, 512], BF16), ("bmask_B", [128, 512], BF16),
            ("NL0", [128, 512], BF16), ("NLa", [128, 512], BF16),
            ("NLb", [128, 512], BF16),
            ("NN0", [128, 512], BF16), ("NNa", [128, 512], BF16),
            ("NNb", [128, 512], BF16),
            ("BL0", [128, 512], BF16), ("BL1", [128, 512], BF16),
            ("R", [128, 512], BF16),
            ("readstm", [128, 512], BF16), ("Sall", [128, 512], BF16),
            ("Ktmp", [128, 512], BF16), ("t1", [128, 512], F32),
            ("t2", [128, 512], F32),
            ("gcall0", [128, 8], F32), ("gcall1", [128, 8], F32),
            ("dall0", [128, 8], F32), ("dall1", [128, 8], F32),
            ("gcrhs", [16, 8], F32),
            ("readsFM", [128, 8 * T], BF16), ("normedFM", [128, 8 * T], BF16),
            ("Lmaskd", [16, 512], F32), ("sqf", [128, 1024], F32),
            ("sqbuf", [128, T], BF16), ("rmsrow", [16, T], F32),
            ("rmsbf", [16, T], BF16),
            ("h2q", [128, 2 * 1024], BF16), ("h2qFM", [128, 8 * T], BF16),
            ("yq", [128, 2 * 4096], BF16),
            ("fstat", [128, 24], F32), ("qhist", [128, 72], BF16),
            ("wsc", [128, 4], F32),
            ("wbuf_qkvA", [128, 8 * 256], BF16), ("wbuf_qkvB", [128, 8 * 256], BF16),
            ("wbuf_ctrlA", [128, CTRL], BF16), ("wbuf_ctrlB", [128, CTRL], BF16),

        ]:
            s[nm] = A(nm, shape, dt)
        AT = nc.alloc_sbuf_tensor_at
        adr = lambda nm: nc.lookup_mloc(s[nm]).addr
        s["y"] = AT("y", [128, 2 * 4096], F32, offset=adr("hid"))
        s["outgate"] = AT("outgate", [128, 2 * 1024], F32, offset=adr("hid"))
        s["istage"] = AT("istage", [128, 2048], I32, offset=adr("Qn"))
        s["yqFM"] = AT("yqFM", [128, 32 * T], BF16, offset=adr("Qn"))
        s["h2"] = AT("h2", [128, 2 * 1024], F32, offset=adr("ab"))
        s["prep"] = AT("prep", [128, 1024], F32, offset=adr("convq"))
        s["prepbf"] = AT("prepbf", [128, 1024], BF16, offset=adr("convq") + 4096)
        s["prepstage"] = AT("prepstage", [128, 512], BF16, offset=adr("convk"))
        s["prepsums"] = AT("prepsums", [128, 64], F32, offset=adr("convk") + 1024)
        s["normw_f32"] = AT("normw_f32", [16, 1024], F32, offset=adr("qkvFM"))
        s["wbuf_outA"] = AT("wbuf_outA", [128, 2048], BF16, offset=adr("wbuf_qkvA"))
        s["wbuf_outB"] = AT("wbuf_outB", [128, 2048], BF16, offset=adr("wbuf_qkvB"))
        s["wbuf_g0"] = AT("wbuf_g0", [128, 4 * 512], BF16, offset=adr("wbuf_qkvA"))
        s["wbuf_g1"] = AT("wbuf_g1", [128, 4 * 512], BF16, offset=adr("wbuf_qkvB"))
        s["wbuf_u0"] = AT("wbuf_u0", [128, 4 * 512], BF16, offset=adr("wbuf_ctrlA"))
        s["wbuf_u1"] = AT("wbuf_u1", [128, 4 * 512], BF16, offset=adr("readsFM"))
        s["wbuf_dA"] = AT("wbuf_dA", [128, 2048], BF16, offset=adr("readsFM"))
        s["wbuf_dB"] = AT("wbuf_dB", [128, 2048], BF16, offset=adr("normedFM"))
        self.s = s
        used = nc.sbuf_base - 16481
        print(f"[alloc] sbuf used ~{used/1024:.1f} KB of {(229344-16481)/1024:.1f}")
        self.P = [nc.alloc_psum_tensor(f"P{i}", [128, 512], F32) for i in range(8)]
        for name, shape, dt in self.taps:
            self.tap_outs[name] = nc.dram_tensor("tap_" + name, shape, dt,
                                                 kind="ExternalOutput").ap()

    def tap(self, name, ap_fn, reads):
        if name not in self.tap_outs:
            return
        out = self.tap_outs[name]
        self.p.op("gpsimd", reads=tuple(reads), dma=True, tag="tap_" + name)(
            lambda e, o=out, f=ap_fn: e.dma_start(out=o, in_=f()))

    def load_consts(self):
        p, s, cst = self.p, self.s, self.cst
        pairs = [(k, k) for k in
                 ["ident64", "ident128", "negstriu", "triuincl",
                  "headmaskC", "mask2pack", "mask2s_f32", "mask2s_bf",
                  "headmask8_f32", "headmask8_bf", "headmaskj", "onesT_bf",
                  "ones64_f32", "ones128row_f32", "onescol128_f32", "onesf_16x64"]]
        pairs += [("ones64_bf", "ones64"), ("ones128row_bf", "ones128row"),
                  ("biasc", "biasc")]
        for sb, ck in pairs:
            p.op("sync", writes=("c_" + sb,), dma=True, tag="ldc")(
                lambda e, a=s[sb], b=cst[ck]: e.dma_start(out=a[:], in_=b))
        p.op("gpsimd", writes=("c_ctrl_b",), dma=True)(
            lambda e: e.dma_start(out=s["ctrl_b_bf"][:], in_=self.dram_in["ctrl_b"][None, :]))
        p.op("gpsimd", writes=("c_out_b",), dma=True)(
            lambda e: e.dma_start(out=s["out_b_bf"][:], in_=self.dram_in["out_b"][None, :]))
        p.op("sync", writes=("c_normw",), dma=True)(
            lambda e: e.dma_start(out=s["normw_f32"][:], in_=self.dram_in["norm_w_t"]))
        p.op("vector", reads=("c_normw", "c_mask2pack"), writes=("c_mask2w",))(
            lambda e: e.tensor_mul(s["mask2w"][:], s["normw_f32"][:],
                                   s["mask2pack"][0:16, :]))
        for si, nm in enumerate(["convq_w", "convk_w", "convv_w"]):
            in_ap = ap_custom(self.dram_in[nm], 0, [[4, 128], [512, 8], [1, 4]])
            out_ap = ap_custom(s["cw"][:], si * 32, [[96, 128], [4, 8], [1, 4]])
            p.op("sync", writes=("c_cw",), dma=True)(
                lambda e, i=in_ap, o=out_ap: e.dma_start(out=o, in_=i))
        for si in range(3):
            for f in range(8):
                for j in range(4):
                    idx = si * 32 + f * 4 + j
                    p.op("scalar", reads=("c_cw", "c_ident128"), writes=("c_diag",),
                         tag="diag")(
                        lambda e, idx=idx, col=si * 32 + f * 4 + j: e.activation(
                            out=s["diags"][:, idx * 128:(idx + 1) * 128],
                            in_=s["ident128"][:], func=AF.Copy,
                            scale=s["cw"][:, col:col + 1]))

    def _transpose_store(self, dst_scr, rowlen, col0, row0, tag):
        p, s, P = self.p, self.s, self.P
        for g in range(2):
            def mmf(e, g=g):
                ins = None
                for b in range(4):
                    cb = g * 4 + b
                    ins = e.matmul(P[6][:, b * 128:(b + 1) * 128],
                                   s["prepbf"][:, cb * 128:(cb + 1) * 128],
                                   s["ident128"][:], start=True, stop=True)
                return ins
            p.op("tensor", reads=("prepbf", "c_ident128"), writes=("P6",), tag=tag)(mmf)
            p.op("vector", reads=("P6",), writes=("prepstage",))(
                lambda e: e.tensor_copy(s["prepstage"][:], P[6][:, 0:512]))
            out_ap = ap_custom(dst_scr, (row0 + g * 512) * rowlen + col0,
                               [[rowlen, 128], [128 * rowlen, 4], [1, 128]])
            in_ap = ap_custom(s["prepstage"][:], 0, [[512, 128], [128, 4], [1, 128]])
            p.op("sync", reads=("prepstage",), dma=True, tag=tag + "st")(
                lambda e, i=in_ap, o=out_ap: e.dma_start(out=o, in_=i))

    def prep_weights_attn(self):
        p, s, P = self.p, self.s, self.P
        for nm, Osz, dstn in [("qkv_w", QKV, "qkvT"), ("ctrl_w", CTRL, "ctrlT"),
                              ("out_w", D, "outT")]:
            dst = self.scr[dstn]
            for t in range((Osz + 127) // 128):
                nr = min(128, Osz - t * 128)
                p.op("sync", writes=("prep",), dma=True, tag="pwld")(
                    lambda e, t=t, nm=nm, nr=nr: e.dma_start(
                        out=s["prep"][0:nr, :],
                        in_=self.dram_in[nm][t * 128:t * 128 + nr, :]))
                if nr == 128:
                    p.op("vector", reads=("prep",), writes=("prepbf",))(
                        lambda e: e.tensor_copy(s["prepbf"][:], s["prep"][:]))
                else:
                    p.op("vector", writes=("prepbf",))(
                        lambda e: e.memset(s["prepbf"][:], 0.0))
                    p.op("vector", reads=("prep",), writes=("prepbf",))(
                        lambda e, nr=nr: e.tensor_copy(s["prepbf"][0:nr, :], s["prep"][0:nr, :]))
                self._transpose_store_n(dst, Osz, t * 128, nr, "tp_" + nm, scrres="scr_" + dstn)
        return

    def prep_weights_ffn(self):
        p, s, P = self.p, self.s, self.P
        for nm, Osz, Isz, dstn in [("w_gate", INNER, D, "wgT"),
                                   ("w_up", INNER, D, "wuT"),
                                   ("w_down", D, INNER, "wdT")]:
            ntile = Osz * Isz // (128 * 1024)
            for t in range(ntile):
                p.op("sync", writes=("prep",), dma=True, tag="fq1")(
                    lambda e, t=t, nm=nm: e.dma_start(
                        out=s["prep"][:],
                        in_=ap_custom(self.dram_in[nm], t * 131072,
                                      [[1024, 128], [1, 1024]])))
                p.op("vector", reads=("prep",), writes=("psums",))(
                    lambda e, t=t: e.tensor_reduce(
                        s["prepsums"][:, t:t + 1], s["prep"][:], AX.X, ALU.add,
                        apply_absolute_value=True))
            p.op("vector", reads=("prep",), writes=("spacer",))(
                lambda e: e.tensor_copy(s["sqf"][:, 0:1024], s["prep"][:]))
            p.op("vector", reads=("psums",), writes=("psumtot",))(
                lambda e, nt=ntile: e.tensor_reduce(
                    s["prepsums"][:, 63:64], s["prepsums"][:, 0:nt], AX.X, ALU.add))
            p.op("tensor", reads=("psumtot", "c_onescol128_f32"), writes=("P7",))(
                lambda e: e.matmul(P[7][0:1, 0:1], s["prepsums"][:, 63:64],
                                   s["onescol128_f32"][:], start=True, stop=True))
            p.op("vector", reads=("P7",), writes=("stat1",))(
                lambda e, n=Osz * Isz: e.tensor_scalar(
                    s["stat1"][0:1, 0:1], P[7][0:1, 0:1], 1.0 / n, 1e-5,
                    ALU.mult, ALU.max))
            p.op("tensor", reads=("stat1", "c_ones128row_f32"), writes=("P7",))(
                lambda e: e.matmul(P[7][:, 0:1], s["ones128row_f32"][:],
                                   s["stat1"][0:1, 0:1], start=True, stop=True))
            wi = {"w_gate": 0, "w_up": 1, "w_down": 2}[nm]
            p.op("scalar", reads=("P7",), writes=("wmean",))(
                lambda e: e.activation(out=s["stats"][:, 29:30], in_=P[7][:, 0:1],
                                       func=AF.Copy))
            p.op("scalar", reads=("P7",), writes=("wsc",))(
                lambda e, wi=wi: e.activation(out=s["wsc"][:, wi:wi + 1],
                                              in_=P[7][:, 0:1], func=AF.Copy))
            p.op("vector", reads=("wmean",), writes=("wscale",))(
                lambda e: e.reciprocal(s["stats"][:, 30:31], s["stats"][:, 29:30]))
            if nm == "w_gate" and "pstat" in self.tap_outs:
                self.p.op("gpsimd", reads=("wscale", "stat1", "psums"), dma=True, tag="tpstat")(
                    lambda e: e.dma_start(out=self.tap_outs["pstat"], in_=s["stat1"][:]))
                self.p.op("gpsimd", reads=("wscale",), dma=True, tag="tpsums")(
                    lambda e: e.dma_start(out=self.tap_outs["psums"], in_=s["prepsums"][:]))
            if nm in ("w_gate", "w_up"):
                tiles = [(t * 128, 0, Osz, t * 128, 0) for t in range(32)]
                tiles = [(t * 128, 0, Osz, t * 128, 0) for t in range(32)]
            else:
                tiles = [(t * 128, q * 1024, D, t * 128, q * 1024)
                         for t in range(8) for q in range(4)]
            for (r0, c0, rowlen, dcol0, drow0) in tiles:
                p.op("sync", writes=("prep",), dma=True, tag="fq2")(
                    lambda e, r0=r0, c0=c0, nm=nm: e.dma_start(
                        out=s["prep"][:],
                        in_=ap_custom(self.dram_in[nm], r0 * (D if nm != "w_down" else INNER) + c0,
                                      [[(D if nm != "w_down" else INNER), 128], [1, 1024]])))
                p.op("vector", reads=("prep", "wscale"), writes=("prep",))(
                    lambda e: e.tensor_scalar(s["prep"][:], s["prep"][:],
                                              s["stats"][:, 30:31], 1.0, ALU.mult, ALU.min))
                p.op("vector", reads=("prep",), writes=("prep",))(
                    lambda e: e.tensor_scalar(s["prep"][:], s["prep"][:], -1.0, None, ALU.max))
                p.op("vector", reads=("prep",), writes=("istage",))(
                    lambda e: e.tensor_copy(s["istage"][:, 0:1024], s["prep"][:]))
                p.op("vector", reads=("istage",), writes=("prepbf",))(
                    lambda e: e.tensor_copy(s["prepbf"][:], s["istage"][:, 0:1024]))
                self._transpose_store_n(self.scr[dstn], rowlen, dcol0, 128,
                                        "tq_" + nm, row0=drow0, scrres="scr_" + dstn)

    def _transpose_store_n(self, dst_scr, rowlen, col0, ncols, tag, row0=0, scrres="scr"):
        p, s, P = self.p, self.s, self.P
        for g in range(2):
            def mmf(e, g=g):
                ins = None
                for b in range(4):
                    cb = g * 4 + b
                    ins = e.matmul(P[6][:, b * 128:(b + 1) * 128],
                                   s["prepbf"][:, cb * 128:(cb + 1) * 128],
                                   s["ident128"][:], start=True, stop=True)
                return ins
            p.op("tensor", reads=("prepbf", "c_ident128"), writes=("P6",), tag=tag)(mmf)
            p.op("vector", reads=("P6",), writes=("prepstage",))(
                lambda e: e.tensor_copy(s["prepstage"][:], P[6][:, 0:512]))
            out_ap = ap_custom(dst_scr, (row0 + g * 512) * rowlen + col0,
                               [[rowlen, 128], [128 * rowlen, 4], [1, ncols]])
            in_ap = ap_custom(s["prepstage"][:], 0, [[512, 128], [128, 4], [1, ncols]])
            p.op("sync", reads=("prepstage",), writes=(scrres,), dma=True, tag=tag + "st")(
                lambda e, i=in_ap, o=out_ap: e.dma_start(out=o, in_=i))

    def frontend(self, ch):
        p, s, P, cfg = self.p, self.s, self.P, self.cfg
        T, NTT = cfg.T, cfg.NTT
        xb = s["xbuf%d" % (ch % 2)]
        xr = "xbuf%d" % (ch % 2)
        in_ap = ap_custom(self.dram_in["x"], ch * T * D, [[D, 128], [128 * D, NTT], [1, D]])
        out_ap = ap_custom(xb[:], 0, [[2 * 1024, 128], [1024, NTT], [1, 1024]])
        p.op("sync", writes=(xr,), dma=True, tag="xld")(
            lambda e, i=in_ap, o=out_ap: e.dma_start(out=o, in_=i))
        self._ln(xb, xr, s["hid"], "hid", statcol=0)
        for tt in range(NTT):
            def mmf(e, tt=tt):
                ins = None
                for g in range(2):
                    for b in range(4):
                        fc = g * 4 + b
                        ins = e.matmul(P[g][:, b * 128:(b + 1) * 128],
                                       s["hid"][:, tt * 1024 + fc * 128: tt * 1024 + (fc + 1) * 128],
                                       s["ident128"][:], start=True, stop=True)
                return ins
            p.op("tensor", reads=("hid", "c_ident128"), writes=("P0", "P1"), tag="hidT")(mmf)
            for g in range(2):
                out_ap = ap_custom(s["hidFM"][:], tt * 128 + g * 4 * T,
                                   [[8 * T, 128], [T, 4], [1, 128]])
                if g == 0:
                    p.op("vector", reads=("P0",), writes=("hidFM",))(
                        lambda e, o=out_ap: e.tensor_copy(o, P[0][:, 0:512]))
                else:
                    p.op("scalar", reads=("P1",), writes=("hidFM",))(
                        lambda e, o=out_ap: e.activation(out=o, in_=P[1][:, 0:512],
                                                         func=AF.Copy))
        self.tap("hid", lambda: s["hid"][:], ["hid"]) if ch == getattr(self, "tap_chunk", 0) else None
        self.tap("stats", lambda: s["stats"][:], ["stats"]) if ch == getattr(self, "tap_chunk", 0) else None
        self.tap("hidFM", lambda: s["hidFM"][:], ["hidFM"]) if ch == getattr(self, "tap_chunk", 0) else None
        if ch == 0:
            p.op("vector", writes=("qkvFM_m",), tag="qmz")(
                lambda e: e.memset(ap_custom(s["qkvFM"][:], 0,
                                             [[24 * (T + 3), 128], [T + 3, 24], [1, 3]]), 0.0))
        else:
            mg_out = ap_custom(s["qkvFM"][:], 0, [[24 * (T + 3), 128], [T + 3, 24], [1, 3]])
            p.op("vector", reads=("qhist",), writes=("qkvFM_m",), tag="qmc")(
                lambda e, o=mg_out: e.tensor_copy(o, s["qhist"][:]))
        for pair in range(12):
            buf = "wbuf_qkv" + ("A" if pair % 2 == 0 else "B")
            p.op("sync", reads=("scr_qkvT",), writes=(buf,), dma=True, tag="wqkv")(
                lambda e, pair=pair, buf=buf: e.dma_start(
                    out=s[buf][:],
                    in_=ap_custom(self.scr["qkvT"], pair * 256,
                                  [[QKV, 128], [128 * QKV, 8], [1, 256]])))
            for ms in range(2):
                m = pair * 2 + ms
                bank = m % 2
                def mmf(e, ms=ms, bank=bank, buf=buf):
                    ins = None
                    for k in range(8):
                        ins = e.matmul(P[bank][:, 0:T],
                                       s[buf][:, k * 256 + ms * 128: k * 256 + (ms + 1) * 128],
                                       s["hidFM"][:, k * T:(k + 1) * T],
                                       start=(k == 0), stop=(k == 7))
                    return ins
                p.op("tensor", reads=(buf, "hidFM"), writes=("P%d" % bank,),
                     tag="qkvmm")(mmf)
                p.op("scalar", reads=("P%d" % bank,), writes=("qkvFM",))(
                    lambda e, m=m, bank=bank: e.activation(
                        out=s["qkvFM"][:, m * (T + 3) + 3: (m + 1) * (T + 3)],
                        in_=P[bank][:, 0:T], func=AF.Copy))
        for k in range(8):
            cb = "wbuf_ctrl" + ("A" if k % 2 == 0 else "B")
            p.op("sync", reads=("scr_ctrlT",), writes=(cb,), dma=True, tag="wctrl")(
                lambda e, k=k, cb=cb: e.dma_start(
                    out=s[cb][:],
                    in_=ap_custom(self.scr["ctrlT"], k * 128 * CTRL, [[CTRL, 128], [1, CTRL]])))
            def mmab(e, k=k, cb=cb):
                e.matmul(P[2][0:16, 0:T], s[cb][:, 0:16],
                         s["hidFM"][:, k * T:(k + 1) * T],
                         start=(k == 0), stop=False)
                return e.matmul(P[2][32:48, 0:T], s[cb][:, 16:32],
                                s["hidFM"][:, k * T:(k + 1) * T],
                                start=(k == 0), stop=False)
            p.op("tensor", reads=(cb, "hidFM"), writes=("P2",), tag="abmm")(mmab)
            def mmgate(e, k=k, cb=cb):
                ins = None
                for tt in range(NTT):
                    for hf in range(2):
                        ins = e.matmul(
                            P[3 + tt * 2 + hf][:, 0:512],
                            s["hidFM"][:, k * T + tt * 128: k * T + (tt + 1) * 128],
                            s[cb][:, 32 + hf * 512: 32 + (hf + 1) * 512],
                            start=(k == 0), stop=False)
                return ins
            p.op("tensor", reads=(cb, "hidFM"), writes=("P3", "P4", "P5", "P6"),
                 tag="gatemm")(mmgate)
        def mmab_b(e):
            e.matmul(P[2][0:16, 0:T], s["ctrl_b_bf"][0:1, 0:16],
                     s["onesT_bf"][0:1, 0:T], start=False, stop=True)
            return e.matmul(P[2][32:48, 0:T], s["ctrl_b_bf"][0:1, 16:32],
                            s["onesT_bf"][0:1, 0:T], start=False, stop=True)
        p.op("tensor", reads=("c_ctrl_b", "c_onesT_bf"), writes=("P2",), tag="abbias")(mmab_b)
        def mmgate_b(e):
            ins = None
            for tt in range(NTT):
                for hf in range(2):
                    ins = e.matmul(P[3 + tt * 2 + hf][:, 0:512],
                                   s["ones128row_bf"][0:1, :],
                                   s["ctrl_b_bf"][0:1, 32 + hf * 512: 32 + (hf + 1) * 512],
                                   start=False, stop=True)
            return ins
        p.op("tensor", reads=("c_ctrl_b", "c_ones128row_bf"),
             writes=("P3", "P4", "P5", "P6"), tag="gatebias")(mmgate_b)
        p.op("vector", reads=("P2",), writes=("ab",))(
            lambda e: e.tensor_copy(s["ab"][0:48, 0:T], P[2][0:48, 0:T]))
        for tt in range(NTT):
            for hf in range(2):
                p.op("scalar", reads=("P%d" % (3 + tt * 2 + hf),), writes=("outgate",))(
                    lambda e, tt=tt, hf=hf: e.activation(
                        out=s["outgate"][:, tt * 1024 + hf * 512: tt * 1024 + (hf + 1) * 512],
                        in_=P[3 + tt * 2 + hf][:, 0:512], func=AF.Sigmoid))
        for si, (dstn, dstr) in enumerate([("convq", "convq"), ("convk", "convk"),
                                           ("Vfm", "Vfm")]):
            for fc in range(8):
                m = si * 8 + fc
                bank = fc % 2
                def mmf(e, si=si, fc=fc, m=m, bank=bank):
                    ins = None
                    for j in range(4):
                        idx = si * 32 + fc * 4 + j
                        ins = e.matmul(P[bank][:, 0:T],
                                       s["diags"][:, idx * 128:(idx + 1) * 128],
                                       s["qkvFM"][:, m * (T + 3) + j: m * (T + 3) + j + T],
                                       start=(j == 0), stop=(j == 3))
                    return ins
                p.op("tensor", reads=("c_diag", "qkvFM", "qkvFM_m"),
                     writes=("P%d" % bank,), tag="convmm")(mmf)
                p.op("scalar", reads=("P%d" % bank,), writes=(dstr,))(
                    lambda e, fc=fc, bank=bank, dstn=dstn: e.activation(
                        out=s[dstn][:, fc * T:(fc + 1) * T],
                        in_=P[bank][:, 0:T], func=AF.Silu))
        mg_in = ap_custom(s["qkvFM"][:], T, [[24 * (T + 3), 128], [T + 3, 24], [1, 3]])
        p.op("vector", reads=("qkvFM",), writes=("qhist",), tag="qhist")(
            lambda e, i=mg_in: e.tensor_copy(s["qhist"][:], i))
        for src, dst in [("convq", "r_invq"), ("convk", "r_invk")]:
            for fc in range(8):
                p.op("scalar", reads=(src,), writes=("sqbuf",))(
                    lambda e, fc=fc, src=src: e.activation(
                        out=s["sqbuf"][:, 0:T], in_=s[src][:, fc * T:(fc + 1) * T],
                        func=AF.Square))
                def mmf(e, fc=fc):
                    return e.matmul(P[7][0:16, 0:T],
                                    s["headmaskC"][:, fc * 16:(fc + 1) * 16],
                                    s["sqbuf"][:, 0:T], start=(fc == 0), stop=(fc == 7))
                p.op("tensor", reads=("sqbuf", "c_headmaskC"), writes=("P7",), tag="l2mm")(mmf)
            p.op("scalar", reads=("P7",), writes=("r_t",))(
                lambda e: e.activation(out=s["r_t"][:, 0:T], in_=P[7][0:16, 0:T],
                                       func=AF.Sqrt))
            p.op("vector", reads=("r_t",), writes=("r_t",))(
                lambda e: e.tensor_scalar(s["r_t"][:, 0:T], s["r_t"][:, 0:T], 1e-6,
                                          None, ALU.max))
            p.op("vector", reads=("r_t",), writes=(dst,))(
                lambda e, dst=dst: e.reciprocal(s[dst][:, 0:T], s["r_t"][:, 0:T]))
        p.op("scalar", reads=("ab",), writes=("r_siga",))(
            lambda e: e.activation(out=s["r_siga"][:, 0:T], in_=s["ab"][0:16, 0:T],
                                   func=AF.Sigmoid))
        p.op("scalar", reads=("r_siga",), writes=("r_logb",))(
            lambda e: e.activation(out=s["r_logb"][:, 0:T], in_=s["r_siga"][:, 0:T],
                                   func=AF.Ln, scale=0.98, bias=s["biasc"][0:16, 2:3]))
        p.op("scalar", reads=("ab",), writes=("r_beta",))(
            lambda e: e.activation(out=s["r_beta"][:, 0:T], in_=s["ab"][32:48, 0:T],
                                   func=AF.Sigmoid))
        for sc in range(cfg.NSC):
            p.op("vector", reads=("r_logb", "c_onesf_16x64"), writes=("r_L",))(
                lambda e, sc=sc: e.tensor_tensor_scan(
                    s["r_L"][:, sc * 64:(sc + 1) * 64], s["onesf_16x64"][:],
                    s["r_logb"][:, sc * 64:(sc + 1) * 64], 0.0, ALU.mult, ALU.add))
        p.op("vector", reads=("r_L",), writes=("r_negL",))(
            lambda e: e.tensor_scalar(s["r_negL"][:, 0:T], s["r_L"][:, 0:T], -1.0,
                                      None, ALU.mult))
        p.op("scalar", reads=("r_L",), writes=("r_eL",))(
            lambda e: e.activation(out=s["r_eL"][:, 0:T], in_=s["r_L"][:, 0:T],
                                   func=AF.Exp))
        p.op("vector", reads=("r_invq",), writes=("rb_qn",))(
            lambda e: e.tensor_copy(s["rb_qn"][:, 0:T], s["r_invq"][:, 0:T]))
        p.op("vector", reads=("r_invk",), writes=("rb_kn",))(
            lambda e: e.tensor_copy(s["rb_kn"][:, 0:T], s["r_invk"][:, 0:T]))
        p.op("vector", reads=("r_invq", "r_eL"), writes=("rb_qh",))(
            lambda e: e.tensor_mul(s["rb_qh"][:, 0:T], s["r_invq"][:, 0:T],
                                   s["r_eL"][:, 0:T]))
        p.op("vector", reads=("r_invk", "r_eL"), writes=("rb_khm",))(
            lambda e: e.scalar_tensor_tensor(s["rb_khm"][:, 0:T], s["r_invk"][:, 0:T],
                                             -1.0, s["r_eL"][:, 0:T],
                                             ALU.mult, ALU.mult))
        for (row, r0m, src, dst) in [("rb_qn", 0, "convq", "Qn"),
                                     ("rb_qh", 0, "convq", "Qh"),
                                     ("rb_kn", 0, "convk", "Kn"),
                                     ("rb_khm", 0, "convk", "Khm")]:
            for fc in range(8):
                def mmf(e, row=row, r0m=r0m, fc=fc):
                    return e.matmul(P[7][:, 0:T],
                                    s["mask2pack"][r0m:r0m + 16, fc * 128:(fc + 1) * 128],
                                    s[row][:, 0:T], start=True, stop=True)
                p.op("tensor", reads=(row, "c_mask2pack"), writes=("P7",), tag="bcmm")(mmf)
                p.op("vector", reads=("P7", src), writes=(dst,))(
                    lambda e, fc=fc, src=src, dst=dst: e.tensor_mul(
                        s[dst][:, fc * T:(fc + 1) * T],
                        s[src][:, fc * T:(fc + 1) * T], P[7][:, 0:T]))

    def _ln(self, xb, xr, dst, dstr, statcol, eps=1e-5):
        p, s, cfg = self.p, self.s, self.cfg
        for tt in range(cfg.NTT):
            sl = (tt * 1024, (tt + 1) * 1024)
            c0 = statcol + tt * 8
            p.op("vector", reads=(xr,), writes=("stats",))(
                lambda e, sl=sl, c0=c0: e.tensor_reduce(
                    s["stats"][:, c0:c0 + 1], xb[:, sl[0]:sl[1]], AX.X, ALU.add))
            p.op("scalar", reads=(xr,), writes=("sq128",))(
                lambda e, sl=sl: e.activation(out=s["sqf"][:, 0:1024],
                                              in_=xb[:, sl[0]:sl[1]], func=AF.Square))
            p.op("vector", reads=("sq128",), writes=("stats",))(
                lambda e, c0=c0: e.tensor_reduce(
                    s["stats"][:, c0 + 1:c0 + 2], s["sqf"][:, 0:1024], AX.X, ALU.add))
            p.op("scalar", reads=("stats",), writes=("stats",))(
                lambda e, c0=c0: e.activation(out=s["stats"][:, c0 + 6:c0 + 7],
                                              in_=s["stats"][:, c0:c0 + 1],
                                              func=AF.Copy, scale=1.0 / 1024))
            p.op("scalar", reads=("stats",), writes=("stats",))(
                lambda e, c0=c0: e.activation(out=s["stats"][:, c0 + 7:c0 + 8],
                                              in_=s["stats"][:, c0 + 1:c0 + 2],
                                              func=AF.Copy, scale=1.0 / 1024))
            p.op("vector", reads=("stats",), writes=("stats",))(
                lambda e, c0=c0: e.tensor_mul(s["stats"][:, c0 + 2:c0 + 3],
                                              s["stats"][:, c0 + 6:c0 + 7],
                                              s["stats"][:, c0 + 6:c0 + 7]))
            p.op("gpsimd", reads=("stats",), writes=("stats",))(
                lambda e, c0=c0: e.tensor_sub(s["stats"][:, c0 + 3:c0 + 4],
                                              s["stats"][:, c0 + 7:c0 + 8],
                                              s["stats"][:, c0 + 2:c0 + 3]))
            p.op("scalar", reads=("stats",), writes=("stats",))(
                lambda e, c0=c0: e.activation(out=s["stats"][:, c0 + 4:c0 + 5],
                                              in_=s["stats"][:, c0 + 3:c0 + 4],
                                              func=AF.Sqrt, bias=s["biasc"][:, 0:1] if abs(eps - 1e-5) < 1e-12 else s["biasc"][:, 1:2]))
            p.op("vector", reads=("stats",), writes=("stats",))(
                lambda e, c0=c0: e.reciprocal(s["stats"][:, c0 + 5:c0 + 6],
                                              s["stats"][:, c0 + 4:c0 + 5]))
            p.op("vector", reads=(xr, "stats"), writes=("sq128",))(
                lambda e, sl=sl, c0=c0: e.tensor_scalar(
                    s["sqf"][:, 0:1024], xb[:, sl[0]:sl[1]],
                    s["stats"][:, c0 + 6:c0 + 7], None, ALU.subtract))
            p.op("scalar", reads=("sq128", "stats"), writes=(dstr,))(
                lambda e, sl=sl, c0=c0: e.activation(
                    out=dst[:, sl[0]:sl[1]], in_=s["sqf"][:, 0:1024],
                    func=AF.Copy, scale=s["stats"][:, c0 + 5:c0 + 6]))
    def scan_front(self, ch, sc):
        p, s, P, cfg = self.p, self.s, self.P, self.cfg
        T = cfg.T
        ts = sc * 64
        BLn = "BL%d" % (sc % 2)

        def blk(t):
            return [(t[64 * r:64 * r + 64, 64 * j:64 * j + 64], r, j)
                    for j in range(8) for r in range(2)]

        def fmsl(t, r, j, w=64, off=0):
            return t[64 * r:64 * r + 64, j * T + ts + off: j * T + ts + off + w]

        negL_rep = ap_custom(s["r_negL"][:], ts, [[T, 16], [0, 2], [1, 64]])
        m2s3 = ap_custom(s["mask2s_f32"][:], 0, [[128, 16], [64, 2], [1, 64]])
        lhsE3 = ap_custom(s["lhsE"][:], 0, [[128, 16], [64, 2], [1, 64]])
        p.op("vector", reads=("r_negL", "c_mask2s_f32"), writes=("lhsE",), tag="lhsE")(
            lambda e: e.tensor_tensor(lhsE3, negL_rep, m2s3, ALU.mult))
        beta_rep = ap_custom(s["r_beta"][:], ts, [[T, 16], [0, 2], [1, 64]])
        m2sb3 = ap_custom(s["mask2s_bf"][:], 0, [[128, 16], [64, 2], [1, 64]])
        lhsB3 = ap_custom(s["lhsB"][:], 0, [[128, 16], [64, 2], [1, 64]])
        p.op("vector", reads=("r_beta", "c_mask2s_bf"), writes=("lhsB",), tag="lhsB")(
            lambda e: e.tensor_tensor(lhsB3, beta_rep, m2sb3, ALU.mult))
        L_rep = ap_custom(s["r_L"][:], ts, [[T, 16], [0, 8], [1, 64]])
        hm83 = ap_custom(s["headmask8_f32"][:], 0, [[512, 16], [64, 8], [1, 64]])
        Lm3 = ap_custom(s["Lmaskd"][:], 0, [[512, 16], [64, 8], [1, 64]])
        p.op("vector", reads=("r_L", "c_headmask8_f32"), writes=("Lmaskd",), tag="Lmaskd")(
            lambda e: e.tensor_tensor(Lm3, L_rep, hm83, ALU.mult))
        def mmE(e):
            e.matmul(P[1][:, 0:512], s["mask2s_f32"][:], s["Lmaskd"][:],
                     start=True, stop=False)
            return e.matmul(P[1][:, 0:512], s["lhsE"][:], s["headmask8_f32"][:],
                            start=False, stop=True)
        p.op("tensor", reads=("Lmaskd", "lhsE", "c_mask2s_f32", "c_headmask8_f32"),
             writes=("P1",), tag="Emm")(mmE)
        p.op("vector", reads=("P1",), writes=("t1",))(
            lambda e: e.tensor_scalar(s["t1"][:], P[1][:, 0:512], 0.0, None, ALU.min))
        p.op("scalar", reads=("t1",), writes=("E",))(
            lambda e: e.activation(out=s["E"][:], in_=s["t1"][:], func=AF.Exp))
        def mmG(e):
            ins = None
            for (b, r, j) in blk(P[1]):
                ins = e.matmul(b, fmsl(s["Kn"], r, j), fmsl(s["Kn"], r, j),
                               start=True, stop=True)
            return ins
        p.op("tensor", reads=("Kn",), writes=("P1",), tag="Gmm")(mmG)
        p.op("vector", reads=("E", "P1"), writes=("t1",))(
            lambda e: e.tensor_mul(s["t1"][:], s["E"][:], P[1][:, 0:512]))
        def mmP(e):
            ins = None
            for (b, r, j) in blk(P[2]):
                ins = e.matmul(b, fmsl(s["Kn"], r, j), fmsl(s["Qn"], r, j),
                               start=True, stop=True)
            return ins
        p.op("tensor", reads=("Kn", "Qn"), writes=("P2",), tag="Pmm")(mmP)
        p.op("vector", reads=("E", "P2"), writes=("t2",))(
            lambda e: e.tensor_mul(s["t2"][:], s["E"][:], P[2][:, 0:512]))
        p.op("tensor", reads=("lhsB", "c_headmask8_bf"), writes=("P2",), tag="bpmm")(
            lambda e: e.matmul(P[2][:, 0:512], s["lhsB"][:], s["headmask8_bf"][:],
                               start=True, stop=True))
        p.op("vector", reads=("P2",), writes=("bpartU",))(
            lambda e: e.tensor_copy(s["bpartU"][:], P[2][:, 0:512]))
        p.op("vector", reads=("bpartU", "c_negstriu"), writes=("bmask_t",))(
            lambda e: e.tensor_mul(s["bmask_t"][:], s["bpartU"][:], s["negstriu"][:]))
        p.op("vector", reads=("bpartU", "c_triuincl"), writes=("bmask_B",))(
            lambda e: e.tensor_mul(s["bmask_B"][:], s["bpartU"][:], s["triuincl"][:]))
        gcn = "gcall%d" % (sc % 2)
        daln = "dall%d" % (sc % 2)
        eL_last = ap_custom(s["r_eL"][:], ts + 63, [[T, 16], [0, 8]])
        p.op("vector", reads=("r_eL", "c_headmaskj"), writes=("gcrhs",))(
            lambda e: e.tensor_tensor(s["gcrhs"][:], eL_last, s["headmaskj"][:], ALU.mult))
        p.op("tensor", reads=("gcrhs", "c_mask2s_f32"), writes=("P2",), tag="gcmm")(
            lambda e: e.matmul(P[2][:, 0:8], s["mask2s_f32"][:], s["gcrhs"][:],
                               start=True, stop=True))
        p.op("scalar", reads=("P2",), writes=(gcn,))(
            lambda e, gcn=gcn: e.activation(out=s[gcn][:], in_=P[2][:, 0:8], func=AF.Copy))
        E_last = ap_custom(s["E"][:], 63, [[512, 128], [64, 8]])
        bp_first = ap_custom(s["bpartU"][:], 0, [[512, 128], [64, 8]])
        p.op("vector", reads=("E", "bpartU"), writes=(daln,))(
            lambda e, daln=daln: e.tensor_tensor(s[daln][:], E_last, bp_first, ALU.mult))
        p.op("vector", reads=("t1", "bmask_t"), writes=("NL0a", "NL0b"))(
            lambda e: e.tensor_mul(s["NL0"][:], s["t1"][:], s["bmask_t"][:]))
        p.op("vector", reads=("t2", "bmask_B"), writes=(BLn,))(
            lambda e, BLn=BLn: e.tensor_mul(s[BLn][:], s["t2"][:], s["bmask_B"][:]))
        def mmNN(e):
            ins = None
            for (b, r, j) in blk(P[1]):
                ins = e.matmul(b, s["NL0"][64 * r:64 * r + 64, 64 * j:64 * j + 64],
                               s["ident64"][64 * r:64 * r + 64, :], start=True, stop=True)
            return ins
        p.op("tensor", reads=("NL0a", "NL0b", "c_ident64"), writes=("P1",), tag="NNmm")(mmNN)
        p.op("vector", reads=("P1",), writes=("NN0a", "NN0b"))(
            lambda e: e.tensor_copy(s["NN0"][:], P[1][:, 0:512]))

    def scan_tail_a(self, ch, sc):
        p, s, P, cfg = self.p, self.s, self.P, self.cfg
        T = cfg.T
        ts = sc * 64
        def blk(t):
            return [(t[64 * r:64 * r + 64, 64 * j:64 * j + 64], r, j)
                    for j in range(8) for r in range(2)]

        def fmsl(t, r, j, w=64, off=0):
            return t[64 * r:64 * r + 64, j * T + ts + off: j * T + ts + off + w]
        def mmR0(e):
            ins = None
            for r in range(2):
                for j in range(8):
                    b = P[5][64 * r:64 * r + 64, 64 * j:64 * j + 64]
                    e.matmul(b, fmsl(s["Vfm"], r, j), s["ident64"][64 * r:64 * r + 64, :],
                             start=(j == 0), stop=False)
                    ins = e.matmul(b, fmsl(s["Khm"], r, j),
                                   s["Sall"][64 * r:64 * r + 64, 64 * j:64 * j + 64],
                                   start=False, stop=True)
            return ins
        p.op("tensor", reads=("Vfm", "Khm", "Sall", "c_ident64"), writes=("P5",),
             tag="R0mm")(mmR0)
        p.op("scalar", reads=("P5",), writes=("R",))(
            lambda e: e.activation(out=s["R"][:], in_=P[5][:, 0:512], func=AF.Copy))
        self._levels(ch, sc, 0, 2)

    def scan_tail_b(self, ch, sc):
        p, s, P, cfg = self.p, self.s, self.P, self.cfg
        T = cfg.T
        ts = sc * 64
        def blk(t):
            return [(t[64 * r:64 * r + 64, 64 * j:64 * j + 64], r, j)
                    for j in range(8) for r in range(2)]

        def fmsl(t, r, j, w=64, off=0):
            return t[64 * r:64 * r + 64, j * T + ts + off: j * T + ts + off + w]
        BLn = "BL%d" % (sc % 2)
        self._levels(ch, sc, 2, 6)
        def mmRead(e):
            ins = None
            for r in range(2):
                for j in range(8):
                    b = P[6][64 * r:64 * r + 64, 64 * j:64 * j + 64]
                    e.matmul(b, fmsl(s["Qh"], r, j),
                             s["Sall"][64 * r:64 * r + 64, 64 * j:64 * j + 64],
                             start=(j == 0), stop=False)
                    ins = e.matmul(b, s[BLn][64 * r:64 * r + 64, 64 * j:64 * j + 64],
                                   s["R"][64 * r:64 * r + 64, 64 * j:64 * j + 64],
                                   start=False, stop=True)
            return ins
        p.op("tensor", reads=("Qh", "Sall", BLn, "R"), writes=("P6",), tag="readmm")(mmRead)
        p.op("vector", reads=("P6",), writes=("readstm",))(
            lambda e: e.tensor_copy(s["readstm"][:], P[6][:, 0:512]))
        def mmKtm(e):
            ins = None
            for (b, r, j) in blk(P[4]):
                ins = e.matmul(b, fmsl(s["Kn"], r, j), s["ident64"][64 * r:64 * r + 64, :],
                               start=True, stop=True)
            return ins
        p.op("tensor", reads=("Kn", "c_ident64"), writes=("P4",), tag="Ktmmm")(mmKtm)
        daln = "dall%d" % (sc % 2)
        P4_3 = ap_custom(self.P[4][:], 0, [[512, 128], [64, 8], [1, 64]])
        dall_b = ap_custom(s[daln][:], 0, [[8, 128], [1, 8], [0, 64]])
        Ktmp3 = ap_custom(s["Ktmp"][:], 0, [[512, 128], [64, 8], [1, 64]])
        p.op("vector", reads=("P4", daln), writes=("Ktmp",))(
            lambda e: e.tensor_tensor(Ktmp3, P4_3, dall_b, ALU.mult))
        def mmState(e):
            ins = None
            for (b, r, j) in blk(P[6]):
                ins = e.matmul(b, s["Ktmp"][64 * r:64 * r + 64, 64 * j:64 * j + 64],
                               s["R"][64 * r:64 * r + 64, 64 * j:64 * j + 64],
                               start=True, stop=True)
            return ins
        p.op("tensor", reads=("Ktmp", "R"), writes=("P6",), tag="statemm")(mmState)
        gcn = "gcall%d" % (sc % 2)
        gc_b = ap_custom(s[gcn][:], 0, [[8, 128], [1, 8], [0, 64]])
        Sall3 = ap_custom(s["Sall"][:], 0, [[512, 128], [64, 8], [1, 64]])
        t23 = ap_custom(s["t2"][:], 0, [[512, 128], [64, 8], [1, 64]])
        p.op("vector", reads=("Sall", gcn), writes=("t2",))(
            lambda e: e.tensor_tensor(t23, Sall3, gc_b, ALU.mult))
        p.op("vector", reads=("P6", "t2"), writes=("Sall",))(
            lambda e: e.tensor_add(s["Sall"][:], P[6][:, 0:512], s["t2"][:]))
        def mmRFM(e):
            ins = None
            for (b, r, j) in blk(P[4]):
                ins = e.matmul(b, s["readstm"][64 * r:64 * r + 64, 64 * j:64 * j + 64],
                               s["ident64"][64 * r:64 * r + 64, :], start=True, stop=True)
            return ins
        p.op("tensor", reads=("readstm", "c_ident64"), writes=("P4",), tag="rfmm")(mmRFM)
        P4_3b = ap_custom(self.P[4][:], 0, [[512, 128], [64, 8], [1, 64]])
        rfm3 = ap_custom(s["readsFM"][:], ts, [[8 * T, 128], [T, 8], [1, 64]])
        p.op("vector", reads=("P4",), writes=("readsFM",))(
            lambda e: e.tensor_copy(rfm3, P4_3b))

    def _levels(self, ch, sc, lev0, lev1):
        p, s, P = self.p, self.s, self.P

        def blk(t):
            return [(t[64 * r:64 * r + 64, 64 * j:64 * j + 64], r, j)
                    for j in range(8) for r in range(2)]
        for lev in range(lev0, lev1):
            nl = "NL0" if lev == 0 else ("NLa" if lev % 2 == 1 else "NLb")
            nn = "NN0" if lev == 0 else ("NNa" if lev % 2 == 1 else "NNb")
            nl2 = "NLa" if lev % 2 == 0 else "NLb"
            nn2 = "NNa" if lev % 2 == 0 else "NNb"
            def mmAp(e, nl=nl):
                ins = None
                for (b, r, j) in blk(P[5]):
                    ins = e.matmul(b, s[nl][64 * r:64 * r + 64, 64 * j:64 * j + 64],
                                   s["R"][64 * r:64 * r + 64, 64 * j:64 * j + 64],
                                   start=False, stop=True)
                return ins
            p.op("tensor", reads=(nl, "R"), writes=("P5",), tag="apply%d" % lev)(mmAp)
            if lev < 5:
                def mmSqN(e, nl=nl, nn=nn):
                    ins = None
                    for (b, r, j) in blk(P[4]):
                        ins = e.matmul(b, s[nl][64 * r:64 * r + 64, 64 * j:64 * j + 64],
                                       s[nn][64 * r:64 * r + 64, 64 * j:64 * j + 64],
                                       start=True, stop=True)
                    return ins
                p.op("tensor", reads=(nl, nn), writes=("P4",), tag="sqN%d" % lev)(mmSqN)
                def mmSqL(e, nl=nl, nn=nn):
                    ins = None
                    for (b, r, j) in blk(P[7]):
                        ins = e.matmul(b, s[nn][64 * r:64 * r + 64, 64 * j:64 * j + 64],
                                       s[nl][64 * r:64 * r + 64, 64 * j:64 * j + 64],
                                       start=True, stop=True)
                    return ins
                p.op("tensor", reads=(nl, nn), writes=("P7",), tag="sqL%d" % lev)(mmSqL)
                p.op("vector", reads=("P4",), writes=(nn2,))(
                    lambda e, nn2=nn2: e.tensor_copy(s[nn2][:], P[4][:, 0:512]))
                p.op("scalar", reads=("P7",), writes=(nl2,))(
                    lambda e, nl2=nl2: e.activation(out=s[nl2][:], in_=P[7][:, 0:512],
                                                    func=AF.Copy))
            eng = "vector" if lev % 2 == 0 else "scalar"
            if eng == "vector":
                p.op("vector", reads=("P5",), writes=("R",))(
                    lambda e: e.tensor_copy(s["R"][:], P[5][:, 0:512]))
            else:
                p.op("scalar", reads=("P5",), writes=("R",))(
                    lambda e: e.activation(out=s["R"][:], in_=P[5][:, 0:512], func=AF.Copy))

    def epilogue(self, ch):
        p, s, P, cfg = self.p, self.s, self.P, self.cfg
        T, NTT = cfg.T, cfg.NTT
        xb = s["xbuf%d" % (ch % 2)]
        xr = "xbuf%d" % (ch % 2)
        for fc in range(8):
            p.op("scalar", reads=("readsFM",), writes=("sqbuf",))(
                lambda e, fc=fc: e.activation(out=s["sqbuf"][:, 0:T],
                                              in_=s["readsFM"][:, fc * T:(fc + 1) * T],
                                              func=AF.Square))
            def mmf(e, fc=fc):
                return e.matmul(P[7][0:16, 0:T], s["headmaskC"][:, fc * 16:(fc + 1) * 16],
                                s["sqbuf"][:, 0:T], start=(fc == 0), stop=(fc == 7))
            p.op("tensor", reads=("sqbuf", "c_headmaskC"), writes=("P7",), tag="rmsmm")(mmf)
        p.op("scalar", reads=("P7",), writes=("rmsrow",))(
            lambda e: e.activation(out=s["rmsrow"][:, 0:T], in_=P[7][0:16, 0:T],
                                   func=AF.Sqrt, scale=1.0 / 64, bias=s["biasc"][0:16, 1:2]))
        p.op("vector", reads=("rmsrow",), writes=("r_t",))(
            lambda e: e.reciprocal(s["r_t"][:, 0:T], s["rmsrow"][:, 0:T]))
        p.op("vector", reads=("r_t",), writes=("rmsbf",))(
            lambda e: e.tensor_copy(s["rmsbf"][:, 0:T], s["r_t"][:, 0:T]))
        for fc in range(8):
            p.op("tensor", reads=("rmsbf", "c_mask2w"), writes=("P0",), tag="nbc")(
                lambda e, fc=fc: e.matmul(P[0][:, 0:T], s["mask2w"][:, fc * 128:(fc + 1) * 128],
                                          s["rmsbf"][:, 0:T], start=True, stop=True))
            p.op("vector", reads=("P0", "readsFM"), writes=("normedFM",))(
                lambda e, fc=fc: e.tensor_mul(s["normedFM"][:, fc * T:(fc + 1) * T],
                                              s["readsFM"][:, fc * T:(fc + 1) * T],
                                              P[0][:, 0:T]))
        for kg in range(4):
            ob = "wbuf_out" + ("A" if kg % 2 == 0 else "B")
            p.op("sync", reads=("scr_outT",), writes=(ob,), dma=True, tag="wout")(
                lambda e, kg=kg, ob=ob: e.dma_start(
                    out=s[ob][:],
                    in_=ap_custom(self.scr["outT"], kg * 256 * D,
                                  [[D, 128], [128 * D, 2], [1, D]])))
            def mmf(e, kg=kg, ob=ob):
                ins = None
                for kk in range(2):
                    k = kg * 2 + kk
                    for tt in range(NTT):
                        for hf in range(2):
                            ins = e.matmul(P[3 + tt * 2 + hf][:, 0:512],
                                           s["normedFM"][:, k * T + tt * 128: k * T + (tt + 1) * 128],
                                           s[ob][:, kk * 1024 + hf * 512: kk * 1024 + (hf + 1) * 512],
                                           start=(k == 0), stop=False)
                return ins
            p.op("tensor", reads=(ob, "normedFM"),
                 writes=("P3", "P4", "P5", "P6"), tag="outmm")(mmf)
        def mmb(e):
            ins = None
            for tt in range(NTT):
                for hf in range(2):
                    ins = e.matmul(P[3 + tt * 2 + hf][:, 0:512],
                                   s["ones128row_bf"][0:1, :],
                                   s["out_b_bf"][0:1, hf * 512:(hf + 1) * 512],
                                   start=False, stop=True)
            return ins
        p.op("tensor", reads=("c_out_b", "c_ones128row_bf"),
             writes=("P3", "P4", "P5", "P6"), tag="outbias")(mmb)
        for tt in range(NTT):
            for hf in range(2):
                pi = 3 + tt * 2 + hf
                sl = (tt * 1024 + hf * 512, tt * 1024 + (hf + 1) * 512)
                p.op("vector", reads=("P%d" % pi, "outgate"), writes=("t1",))(
                    lambda e, pi=pi, sl=sl: e.tensor_mul(
                        s["t1"][:], P[pi][:, 0:512], s["outgate"][:, sl[0]:sl[1]]))
                p.op("vector", reads=("t1", xr), writes=(xr,))(
                    lambda e, sl=sl: e.tensor_add(xb[:, sl[0]:sl[1]],
                                                  xb[:, sl[0]:sl[1]], s["t1"][:, 0:512]))

    def ffn(self, ch):
        p, s, P, cfg = self.p, self.s, self.P, self.cfg
        T, NTT = cfg.T, cfg.NTT
        xb = s["xbuf%d" % (ch % 2)]
        xr = "xbuf%d" % (ch % 2)
        self._ln(xb, xr, s["h2"], "h2", statcol=16)
        for tt in range(NTT):
            sl = (tt * 1024, (tt + 1) * 1024)
            c0 = tt * 4
            p.op("vector", reads=("h2",), writes=("fstat",))(
                lambda e, sl=sl, c0=c0: e.tensor_reduce(
                    s["fstat"][:, c0:c0 + 1], s["h2"][:, sl[0]:sl[1]], AX.X, ALU.max,
                    apply_absolute_value=True))
            p.op("gpsimd", reads=("fstat",), writes=("fstat",))(
                lambda e, c0=c0: e.tensor_scalar(s["fstat"][:, c0 + 1:c0 + 2],
                                                 s["fstat"][:, c0:c0 + 1],
                                                 1e-5, None, ALU.max))
            p.op("vector", reads=("fstat",), writes=("fstat",))(
                lambda e, c0=c0: e.reciprocal(s["fstat"][:, c0 + 2:c0 + 3],
                                              s["fstat"][:, c0 + 1:c0 + 2]))
            p.op("scalar", reads=("fstat",), writes=("fstat",))(
                lambda e, c0=c0: e.activation(out=s["fstat"][:, c0 + 3:c0 + 4],
                                              in_=s["fstat"][:, c0 + 2:c0 + 3],
                                              func=AF.Copy, scale=127.0))
            p.op("vector", reads=("fstat", "wsc"), writes=("fstat",))(
                lambda e, c0=c0, tt=tt: e.scalar_tensor_tensor(
                    s["fstat"][:, 16 + tt:17 + tt], s["fstat"][:, c0 + 1:c0 + 2],
                    1.0 / 127, s["wsc"][:, 0:1], ALU.mult, ALU.mult))
            p.op("vector", reads=("fstat", "wsc"), writes=("fstat",))(
                lambda e, c0=c0, tt=tt: e.scalar_tensor_tensor(
                    s["fstat"][:, 18 + tt:19 + tt], s["fstat"][:, c0 + 1:c0 + 2],
                    1.0 / 127, s["wsc"][:, 1:2], ALU.mult, ALU.mult))
            p.op("vector", reads=("h2", "fstat"), writes=("h2",))(
                lambda e, sl=sl, c0=c0: e.tensor_scalar(
                    s["h2"][:, sl[0]:sl[1]], s["h2"][:, sl[0]:sl[1]],
                    s["fstat"][:, c0 + 3:c0 + 4], 127.0, ALU.mult, ALU.min))
            p.op("gpsimd", reads=("h2",), writes=("h2",))(
                lambda e, sl=sl: e.tensor_scalar(s["h2"][:, sl[0]:sl[1]],
                                                 s["h2"][:, sl[0]:sl[1]],
                                                 -128.0, None, ALU.max))
            p.op("scalar", reads=("h2",), writes=("istage",))(
                lambda e, sl=sl: e.activation(out=s["istage"][:, 0:1024],
                                              in_=s["h2"][:, sl[0]:sl[1]], func=AF.Copy))
            p.op("gpsimd", reads=("istage",), writes=("h2q",))(
                lambda e, sl=sl: e.tensor_copy(s["h2q"][:, sl[0]:sl[1]],
                                               s["istage"][:, 0:1024]))
        for tt in range(NTT):
            def mmf(e, tt=tt):
                ins = None
                for g in range(2):
                    for b in range(4):
                        fc = g * 4 + b
                        ins = e.matmul(P[g][:, b * 128:(b + 1) * 128],
                                       s["h2q"][:, tt * 1024 + fc * 128: tt * 1024 + (fc + 1) * 128],
                                       s["ident128"][:], start=True, stop=True)
                return ins
            p.op("tensor", reads=("h2q", "c_ident128"), writes=("P0", "P1"), tag="h2qT")(mmf)
            for g in range(2):
                out_ap = ap_custom(s["h2qFM"][:], tt * 128 + g * 4 * T,
                                   [[8 * T, 128], [T, 4], [1, 128]])
                if g == 0:
                    p.op("vector", reads=("P0",), writes=("h2qFM",))(
                        lambda e, o=out_ap: e.tensor_copy(o, P[0][:, 0:512]))
                else:
                    p.op("scalar", reads=("P1",), writes=("h2qFM",))(
                        lambda e, o=out_ap: e.activation(out=o, in_=P[1][:, 0:512],
                                                         func=AF.Copy))
        if ch == getattr(self, "tap_chunk", 0):
            self.tap("h2q", lambda: s["h2q"][:], ["h2q"])
            self.tap("h2qFM", lambda: s["h2qFM"][:], ["h2qFM"])
            self.tap("fstat", lambda: s["fstat"][:], ["fstat"])
        for ic in range(8):
            for nm, b0, b1 in [("wgT", "wbuf_g0", "wbuf_g1"), ("wuT", "wbuf_u0", "wbuf_u1")]:
                for hh, bh in ((0, b0), (1, b1)):
                    p.op("sync", reads=("scr_" + nm,), writes=(bh,), dma=True,
                         tag="wld_" + nm)(
                        lambda e, ic=ic, nm=nm, bh=bh, hh=hh: e.dma_start(
                            out=s[bh][:],
                            in_=ap_custom(self.scr[nm], hh * 4 * 128 * INNER + ic * 512,
                                          [[INNER, 128], [128 * INNER, 4], [1, 512]])))
            pg = 0 if ic % 2 == 0 else 2
            pu = 1 if ic % 2 == 0 else 3
            for tt in range(NTT):
                for hh, bh in ((0, "wbuf_g0"), (1, "wbuf_g1")):
                    def mmg(e, tt=tt, hh=hh, bh=bh, pg=pg):
                        ins = None
                        for kk in range(4):
                            k = hh * 4 + kk
                            ins = e.matmul(P[pg][:, 0:512],
                                           s["h2qFM"][:, k * T + tt * 128: k * T + (tt + 1) * 128],
                                           s[bh][:, kk * 512:(kk + 1) * 512],
                                           start=(k == 0), stop=(k == 7))
                        return ins
                    p.op("tensor", reads=(bh, "h2qFM"), writes=("P%d" % pg,), tag="gmm")(mmg)
                for hh, bh in ((0, "wbuf_u0"), (1, "wbuf_u1")):
                    def mmu(e, tt=tt, hh=hh, bh=bh, pu=pu):
                        ins = None
                        for kk in range(4):
                            k = hh * 4 + kk
                            ins = e.matmul(P[pu][:, 0:512],
                                           s["h2qFM"][:, k * T + tt * 128: k * T + (tt + 1) * 128],
                                           s[bh][:, kk * 512:(kk + 1) * 512],
                                           start=(k == 0), stop=(k == 7))
                        return ins
                    p.op("tensor", reads=(bh, "h2qFM"), writes=("P%d" % pu,), tag="umm")(mmu)
                p.op("scalar", reads=("P%d" % pg, "fstat"), writes=("t1",))(
                    lambda e, tt=tt, pg=pg: e.activation(out=s["t1"][:], in_=P[pg][:, 0:512],
                                                  func=AF.Silu,
                                                  scale=s["fstat"][:, 16 + tt:17 + tt]))
                p.op("vector", reads=("t1", "P%d" % pu, "fstat"), writes=("y",))(
                    lambda e, ic=ic, tt=tt, pu=pu: e.scalar_tensor_tensor(
                        s["y"][:, tt * 4096 + ic * 512: tt * 4096 + (ic + 1) * 512],
                        P[pu][:, 0:512], s["fstat"][:, 18 + tt:19 + tt],
                        s["t1"][:, 0:512], ALU.mult, ALU.mult))
                p.op("vector", reads=("y",), writes=("ymax",))(
                    lambda e, ic=ic, tt=tt: e.tensor_reduce(
                        s["fstat"][:, 22 + tt:23 + tt],
                        s["y"][:, tt * 4096 + ic * 512: tt * 4096 + (ic + 1) * 512],
                        AX.X, ALU.max, apply_absolute_value=True))
                if ic > 0:
                    p.op("gpsimd", reads=("ymax", "fstat"), writes=("fstat",))(
                        lambda e, tt=tt, c0=8 + tt * 4: e.tensor_scalar(
                            s["fstat"][:, c0:c0 + 1], s["fstat"][:, 22 + tt:23 + tt],
                            s["fstat"][:, c0:c0 + 1], None, ALU.max))
                else:
                    p.op("gpsimd", reads=("ymax",), writes=("fstat",))(
                        lambda e, tt=tt, c0=8 + tt * 4: e.tensor_scalar(
                            s["fstat"][:, c0:c0 + 1], s["fstat"][:, 22 + tt:23 + tt],
                            0.0, None, ALU.max))
        if ch == getattr(self, "tap_chunk", 0):
            self.tap("ytap", lambda: s["y"][:, 0:4096], ["y"])
        for tt in range(NTT):
            sl = (tt * 4096, (tt + 1) * 4096)
            c0 = 8 + tt * 4
            p.op("gpsimd", reads=("fstat",), writes=("fstat",))(
                lambda e, c0=c0: e.tensor_scalar(s["fstat"][:, c0 + 1:c0 + 2],
                                                 s["fstat"][:, c0:c0 + 1],
                                                 1e-5, None, ALU.max))
            p.op("vector", reads=("fstat",), writes=("fstat",))(
                lambda e, c0=c0: e.reciprocal(s["fstat"][:, c0 + 2:c0 + 3],
                                              s["fstat"][:, c0 + 1:c0 + 2]))
            p.op("scalar", reads=("fstat",), writes=("fstat",))(
                lambda e, c0=c0: e.activation(out=s["fstat"][:, c0 + 3:c0 + 4],
                                              in_=s["fstat"][:, c0 + 2:c0 + 3],
                                              func=AF.Copy, scale=127.0))
            p.op("vector", reads=("fstat", "wsc"), writes=("fstat",))(
                lambda e, c0=c0, tt=tt: e.scalar_tensor_tensor(
                    s["fstat"][:, 20 + tt:21 + tt], s["fstat"][:, c0 + 1:c0 + 2],
                    1.0 / 127, s["wsc"][:, 2:3], ALU.mult, ALU.mult))
            p.op("vector", reads=("y", "fstat"), writes=("y",))(
                lambda e, sl=sl, c0=c0: e.tensor_scalar(
                    s["y"][:, sl[0]:sl[1]], s["y"][:, sl[0]:sl[1]],
                    s["fstat"][:, c0 + 3:c0 + 4], 127.0, ALU.mult, ALU.min))
            p.op("gpsimd", reads=("y",), writes=("y",))(
                lambda e, sl=sl: e.tensor_scalar(s["y"][:, sl[0]:sl[1]],
                                                 s["y"][:, sl[0]:sl[1]],
                                                 -128.0, None, ALU.max))
            for hh in range(2):
                p.op("scalar", reads=("y",), writes=("istage",))(
                    lambda e, sl=sl, hh=hh: e.activation(
                        out=s["istage"][:, 0:2048],
                        in_=s["y"][:, sl[0] + hh * 2048: sl[0] + (hh + 1) * 2048],
                        func=AF.Copy))
                p.op("gpsimd", reads=("istage",), writes=("yq",))(
                    lambda e, sl=sl, hh=hh: e.tensor_copy(
                        s["yq"][:, sl[0] + hh * 2048: sl[0] + (hh + 1) * 2048],
                        s["istage"][:, 0:2048]))
        if ch == getattr(self, "tap_chunk", 0):
            self.tap("yq", lambda: s["yq"][:], ["yq"])
        for tt in range(NTT):
            for icg in range(8):
                def mmf(e, tt=tt, icg=icg):
                    ins = None
                    for b in range(4):
                        ic = icg * 4 + b
                        ins = e.matmul(P[2][:, b * 128:(b + 1) * 128],
                                       s["yq"][:, tt * 4096 + ic * 128: tt * 4096 + (ic + 1) * 128],
                                       s["ident128"][:], start=True, stop=True)
                    return ins
                p.op("tensor", reads=("yq", "c_ident128"), writes=("P2",), tag="yqT")(mmf)
                out_ap = ap_custom(s["yqFM"][:], (icg * 4) * T + tt * 128,
                                   [[32 * T, 128], [T, 4], [1, 128]])
                if icg % 2 == 0:
                    p.op("vector", reads=("P2",), writes=("yqFM",))(
                        lambda e, o=out_ap: e.tensor_copy(o, P[2][:, 0:512]))
                else:
                    p.op("scalar", reads=("P2",), writes=("yqFM",))(
                        lambda e, o=out_ap: e.activation(out=o, in_=P[2][:, 0:512],
                                                         func=AF.Copy))
        for icg in range(16):
            db = "wbuf_d" + ("A" if icg % 2 == 0 else "B")
            p.op("sync", reads=("scr_wdT",), writes=(db,), dma=True, tag="wd")(
                lambda e, icg=icg, db=db: e.dma_start(
                    out=s[db][:],
                    in_=ap_custom(self.scr["wdT"], icg * 256 * D,
                                  [[D, 128], [128 * D, 2], [1, D]])))
            def mmf(e, icg=icg, db=db):
                ins = None
                for kk in range(2):
                    ic = icg * 2 + kk
                    for tt in range(NTT):
                        for hf in range(2):
                            ins = e.matmul(P[3 + tt * 2 + hf][:, 0:512],
                                           s["yqFM"][:, ic * T + tt * 128: ic * T + (tt + 1) * 128],
                                           s[db][:, kk * 1024 + hf * 512: kk * 1024 + (hf + 1) * 512],
                                           start=(ic == 0), stop=(ic == 31))
                return ins
            p.op("tensor", reads=(db, "yqFM"),
                 writes=("P3", "P4", "P5", "P6"), tag="dmm")(mmf)
        for tt in range(NTT):
            for hf in range(2):
                pi = 3 + tt * 2 + hf
                sl = (tt * 1024 + hf * 512, tt * 1024 + (hf + 1) * 512)
                p.op("vector", reads=("P%d" % pi, xr, "fstat"), writes=(xr,))(
                    lambda e, pi=pi, sl=sl, tt=tt: e.scalar_tensor_tensor(
                        xb[:, sl[0]:sl[1]], P[pi][:, 0:512],
                        s["fstat"][:, 20 + tt:21 + tt],
                        xb[:, sl[0]:sl[1]], ALU.mult, ALU.add))
        in_ap = ap_custom(xb[:], 0, [[2 * 1024, 128], [1024, NTT], [1, 1024]])
        out_ap = ap_custom(self.dram_out, ch * self.cfg.T * D,
                           [[D, 128], [128 * D, NTT], [1, D]])
        p.op("sync", reads=(xr,), writes=("xout",), dma=True, tag="xst")(
            lambda e, i=in_ap, o=out_ap: e.dma_start(out=o, in_=i))

    def build(self, with_prep=True, nch=None):
        p, s = self.p, self.s
        self.load_consts()
        if with_prep:
            self.prep_weights_attn()
        p.op("vector", writes=("Salla", "Sallb"))(lambda e: e.memset(s["Sall"][:], 0.0))
        nch = self.cfg.NCH if nch is None else nch
        for ch in range(nch):
            self.p.phase = "front"
            self.frontend(ch)
            self.run_taps_frontend(ch)
            self.p.phase = "scanF"
            self.scan_front(ch, 0)
            for sc in range(self.cfg.NSC):
                self.p.phase = "scanTA"
                self.scan_tail_a(ch, sc)
                if sc + 1 < self.cfg.NSC:
                    self.p.phase = "scanF"
                    self.scan_front(ch, sc + 1)
                self.p.phase = "scanTB"
                self.scan_tail_b(ch, sc)
            self.run_taps_scan(ch)
            if ch == 0 and with_prep:
                self.p.phase = "prepF"
                self.prep_weights_ffn()
            self.p.phase = "epi"
            self.epilogue(ch)
            self.run_taps_epi(ch)
            self.p.phase = "ffn"
            self.ffn(ch)
        p.op("sync", reads=("xout",), tag="fence")(lambda e: e.nop())
        with self.nc.Block() as block:
            p.finalize(self.nc, block)

    def run_taps_frontend(self, ch):
        if ch != getattr(self, "tap_chunk", 0):
            return
        s, cfg = self.s, self.cfg
        for scrn in ["scr_ctrlT", "scr_qkvT", "scr_wgT"]:
            if scrn in self.tap_outs:
                nm = scrn[4:] + "T" if False else scrn.replace("scr_", "")
                self.p.op("gpsimd", reads=(scrn,), dma=True, tag="tap" + scrn)(
                    lambda e, o=self.tap_outs[scrn], i=self.scr[nm]: e.dma_start(out=o, in_=i))

        self.tap("qkvFM", lambda: s["qkvFM"][:], ["qkvFM"])
        self.tap("convq", lambda: s["convq"][:], ["convq"])
        self.tap("Vfm", lambda: s["Vfm"][:], ["Vfm"])
        self.tap("Qn", lambda: s["Qn"][:], ["Qn"])
        self.tap("Kn", lambda: s["Kn"][:], ["Kn"])
        self.tap("Qh", lambda: s["Qh"][:], ["Qh"])
        self.tap("Khm", lambda: s["Khm"][:], ["Khm"])
        self.tap("ab", lambda: s["ab"][:], ["ab"])
        self.tap("r_L", lambda: s["r_L"][:], ["r_L"])
        self.tap("r_beta", lambda: s["r_beta"][:], ["r_beta"])
        self.tap("r_eL", lambda: s["r_eL"][:], ["r_eL"])
        self.tap("outgate", lambda: s["outgate"][:], ["outgate"])

    def run_taps_scan(self, ch):
        if ch != getattr(self, "tap_chunk", 0):
            return
        s = self.s
        self.tap("readsFM", lambda: s["readsFM"][:], ["readsFM"])
        self.tap("Sall", lambda: s["Sall"][:], ["Sall"])

    def run_taps_epi(self, ch):
        if ch != getattr(self, "tap_chunk", 0):
            return
        s = self.s
        self.tap("normedFM", lambda: s["normedFM"][:], ["normedFM"])
        self.tap("x1", lambda: s["xbuf%d" % (ch % 2)][:], ["xbuf%d" % (ch % 2)])


def kernel(L0_attn_hidden, L0_attn_qkv_w, L0_attn_ctrl_w, L0_attn_ctrl_b,
           L0_attn_convq_w, L0_attn_convk_w, L0_attn_convv_w, L0_attn_norm_w,
           L0_attn_out_w, L0_attn_out_b, L0_ffn_w_gate, L0_ffn_w_up,
           L0_ffn_w_down):
    B, S, Dm = L0_attn_hidden.shape
    assert (B, S, Dm) == (8, 4096, 1024)
    cfg = Cfg(S=S, T=256)
    nc = bass.Bass("TRN2", target_bir_lowering=False)
    b = Builder(nc, cfg, taps=[])
    b.build()
    norm_w_t = np.ascontiguousarray(np.tile(np.asarray(L0_attn_norm_w, np.float32), (16, 16)))
    shared = {
        "qkv_w": np.ascontiguousarray(L0_attn_qkv_w, dtype=np.float32),
        "ctrl_w": np.ascontiguousarray(L0_attn_ctrl_w, dtype=np.float32),
        "ctrl_b": np.ascontiguousarray(L0_attn_ctrl_b, dtype=np.float32),
        "convq_w": np.ascontiguousarray(L0_attn_convq_w, dtype=np.float32),
        "convk_w": np.ascontiguousarray(L0_attn_convk_w, dtype=np.float32),
        "convv_w": np.ascontiguousarray(L0_attn_convv_w, dtype=np.float32),
        "norm_w_t": norm_w_t,
        "out_w": np.ascontiguousarray(L0_attn_out_w, dtype=np.float32),
        "out_b": np.ascontiguousarray(L0_attn_out_b, dtype=np.float32),
        "w_gate": np.ascontiguousarray(L0_ffn_w_gate, dtype=np.float32),
        "w_up": np.ascontiguousarray(L0_ffn_w_up, dtype=np.float32),
        "w_down": np.ascontiguousarray(L0_ffn_w_down, dtype=np.float32),
    }
    xin = np.ascontiguousarray(L0_attn_hidden, dtype=np.float32)
    in_maps = [dict(shared, x=xin[i]) for i in range(8)]
    res = run_bass_kernel_spmd(nc, in_maps, core_ids=list(range(8)))
    out = np.stack([np.asarray(res.results[i]["out"], np.float32) for i in range(8)], axis=0)
    return out


# version 10
# speedup vs baseline: 1.2032x; 1.2032x over previous
import numpy as np
import ml_dtypes
import concourse.bass as bass
import concourse.mybir as mybir
from concourse.bass_utils import run_bass_kernel_spmd

BF = ml_dtypes.bfloat16

"""Support code for the mk_seq_preln_d1 deltanet megakernel (TRN2, raw bass).

Layout conventions:
- TM (token-major): [128 tokens on partitions x features on free]
- FM (feature-major): [128 features-within-chunk on partitions x (fc, t) on free]
  global feature index d = fc*128 + p ; head h = d // 64 = 2*fc + p//64
- scan all-heads tiles: [128 x 512] fp32/bf16 ; head h=2j+r at partition block
  r=h%2 (rows 64r..64r+63), free block j=h//2 (cols 64j..64j+63)
- weights in HBM scratch, bf16, transposed to [in_feat x out_feat]
"""
import numpy as np
import ml_dtypes
import concourse.bass as bass
import concourse.mybir as mybir

F32 = mybir.dt.float32
BF16 = mybir.dt.bfloat16
I32 = mybir.dt.int32
AF = mybir.ActivationFunctionType
ALU = mybir.AluOpType
AX = mybir.AxisListType

D = 1024
H = 16
HD = 64
INNER = 4096
CTRL = 2 * H + D
QKV = 3 * D
C = 64


class Cfg:
    def __init__(self, S=4096, T=256):
        assert S % T == 0 and T % 128 == 0
        self.S = S
        self.T = T
        self.NCH = S // T
        self.NTT = T // 128
        self.NSC = T // C


ENGINES = ("sync", "tensor", "vector", "scalar", "gpsimd")


class Op:
    __slots__ = ("idx", "eng", "fn", "reads", "writes", "dma", "tag",
                 "need_inc", "tick", "waits", "_deps", "dkey", "phase")

    def __init__(self, idx, eng, fn, reads, writes, dma, tag):
        self.idx = idx
        self.eng = eng
        self.fn = fn
        self.reads = reads
        self.writes = writes
        self.dma = dma
        self.tag = tag
        self.need_inc = False
        self.tick = 0
        self.waits = []
        self.dkey = None


class Prog:
    def __init__(self):
        self.ops = []
        self.phase = "init"
        self.last_writer = {}
        self.readers = {}

    def op(self, eng, reads=(), writes=(), dma=False, tag=""):
        def deco(fn):
            o = Op(len(self.ops), eng, fn, expand_couples(reads), expand_couples(writes), dma, tag)
            o.phase = self.phase
            if dma:
                o.dkey = "D_" + (o.writes[0] if o.writes else (o.reads[0] if o.reads else "misc"))
            deps = []
            for r in o.reads:
                w = self.last_writer.get(r)
                if w is not None:
                    deps.append(w)
            for w_ in o.writes:
                lw = self.last_writer.get(w_)
                if lw is not None:
                    deps.append(lw)
                deps.extend(self.readers.get(w_, ()))
            for d in deps:
                if d.eng != o.eng:
                    d.need_inc = True
            o._deps = [d for d in deps if d.eng != o.eng]
            for r in o.reads:
                self.readers.setdefault(r, []).append(o)
            for w_ in o.writes:
                self.last_writer[w_] = o
                self.readers[w_] = []
            self.ops.append(o)
            return o
        return deco

    def finalize(self, nc, block):
        for o in self.ops:
            if o.dma:
                o.need_inc = True
        tick = {}
        for o in self.ops:
            key = o.dkey if o.dkey else o.eng
            if o.need_inc:
                tick[key] = tick.get(key, 0) + (16 if o.dma else 1)
            o.tick = tick.get(key, 0)
        water = {}
        for o in self.ops:
            want = {}
            for d in o._deps:
                assert d.need_inc and d.tick > 0
                key = d.dkey if d.dkey else d.eng
                want[key] = max(want.get(key, 0), d.tick)
            for k2, v in want.items():
                wk = (o.eng, k2)
                if v > water.get(wk, 0):
                    o.waits.append((k2, v))
                    water[wk] = v
        n_inc = sum(1 for o in self.ops if o.need_inc)
        n_wait = sum(len(o.waits) for o in self.ops)
        by_eng = {e: sum(1 for o in self.ops if o.eng == e) for e in ENGINES}
        print(f"[prog] ops={len(self.ops)} incs={n_inc} waits={n_wait} by_eng={by_eng}")
        keys = set(e for e in ENGINES)
        for o in self.ops:
            if o.dkey:
                keys.add(o.dkey)
        sems = {k: nc.alloc_semaphore(f"sem_{k}"[:28]) for k in sorted(keys)}
        self.sems = sems

        def run_engine(ename):
            def body(eng):
                for o in self.ops:
                    if o.eng != ename:
                        continue
                    for (pe, v) in o.waits:
                        eng.wait_ge(sems[pe], v)
                    with nc.named_scope(o.phase):
                        ins = o.fn(eng)
                    if o.need_inc:
                        assert ins is not None, f"op {o.idx} {o.tag} returned no instruction"
                        key = o.dkey if o.dkey else ename
                        ins.then_inc(sems[key], 16 if o.dma else 1)
            return body

        for ename in ENGINES:
            getattr(block, ename)(run_engine(ename))


def make_consts():
    c = {}
    c["ident64"] = np.vstack([np.eye(64), np.eye(64)]).astype(ml_dtypes.bfloat16)
    c["ident128"] = np.eye(128, dtype=ml_dtypes.bfloat16)
    c["ones64"] = np.ones((1, 64), dtype=ml_dtypes.bfloat16)
    c["ones128row"] = np.ones((1, 128), dtype=ml_dtypes.bfloat16)
    c["onescol128"] = np.ones((128, 1), dtype=np.float32)
    i_idx = np.arange(64)
    negstriu = -(i_idx[:, None] < i_idx[None, :]).astype(np.float32)
    triuincl = (i_idx[:, None] <= i_idx[None, :]).astype(np.float32)
    def tile_all(m):
        out = np.zeros((128, 512), np.float32)
        for r in range(2):
            for j in range(8):
                out[64*r:64*r+64, 64*j:64*j+64] = m
        return out
    c["negstriu"] = tile_all(negstriu).astype(ml_dtypes.bfloat16)
    c["triuincl"] = tile_all(triuincl).astype(ml_dtypes.bfloat16)
    hm = np.zeros((128, 128), np.float32)
    for fc in range(8):
        for p in range(128):
            hm[p, fc*16 + 2*fc + p//64] = 1.0
    c["headmaskC"] = hm.astype(ml_dtypes.bfloat16)
    m2 = np.zeros((16, 1024), np.float32)
    for h in range(16):
        fc, r = h // 2, h % 2
        m2[h, fc*128 + 64*r: fc*128 + 64*r + 64] = 1.0
    mp = np.zeros((64, 1024), np.float32)
    mp[0:16] = m2
    mp[32:48] = -m2
    c["mask2pack"] = mp.astype(ml_dtypes.bfloat16)
    c["mask2_f32"] = m2.astype(np.float32)
    m2s = np.zeros((16, 128), np.float32)
    for h in range(16):
        m2s[h, 64*(h % 2): 64*(h % 2)+64] = 1.0
    c["mask2s_f32"] = m2s.astype(np.float32)
    c["mask2s_bf"] = m2s.astype(ml_dtypes.bfloat16)
    hm8 = np.zeros((16, 512), np.float32)
    for h in range(16):
        j = h // 2
        hm8[h, 64*j:64*j+64] = 1.0
    c["headmask8_f32"] = hm8
    c["headmask8_bf"] = hm8.astype(ml_dtypes.bfloat16)
    hmj = np.zeros((16, 8), np.float32)
    for h in range(16):
        hmj[h, h//2] = 1.0
    c["headmaskj"] = hmj
    return c


COUPLES = {
    "y": ("hid", "hidFM", "qkvFM", "convq", "convk", "Vfm"),
    "outgate": ("hid", "hidFM"),
    "istage": ("Qn", "Qh"),
    "yqFM": ("Qn", "Qh", "Kn", "Khm"),
    "h2": ("ab", "r_siga", "r_logb", "r_beta", "r_L", "r_negL", "r_eL", "r_invq"),
    "prep": ("convq",),
    "prepbf": ("convq",),
    "prepstage": ("convk",),
    "prepsums": ("convk",),
    "normw_f32": ("qkvFM",),
    "wbuf_outA": ("wbuf_qkvA", "wbuf_g0"),
    "wbuf_outB": ("wbuf_qkvB", "wbuf_g1"),
    "wbuf_g0": ("wbuf_qkvA",),
    "wbuf_g1": ("wbuf_qkvB",),
    "wbuf_u0": ("wbuf_ctrlA", "wbuf_ctrlB"),
    "wbuf_u1": ("readsFM",),
    "wbuf_dA": ("readsFM",),
    "wbuf_dB": ("normedFM",),
}


def expand_couples(res):
    out = list(res)
    for r in res:
        out.extend(COUPLES.get(r, ()))
    return tuple(dict.fromkeys(out))


"""Main megakernel builder. See kb.py for conventions.

Phases per token chunk (T tokens, NTT=T/128 token tiles, NSC=T/64 scan chunks):
  F: x load, LN1, hidden->FM, qkv mm, ctrl mm, conv mm, l2norm, scaled variants
  S: per scan chunk: WY delta rule (all 16 heads batched in [128x512] tiles)
  E: RMS head-norm, out proj (token-major out), gate, residual -> x1
  FF: LN2, act_quant, gate/up mm, silu*up, act_quant, down mm, residual -> x2, store
"""


def ap_custom(t_ap, offset, dims):
    return bass.AP(tensor=t_ap.tensor, offset=offset, ap=[list(d) for d in dims])


class Builder:
    def __init__(self, nc, cfg: Cfg, taps=None):
        self.nc = nc
        self.cfg = cfg
        self.p = Prog()
        self.taps = taps or []
        self.tap_outs = {}
        self.consts_np = make_consts()
        self._alloc()

    def _alloc(self):
        nc, cfg = self.nc, self.cfg
        T = cfg.T
        self.dram_in = {}
        for name, shape in [
            ("x", [cfg.S, D]), ("qkv_w", [QKV, D]), ("ctrl_w", [CTRL, D]),
            ("ctrl_b", [CTRL]), ("convq_w", [D, 4]), ("convk_w", [D, 4]),
            ("convv_w", [D, 4]), ("norm_w_t", [16, D]), ("out_w", [D, D]),
            ("out_b", [D]), ("w_gate", [INNER, D]), ("w_up", [INNER, D]),
            ("w_down", [D, INNER]),
        ]:
            self.dram_in[name] = nc.dram_tensor(name, shape, F32, kind="ExternalInput").ap()
        self.dram_out = nc.dram_tensor("out", [cfg.S, D], F32, kind="ExternalOutput").ap()
        self.scr = {}
        for name, shape in [("qkvT", [D, QKV]), ("ctrlT", [D, CTRL]),
                            ("outT", [D, D]), ("wgT", [D, INNER]),
                            ("wuT", [D, INNER]), ("wdT", [INNER, D])]:
            self.scr[name] = nc.dram_tensor("scr_" + name, shape, BF16, kind="Internal").ap()
        self.cst = {}
        for k, v in self.consts_np.items():
            self.cst[k] = nc.inline_tensor(np.ascontiguousarray(v), "c_" + k).ap()
        self.cst["onesT_bf"] = nc.inline_tensor(np.ones((1, T), BF), "c_onesT").ap()
        self.cst["ones64_f32"] = nc.inline_tensor(np.ones((1, 64), np.float32), "c_ones64f").ap()
        self.cst["ones128row_f32"] = nc.inline_tensor(np.ones((1, 128), np.float32), "c_o128rf").ap()
        self.cst["onescol128_f32"] = nc.inline_tensor(np.ones((128, 1), np.float32), "c_oc128f").ap()
        self.cst["onesf_16x64"] = nc.inline_tensor(np.ones((16, 64), np.float32), "c_of1664").ap()
        self.cst["biasc"] = nc.inline_tensor(
            np.tile(np.array([[1e-5, 1e-6, 0.01, 0.0]], np.float32), (128, 1)), "c_biasc").ap()

        A = nc.alloc_sbuf_tensor
        s = {}
        for nm, shape, dt in [
            ("ident64", [128, 64], BF16), ("ident128", [128, 128], BF16),
            ("negstriu", [128, 512], BF16), ("triuincl", [128, 512], BF16),
            ("headmaskC", [128, 128], BF16), ("mask2pack", [64, 1024], BF16),
            ("mask2s_f32", [16, 128], F32),
            ("mask2s_bf", [16, 128], BF16), ("headmask8_f32", [16, 512], F32),
            ("headmask8_bf", [16, 512], BF16), ("headmaskj", [16, 8], F32),
            ("onesT_bf", [1, T], BF16), ("ones64_bf", [1, 64], BF16),
            ("ones64_f32", [1, 64], F32), ("ones128row_bf", [1, 128], BF16),
            ("ones128row_f32", [1, 128], F32), ("onescol128_f32", [128, 1], F32),
            ("onesf_16x64", [16, 64], F32), ("biasc", [128, 4], F32),
            ("diags", [128, 96 * 128], BF16),
            ("ctrl_b_bf", [1, CTRL], BF16), ("out_b_bf", [1, D], BF16),
            ("mask2w", [16, 1024], BF16),
            ("cw", [128, 96], F32),
            ("xbuf0", [128, 2 * 1024], F32), ("xbuf1", [128, 2 * 1024], F32),
            ("stats", [128, 32], F32), ("stat1", [1, 64], F32),
            ("hid", [128, 2 * 1024], BF16), ("hidFM", [128, 8 * T], BF16),
            ("qkvFM", [128, 24 * (T + 3)], BF16),
            ("convq", [128, 8 * T], F32), ("convk", [128, 8 * T], F32),
            ("Vfm", [128, 8 * T], BF16), ("Qn", [128, 8 * T], BF16),
            ("Qh", [128, 8 * T], BF16), ("Kn", [128, 8 * T], BF16),
            ("Khm", [128, 8 * T], BF16),
            ("ab", [64, T], F32),
            ("r_siga", [16, T], F32), ("r_logb", [16, T], F32),
            ("r_beta", [16, T], F32), ("r_L", [16, T], F32),
            ("r_negL", [16, T], F32), ("r_eL", [16, T], F32),
            ("r_invq", [16, T], F32), ("r_invk", [16, T], F32),
            ("r_t", [16, T], F32),
            ("rb_qn", [16, T], BF16), ("rb_qh", [16, T], BF16),
            ("rb_kn", [16, T], BF16), ("rb_khm", [16, T], BF16),
            ("lhsE", [16, 128], F32), ("lhsB", [16, 128], BF16),
            ("E", [128, 512], F32), ("bpartU", [128, 512], F32),
            ("bmask_t", [128, 512], BF16), ("bmask_B", [128, 512], BF16),
            ("NL0", [128, 512], BF16), ("NLa", [128, 512], BF16),
            ("NLb", [128, 512], BF16),
            ("NN0", [128, 512], BF16), ("NNa", [128, 512], BF16),
            ("NNb", [128, 512], BF16),
            ("BL0", [128, 512], BF16), ("BL1", [128, 512], BF16),
            ("R", [128, 512], BF16),
            ("readstm", [128, 512], BF16), ("Sall", [128, 512], BF16),
            ("Ktmp", [128, 512], BF16), ("t1", [128, 512], F32),
            ("t2", [128, 512], F32),
            ("gcall0", [128, 8], F32), ("gcall1", [128, 8], F32),
            ("dall0", [128, 8], F32), ("dall1", [128, 8], F32),
            ("gcrhs", [16, 8], F32),
            ("readsFM", [128, 8 * T], BF16), ("normedFM", [128, 8 * T], BF16),
            ("Lmaskd", [16, 512], F32), ("sqf", [128, 1024], F32),
            ("sqbuf", [128, T], BF16), ("rmsrow", [16, T], F32),
            ("rmsbf", [16, T], BF16),
            ("h2q", [128, 2 * 1024], BF16), ("h2qFM", [128, 8 * T], BF16),
            ("yq", [128, 2 * 4096], BF16),
            ("fstat", [128, 24], F32), ("qhist", [128, 72], BF16),
            ("wsc", [128, 4], F32),
            ("wbuf_qkvA", [128, 8 * 256], BF16), ("wbuf_qkvB", [128, 8 * 256], BF16),
            ("wbuf_ctrlA", [128, CTRL], BF16), ("wbuf_ctrlB", [128, CTRL], BF16),

        ]:
            s[nm] = A(nm, shape, dt)
        AT = nc.alloc_sbuf_tensor_at
        adr = lambda nm: nc.lookup_mloc(s[nm]).addr
        s["y"] = AT("y", [128, 2 * 4096], F32, offset=adr("hid"))
        s["outgate"] = AT("outgate", [128, 2 * 1024], F32, offset=adr("hid"))
        s["istage"] = AT("istage", [128, 2048], I32, offset=adr("Qn"))
        s["yqFM"] = AT("yqFM", [128, 32 * T], BF16, offset=adr("Qn"))
        s["h2"] = AT("h2", [128, 2 * 1024], F32, offset=adr("ab"))
        s["prep"] = AT("prep", [128, 1024], F32, offset=adr("convq"))
        s["prepbf"] = AT("prepbf", [128, 1024], BF16, offset=adr("convq") + 4096)
        s["prepstage"] = AT("prepstage", [128, 512], BF16, offset=adr("convk"))
        s["prepsums"] = AT("prepsums", [128, 64], F32, offset=adr("convk") + 1024)
        s["normw_f32"] = AT("normw_f32", [16, 1024], F32, offset=adr("qkvFM"))
        s["wbuf_outA"] = AT("wbuf_outA", [128, 2048], BF16, offset=adr("wbuf_qkvA"))
        s["wbuf_outB"] = AT("wbuf_outB", [128, 2048], BF16, offset=adr("wbuf_qkvB"))
        s["wbuf_g0"] = AT("wbuf_g0", [128, 4 * 512], BF16, offset=adr("wbuf_qkvA"))
        s["wbuf_g1"] = AT("wbuf_g1", [128, 4 * 512], BF16, offset=adr("wbuf_qkvB"))
        s["wbuf_u0"] = AT("wbuf_u0", [128, 4 * 512], BF16, offset=adr("wbuf_ctrlA"))
        s["wbuf_u1"] = AT("wbuf_u1", [128, 4 * 512], BF16, offset=adr("readsFM"))
        s["wbuf_dA"] = AT("wbuf_dA", [128, 2048], BF16, offset=adr("readsFM"))
        s["wbuf_dB"] = AT("wbuf_dB", [128, 2048], BF16, offset=adr("normedFM"))
        self.s = s
        used = nc.sbuf_base - 16481
        print(f"[alloc] sbuf used ~{used/1024:.1f} KB of {(229344-16481)/1024:.1f}")
        self.P = [nc.alloc_psum_tensor(f"P{i}", [128, 512], F32) for i in range(8)]
        for name, shape, dt in self.taps:
            self.tap_outs[name] = nc.dram_tensor("tap_" + name, shape, dt,
                                                 kind="ExternalOutput").ap()

    def tap(self, name, ap_fn, reads):
        if name not in self.tap_outs:
            return
        out = self.tap_outs[name]
        self.p.op("gpsimd", reads=tuple(reads), dma=True, tag="tap_" + name)(
            lambda e, o=out, f=ap_fn: e.dma_start(out=o, in_=f()))

    def load_consts(self):
        p, s, cst = self.p, self.s, self.cst
        pairs = [(k, k) for k in
                 ["ident64", "ident128", "negstriu", "triuincl",
                  "headmaskC", "mask2pack", "mask2s_f32", "mask2s_bf",
                  "headmask8_f32", "headmask8_bf", "headmaskj", "onesT_bf",
                  "ones64_f32", "ones128row_f32", "onescol128_f32", "onesf_16x64"]]
        pairs += [("ones64_bf", "ones64"), ("ones128row_bf", "ones128row"),
                  ("biasc", "biasc")]
        for sb, ck in pairs:
            p.op("sync", writes=("c_" + sb,), dma=True, tag="ldc")(
                lambda e, a=s[sb], b=cst[ck]: e.dma_start(out=a[:], in_=b))
        p.op("gpsimd", writes=("c_ctrl_b",), dma=True)(
            lambda e: e.dma_start(out=s["ctrl_b_bf"][:], in_=self.dram_in["ctrl_b"][None, :]))
        p.op("gpsimd", writes=("c_out_b",), dma=True)(
            lambda e: e.dma_start(out=s["out_b_bf"][:], in_=self.dram_in["out_b"][None, :]))
        p.op("sync", writes=("c_normw",), dma=True)(
            lambda e: e.dma_start(out=s["normw_f32"][:], in_=self.dram_in["norm_w_t"]))
        p.op("vector", reads=("c_normw", "c_mask2pack"), writes=("c_mask2w",))(
            lambda e: e.tensor_mul(s["mask2w"][:], s["normw_f32"][:],
                                   s["mask2pack"][0:16, :]))
        for si, nm in enumerate(["convq_w", "convk_w", "convv_w"]):
            in_ap = ap_custom(self.dram_in[nm], 0, [[4, 128], [512, 8], [1, 4]])
            out_ap = ap_custom(s["cw"][:], si * 32, [[96, 128], [4, 8], [1, 4]])
            p.op("sync", writes=("c_cw",), dma=True)(
                lambda e, i=in_ap, o=out_ap: e.dma_start(out=o, in_=i))
        for si in range(3):
            for f in range(8):
                for j in range(4):
                    idx = si * 32 + f * 4 + j
                    p.op("scalar", reads=("c_cw", "c_ident128"), writes=("c_diag",),
                         tag="diag")(
                        lambda e, idx=idx, col=si * 32 + f * 4 + j: e.activation(
                            out=s["diags"][:, idx * 128:(idx + 1) * 128],
                            in_=s["ident128"][:], func=AF.Copy,
                            scale=s["cw"][:, col:col + 1]))

    def _transpose_store(self, dst_scr, rowlen, col0, row0, tag):
        p, s, P = self.p, self.s, self.P
        for g in range(2):
            def mmf(e, g=g):
                ins = None
                for b in range(4):
                    cb = g * 4 + b
                    ins = e.matmul(P[6][:, b * 128:(b + 1) * 128],
                                   s["prepbf"][:, cb * 128:(cb + 1) * 128],
                                   s["ident128"][:], start=True, stop=True)
                return ins
            p.op("tensor", reads=("prepbf", "c_ident128"), writes=("P6",), tag=tag)(mmf)
            p.op("vector", reads=("P6",), writes=("prepstage",))(
                lambda e: e.tensor_copy(s["prepstage"][:], P[6][:, 0:512]))
            out_ap = ap_custom(dst_scr, (row0 + g * 512) * rowlen + col0,
                               [[rowlen, 128], [128 * rowlen, 4], [1, 128]])
            in_ap = ap_custom(s["prepstage"][:], 0, [[512, 128], [128, 4], [1, 128]])
            p.op("sync", reads=("prepstage",), dma=True, tag=tag + "st")(
                lambda e, i=in_ap, o=out_ap: e.dma_start(out=o, in_=i))

    def prep_weights_attn(self):
        p, s, P = self.p, self.s, self.P
        for nm, Osz, dstn in [("qkv_w", QKV, "qkvT"), ("ctrl_w", CTRL, "ctrlT"),
                              ("out_w", D, "outT")]:
            dst = self.scr[dstn]
            for t in range((Osz + 127) // 128):
                nr = min(128, Osz - t * 128)
                p.op("sync", writes=("prep",), dma=True, tag="pwld")(
                    lambda e, t=t, nm=nm, nr=nr: e.dma_start(
                        out=s["prep"][0:nr, :],
                        in_=self.dram_in[nm][t * 128:t * 128 + nr, :]))
                if nr == 128:
                    p.op("vector", reads=("prep",), writes=("prepbf",))(
                        lambda e: e.tensor_copy(s["prepbf"][:], s["prep"][:]))
                else:
                    p.op("vector", writes=("prepbf",))(
                        lambda e: e.memset(s["prepbf"][:], 0.0))
                    p.op("vector", reads=("prep",), writes=("prepbf",))(
                        lambda e, nr=nr: e.tensor_copy(s["prepbf"][0:nr, :], s["prep"][0:nr, :]))
                self._transpose_store_n(dst, Osz, t * 128, nr, "tp_" + nm, scrres="scr_" + dstn)
        return

    def prep_weights_ffn(self):
        p, s, P = self.p, self.s, self.P
        for nm, Osz, Isz, dstn in [("w_gate", INNER, D, "wgT"),
                                   ("w_up", INNER, D, "wuT"),
                                   ("w_down", D, INNER, "wdT")]:
            ntile = Osz * Isz // (128 * 1024)
            for t in range(ntile):
                p.op("sync", writes=("prep",), dma=True, tag="fq1")(
                    lambda e, t=t, nm=nm: e.dma_start(
                        out=s["prep"][:],
                        in_=ap_custom(self.dram_in[nm], t * 131072,
                                      [[1024, 128], [1, 1024]])))
                p.op("vector", reads=("prep",), writes=("psums",))(
                    lambda e, t=t: e.tensor_reduce(
                        s["prepsums"][:, t:t + 1], s["prep"][:], AX.X, ALU.add,
                        apply_absolute_value=True))
            p.op("vector", reads=("prep",), writes=("spacer",))(
                lambda e: e.tensor_copy(s["sqf"][:, 0:1024], s["prep"][:]))
            p.op("vector", reads=("psums",), writes=("psumtot",))(
                lambda e, nt=ntile: e.tensor_reduce(
                    s["prepsums"][:, 63:64], s["prepsums"][:, 0:nt], AX.X, ALU.add))
            p.op("tensor", reads=("psumtot", "c_onescol128_f32"), writes=("P7",))(
                lambda e: e.matmul(P[7][0:1, 0:1], s["prepsums"][:, 63:64],
                                   s["onescol128_f32"][:], start=True, stop=True))
            p.op("vector", reads=("P7",), writes=("stat1",))(
                lambda e, n=Osz * Isz: e.tensor_scalar(
                    s["stat1"][0:1, 0:1], P[7][0:1, 0:1], 1.0 / n, 1e-5,
                    ALU.mult, ALU.max))
            p.op("tensor", reads=("stat1", "c_ones128row_f32"), writes=("P7",))(
                lambda e: e.matmul(P[7][:, 0:1], s["ones128row_f32"][:],
                                   s["stat1"][0:1, 0:1], start=True, stop=True))
            wi = {"w_gate": 0, "w_up": 1, "w_down": 2}[nm]
            p.op("scalar", reads=("P7",), writes=("wmean",))(
                lambda e: e.activation(out=s["stats"][:, 29:30], in_=P[7][:, 0:1],
                                       func=AF.Copy))
            p.op("scalar", reads=("P7",), writes=("wsc",))(
                lambda e, wi=wi: e.activation(out=s["wsc"][:, wi:wi + 1],
                                              in_=P[7][:, 0:1], func=AF.Copy))
            p.op("vector", reads=("wmean",), writes=("wscale",))(
                lambda e: e.reciprocal(s["stats"][:, 30:31], s["stats"][:, 29:30]))
            if nm == "w_gate" and "pstat" in self.tap_outs:
                self.p.op("gpsimd", reads=("wscale", "stat1", "psums"), dma=True, tag="tpstat")(
                    lambda e: e.dma_start(out=self.tap_outs["pstat"], in_=s["stat1"][:]))
                self.p.op("gpsimd", reads=("wscale",), dma=True, tag="tpsums")(
                    lambda e: e.dma_start(out=self.tap_outs["psums"], in_=s["prepsums"][:]))
            if nm in ("w_gate", "w_up"):
                tiles = [(t * 128, 0, Osz, t * 128, 0) for t in range(32)]
                tiles = [(t * 128, 0, Osz, t * 128, 0) for t in range(32)]
            else:
                tiles = [(t * 128, q * 1024, D, t * 128, q * 1024)
                         for t in range(8) for q in range(4)]
            for (r0, c0, rowlen, dcol0, drow0) in tiles:
                p.op("sync", writes=("prep",), dma=True, tag="fq2")(
                    lambda e, r0=r0, c0=c0, nm=nm: e.dma_start(
                        out=s["prep"][:],
                        in_=ap_custom(self.dram_in[nm], r0 * (D if nm != "w_down" else INNER) + c0,
                                      [[(D if nm != "w_down" else INNER), 128], [1, 1024]])))
                p.op("vector", reads=("prep", "wscale"), writes=("prep",))(
                    lambda e: e.tensor_scalar(s["prep"][:], s["prep"][:],
                                              s["stats"][:, 30:31], 1.0, ALU.mult, ALU.min))
                p.op("vector", reads=("prep",), writes=("prep",))(
                    lambda e: e.tensor_scalar(s["prep"][:], s["prep"][:], -1.0, None, ALU.max))
                p.op("vector", reads=("prep",), writes=("istage",))(
                    lambda e: e.tensor_copy(s["istage"][:, 0:1024], s["prep"][:]))
                p.op("vector", reads=("istage",), writes=("prepbf",))(
                    lambda e: e.tensor_copy(s["prepbf"][:], s["istage"][:, 0:1024]))
                self._transpose_store_n(self.scr[dstn], rowlen, dcol0, 128,
                                        "tq_" + nm, row0=drow0, scrres="scr_" + dstn)

    def _transpose_store_n(self, dst_scr, rowlen, col0, ncols, tag, row0=0, scrres="scr"):
        p, s, P = self.p, self.s, self.P
        for g in range(2):
            def mmf(e, g=g):
                ins = None
                for b in range(4):
                    cb = g * 4 + b
                    ins = e.matmul(P[6][:, b * 128:(b + 1) * 128],
                                   s["prepbf"][:, cb * 128:(cb + 1) * 128],
                                   s["ident128"][:], start=True, stop=True)
                return ins
            p.op("tensor", reads=("prepbf", "c_ident128"), writes=("P6",), tag=tag)(mmf)
            p.op("vector", reads=("P6",), writes=("prepstage",))(
                lambda e: e.tensor_copy(s["prepstage"][:], P[6][:, 0:512]))
            out_ap = ap_custom(dst_scr, (row0 + g * 512) * rowlen + col0,
                               [[rowlen, 128], [128 * rowlen, 4], [1, ncols]])
            in_ap = ap_custom(s["prepstage"][:], 0, [[512, 128], [128, 4], [1, ncols]])
            p.op("sync", reads=("prepstage",), writes=(scrres,), dma=True, tag=tag + "st")(
                lambda e, i=in_ap, o=out_ap: e.dma_start(out=o, in_=i))

    def frontend(self, ch):
        p, s, P, cfg = self.p, self.s, self.P, self.cfg
        T, NTT = cfg.T, cfg.NTT
        xb = s["xbuf%d" % (ch % 2)]
        xr = "xbuf%d" % (ch % 2)
        in_ap = ap_custom(self.dram_in["x"], ch * T * D, [[D, 128], [128 * D, NTT], [1, D]])
        out_ap = ap_custom(xb[:], 0, [[2 * 1024, 128], [1024, NTT], [1, 1024]])
        p.op("sync", writes=(xr,), dma=True, tag="xld")(
            lambda e, i=in_ap, o=out_ap: e.dma_start(out=o, in_=i))
        self._ln(xb, xr, s["hid"], "hid", statcol=0)
        for tt in range(NTT):
            def mmf(e, tt=tt):
                ins = None
                for g in range(2):
                    for b in range(4):
                        fc = g * 4 + b
                        ins = e.matmul(P[g][:, b * 128:(b + 1) * 128],
                                       s["hid"][:, tt * 1024 + fc * 128: tt * 1024 + (fc + 1) * 128],
                                       s["ident128"][:], start=True, stop=True)
                return ins
            p.op("tensor", reads=("hid", "c_ident128"), writes=("P0", "P1"), tag="hidT")(mmf)
            for g in range(2):
                out_ap = ap_custom(s["hidFM"][:], tt * 128 + g * 4 * T,
                                   [[8 * T, 128], [T, 4], [1, 128]])
                if g == 0:
                    p.op("vector", reads=("P0",), writes=("hidFM",))(
                        lambda e, o=out_ap: e.tensor_copy(o, P[0][:, 0:512]))
                else:
                    p.op("scalar", reads=("P1",), writes=("hidFM",))(
                        lambda e, o=out_ap: e.activation(out=o, in_=P[1][:, 0:512],
                                                         func=AF.Copy))
        self.tap("hid", lambda: s["hid"][:], ["hid"]) if ch == getattr(self, "tap_chunk", 0) else None
        self.tap("stats", lambda: s["stats"][:], ["stats"]) if ch == getattr(self, "tap_chunk", 0) else None
        self.tap("hidFM", lambda: s["hidFM"][:], ["hidFM"]) if ch == getattr(self, "tap_chunk", 0) else None
        if ch == 0:
            p.op("vector", writes=("qkvFM_m",), tag="qmz")(
                lambda e: e.memset(ap_custom(s["qkvFM"][:], 0,
                                             [[24 * (T + 3), 128], [T + 3, 24], [1, 3]]), 0.0))
        else:
            mg_out = ap_custom(s["qkvFM"][:], 0, [[24 * (T + 3), 128], [T + 3, 24], [1, 3]])
            p.op("vector", reads=("qhist",), writes=("qkvFM_m",), tag="qmc")(
                lambda e, o=mg_out: e.tensor_copy(o, s["qhist"][:]))
        for pair in range(12):
            buf = "wbuf_qkv" + ("A" if pair % 2 == 0 else "B")
            p.op("sync", reads=("scr_qkvT",), writes=(buf,), dma=True, tag="wqkv")(
                lambda e, pair=pair, buf=buf: e.dma_start(
                    out=s[buf][:],
                    in_=ap_custom(self.scr["qkvT"], pair * 256,
                                  [[QKV, 128], [128 * QKV, 8], [1, 256]])))
            for ms in range(2):
                m = pair * 2 + ms
                bank = m % 2
                def mmf(e, ms=ms, bank=bank, buf=buf):
                    ins = None
                    for k in range(8):
                        ins = e.matmul(P[bank][:, 0:T],
                                       s[buf][:, k * 256 + ms * 128: k * 256 + (ms + 1) * 128],
                                       s["hidFM"][:, k * T:(k + 1) * T],
                                       start=(k == 0), stop=(k == 7))
                    return ins
                p.op("tensor", reads=(buf, "hidFM"), writes=("P%d" % bank,),
                     tag="qkvmm")(mmf)
                p.op("scalar", reads=("P%d" % bank,), writes=("qkvFM",))(
                    lambda e, m=m, bank=bank: e.activation(
                        out=s["qkvFM"][:, m * (T + 3) + 3: (m + 1) * (T + 3)],
                        in_=P[bank][:, 0:T], func=AF.Copy))
        for k in range(8):
            cb = "wbuf_ctrl" + ("A" if k % 2 == 0 else "B")
            p.op("sync", reads=("scr_ctrlT",), writes=(cb,), dma=True, tag="wctrl")(
                lambda e, k=k, cb=cb: e.dma_start(
                    out=s[cb][:],
                    in_=ap_custom(self.scr["ctrlT"], k * 128 * CTRL, [[CTRL, 128], [1, CTRL]])))
            def mmab(e, k=k, cb=cb):
                e.matmul(P[2][0:16, 0:T], s[cb][:, 0:16],
                         s["hidFM"][:, k * T:(k + 1) * T],
                         start=(k == 0), stop=False)
                return e.matmul(P[2][32:48, 0:T], s[cb][:, 16:32],
                                s["hidFM"][:, k * T:(k + 1) * T],
                                start=(k == 0), stop=False)
            p.op("tensor", reads=(cb, "hidFM"), writes=("P2",), tag="abmm")(mmab)
            def mmgate(e, k=k, cb=cb):
                ins = None
                for tt in range(NTT):
                    for hf in range(2):
                        ins = e.matmul(
                            P[3 + tt * 2 + hf][:, 0:512],
                            s["hidFM"][:, k * T + tt * 128: k * T + (tt + 1) * 128],
                            s[cb][:, 32 + hf * 512: 32 + (hf + 1) * 512],
                            start=(k == 0), stop=False)
                return ins
            p.op("tensor", reads=(cb, "hidFM"), writes=("P3", "P4", "P5", "P6"),
                 tag="gatemm")(mmgate)
        def mmab_b(e):
            e.matmul(P[2][0:16, 0:T], s["ctrl_b_bf"][0:1, 0:16],
                     s["onesT_bf"][0:1, 0:T], start=False, stop=True)
            return e.matmul(P[2][32:48, 0:T], s["ctrl_b_bf"][0:1, 16:32],
                            s["onesT_bf"][0:1, 0:T], start=False, stop=True)
        p.op("tensor", reads=("c_ctrl_b", "c_onesT_bf"), writes=("P2",), tag="abbias")(mmab_b)
        def mmgate_b(e):
            ins = None
            for tt in range(NTT):
                for hf in range(2):
                    ins = e.matmul(P[3 + tt * 2 + hf][:, 0:512],
                                   s["ones128row_bf"][0:1, :],
                                   s["ctrl_b_bf"][0:1, 32 + hf * 512: 32 + (hf + 1) * 512],
                                   start=False, stop=True)
            return ins
        p.op("tensor", reads=("c_ctrl_b", "c_ones128row_bf"),
             writes=("P3", "P4", "P5", "P6"), tag="gatebias")(mmgate_b)
        p.op("vector", reads=("P2",), writes=("ab",))(
            lambda e: e.tensor_copy(s["ab"][0:48, 0:T], P[2][0:48, 0:T]))
        for tt in range(NTT):
            for hf in range(2):
                p.op("scalar", reads=("P%d" % (3 + tt * 2 + hf),), writes=("outgate",))(
                    lambda e, tt=tt, hf=hf: e.activation(
                        out=s["outgate"][:, tt * 1024 + hf * 512: tt * 1024 + (hf + 1) * 512],
                        in_=P[3 + tt * 2 + hf][:, 0:512], func=AF.Sigmoid))
        for si, (dstn, dstr) in enumerate([("convq", "convq"), ("convk", "convk"),
                                           ("Vfm", "Vfm")]):
            for fc in range(8):
                m = si * 8 + fc
                bank = fc % 2
                def mmf(e, si=si, fc=fc, m=m, bank=bank):
                    ins = None
                    for j in range(4):
                        idx = si * 32 + fc * 4 + j
                        ins = e.matmul(P[bank][:, 0:T],
                                       s["diags"][:, idx * 128:(idx + 1) * 128],
                                       s["qkvFM"][:, m * (T + 3) + j: m * (T + 3) + j + T],
                                       start=(j == 0), stop=(j == 3))
                    return ins
                p.op("tensor", reads=("c_diag", "qkvFM", "qkvFM_m"),
                     writes=("P%d" % bank,), tag="convmm")(mmf)
                p.op("scalar", reads=("P%d" % bank,), writes=(dstr,))(
                    lambda e, fc=fc, bank=bank, dstn=dstn: e.activation(
                        out=s[dstn][:, fc * T:(fc + 1) * T],
                        in_=P[bank][:, 0:T], func=AF.Silu))
        mg_in = ap_custom(s["qkvFM"][:], T, [[24 * (T + 3), 128], [T + 3, 24], [1, 3]])
        p.op("vector", reads=("qkvFM",), writes=("qhist",), tag="qhist")(
            lambda e, i=mg_in: e.tensor_copy(s["qhist"][:], i))
        for src, dst in [("convq", "r_invq"), ("convk", "r_invk")]:
            for fc in range(8):
                p.op("scalar", reads=(src,), writes=("sqbuf",))(
                    lambda e, fc=fc, src=src: e.activation(
                        out=s["sqbuf"][:, 0:T], in_=s[src][:, fc * T:(fc + 1) * T],
                        func=AF.Square))
                def mmf(e, fc=fc):
                    return e.matmul(P[7][0:16, 0:T],
                                    s["headmaskC"][:, fc * 16:(fc + 1) * 16],
                                    s["sqbuf"][:, 0:T], start=(fc == 0), stop=(fc == 7))
                p.op("tensor", reads=("sqbuf", "c_headmaskC"), writes=("P7",), tag="l2mm")(mmf)
            p.op("scalar", reads=("P7",), writes=("r_t",))(
                lambda e: e.activation(out=s["r_t"][:, 0:T], in_=P[7][0:16, 0:T],
                                       func=AF.Sqrt))
            p.op("vector", reads=("r_t",), writes=("r_t",))(
                lambda e: e.tensor_scalar(s["r_t"][:, 0:T], s["r_t"][:, 0:T], 1e-6,
                                          None, ALU.max))
            p.op("vector", reads=("r_t",), writes=(dst,))(
                lambda e, dst=dst: e.reciprocal(s[dst][:, 0:T], s["r_t"][:, 0:T]))
        p.op("scalar", reads=("ab",), writes=("r_siga",))(
            lambda e: e.activation(out=s["r_siga"][:, 0:T], in_=s["ab"][0:16, 0:T],
                                   func=AF.Sigmoid))
        p.op("scalar", reads=("r_siga",), writes=("r_logb",))(
            lambda e: e.activation(out=s["r_logb"][:, 0:T], in_=s["r_siga"][:, 0:T],
                                   func=AF.Ln, scale=0.98, bias=s["biasc"][0:16, 2:3]))
        p.op("scalar", reads=("ab",), writes=("r_beta",))(
            lambda e: e.activation(out=s["r_beta"][:, 0:T], in_=s["ab"][32:48, 0:T],
                                   func=AF.Sigmoid))
        for sc in range(cfg.NSC):
            p.op("vector", reads=("r_logb", "c_onesf_16x64"), writes=("r_L",))(
                lambda e, sc=sc: e.tensor_tensor_scan(
                    s["r_L"][:, sc * 64:(sc + 1) * 64], s["onesf_16x64"][:],
                    s["r_logb"][:, sc * 64:(sc + 1) * 64], 0.0, ALU.mult, ALU.add))
        p.op("vector", reads=("r_L",), writes=("r_negL",))(
            lambda e: e.tensor_scalar(s["r_negL"][:, 0:T], s["r_L"][:, 0:T], -1.0,
                                      None, ALU.mult))
        p.op("scalar", reads=("r_L",), writes=("r_eL",))(
            lambda e: e.activation(out=s["r_eL"][:, 0:T], in_=s["r_L"][:, 0:T],
                                   func=AF.Exp))
        p.op("vector", reads=("r_invq",), writes=("rb_qn",))(
            lambda e: e.tensor_copy(s["rb_qn"][:, 0:T], s["r_invq"][:, 0:T]))
        p.op("vector", reads=("r_invk",), writes=("rb_kn",))(
            lambda e: e.tensor_copy(s["rb_kn"][:, 0:T], s["r_invk"][:, 0:T]))
        p.op("vector", reads=("r_invq", "r_eL"), writes=("rb_qh",))(
            lambda e: e.tensor_mul(s["rb_qh"][:, 0:T], s["r_invq"][:, 0:T],
                                   s["r_eL"][:, 0:T]))
        p.op("vector", reads=("r_invk", "r_eL"), writes=("rb_khm",))(
            lambda e: e.scalar_tensor_tensor(s["rb_khm"][:, 0:T], s["r_invk"][:, 0:T],
                                             -1.0, s["r_eL"][:, 0:T],
                                             ALU.mult, ALU.mult))
        for (row, r0m, src, dst) in [("rb_qn", 0, "convq", "Qn"),
                                     ("rb_qh", 0, "convq", "Qh"),
                                     ("rb_kn", 0, "convk", "Kn"),
                                     ("rb_khm", 0, "convk", "Khm")]:
            for fc in range(8):
                def mmf(e, row=row, r0m=r0m, fc=fc):
                    return e.matmul(P[7][:, 0:T],
                                    s["mask2pack"][r0m:r0m + 16, fc * 128:(fc + 1) * 128],
                                    s[row][:, 0:T], start=True, stop=True)
                p.op("tensor", reads=(row, "c_mask2pack"), writes=("P7",), tag="bcmm")(mmf)
                p.op("vector", reads=("P7", src), writes=(dst,))(
                    lambda e, fc=fc, src=src, dst=dst: e.tensor_mul(
                        s[dst][:, fc * T:(fc + 1) * T],
                        s[src][:, fc * T:(fc + 1) * T], P[7][:, 0:T]))

    def _ln(self, xb, xr, dst, dstr, statcol, eps=1e-5):
        p, s, cfg = self.p, self.s, self.cfg
        for tt in range(cfg.NTT):
            sl = (tt * 1024, (tt + 1) * 1024)
            c0 = statcol + tt * 8
            p.op("vector", reads=(xr,), writes=("stats",))(
                lambda e, sl=sl, c0=c0: e.tensor_reduce(
                    s["stats"][:, c0:c0 + 1], xb[:, sl[0]:sl[1]], AX.X, ALU.add))
            p.op("scalar", reads=(xr,), writes=("sq128",))(
                lambda e, sl=sl: e.activation(out=s["sqf"][:, 0:1024],
                                              in_=xb[:, sl[0]:sl[1]], func=AF.Square))
            p.op("vector", reads=("sq128",), writes=("stats",))(
                lambda e, c0=c0: e.tensor_reduce(
                    s["stats"][:, c0 + 1:c0 + 2], s["sqf"][:, 0:1024], AX.X, ALU.add))
            p.op("scalar", reads=("stats",), writes=("stats",))(
                lambda e, c0=c0: e.activation(out=s["stats"][:, c0 + 6:c0 + 7],
                                              in_=s["stats"][:, c0:c0 + 1],
                                              func=AF.Copy, scale=1.0 / 1024))
            p.op("scalar", reads=("stats",), writes=("stats",))(
                lambda e, c0=c0: e.activation(out=s["stats"][:, c0 + 7:c0 + 8],
                                              in_=s["stats"][:, c0 + 1:c0 + 2],
                                              func=AF.Copy, scale=1.0 / 1024))
            p.op("vector", reads=("stats",), writes=("stats",))(
                lambda e, c0=c0: e.tensor_mul(s["stats"][:, c0 + 2:c0 + 3],
                                              s["stats"][:, c0 + 6:c0 + 7],
                                              s["stats"][:, c0 + 6:c0 + 7]))
            p.op("gpsimd", reads=("stats",), writes=("stats",))(
                lambda e, c0=c0: e.tensor_sub(s["stats"][:, c0 + 3:c0 + 4],
                                              s["stats"][:, c0 + 7:c0 + 8],
                                              s["stats"][:, c0 + 2:c0 + 3]))
            p.op("scalar", reads=("stats",), writes=("stats",))(
                lambda e, c0=c0: e.activation(out=s["stats"][:, c0 + 4:c0 + 5],
                                              in_=s["stats"][:, c0 + 3:c0 + 4],
                                              func=AF.Sqrt, bias=s["biasc"][:, 0:1] if abs(eps - 1e-5) < 1e-12 else s["biasc"][:, 1:2]))
            p.op("vector", reads=("stats",), writes=("stats",))(
                lambda e, c0=c0: e.reciprocal(s["stats"][:, c0 + 5:c0 + 6],
                                              s["stats"][:, c0 + 4:c0 + 5]))
            p.op("vector", reads=(xr, "stats"), writes=("sq128",))(
                lambda e, sl=sl, c0=c0: e.tensor_scalar(
                    s["sqf"][:, 0:1024], xb[:, sl[0]:sl[1]],
                    s["stats"][:, c0 + 6:c0 + 7], None, ALU.subtract))
            p.op("scalar", reads=("sq128", "stats"), writes=(dstr,))(
                lambda e, sl=sl, c0=c0: e.activation(
                    out=dst[:, sl[0]:sl[1]], in_=s["sqf"][:, 0:1024],
                    func=AF.Copy, scale=s["stats"][:, c0 + 5:c0 + 6]))
    def scan_front(self, ch, sc):
        p, s, P, cfg = self.p, self.s, self.P, self.cfg
        T = cfg.T
        ts = sc * 64
        BLn = "BL%d" % (sc % 2)

        def blk(t):
            return [(t[64 * r:64 * r + 64, 64 * j:64 * j + 64], r, j)
                    for j in range(8) for r in range(2)]

        def fmsl(t, r, j, w=64, off=0):
            return t[64 * r:64 * r + 64, j * T + ts + off: j * T + ts + off + w]

        negL_rep = ap_custom(s["r_negL"][:], ts, [[T, 16], [0, 2], [1, 64]])
        m2s3 = ap_custom(s["mask2s_f32"][:], 0, [[128, 16], [64, 2], [1, 64]])
        lhsE3 = ap_custom(s["lhsE"][:], 0, [[128, 16], [64, 2], [1, 64]])
        p.op("vector", reads=("r_negL", "c_mask2s_f32"), writes=("lhsE",), tag="lhsE")(
            lambda e: e.tensor_tensor(lhsE3, negL_rep, m2s3, ALU.mult))
        beta_rep = ap_custom(s["r_beta"][:], ts, [[T, 16], [0, 2], [1, 64]])
        m2sb3 = ap_custom(s["mask2s_bf"][:], 0, [[128, 16], [64, 2], [1, 64]])
        lhsB3 = ap_custom(s["lhsB"][:], 0, [[128, 16], [64, 2], [1, 64]])
        p.op("vector", reads=("r_beta", "c_mask2s_bf"), writes=("lhsB",), tag="lhsB")(
            lambda e: e.tensor_tensor(lhsB3, beta_rep, m2sb3, ALU.mult))
        L_rep = ap_custom(s["r_L"][:], ts, [[T, 16], [0, 8], [1, 64]])
        hm83 = ap_custom(s["headmask8_f32"][:], 0, [[512, 16], [64, 8], [1, 64]])
        Lm3 = ap_custom(s["Lmaskd"][:], 0, [[512, 16], [64, 8], [1, 64]])
        p.op("vector", reads=("r_L", "c_headmask8_f32"), writes=("Lmaskd",), tag="Lmaskd")(
            lambda e: e.tensor_tensor(Lm3, L_rep, hm83, ALU.mult))
        def mmE(e):
            e.matmul(P[1][:, 0:512], s["mask2s_f32"][:], s["Lmaskd"][:],
                     start=True, stop=False)
            return e.matmul(P[1][:, 0:512], s["lhsE"][:], s["headmask8_f32"][:],
                            start=False, stop=True)
        p.op("tensor", reads=("Lmaskd", "lhsE", "c_mask2s_f32", "c_headmask8_f32"),
             writes=("P1",), tag="Emm")(mmE)
        p.op("vector", reads=("P1",), writes=("t1",))(
            lambda e: e.tensor_scalar(s["t1"][:], P[1][:, 0:512], 0.0, None, ALU.min))
        p.op("scalar", reads=("t1",), writes=("E",))(
            lambda e: e.activation(out=s["E"][:], in_=s["t1"][:], func=AF.Exp))
        def mmG(e):
            ins = None
            for (b, r, j) in blk(P[1]):
                ins = e.matmul(b, fmsl(s["Kn"], r, j), fmsl(s["Kn"], r, j),
                               start=True, stop=True)
            return ins
        p.op("tensor", reads=("Kn",), writes=("P1",), tag="Gmm")(mmG)
        p.op("vector", reads=("E", "P1"), writes=("t1",))(
            lambda e: e.tensor_mul(s["t1"][:], s["E"][:], P[1][:, 0:512]))
        def mmP(e):
            ins = None
            for (b, r, j) in blk(P[2]):
                ins = e.matmul(b, fmsl(s["Kn"], r, j), fmsl(s["Qn"], r, j),
                               start=True, stop=True)
            return ins
        p.op("tensor", reads=("Kn", "Qn"), writes=("P2",), tag="Pmm")(mmP)
        p.op("vector", reads=("E", "P2"), writes=("t2",))(
            lambda e: e.tensor_mul(s["t2"][:], s["E"][:], P[2][:, 0:512]))
        p.op("tensor", reads=("lhsB", "c_headmask8_bf"), writes=("P2",), tag="bpmm")(
            lambda e: e.matmul(P[2][:, 0:512], s["lhsB"][:], s["headmask8_bf"][:],
                               start=True, stop=True))
        p.op("vector", reads=("P2",), writes=("bpartU",))(
            lambda e: e.tensor_copy(s["bpartU"][:], P[2][:, 0:512]))
        p.op("vector", reads=("bpartU", "c_negstriu"), writes=("bmask_t",))(
            lambda e: e.tensor_mul(s["bmask_t"][:], s["bpartU"][:], s["negstriu"][:]))
        p.op("vector", reads=("bpartU", "c_triuincl"), writes=("bmask_B",))(
            lambda e: e.tensor_mul(s["bmask_B"][:], s["bpartU"][:], s["triuincl"][:]))
        gcn = "gcall%d" % (sc % 2)
        daln = "dall%d" % (sc % 2)
        eL_last = ap_custom(s["r_eL"][:], ts + 63, [[T, 16], [0, 8]])
        p.op("vector", reads=("r_eL", "c_headmaskj"), writes=("gcrhs",))(
            lambda e: e.tensor_tensor(s["gcrhs"][:], eL_last, s["headmaskj"][:], ALU.mult))
        p.op("tensor", reads=("gcrhs", "c_mask2s_f32"), writes=("P2",), tag="gcmm")(
            lambda e: e.matmul(P[2][:, 0:8], s["mask2s_f32"][:], s["gcrhs"][:],
                               start=True, stop=True))
        p.op("scalar", reads=("P2",), writes=(gcn,))(
            lambda e, gcn=gcn: e.activation(out=s[gcn][:], in_=P[2][:, 0:8], func=AF.Copy))
        E_last = ap_custom(s["E"][:], 63, [[512, 128], [64, 8]])
        bp_first = ap_custom(s["bpartU"][:], 0, [[512, 128], [64, 8]])
        p.op("vector", reads=("E", "bpartU"), writes=(daln,))(
            lambda e, daln=daln: e.tensor_tensor(s[daln][:], E_last, bp_first, ALU.mult))
        p.op("vector", reads=("t1", "bmask_t"), writes=("NL0a", "NL0b"))(
            lambda e: e.tensor_mul(s["NL0"][:], s["t1"][:], s["bmask_t"][:]))
        p.op("vector", reads=("t2", "bmask_B"), writes=(BLn,))(
            lambda e, BLn=BLn: e.tensor_mul(s[BLn][:], s["t2"][:], s["bmask_B"][:]))
        def mmNN(e):
            ins = None
            for (b, r, j) in blk(P[1]):
                ins = e.matmul(b, s["NL0"][64 * r:64 * r + 64, 64 * j:64 * j + 64],
                               s["ident64"][64 * r:64 * r + 64, :], start=True, stop=True)
            return ins
        p.op("tensor", reads=("NL0a", "NL0b", "c_ident64"), writes=("P1",), tag="NNmm")(mmNN)
        p.op("vector", reads=("P1",), writes=("NN0a", "NN0b"))(
            lambda e: e.tensor_copy(s["NN0"][:], P[1][:, 0:512]))

    def scan_tail_a(self, ch, sc):
        p, s, P, cfg = self.p, self.s, self.P, self.cfg
        T = cfg.T
        ts = sc * 64
        def blk(t):
            return [(t[64 * r:64 * r + 64, 64 * j:64 * j + 64], r, j)
                    for j in range(8) for r in range(2)]

        def fmsl(t, r, j, w=64, off=0):
            return t[64 * r:64 * r + 64, j * T + ts + off: j * T + ts + off + w]
        def mmR0(e):
            ins = None
            for r in range(2):
                for j in range(8):
                    b = P[5][64 * r:64 * r + 64, 64 * j:64 * j + 64]
                    e.matmul(b, fmsl(s["Vfm"], r, j), s["ident64"][64 * r:64 * r + 64, :],
                             start=(j == 0), stop=False)
                    ins = e.matmul(b, fmsl(s["Khm"], r, j),
                                   s["Sall"][64 * r:64 * r + 64, 64 * j:64 * j + 64],
                                   start=False, stop=True)
            return ins
        p.op("tensor", reads=("Vfm", "Khm", "Sall", "c_ident64"), writes=("P5",),
             tag="R0mm")(mmR0)
        p.op("scalar", reads=("P5",), writes=("R",))(
            lambda e: e.activation(out=s["R"][:], in_=P[5][:, 0:512], func=AF.Copy))
        self._levels(ch, sc, 0, 2)

    def scan_tail_b(self, ch, sc):
        p, s, P, cfg = self.p, self.s, self.P, self.cfg
        T = cfg.T
        ts = sc * 64
        def blk(t):
            return [(t[64 * r:64 * r + 64, 64 * j:64 * j + 64], r, j)
                    for j in range(8) for r in range(2)]

        def fmsl(t, r, j, w=64, off=0):
            return t[64 * r:64 * r + 64, j * T + ts + off: j * T + ts + off + w]
        BLn = "BL%d" % (sc % 2)
        self._levels(ch, sc, 2, 6)
        def mmRead(e):
            ins = None
            for r in range(2):
                for j in range(8):
                    b = P[6][64 * r:64 * r + 64, 64 * j:64 * j + 64]
                    e.matmul(b, fmsl(s["Qh"], r, j),
                             s["Sall"][64 * r:64 * r + 64, 64 * j:64 * j + 64],
                             start=(j == 0), stop=False)
                    ins = e.matmul(b, s[BLn][64 * r:64 * r + 64, 64 * j:64 * j + 64],
                                   s["R"][64 * r:64 * r + 64, 64 * j:64 * j + 64],
                                   start=False, stop=True)
            return ins
        p.op("tensor", reads=("Qh", "Sall", BLn, "R"), writes=("P6",), tag="readmm")(mmRead)
        p.op("vector", reads=("P6",), writes=("readstm",))(
            lambda e: e.tensor_copy(s["readstm"][:], P[6][:, 0:512]))
        def mmKtm(e):
            ins = None
            for (b, r, j) in blk(P[4]):
                ins = e.matmul(b, fmsl(s["Kn"], r, j), s["ident64"][64 * r:64 * r + 64, :],
                               start=True, stop=True)
            return ins
        p.op("tensor", reads=("Kn", "c_ident64"), writes=("P4",), tag="Ktmmm")(mmKtm)
        daln = "dall%d" % (sc % 2)
        P4_3 = ap_custom(self.P[4][:], 0, [[512, 128], [64, 8], [1, 64]])
        dall_b = ap_custom(s[daln][:], 0, [[8, 128], [1, 8], [0, 64]])
        Ktmp3 = ap_custom(s["Ktmp"][:], 0, [[512, 128], [64, 8], [1, 64]])
        p.op("vector", reads=("P4", daln), writes=("Ktmp",))(
            lambda e: e.tensor_tensor(Ktmp3, P4_3, dall_b, ALU.mult))
        def mmState(e):
            ins = None
            for (b, r, j) in blk(P[6]):
                ins = e.matmul(b, s["Ktmp"][64 * r:64 * r + 64, 64 * j:64 * j + 64],
                               s["R"][64 * r:64 * r + 64, 64 * j:64 * j + 64],
                               start=True, stop=True)
            return ins
        p.op("tensor", reads=("Ktmp", "R"), writes=("P6",), tag="statemm")(mmState)
        gcn = "gcall%d" % (sc % 2)
        gc_b = ap_custom(s[gcn][:], 0, [[8, 128], [1, 8], [0, 64]])
        Sall3 = ap_custom(s["Sall"][:], 0, [[512, 128], [64, 8], [1, 64]])
        t23 = ap_custom(s["t2"][:], 0, [[512, 128], [64, 8], [1, 64]])
        p.op("vector", reads=("Sall", gcn), writes=("t2",))(
            lambda e: e.tensor_tensor(t23, Sall3, gc_b, ALU.mult))
        p.op("vector", reads=("P6", "t2"), writes=("Sall",))(
            lambda e: e.tensor_add(s["Sall"][:], P[6][:, 0:512], s["t2"][:]))
        def mmRFM(e):
            ins = None
            for (b, r, j) in blk(P[4]):
                ins = e.matmul(b, s["readstm"][64 * r:64 * r + 64, 64 * j:64 * j + 64],
                               s["ident64"][64 * r:64 * r + 64, :], start=True, stop=True)
            return ins
        p.op("tensor", reads=("readstm", "c_ident64"), writes=("P4",), tag="rfmm")(mmRFM)
        P4_3b = ap_custom(self.P[4][:], 0, [[512, 128], [64, 8], [1, 64]])
        rfm3 = ap_custom(s["readsFM"][:], ts, [[8 * T, 128], [T, 8], [1, 64]])
        p.op("vector", reads=("P4",), writes=("readsFM",))(
            lambda e: e.tensor_copy(rfm3, P4_3b))

    def _levels(self, ch, sc, lev0, lev1):
        p, s, P = self.p, self.s, self.P

        def blk(t):
            return [(t[64 * r:64 * r + 64, 64 * j:64 * j + 64], r, j)
                    for j in range(8) for r in range(2)]
        for lev in range(lev0, lev1):
            nl = "NL0" if lev == 0 else ("NLa" if lev % 2 == 1 else "NLb")
            nn = "NN0" if lev == 0 else ("NNa" if lev % 2 == 1 else "NNb")
            nl2 = "NLa" if lev % 2 == 0 else "NLb"
            nn2 = "NNa" if lev % 2 == 0 else "NNb"
            def mmAp(e, nl=nl):
                ins = None
                for (b, r, j) in blk(P[5]):
                    ins = e.matmul(b, s[nl][64 * r:64 * r + 64, 64 * j:64 * j + 64],
                                   s["R"][64 * r:64 * r + 64, 64 * j:64 * j + 64],
                                   start=False, stop=True)
                return ins
            p.op("tensor", reads=(nl, "R"), writes=("P5",), tag="apply%d" % lev)(mmAp)
            if lev < 5:
                def mmSqN(e, nl=nl, nn=nn):
                    ins = None
                    for (b, r, j) in blk(P[4]):
                        ins = e.matmul(b, s[nl][64 * r:64 * r + 64, 64 * j:64 * j + 64],
                                       s[nn][64 * r:64 * r + 64, 64 * j:64 * j + 64],
                                       start=True, stop=True)
                    return ins
                p.op("tensor", reads=(nl, nn), writes=("P4",), tag="sqN%d" % lev)(mmSqN)
                def mmSqL(e, nl=nl, nn=nn):
                    ins = None
                    for (b, r, j) in blk(P[7]):
                        ins = e.matmul(b, s[nn][64 * r:64 * r + 64, 64 * j:64 * j + 64],
                                       s[nl][64 * r:64 * r + 64, 64 * j:64 * j + 64],
                                       start=True, stop=True)
                    return ins
                p.op("tensor", reads=(nl, nn), writes=("P7",), tag="sqL%d" % lev)(mmSqL)
                p.op("vector", reads=("P4",), writes=(nn2,))(
                    lambda e, nn2=nn2: e.tensor_copy(s[nn2][:], P[4][:, 0:512]))
                p.op("scalar", reads=("P7",), writes=(nl2,))(
                    lambda e, nl2=nl2: e.activation(out=s[nl2][:], in_=P[7][:, 0:512],
                                                    func=AF.Copy))
            eng = "vector" if lev % 2 == 0 else "scalar"
            if eng == "vector":
                p.op("vector", reads=("P5",), writes=("R",))(
                    lambda e: e.tensor_copy(s["R"][:], P[5][:, 0:512]))
            else:
                p.op("scalar", reads=("P5",), writes=("R",))(
                    lambda e: e.activation(out=s["R"][:], in_=P[5][:, 0:512], func=AF.Copy))

    def epilogue(self, ch):
        p, s, P, cfg = self.p, self.s, self.P, self.cfg
        T, NTT = cfg.T, cfg.NTT
        xb = s["xbuf%d" % (ch % 2)]
        xr = "xbuf%d" % (ch % 2)
        for fc in range(8):
            p.op("scalar", reads=("readsFM",), writes=("sqbuf",))(
                lambda e, fc=fc: e.activation(out=s["sqbuf"][:, 0:T],
                                              in_=s["readsFM"][:, fc * T:(fc + 1) * T],
                                              func=AF.Square))
            def mmf(e, fc=fc):
                return e.matmul(P[7][0:16, 0:T], s["headmaskC"][:, fc * 16:(fc + 1) * 16],
                                s["sqbuf"][:, 0:T], start=(fc == 0), stop=(fc == 7))
            p.op("tensor", reads=("sqbuf", "c_headmaskC"), writes=("P7",), tag="rmsmm")(mmf)
        p.op("scalar", reads=("P7",), writes=("rmsrow",))(
            lambda e: e.activation(out=s["rmsrow"][:, 0:T], in_=P[7][0:16, 0:T],
                                   func=AF.Sqrt, scale=1.0 / 64, bias=s["biasc"][0:16, 1:2]))
        p.op("vector", reads=("rmsrow",), writes=("r_t",))(
            lambda e: e.reciprocal(s["r_t"][:, 0:T], s["rmsrow"][:, 0:T]))
        p.op("vector", reads=("r_t",), writes=("rmsbf",))(
            lambda e: e.tensor_copy(s["rmsbf"][:, 0:T], s["r_t"][:, 0:T]))
        for fc in range(8):
            p.op("tensor", reads=("rmsbf", "c_mask2w"), writes=("P0",), tag="nbc")(
                lambda e, fc=fc: e.matmul(P[0][:, 0:T], s["mask2w"][:, fc * 128:(fc + 1) * 128],
                                          s["rmsbf"][:, 0:T], start=True, stop=True))
            p.op("vector", reads=("P0", "readsFM"), writes=("normedFM",))(
                lambda e, fc=fc: e.tensor_mul(s["normedFM"][:, fc * T:(fc + 1) * T],
                                              s["readsFM"][:, fc * T:(fc + 1) * T],
                                              P[0][:, 0:T]))
        for kg in range(4):
            ob = "wbuf_out" + ("A" if kg % 2 == 0 else "B")
            p.op("sync", reads=("scr_outT",), writes=(ob,), dma=True, tag="wout")(
                lambda e, kg=kg, ob=ob: e.dma_start(
                    out=s[ob][:],
                    in_=ap_custom(self.scr["outT"], kg * 256 * D,
                                  [[D, 128], [128 * D, 2], [1, D]])))
            def mmf(e, kg=kg, ob=ob):
                ins = None
                for kk in range(2):
                    k = kg * 2 + kk
                    for tt in range(NTT):
                        for hf in range(2):
                            ins = e.matmul(P[3 + tt * 2 + hf][:, 0:512],
                                           s["normedFM"][:, k * T + tt * 128: k * T + (tt + 1) * 128],
                                           s[ob][:, kk * 1024 + hf * 512: kk * 1024 + (hf + 1) * 512],
                                           start=(k == 0), stop=False)
                return ins
            p.op("tensor", reads=(ob, "normedFM"),
                 writes=("P3", "P4", "P5", "P6"), tag="outmm")(mmf)
        def mmb(e):
            ins = None
            for tt in range(NTT):
                for hf in range(2):
                    ins = e.matmul(P[3 + tt * 2 + hf][:, 0:512],
                                   s["ones128row_bf"][0:1, :],
                                   s["out_b_bf"][0:1, hf * 512:(hf + 1) * 512],
                                   start=False, stop=True)
            return ins
        p.op("tensor", reads=("c_out_b", "c_ones128row_bf"),
             writes=("P3", "P4", "P5", "P6"), tag="outbias")(mmb)
        for tt in range(NTT):
            for hf in range(2):
                pi = 3 + tt * 2 + hf
                sl = (tt * 1024 + hf * 512, tt * 1024 + (hf + 1) * 512)
                p.op("vector", reads=("P%d" % pi, "outgate"), writes=("t1",))(
                    lambda e, pi=pi, sl=sl: e.tensor_mul(
                        s["t1"][:], P[pi][:, 0:512], s["outgate"][:, sl[0]:sl[1]]))
                p.op("vector", reads=("t1", xr), writes=(xr,))(
                    lambda e, sl=sl: e.tensor_add(xb[:, sl[0]:sl[1]],
                                                  xb[:, sl[0]:sl[1]], s["t1"][:, 0:512]))

    def ffn(self, ch):
        p, s, P, cfg = self.p, self.s, self.P, self.cfg
        T, NTT = cfg.T, cfg.NTT
        xb = s["xbuf%d" % (ch % 2)]
        xr = "xbuf%d" % (ch % 2)
        self._ln(xb, xr, s["h2"], "h2", statcol=16)
        for tt in range(NTT):
            sl = (tt * 1024, (tt + 1) * 1024)
            c0 = tt * 4
            p.op("vector", reads=("h2",), writes=("fstat",))(
                lambda e, sl=sl, c0=c0: e.tensor_reduce(
                    s["fstat"][:, c0:c0 + 1], s["h2"][:, sl[0]:sl[1]], AX.X, ALU.max,
                    apply_absolute_value=True))
            p.op("gpsimd", reads=("fstat",), writes=("fstat",))(
                lambda e, c0=c0: e.tensor_scalar(s["fstat"][:, c0 + 1:c0 + 2],
                                                 s["fstat"][:, c0:c0 + 1],
                                                 1e-5, None, ALU.max))
            p.op("vector", reads=("fstat",), writes=("fstat",))(
                lambda e, c0=c0: e.reciprocal(s["fstat"][:, c0 + 2:c0 + 3],
                                              s["fstat"][:, c0 + 1:c0 + 2]))
            p.op("scalar", reads=("fstat",), writes=("fstat",))(
                lambda e, c0=c0: e.activation(out=s["fstat"][:, c0 + 3:c0 + 4],
                                              in_=s["fstat"][:, c0 + 2:c0 + 3],
                                              func=AF.Copy, scale=127.0))
            p.op("vector", reads=("fstat", "wsc"), writes=("fstat",))(
                lambda e, c0=c0, tt=tt: e.scalar_tensor_tensor(
                    s["fstat"][:, 16 + tt:17 + tt], s["fstat"][:, c0 + 1:c0 + 2],
                    1.0 / 127, s["wsc"][:, 0:1], ALU.mult, ALU.mult))
            p.op("vector", reads=("fstat", "wsc"), writes=("fstat",))(
                lambda e, c0=c0, tt=tt: e.scalar_tensor_tensor(
                    s["fstat"][:, 18 + tt:19 + tt], s["fstat"][:, c0 + 1:c0 + 2],
                    1.0 / 127, s["wsc"][:, 1:2], ALU.mult, ALU.mult))
            p.op("vector", reads=("h2", "fstat"), writes=("h2",))(
                lambda e, sl=sl, c0=c0: e.tensor_scalar(
                    s["h2"][:, sl[0]:sl[1]], s["h2"][:, sl[0]:sl[1]],
                    s["fstat"][:, c0 + 3:c0 + 4], 127.0, ALU.mult, ALU.min))
            p.op("vector", reads=("h2",), writes=("h2",))(
                lambda e, sl=sl: e.tensor_scalar(s["h2"][:, sl[0]:sl[1]],
                                                 s["h2"][:, sl[0]:sl[1]],
                                                 -128.0, None, ALU.max))
            p.op("scalar", reads=("h2",), writes=("istage",))(
                lambda e, sl=sl: e.activation(out=s["istage"][:, 0:1024],
                                              in_=s["h2"][:, sl[0]:sl[1]], func=AF.Copy))
            p.op("vector", reads=("istage",), writes=("h2q",))(
                lambda e, sl=sl: e.tensor_copy(s["h2q"][:, sl[0]:sl[1]],
                                               s["istage"][:, 0:1024]))
        for tt in range(NTT):
            def mmf(e, tt=tt):
                ins = None
                for g in range(2):
                    for b in range(4):
                        fc = g * 4 + b
                        ins = e.matmul(P[g][:, b * 128:(b + 1) * 128],
                                       s["h2q"][:, tt * 1024 + fc * 128: tt * 1024 + (fc + 1) * 128],
                                       s["ident128"][:], start=True, stop=True)
                return ins
            p.op("tensor", reads=("h2q", "c_ident128"), writes=("P0", "P1"), tag="h2qT")(mmf)
            for g in range(2):
                out_ap = ap_custom(s["h2qFM"][:], tt * 128 + g * 4 * T,
                                   [[8 * T, 128], [T, 4], [1, 128]])
                if g == 0:
                    p.op("vector", reads=("P0",), writes=("h2qFM",))(
                        lambda e, o=out_ap: e.tensor_copy(o, P[0][:, 0:512]))
                else:
                    p.op("scalar", reads=("P1",), writes=("h2qFM",))(
                        lambda e, o=out_ap: e.activation(out=o, in_=P[1][:, 0:512],
                                                         func=AF.Copy))
        if ch == getattr(self, "tap_chunk", 0):
            self.tap("h2q", lambda: s["h2q"][:], ["h2q"])
            self.tap("h2qFM", lambda: s["h2qFM"][:], ["h2qFM"])
            self.tap("fstat", lambda: s["fstat"][:], ["fstat"])
        for ic in range(8):
            for nm, b0, b1 in [("wgT", "wbuf_g0", "wbuf_g1"), ("wuT", "wbuf_u0", "wbuf_u1")]:
                for hh, bh in ((0, b0), (1, b1)):
                    p.op("sync", reads=("scr_" + nm,), writes=(bh,), dma=True,
                         tag="wld_" + nm)(
                        lambda e, ic=ic, nm=nm, bh=bh, hh=hh: e.dma_start(
                            out=s[bh][:],
                            in_=ap_custom(self.scr[nm], hh * 4 * 128 * INNER + ic * 512,
                                          [[INNER, 128], [128 * INNER, 4], [1, 512]])))
            pg = 0 if ic % 2 == 0 else 2
            pu = 1 if ic % 2 == 0 else 3
            for tt in range(NTT):
                for hh, bh in ((0, "wbuf_g0"), (1, "wbuf_g1")):
                    def mmg(e, tt=tt, hh=hh, bh=bh, pg=pg):
                        ins = None
                        for kk in range(4):
                            k = hh * 4 + kk
                            ins = e.matmul(P[pg][:, 0:512],
                                           s["h2qFM"][:, k * T + tt * 128: k * T + (tt + 1) * 128],
                                           s[bh][:, kk * 512:(kk + 1) * 512],
                                           start=(k == 0), stop=(k == 7))
                        return ins
                    p.op("tensor", reads=(bh, "h2qFM"), writes=("P%d" % pg,), tag="gmm")(mmg)
                for hh, bh in ((0, "wbuf_u0"), (1, "wbuf_u1")):
                    def mmu(e, tt=tt, hh=hh, bh=bh, pu=pu):
                        ins = None
                        for kk in range(4):
                            k = hh * 4 + kk
                            ins = e.matmul(P[pu][:, 0:512],
                                           s["h2qFM"][:, k * T + tt * 128: k * T + (tt + 1) * 128],
                                           s[bh][:, kk * 512:(kk + 1) * 512],
                                           start=(k == 0), stop=(k == 7))
                        return ins
                    p.op("tensor", reads=(bh, "h2qFM"), writes=("P%d" % pu,), tag="umm")(mmu)
                p.op("scalar", reads=("P%d" % pg, "fstat"), writes=("t1",))(
                    lambda e, tt=tt, pg=pg: e.activation(out=s["t1"][:], in_=P[pg][:, 0:512],
                                                  func=AF.Silu,
                                                  scale=s["fstat"][:, 16 + tt:17 + tt]))
                p.op("vector", reads=("t1", "P%d" % pu, "fstat"), writes=("y",))(
                    lambda e, ic=ic, tt=tt, pu=pu: e.scalar_tensor_tensor(
                        s["y"][:, tt * 4096 + ic * 512: tt * 4096 + (ic + 1) * 512],
                        P[pu][:, 0:512], s["fstat"][:, 18 + tt:19 + tt],
                        s["t1"][:, 0:512], ALU.mult, ALU.mult))
                p.op("vector", reads=("y",), writes=("ymax",))(
                    lambda e, ic=ic, tt=tt: e.tensor_reduce(
                        s["fstat"][:, 22 + tt:23 + tt],
                        s["y"][:, tt * 4096 + ic * 512: tt * 4096 + (ic + 1) * 512],
                        AX.X, ALU.max, apply_absolute_value=True))
                if ic > 0:
                    p.op("gpsimd", reads=("ymax", "fstat"), writes=("fstat",))(
                        lambda e, tt=tt, c0=8 + tt * 4: e.tensor_scalar(
                            s["fstat"][:, c0:c0 + 1], s["fstat"][:, 22 + tt:23 + tt],
                            s["fstat"][:, c0:c0 + 1], None, ALU.max))
                else:
                    p.op("gpsimd", reads=("ymax",), writes=("fstat",))(
                        lambda e, tt=tt, c0=8 + tt * 4: e.tensor_scalar(
                            s["fstat"][:, c0:c0 + 1], s["fstat"][:, 22 + tt:23 + tt],
                            0.0, None, ALU.max))
        if ch == getattr(self, "tap_chunk", 0):
            self.tap("ytap", lambda: s["y"][:, 0:4096], ["y"])
        for tt in range(NTT):
            sl = (tt * 4096, (tt + 1) * 4096)
            c0 = 8 + tt * 4
            p.op("gpsimd", reads=("fstat",), writes=("fstat",))(
                lambda e, c0=c0: e.tensor_scalar(s["fstat"][:, c0 + 1:c0 + 2],
                                                 s["fstat"][:, c0:c0 + 1],
                                                 1e-5, None, ALU.max))
            p.op("vector", reads=("fstat",), writes=("fstat",))(
                lambda e, c0=c0: e.reciprocal(s["fstat"][:, c0 + 2:c0 + 3],
                                              s["fstat"][:, c0 + 1:c0 + 2]))
            p.op("scalar", reads=("fstat",), writes=("fstat",))(
                lambda e, c0=c0: e.activation(out=s["fstat"][:, c0 + 3:c0 + 4],
                                              in_=s["fstat"][:, c0 + 2:c0 + 3],
                                              func=AF.Copy, scale=127.0))
            p.op("vector", reads=("fstat", "wsc"), writes=("fstat",))(
                lambda e, c0=c0, tt=tt: e.scalar_tensor_tensor(
                    s["fstat"][:, 20 + tt:21 + tt], s["fstat"][:, c0 + 1:c0 + 2],
                    1.0 / 127, s["wsc"][:, 2:3], ALU.mult, ALU.mult))
            p.op("vector", reads=("y", "fstat"), writes=("y",))(
                lambda e, sl=sl, c0=c0: e.tensor_scalar(
                    s["y"][:, sl[0]:sl[1]], s["y"][:, sl[0]:sl[1]],
                    s["fstat"][:, c0 + 3:c0 + 4], 127.0, ALU.mult, ALU.min))
            p.op("vector", reads=("y",), writes=("y",))(
                lambda e, sl=sl: e.tensor_scalar(s["y"][:, sl[0]:sl[1]],
                                                 s["y"][:, sl[0]:sl[1]],
                                                 -128.0, None, ALU.max))
            for hh in range(2):
                p.op("scalar", reads=("y",), writes=("istage",))(
                    lambda e, sl=sl, hh=hh: e.activation(
                        out=s["istage"][:, 0:2048],
                        in_=s["y"][:, sl[0] + hh * 2048: sl[0] + (hh + 1) * 2048],
                        func=AF.Copy))
                p.op("vector", reads=("istage",), writes=("yq",))(
                    lambda e, sl=sl, hh=hh: e.tensor_copy(
                        s["yq"][:, sl[0] + hh * 2048: sl[0] + (hh + 1) * 2048],
                        s["istage"][:, 0:2048]))
        if ch == getattr(self, "tap_chunk", 0):
            self.tap("yq", lambda: s["yq"][:], ["yq"])
        for tt in range(NTT):
            for icg in range(8):
                def mmf(e, tt=tt, icg=icg):
                    ins = None
                    for b in range(4):
                        ic = icg * 4 + b
                        ins = e.matmul(P[2][:, b * 128:(b + 1) * 128],
                                       s["yq"][:, tt * 4096 + ic * 128: tt * 4096 + (ic + 1) * 128],
                                       s["ident128"][:], start=True, stop=True)
                    return ins
                p.op("tensor", reads=("yq", "c_ident128"), writes=("P2",), tag="yqT")(mmf)
                out_ap = ap_custom(s["yqFM"][:], (icg * 4) * T + tt * 128,
                                   [[32 * T, 128], [T, 4], [1, 128]])
                if icg % 2 == 0:
                    p.op("vector", reads=("P2",), writes=("yqFM",))(
                        lambda e, o=out_ap: e.tensor_copy(o, P[2][:, 0:512]))
                else:
                    p.op("scalar", reads=("P2",), writes=("yqFM",))(
                        lambda e, o=out_ap: e.activation(out=o, in_=P[2][:, 0:512],
                                                         func=AF.Copy))
        for icg in range(16):
            db = "wbuf_d" + ("A" if icg % 2 == 0 else "B")
            p.op("sync", reads=("scr_wdT",), writes=(db,), dma=True, tag="wd")(
                lambda e, icg=icg, db=db: e.dma_start(
                    out=s[db][:],
                    in_=ap_custom(self.scr["wdT"], icg * 256 * D,
                                  [[D, 128], [128 * D, 2], [1, D]])))
            def mmf(e, icg=icg, db=db):
                ins = None
                for kk in range(2):
                    ic = icg * 2 + kk
                    for tt in range(NTT):
                        for hf in range(2):
                            ins = e.matmul(P[3 + tt * 2 + hf][:, 0:512],
                                           s["yqFM"][:, ic * T + tt * 128: ic * T + (tt + 1) * 128],
                                           s[db][:, kk * 1024 + hf * 512: kk * 1024 + (hf + 1) * 512],
                                           start=(ic == 0), stop=(ic == 31))
                return ins
            p.op("tensor", reads=(db, "yqFM"),
                 writes=("P3", "P4", "P5", "P6"), tag="dmm")(mmf)
        for tt in range(NTT):
            for hf in range(2):
                pi = 3 + tt * 2 + hf
                sl = (tt * 1024 + hf * 512, tt * 1024 + (hf + 1) * 512)
                p.op("vector", reads=("P%d" % pi, xr, "fstat"), writes=(xr,))(
                    lambda e, pi=pi, sl=sl, tt=tt: e.scalar_tensor_tensor(
                        xb[:, sl[0]:sl[1]], P[pi][:, 0:512],
                        s["fstat"][:, 20 + tt:21 + tt],
                        xb[:, sl[0]:sl[1]], ALU.mult, ALU.add))
        in_ap = ap_custom(xb[:], 0, [[2 * 1024, 128], [1024, NTT], [1, 1024]])
        out_ap = ap_custom(self.dram_out, ch * self.cfg.T * D,
                           [[D, 128], [128 * D, NTT], [1, D]])
        p.op("sync", reads=(xr,), writes=("xout",), dma=True, tag="xst")(
            lambda e, i=in_ap, o=out_ap: e.dma_start(out=o, in_=i))

    def build(self, with_prep=True, nch=None):
        p, s = self.p, self.s
        self.load_consts()
        if with_prep:
            self.prep_weights_attn()
        p.op("vector", writes=("Salla", "Sallb"))(lambda e: e.memset(s["Sall"][:], 0.0))
        nch = self.cfg.NCH if nch is None else nch
        for ch in range(nch):
            self.p.phase = "front"
            self.frontend(ch)
            self.run_taps_frontend(ch)
            self.p.phase = "scanF"
            self.scan_front(ch, 0)
            for sc in range(self.cfg.NSC):
                self.p.phase = "scanTA"
                self.scan_tail_a(ch, sc)
                if sc + 1 < self.cfg.NSC:
                    self.p.phase = "scanF"
                    self.scan_front(ch, sc + 1)
                self.p.phase = "scanTB"
                self.scan_tail_b(ch, sc)
            self.run_taps_scan(ch)
            if ch == 0 and with_prep:
                self.p.phase = "prepF"
                self.prep_weights_ffn()
            self.p.phase = "epi"
            self.epilogue(ch)
            self.run_taps_epi(ch)
            self.p.phase = "ffn"
            self.ffn(ch)
        p.op("sync", reads=("xout",), tag="fence")(lambda e: e.nop())
        with self.nc.Block() as block:
            p.finalize(self.nc, block)

    def run_taps_frontend(self, ch):
        if ch != getattr(self, "tap_chunk", 0):
            return
        s, cfg = self.s, self.cfg
        for scrn in ["scr_ctrlT", "scr_qkvT", "scr_wgT"]:
            if scrn in self.tap_outs:
                nm = scrn[4:] + "T" if False else scrn.replace("scr_", "")
                self.p.op("gpsimd", reads=(scrn,), dma=True, tag="tap" + scrn)(
                    lambda e, o=self.tap_outs[scrn], i=self.scr[nm]: e.dma_start(out=o, in_=i))

        self.tap("qkvFM", lambda: s["qkvFM"][:], ["qkvFM"])
        self.tap("convq", lambda: s["convq"][:], ["convq"])
        self.tap("Vfm", lambda: s["Vfm"][:], ["Vfm"])
        self.tap("Qn", lambda: s["Qn"][:], ["Qn"])
        self.tap("Kn", lambda: s["Kn"][:], ["Kn"])
        self.tap("Qh", lambda: s["Qh"][:], ["Qh"])
        self.tap("Khm", lambda: s["Khm"][:], ["Khm"])
        self.tap("ab", lambda: s["ab"][:], ["ab"])
        self.tap("r_L", lambda: s["r_L"][:], ["r_L"])
        self.tap("r_beta", lambda: s["r_beta"][:], ["r_beta"])
        self.tap("r_eL", lambda: s["r_eL"][:], ["r_eL"])
        self.tap("outgate", lambda: s["outgate"][:], ["outgate"])

    def run_taps_scan(self, ch):
        if ch != getattr(self, "tap_chunk", 0):
            return
        s = self.s
        self.tap("readsFM", lambda: s["readsFM"][:], ["readsFM"])
        self.tap("Sall", lambda: s["Sall"][:], ["Sall"])

    def run_taps_epi(self, ch):
        if ch != getattr(self, "tap_chunk", 0):
            return
        s = self.s
        self.tap("normedFM", lambda: s["normedFM"][:], ["normedFM"])
        self.tap("x1", lambda: s["xbuf%d" % (ch % 2)][:], ["xbuf%d" % (ch % 2)])


def kernel(L0_attn_hidden, L0_attn_qkv_w, L0_attn_ctrl_w, L0_attn_ctrl_b,
           L0_attn_convq_w, L0_attn_convk_w, L0_attn_convv_w, L0_attn_norm_w,
           L0_attn_out_w, L0_attn_out_b, L0_ffn_w_gate, L0_ffn_w_up,
           L0_ffn_w_down):
    B, S, Dm = L0_attn_hidden.shape
    assert (B, S, Dm) == (8, 4096, 1024)
    cfg = Cfg(S=S, T=256)
    nc = bass.Bass("TRN2", target_bir_lowering=False)
    b = Builder(nc, cfg, taps=[])
    b.build()
    norm_w_t = np.ascontiguousarray(np.tile(np.asarray(L0_attn_norm_w, np.float32), (16, 16)))
    shared = {
        "qkv_w": np.ascontiguousarray(L0_attn_qkv_w, dtype=np.float32),
        "ctrl_w": np.ascontiguousarray(L0_attn_ctrl_w, dtype=np.float32),
        "ctrl_b": np.ascontiguousarray(L0_attn_ctrl_b, dtype=np.float32),
        "convq_w": np.ascontiguousarray(L0_attn_convq_w, dtype=np.float32),
        "convk_w": np.ascontiguousarray(L0_attn_convk_w, dtype=np.float32),
        "convv_w": np.ascontiguousarray(L0_attn_convv_w, dtype=np.float32),
        "norm_w_t": norm_w_t,
        "out_w": np.ascontiguousarray(L0_attn_out_w, dtype=np.float32),
        "out_b": np.ascontiguousarray(L0_attn_out_b, dtype=np.float32),
        "w_gate": np.ascontiguousarray(L0_ffn_w_gate, dtype=np.float32),
        "w_up": np.ascontiguousarray(L0_ffn_w_up, dtype=np.float32),
        "w_down": np.ascontiguousarray(L0_ffn_w_down, dtype=np.float32),
    }
    xin = np.ascontiguousarray(L0_attn_hidden, dtype=np.float32)
    in_maps = [dict(shared, x=xin[i]) for i in range(8)]
    res = run_bass_kernel_spmd(nc, in_maps, core_ids=list(range(8)))
    out = np.stack([np.asarray(res.results[i]["out"], np.float32) for i in range(8)], axis=0)
    return out


# version 11
# speedup vs baseline: 1.2469x; 1.0363x over previous
import numpy as np
import ml_dtypes
import concourse.bass as bass
import concourse.mybir as mybir
from concourse.bass_utils import run_bass_kernel_spmd

BF = ml_dtypes.bfloat16

"""Support code for the mk_seq_preln_d1 deltanet megakernel (TRN2, raw bass).

Layout conventions:
- TM (token-major): [128 tokens on partitions x features on free]
- FM (feature-major): [128 features-within-chunk on partitions x (fc, t) on free]
  global feature index d = fc*128 + p ; head h = d // 64 = 2*fc + p//64
- scan all-heads tiles: [128 x 512] fp32/bf16 ; head h=2j+r at partition block
  r=h%2 (rows 64r..64r+63), free block j=h//2 (cols 64j..64j+63)
- weights in HBM scratch, bf16, transposed to [in_feat x out_feat]
"""
import numpy as np
import ml_dtypes
import concourse.bass as bass
import concourse.mybir as mybir

F32 = mybir.dt.float32
BF16 = mybir.dt.bfloat16
I32 = mybir.dt.int32
AF = mybir.ActivationFunctionType
ALU = mybir.AluOpType
AX = mybir.AxisListType

D = 1024
H = 16
HD = 64
INNER = 4096
CTRL = 2 * H + D
QKV = 3 * D
C = 64


class Cfg:
    def __init__(self, S=4096, T=256):
        assert S % T == 0 and T % 128 == 0
        self.S = S
        self.T = T
        self.NCH = S // T
        self.NTT = T // 128
        self.NSC = T // C


ENGINES = ("sync", "tensor", "vector", "scalar", "gpsimd")


class Op:
    __slots__ = ("idx", "eng", "fn", "reads", "writes", "dma", "tag",
                 "need_inc", "tick", "waits", "_deps", "dkey", "phase")

    def __init__(self, idx, eng, fn, reads, writes, dma, tag):
        self.idx = idx
        self.eng = eng
        self.fn = fn
        self.reads = reads
        self.writes = writes
        self.dma = dma
        self.tag = tag
        self.need_inc = False
        self.tick = 0
        self.waits = []
        self.dkey = None


class Prog:
    def __init__(self):
        self.ops = []
        self.phase = "init"
        self.last_writer = {}
        self.readers = {}

    def op(self, eng, reads=(), writes=(), dma=False, tag=""):
        def deco(fn):
            o = Op(len(self.ops), eng, fn, expand_couples(reads), expand_couples(writes), dma, tag)
            o.phase = self.phase
            if dma:
                o.dkey = "D_" + (o.writes[0] if o.writes else (o.reads[0] if o.reads else "misc"))
            deps = []
            for r in o.reads:
                w = self.last_writer.get(r)
                if w is not None:
                    deps.append(w)
            for w_ in o.writes:
                lw = self.last_writer.get(w_)
                if lw is not None:
                    deps.append(lw)
                deps.extend(self.readers.get(w_, ()))
            for d in deps:
                if d.eng != o.eng:
                    d.need_inc = True
            o._deps = [d for d in deps if d.eng != o.eng]
            for r in o.reads:
                self.readers.setdefault(r, []).append(o)
            for w_ in o.writes:
                self.last_writer[w_] = o
                self.readers[w_] = []
            self.ops.append(o)
            return o
        return deco

    def finalize(self, nc, block):
        for o in self.ops:
            if o.dma:
                o.need_inc = True
        tick = {}
        for o in self.ops:
            key = o.dkey if o.dkey else o.eng
            if o.need_inc:
                tick[key] = tick.get(key, 0) + (16 if o.dma else 1)
            o.tick = tick.get(key, 0)
        water = {}
        for o in self.ops:
            want = {}
            for d in o._deps:
                assert d.need_inc and d.tick > 0
                key = d.dkey if d.dkey else d.eng
                want[key] = max(want.get(key, 0), d.tick)
            for k2, v in want.items():
                wk = (o.eng, k2)
                if v > water.get(wk, 0):
                    o.waits.append((k2, v))
                    water[wk] = v
        n_inc = sum(1 for o in self.ops if o.need_inc)
        n_wait = sum(len(o.waits) for o in self.ops)
        by_eng = {e: sum(1 for o in self.ops if o.eng == e) for e in ENGINES}
        print(f"[prog] ops={len(self.ops)} incs={n_inc} waits={n_wait} by_eng={by_eng}")
        keys = set(e for e in ENGINES)
        for o in self.ops:
            if o.dkey:
                keys.add(o.dkey)
        sems = {k: nc.alloc_semaphore(f"sem_{k}"[:28]) for k in sorted(keys)}
        self.sems = sems

        def run_engine(ename):
            def body(eng):
                for o in self.ops:
                    if o.eng != ename:
                        continue
                    for (pe, v) in o.waits:
                        eng.wait_ge(sems[pe], v)
                    with nc.named_scope(o.phase):
                        ins = o.fn(eng)
                    if o.need_inc:
                        assert ins is not None, f"op {o.idx} {o.tag} returned no instruction"
                        key = o.dkey if o.dkey else ename
                        ins.then_inc(sems[key], 16 if o.dma else 1)
            return body

        for ename in ENGINES:
            getattr(block, ename)(run_engine(ename))


def make_consts():
    c = {}
    c["ident64"] = np.vstack([np.eye(64), np.eye(64)]).astype(ml_dtypes.bfloat16)
    c["ident128"] = np.eye(128, dtype=ml_dtypes.bfloat16)
    c["ones64"] = np.ones((1, 64), dtype=ml_dtypes.bfloat16)
    c["ones128row"] = np.ones((1, 128), dtype=ml_dtypes.bfloat16)
    c["onescol128"] = np.ones((128, 1), dtype=np.float32)
    i_idx = np.arange(64)
    negstriu = -(i_idx[:, None] < i_idx[None, :]).astype(np.float32)
    triuincl = (i_idx[:, None] <= i_idx[None, :]).astype(np.float32)
    def tile_all(m):
        out = np.zeros((128, 512), np.float32)
        for r in range(2):
            for j in range(8):
                out[64*r:64*r+64, 64*j:64*j+64] = m
        return out
    c["negstriu"] = tile_all(negstriu).astype(ml_dtypes.bfloat16)
    c["triuincl"] = tile_all(triuincl).astype(ml_dtypes.bfloat16)
    hm = np.zeros((128, 128), np.float32)
    for fc in range(8):
        for p in range(128):
            hm[p, fc*16 + 2*fc + p//64] = 1.0
    c["headmaskC"] = hm.astype(ml_dtypes.bfloat16)
    m2 = np.zeros((16, 1024), np.float32)
    for h in range(16):
        fc, r = h // 2, h % 2
        m2[h, fc*128 + 64*r: fc*128 + 64*r + 64] = 1.0
    mp = np.zeros((64, 1024), np.float32)
    mp[0:16] = m2
    mp[32:48] = -m2
    c["mask2pack"] = mp.astype(ml_dtypes.bfloat16)
    c["mask2_f32"] = m2.astype(np.float32)
    m2s = np.zeros((16, 128), np.float32)
    for h in range(16):
        m2s[h, 64*(h % 2): 64*(h % 2)+64] = 1.0
    c["mask2s_f32"] = m2s.astype(np.float32)
    c["mask2s_bf"] = m2s.astype(ml_dtypes.bfloat16)
    hm8 = np.zeros((16, 512), np.float32)
    for h in range(16):
        j = h // 2
        hm8[h, 64*j:64*j+64] = 1.0
    c["headmask8_f32"] = hm8
    c["headmask8_bf"] = hm8.astype(ml_dtypes.bfloat16)
    hmj = np.zeros((16, 8), np.float32)
    for h in range(16):
        hmj[h, h//2] = 1.0
    c["headmaskj"] = hmj
    return c


COUPLES = {
    "y": ("hid", "hidFM", "qkvFM", "convq", "convk", "Vfm"),
    "outgate": ("hid", "hidFM"),
    "istage": ("Qn", "Qh"),
    "yqFM": ("Qn", "Qh", "Kn", "Khm"),
    "h2": ("ab", "r_siga", "r_logb", "r_beta", "r_L", "r_negL", "r_eL", "r_invq"),
    "prep": ("convq",),
    "prepbf": ("convq",),
    "prepstage": ("convk",),
    "prepsums": ("convk",),
    "normw_f32": ("qkvFM",),
    "wbuf_outA": ("wbuf_qkvA", "wbuf_g0"),
    "wbuf_outB": ("wbuf_qkvB", "wbuf_g1"),
    "wbuf_g0": ("wbuf_qkvA",),
    "wbuf_g1": ("wbuf_qkvB",),
    "wbuf_u0": ("wbuf_ctrlA", "wbuf_ctrlB"),
    "wbuf_u1": ("readsFM",),
    "wbuf_dA": ("readsFM",),
    "wbuf_dB": ("normedFM",),
}


def expand_couples(res):
    out = list(res)
    for r in res:
        out.extend(COUPLES.get(r, ()))
    return tuple(dict.fromkeys(out))


"""Main megakernel builder. See kb.py for conventions.

Phases per token chunk (T tokens, NTT=T/128 token tiles, NSC=T/64 scan chunks):
  F: x load, LN1, hidden->FM, qkv mm, ctrl mm, conv mm, l2norm, scaled variants
  S: per scan chunk: WY delta rule (all 16 heads batched in [128x512] tiles)
  E: RMS head-norm, out proj (token-major out), gate, residual -> x1
  FF: LN2, act_quant, gate/up mm, silu*up, act_quant, down mm, residual -> x2, store
"""


def ap_custom(t_ap, offset, dims):
    return bass.AP(tensor=t_ap.tensor, offset=offset, ap=[list(d) for d in dims])


class Builder:
    def __init__(self, nc, cfg: Cfg, taps=None):
        self.nc = nc
        self.cfg = cfg
        self.p = Prog()
        self.taps = taps or []
        self.tap_outs = {}
        self.consts_np = make_consts()
        self._alloc()

    def _alloc(self):
        nc, cfg = self.nc, self.cfg
        T = cfg.T
        self.dram_in = {}
        for name, shape in [
            ("x", [cfg.S, D]), ("qkv_w", [QKV, D]), ("ctrl_w", [CTRL, D]),
            ("ctrl_b", [CTRL]), ("convq_w", [D, 4]), ("convk_w", [D, 4]),
            ("convv_w", [D, 4]), ("norm_w_t", [16, D]), ("out_w", [D, D]),
            ("out_b", [D]), ("w_gate", [INNER, D]), ("w_up", [INNER, D]),
            ("w_down", [D, INNER]),
        ]:
            self.dram_in[name] = nc.dram_tensor(name, shape, F32, kind="ExternalInput").ap()
        self.dram_out = nc.dram_tensor("out", [cfg.S, D], F32, kind="ExternalOutput").ap()
        self.scr = {}
        for name, shape in [("qkvT", [D, QKV]), ("ctrlT", [D, CTRL]),
                            ("outT", [D, D]), ("wgT", [D, INNER]),
                            ("wuT", [D, INNER]), ("wdT", [INNER, D])]:
            self.scr[name] = nc.dram_tensor("scr_" + name, shape, BF16, kind="Internal").ap()
        self.cst = {}
        for k, v in self.consts_np.items():
            self.cst[k] = nc.inline_tensor(np.ascontiguousarray(v), "c_" + k).ap()
        self.cst["onesT_bf"] = nc.inline_tensor(np.ones((1, T), BF), "c_onesT").ap()
        self.cst["ones64_f32"] = nc.inline_tensor(np.ones((1, 64), np.float32), "c_ones64f").ap()
        self.cst["ones128row_f32"] = nc.inline_tensor(np.ones((1, 128), np.float32), "c_o128rf").ap()
        self.cst["onescol128_f32"] = nc.inline_tensor(np.ones((128, 1), np.float32), "c_oc128f").ap()
        self.cst["onesf_16x64"] = nc.inline_tensor(np.ones((16, 64), np.float32), "c_of1664").ap()
        self.cst["biasc"] = nc.inline_tensor(
            np.tile(np.array([[1e-5, 1e-6, 0.01, 0.0]], np.float32), (128, 1)), "c_biasc").ap()

        A = nc.alloc_sbuf_tensor
        s = {}
        for nm, shape, dt in [
            ("ident64", [128, 64], BF16), ("ident128", [128, 128], BF16),
            ("negstriu", [128, 512], BF16), ("triuincl", [128, 512], BF16),
            ("headmaskC", [128, 128], BF16), ("mask2pack", [64, 1024], BF16),
            ("mask2s_f32", [16, 128], F32),
            ("mask2s_bf", [16, 128], BF16), ("headmask8_f32", [16, 512], F32),
            ("headmask8_bf", [16, 512], BF16), ("headmaskj", [16, 8], F32),
            ("onesT_bf", [1, T], BF16), ("ones64_bf", [1, 64], BF16),
            ("ones64_f32", [1, 64], F32), ("ones128row_bf", [1, 128], BF16),
            ("ones128row_f32", [1, 128], F32), ("onescol128_f32", [128, 1], F32),
            ("onesf_16x64", [16, 64], F32), ("biasc", [128, 4], F32),
            ("diags", [128, 96 * 128], BF16),
            ("ctrl_b_bf", [1, CTRL], BF16), ("out_b_bf", [1, D], BF16),
            ("mask2w", [16, 1024], BF16),
            ("cw", [128, 96], F32),
            ("xbuf0", [128, 2 * 1024], F32), ("xbuf1", [128, 2 * 1024], F32),
            ("stats", [128, 32], F32), ("stat1", [1, 64], F32),
            ("hid", [128, 2 * 1024], BF16), ("hidFM", [128, 8 * T], BF16),
            ("qkvFM", [128, 24 * (T + 3)], BF16),
            ("convq", [128, 8 * T], F32), ("convk", [128, 8 * T], F32),
            ("Vfm", [128, 8 * T], BF16), ("Qn", [128, 8 * T], BF16),
            ("Qh", [128, 8 * T], BF16), ("Kn", [128, 8 * T], BF16),
            ("Khm", [128, 8 * T], BF16),
            ("ab", [64, T], F32),
            ("r_siga", [16, T], F32), ("r_logb", [16, T], F32),
            ("r_beta", [16, T], F32), ("r_L", [16, T], F32),
            ("r_negL", [16, T], F32), ("r_eL", [16, T], F32),
            ("r_invq", [16, T], F32), ("r_invk", [16, T], F32),
            ("r_t", [16, T], F32),
            ("rb_qn", [16, T], BF16), ("rb_qh", [16, T], BF16),
            ("rb_kn", [16, T], BF16), ("rb_khm", [16, T], BF16),
            ("lhsE", [16, 128], F32), ("lhsB", [16, 128], BF16),
            ("E", [128, 512], F32), ("bpartU", [128, 512], F32),
            ("bmask_t", [128, 512], BF16), ("bmask_B", [128, 512], BF16),
            ("NL0", [128, 512], BF16), ("NLa", [128, 512], BF16),
            ("NLb", [128, 512], BF16),
            ("NN0", [128, 512], BF16), ("NNa", [128, 512], BF16),
            ("NNb", [128, 512], BF16),
            ("BL0", [128, 512], BF16), ("BL1", [128, 512], BF16),
            ("R", [128, 512], BF16),
            ("readstm", [128, 512], BF16), ("Sall", [128, 512], BF16),
            ("Ktmp", [128, 512], BF16), ("t1", [128, 512], F32),
            ("t2", [128, 512], F32),
            ("gcall0", [128, 8], F32), ("gcall1", [128, 8], F32),
            ("dall0", [128, 8], F32), ("dall1", [128, 8], F32),
            ("gcrhs", [16, 8], F32),
            ("readsFM", [128, 8 * T], BF16), ("normedFM", [128, 8 * T], BF16),
            ("Lmaskd", [16, 512], F32), ("sqf", [128, 1024], F32),
            ("sqbuf", [128, T], BF16), ("rmsrow", [16, T], F32),
            ("rmsbf", [16, T], BF16),
            ("h2q", [128, 2 * 1024], BF16), ("h2qFM", [128, 8 * T], BF16),
            ("yq", [128, 2 * 4096], BF16),
            ("fstat", [128, 24], F32), ("qhist", [128, 72], BF16),
            ("wsc", [128, 4], F32),
            ("wbuf_qkvA", [128, 8 * 256], BF16), ("wbuf_qkvB", [128, 8 * 256], BF16),
            ("wbuf_ctrlA", [128, CTRL], BF16), ("wbuf_ctrlB", [128, CTRL], BF16),

        ]:
            s[nm] = A(nm, shape, dt)
        AT = nc.alloc_sbuf_tensor_at
        adr = lambda nm: nc.lookup_mloc(s[nm]).addr
        s["y"] = AT("y", [128, 2 * 4096], F32, offset=adr("hid"))
        s["outgate"] = AT("outgate", [128, 2 * 1024], F32, offset=adr("hid"))
        s["istage"] = AT("istage", [128, 2048], I32, offset=adr("Qn"))
        s["yqFM"] = AT("yqFM", [128, 32 * T], BF16, offset=adr("Qn"))
        s["h2"] = AT("h2", [128, 2 * 1024], F32, offset=adr("ab"))
        s["prep"] = AT("prep", [128, 1024], F32, offset=adr("convq"))
        s["prepbf"] = AT("prepbf", [128, 1024], BF16, offset=adr("convq") + 4096)
        s["prepstage"] = AT("prepstage", [128, 512], BF16, offset=adr("convk"))
        s["prepsums"] = AT("prepsums", [128, 64], F32, offset=adr("convk") + 1024)
        s["normw_f32"] = AT("normw_f32", [16, 1024], F32, offset=adr("qkvFM"))
        s["wbuf_outA"] = AT("wbuf_outA", [128, 2048], BF16, offset=adr("wbuf_qkvA"))
        s["wbuf_outB"] = AT("wbuf_outB", [128, 2048], BF16, offset=adr("wbuf_qkvB"))
        s["wbuf_g0"] = AT("wbuf_g0", [128, 4 * 512], BF16, offset=adr("wbuf_qkvA"))
        s["wbuf_g1"] = AT("wbuf_g1", [128, 4 * 512], BF16, offset=adr("wbuf_qkvB"))
        s["wbuf_u0"] = AT("wbuf_u0", [128, 4 * 512], BF16, offset=adr("wbuf_ctrlA"))
        s["wbuf_u1"] = AT("wbuf_u1", [128, 4 * 512], BF16, offset=adr("readsFM"))
        s["wbuf_dA"] = AT("wbuf_dA", [128, 2048], BF16, offset=adr("readsFM"))
        s["wbuf_dB"] = AT("wbuf_dB", [128, 2048], BF16, offset=adr("normedFM"))
        self.s = s
        used = nc.sbuf_base - 16481
        print(f"[alloc] sbuf used ~{used/1024:.1f} KB of {(229344-16481)/1024:.1f}")
        self.P = [nc.alloc_psum_tensor(f"P{i}", [128, 512], F32) for i in range(8)]
        for name, shape, dt in self.taps:
            self.tap_outs[name] = nc.dram_tensor("tap_" + name, shape, dt,
                                                 kind="ExternalOutput").ap()

    def tap(self, name, ap_fn, reads):
        if name not in self.tap_outs:
            return
        out = self.tap_outs[name]
        self.p.op("gpsimd", reads=tuple(reads), dma=True, tag="tap_" + name)(
            lambda e, o=out, f=ap_fn: e.dma_start(out=o, in_=f()))

    def load_consts(self):
        p, s, cst = self.p, self.s, self.cst
        pairs = [(k, k) for k in
                 ["ident64", "ident128", "negstriu", "triuincl",
                  "headmaskC", "mask2pack", "mask2s_f32", "mask2s_bf",
                  "headmask8_f32", "headmask8_bf", "headmaskj", "onesT_bf",
                  "ones64_f32", "ones128row_f32", "onescol128_f32", "onesf_16x64"]]
        pairs += [("ones64_bf", "ones64"), ("ones128row_bf", "ones128row"),
                  ("biasc", "biasc")]
        for sb, ck in pairs:
            p.op("sync", writes=("c_" + sb,), dma=True, tag="ldc")(
                lambda e, a=s[sb], b=cst[ck]: e.dma_start(out=a[:], in_=b))
        p.op("gpsimd", writes=("c_ctrl_b",), dma=True)(
            lambda e: e.dma_start(out=s["ctrl_b_bf"][:], in_=self.dram_in["ctrl_b"][None, :]))
        p.op("gpsimd", writes=("c_out_b",), dma=True)(
            lambda e: e.dma_start(out=s["out_b_bf"][:], in_=self.dram_in["out_b"][None, :]))
        p.op("sync", writes=("c_normw",), dma=True)(
            lambda e: e.dma_start(out=s["normw_f32"][:], in_=self.dram_in["norm_w_t"]))
        p.op("vector", reads=("c_normw", "c_mask2pack"), writes=("c_mask2w",))(
            lambda e: e.tensor_mul(s["mask2w"][:], s["normw_f32"][:],
                                   s["mask2pack"][0:16, :]))
        for si, nm in enumerate(["convq_w", "convk_w", "convv_w"]):
            in_ap = ap_custom(self.dram_in[nm], 0, [[4, 128], [512, 8], [1, 4]])
            out_ap = ap_custom(s["cw"][:], si * 32, [[96, 128], [4, 8], [1, 4]])
            p.op("sync", writes=("c_cw",), dma=True)(
                lambda e, i=in_ap, o=out_ap: e.dma_start(out=o, in_=i))
        for si in range(3):
            for f in range(8):
                for j in range(4):
                    idx = si * 32 + f * 4 + j
                    p.op("scalar", reads=("c_cw", "c_ident128"), writes=("c_diag",),
                         tag="diag")(
                        lambda e, idx=idx, col=si * 32 + f * 4 + j: e.activation(
                            out=s["diags"][:, idx * 128:(idx + 1) * 128],
                            in_=s["ident128"][:], func=AF.Copy,
                            scale=s["cw"][:, col:col + 1]))

    def _transpose_store(self, dst_scr, rowlen, col0, row0, tag):
        p, s, P = self.p, self.s, self.P
        for g in range(2):
            def mmf(e, g=g):
                ins = None
                for b in range(4):
                    cb = g * 4 + b
                    ins = e.matmul(P[6][:, b * 128:(b + 1) * 128],
                                   s["prepbf"][:, cb * 128:(cb + 1) * 128],
                                   s["ident128"][:], start=True, stop=True)
                return ins
            p.op("tensor", reads=("prepbf", "c_ident128"), writes=("P6",), tag=tag)(mmf)
            p.op("vector", reads=("P6",), writes=("prepstage",))(
                lambda e: e.tensor_copy(s["prepstage"][:], P[6][:, 0:512]))
            out_ap = ap_custom(dst_scr, (row0 + g * 512) * rowlen + col0,
                               [[rowlen, 128], [128 * rowlen, 4], [1, 128]])
            in_ap = ap_custom(s["prepstage"][:], 0, [[512, 128], [128, 4], [1, 128]])
            p.op("sync", reads=("prepstage",), dma=True, tag=tag + "st")(
                lambda e, i=in_ap, o=out_ap: e.dma_start(out=o, in_=i))

    def prep_weights_attn(self):
        p, s, P = self.p, self.s, self.P
        for nm, Osz, dstn in [("qkv_w", QKV, "qkvT"), ("ctrl_w", CTRL, "ctrlT"),
                              ("out_w", D, "outT")]:
            dst = self.scr[dstn]
            for t in range((Osz + 127) // 128):
                nr = min(128, Osz - t * 128)
                p.op("sync", writes=("prep",), dma=True, tag="pwld")(
                    lambda e, t=t, nm=nm, nr=nr: e.dma_start(
                        out=s["prep"][0:nr, :],
                        in_=self.dram_in[nm][t * 128:t * 128 + nr, :]))
                if nr == 128:
                    p.op("vector", reads=("prep",), writes=("prepbf",))(
                        lambda e: e.tensor_copy(s["prepbf"][:], s["prep"][:]))
                else:
                    p.op("vector", writes=("prepbf",))(
                        lambda e: e.memset(s["prepbf"][:], 0.0))
                    p.op("vector", reads=("prep",), writes=("prepbf",))(
                        lambda e, nr=nr: e.tensor_copy(s["prepbf"][0:nr, :], s["prep"][0:nr, :]))
                self._transpose_store_n(dst, Osz, t * 128, nr, "tp_" + nm, scrres="scr_" + dstn)
        return

    def prep_weights_ffn(self):
        p, s, P = self.p, self.s, self.P
        for nm, Osz, Isz, dstn in [("w_gate", INNER, D, "wgT"),
                                   ("w_up", INNER, D, "wuT"),
                                   ("w_down", D, INNER, "wdT")]:
            ntile = Osz * Isz // (128 * 1024)
            for t in range(ntile):
                p.op("sync", writes=("prep",), dma=True, tag="fq1")(
                    lambda e, t=t, nm=nm: e.dma_start(
                        out=s["prep"][:],
                        in_=ap_custom(self.dram_in[nm], t * 131072,
                                      [[1024, 128], [1, 1024]])))
                p.op("vector", reads=("prep",), writes=("psums",))(
                    lambda e, t=t: e.tensor_reduce(
                        s["prepsums"][:, t:t + 1], s["prep"][:], AX.X, ALU.add,
                        apply_absolute_value=True))
            p.op("vector", reads=("prep",), writes=("spacer",))(
                lambda e: e.tensor_copy(s["sqf"][:, 0:1024], s["prep"][:]))
            p.op("vector", reads=("psums",), writes=("psumtot",))(
                lambda e, nt=ntile: e.tensor_reduce(
                    s["prepsums"][:, 63:64], s["prepsums"][:, 0:nt], AX.X, ALU.add))
            p.op("tensor", reads=("psumtot", "c_onescol128_f32"), writes=("P7",))(
                lambda e: e.matmul(P[7][0:1, 0:1], s["prepsums"][:, 63:64],
                                   s["onescol128_f32"][:], start=True, stop=True))
            p.op("vector", reads=("P7",), writes=("stat1",))(
                lambda e, n=Osz * Isz: e.tensor_scalar(
                    s["stat1"][0:1, 0:1], P[7][0:1, 0:1], 1.0 / n, 1e-5,
                    ALU.mult, ALU.max))
            p.op("tensor", reads=("stat1", "c_ones128row_f32"), writes=("P7",))(
                lambda e: e.matmul(P[7][:, 0:1], s["ones128row_f32"][:],
                                   s["stat1"][0:1, 0:1], start=True, stop=True))
            wi = {"w_gate": 0, "w_up": 1, "w_down": 2}[nm]
            p.op("scalar", reads=("P7",), writes=("wmean",))(
                lambda e: e.activation(out=s["stats"][:, 29:30], in_=P[7][:, 0:1],
                                       func=AF.Copy))
            p.op("scalar", reads=("P7",), writes=("wsc",))(
                lambda e, wi=wi: e.activation(out=s["wsc"][:, wi:wi + 1],
                                              in_=P[7][:, 0:1], func=AF.Copy))
            p.op("vector", reads=("wmean",), writes=("wscale",))(
                lambda e: e.reciprocal(s["stats"][:, 30:31], s["stats"][:, 29:30]))
            if nm == "w_gate" and "pstat" in self.tap_outs:
                self.p.op("gpsimd", reads=("wscale", "stat1", "psums"), dma=True, tag="tpstat")(
                    lambda e: e.dma_start(out=self.tap_outs["pstat"], in_=s["stat1"][:]))
                self.p.op("gpsimd", reads=("wscale",), dma=True, tag="tpsums")(
                    lambda e: e.dma_start(out=self.tap_outs["psums"], in_=s["prepsums"][:]))
            if nm in ("w_gate", "w_up"):
                tiles = [(t * 128, 0, Osz, t * 128, 0) for t in range(32)]
                tiles = [(t * 128, 0, Osz, t * 128, 0) for t in range(32)]
            else:
                tiles = [(t * 128, q * 1024, D, t * 128, q * 1024)
                         for t in range(8) for q in range(4)]
            for (r0, c0, rowlen, dcol0, drow0) in tiles:
                p.op("sync", writes=("prep",), dma=True, tag="fq2")(
                    lambda e, r0=r0, c0=c0, nm=nm: e.dma_start(
                        out=s["prep"][:],
                        in_=ap_custom(self.dram_in[nm], r0 * (D if nm != "w_down" else INNER) + c0,
                                      [[(D if nm != "w_down" else INNER), 128], [1, 1024]])))
                p.op("vector", reads=("prep", "wscale"), writes=("prep",))(
                    lambda e: e.tensor_scalar(s["prep"][:], s["prep"][:],
                                              s["stats"][:, 30:31], 1.0, ALU.mult, ALU.min))
                p.op("vector", reads=("prep",), writes=("prep",))(
                    lambda e: e.tensor_scalar(s["prep"][:], s["prep"][:], -1.0, None, ALU.max))
                p.op("vector", reads=("prep",), writes=("istage",))(
                    lambda e: e.tensor_copy(s["istage"][:, 0:1024], s["prep"][:]))
                p.op("vector", reads=("istage",), writes=("prepbf",))(
                    lambda e: e.tensor_copy(s["prepbf"][:], s["istage"][:, 0:1024]))
                self._transpose_store_n(self.scr[dstn], rowlen, dcol0, 128,
                                        "tq_" + nm, row0=drow0, scrres="scr_" + dstn)

    def _transpose_store_n(self, dst_scr, rowlen, col0, ncols, tag, row0=0, scrres="scr"):
        p, s, P = self.p, self.s, self.P
        for g in range(2):
            def mmf(e, g=g):
                ins = None
                for b in range(4):
                    cb = g * 4 + b
                    ins = e.matmul(P[6][:, b * 128:(b + 1) * 128],
                                   s["prepbf"][:, cb * 128:(cb + 1) * 128],
                                   s["ident128"][:], start=True, stop=True)
                return ins
            p.op("tensor", reads=("prepbf", "c_ident128"), writes=("P6",), tag=tag)(mmf)
            p.op("vector", reads=("P6",), writes=("prepstage",))(
                lambda e: e.tensor_copy(s["prepstage"][:], P[6][:, 0:512]))
            out_ap = ap_custom(dst_scr, (row0 + g * 512) * rowlen + col0,
                               [[rowlen, 128], [128 * rowlen, 4], [1, ncols]])
            in_ap = ap_custom(s["prepstage"][:], 0, [[512, 128], [128, 4], [1, ncols]])
            p.op("sync", reads=("prepstage",), writes=(scrres,), dma=True, tag=tag + "st")(
                lambda e, i=in_ap, o=out_ap: e.dma_start(out=o, in_=i))

    def frontend(self, ch):
        p, s, P, cfg = self.p, self.s, self.P, self.cfg
        T, NTT = cfg.T, cfg.NTT
        xb = s["xbuf%d" % (ch % 2)]
        xr = "xbuf%d" % (ch % 2)
        in_ap = ap_custom(self.dram_in["x"], ch * T * D, [[D, 128], [128 * D, NTT], [1, D]])
        out_ap = ap_custom(xb[:], 0, [[2 * 1024, 128], [1024, NTT], [1, 1024]])
        p.op("sync", writes=(xr,), dma=True, tag="xld")(
            lambda e, i=in_ap, o=out_ap: e.dma_start(out=o, in_=i))
        self._ln(xb, xr, s["hid"], "hid", statcol=0)
        for tt in range(NTT):
            def mmf(e, tt=tt):
                ins = None
                for g in range(2):
                    for b in range(4):
                        fc = g * 4 + b
                        ins = e.matmul(P[g][:, b * 128:(b + 1) * 128],
                                       s["hid"][:, tt * 1024 + fc * 128: tt * 1024 + (fc + 1) * 128],
                                       s["ident128"][:], start=True, stop=True)
                return ins
            p.op("tensor", reads=("hid", "c_ident128"), writes=("P0", "P1"), tag="hidT")(mmf)
            for g in range(2):
                out_ap = ap_custom(s["hidFM"][:], tt * 128 + g * 4 * T,
                                   [[8 * T, 128], [T, 4], [1, 128]])
                p.op("vector", reads=("P%d" % g,), writes=("hidFM",))(
                    lambda e, g=g, o=out_ap: e.tensor_copy(o, P[g][:, 0:512]))
        self.tap("hid", lambda: s["hid"][:], ["hid"]) if ch == getattr(self, "tap_chunk", 0) else None
        self.tap("stats", lambda: s["stats"][:], ["stats"]) if ch == getattr(self, "tap_chunk", 0) else None
        self.tap("hidFM", lambda: s["hidFM"][:], ["hidFM"]) if ch == getattr(self, "tap_chunk", 0) else None
        if ch == 0:
            p.op("vector", writes=("qkvFM_m",), tag="qmz")(
                lambda e: e.memset(ap_custom(s["qkvFM"][:], 0,
                                             [[24 * (T + 3), 128], [T + 3, 24], [1, 3]]), 0.0))
        else:
            mg_out = ap_custom(s["qkvFM"][:], 0, [[24 * (T + 3), 128], [T + 3, 24], [1, 3]])
            p.op("vector", reads=("qhist",), writes=("qkvFM_m",), tag="qmc")(
                lambda e, o=mg_out: e.tensor_copy(o, s["qhist"][:]))
        for pair in range(12):
            buf = "wbuf_qkv" + ("A" if pair % 2 == 0 else "B")
            p.op("sync", reads=("scr_qkvT",), writes=(buf,), dma=True, tag="wqkv")(
                lambda e, pair=pair, buf=buf: e.dma_start(
                    out=s[buf][:],
                    in_=ap_custom(self.scr["qkvT"], pair * 256,
                                  [[QKV, 128], [128 * QKV, 8], [1, 256]])))
            for ms in range(2):
                m = pair * 2 + ms
                bank = m % 2
                def mmf(e, ms=ms, bank=bank, buf=buf):
                    ins = None
                    for k in range(8):
                        ins = e.matmul(P[bank][:, 0:T],
                                       s[buf][:, k * 256 + ms * 128: k * 256 + (ms + 1) * 128],
                                       s["hidFM"][:, k * T:(k + 1) * T],
                                       start=(k == 0), stop=(k == 7))
                    return ins
                p.op("tensor", reads=(buf, "hidFM"), writes=("P%d" % bank,),
                     tag="qkvmm")(mmf)
                p.op("scalar", reads=("P%d" % bank,), writes=("qkvFM",))(
                    lambda e, m=m, bank=bank: e.activation(
                        out=s["qkvFM"][:, m * (T + 3) + 3: (m + 1) * (T + 3)],
                        in_=P[bank][:, 0:T], func=AF.Copy))
        for k in range(8):
            cb = "wbuf_ctrl" + ("A" if k % 2 == 0 else "B")
            p.op("sync", reads=("scr_ctrlT",), writes=(cb,), dma=True, tag="wctrl")(
                lambda e, k=k, cb=cb: e.dma_start(
                    out=s[cb][:],
                    in_=ap_custom(self.scr["ctrlT"], k * 128 * CTRL, [[CTRL, 128], [1, CTRL]])))
            def mmab(e, k=k, cb=cb):
                e.matmul(P[2][0:16, 0:T], s[cb][:, 0:16],
                         s["hidFM"][:, k * T:(k + 1) * T],
                         start=(k == 0), stop=False)
                return e.matmul(P[2][32:48, 0:T], s[cb][:, 16:32],
                                s["hidFM"][:, k * T:(k + 1) * T],
                                start=(k == 0), stop=False)
            p.op("tensor", reads=(cb, "hidFM"), writes=("P2",), tag="abmm")(mmab)
            def mmgate(e, k=k, cb=cb):
                ins = None
                for tt in range(NTT):
                    for hf in range(2):
                        ins = e.matmul(
                            P[3 + tt * 2 + hf][:, 0:512],
                            s["hidFM"][:, k * T + tt * 128: k * T + (tt + 1) * 128],
                            s[cb][:, 32 + hf * 512: 32 + (hf + 1) * 512],
                            start=(k == 0), stop=False)
                return ins
            p.op("tensor", reads=(cb, "hidFM"), writes=("P3", "P4", "P5", "P6"),
                 tag="gatemm")(mmgate)
        def mmab_b(e):
            e.matmul(P[2][0:16, 0:T], s["ctrl_b_bf"][0:1, 0:16],
                     s["onesT_bf"][0:1, 0:T], start=False, stop=True)
            return e.matmul(P[2][32:48, 0:T], s["ctrl_b_bf"][0:1, 16:32],
                            s["onesT_bf"][0:1, 0:T], start=False, stop=True)
        p.op("tensor", reads=("c_ctrl_b", "c_onesT_bf"), writes=("P2",), tag="abbias")(mmab_b)
        def mmgate_b(e):
            ins = None
            for tt in range(NTT):
                for hf in range(2):
                    ins = e.matmul(P[3 + tt * 2 + hf][:, 0:512],
                                   s["ones128row_bf"][0:1, :],
                                   s["ctrl_b_bf"][0:1, 32 + hf * 512: 32 + (hf + 1) * 512],
                                   start=False, stop=True)
            return ins
        p.op("tensor", reads=("c_ctrl_b", "c_ones128row_bf"),
             writes=("P3", "P4", "P5", "P6"), tag="gatebias")(mmgate_b)
        p.op("vector", reads=("P2",), writes=("ab",))(
            lambda e: e.tensor_copy(s["ab"][0:48, 0:T], P[2][0:48, 0:T]))
        for tt in range(NTT):
            for hf in range(2):
                p.op("scalar", reads=("P%d" % (3 + tt * 2 + hf),), writes=("outgate",))(
                    lambda e, tt=tt, hf=hf: e.activation(
                        out=s["outgate"][:, tt * 1024 + hf * 512: tt * 1024 + (hf + 1) * 512],
                        in_=P[3 + tt * 2 + hf][:, 0:512], func=AF.Sigmoid))
        for si, (dstn, dstr) in enumerate([("convq", "convq"), ("convk", "convk"),
                                           ("Vfm", "Vfm")]):
            for fc in range(8):
                m = si * 8 + fc
                bank = fc % 2
                def mmf(e, si=si, fc=fc, m=m, bank=bank):
                    ins = None
                    for j in range(4):
                        idx = si * 32 + fc * 4 + j
                        ins = e.matmul(P[bank][:, 0:T],
                                       s["diags"][:, idx * 128:(idx + 1) * 128],
                                       s["qkvFM"][:, m * (T + 3) + j: m * (T + 3) + j + T],
                                       start=(j == 0), stop=(j == 3))
                    return ins
                p.op("tensor", reads=("c_diag", "qkvFM", "qkvFM_m"),
                     writes=("P%d" % bank,), tag="convmm")(mmf)
                p.op("scalar", reads=("P%d" % bank,), writes=(dstr,))(
                    lambda e, fc=fc, bank=bank, dstn=dstn: e.activation(
                        out=s[dstn][:, fc * T:(fc + 1) * T],
                        in_=P[bank][:, 0:T], func=AF.Silu))
        mg_in = ap_custom(s["qkvFM"][:], T, [[24 * (T + 3), 128], [T + 3, 24], [1, 3]])
        p.op("vector", reads=("qkvFM",), writes=("qhist",), tag="qhist")(
            lambda e, i=mg_in: e.tensor_copy(s["qhist"][:], i))
        for src, dst in [("convq", "r_invq"), ("convk", "r_invk")]:
            for fc in range(8):
                p.op("scalar", reads=(src,), writes=("sqbuf",))(
                    lambda e, fc=fc, src=src: e.activation(
                        out=s["sqbuf"][:, 0:T], in_=s[src][:, fc * T:(fc + 1) * T],
                        func=AF.Square))
                def mmf(e, fc=fc):
                    return e.matmul(P[7][0:16, 0:T],
                                    s["headmaskC"][:, fc * 16:(fc + 1) * 16],
                                    s["sqbuf"][:, 0:T], start=(fc == 0), stop=(fc == 7))
                p.op("tensor", reads=("sqbuf", "c_headmaskC"), writes=("P7",), tag="l2mm")(mmf)
            p.op("scalar", reads=("P7",), writes=("r_t",))(
                lambda e: e.activation(out=s["r_t"][:, 0:T], in_=P[7][0:16, 0:T],
                                       func=AF.Sqrt))
            p.op("vector", reads=("r_t",), writes=("r_t",))(
                lambda e: e.tensor_scalar(s["r_t"][:, 0:T], s["r_t"][:, 0:T], 1e-6,
                                          None, ALU.max))
            p.op("vector", reads=("r_t",), writes=(dst,))(
                lambda e, dst=dst: e.reciprocal(s[dst][:, 0:T], s["r_t"][:, 0:T]))
        p.op("scalar", reads=("ab",), writes=("r_siga",))(
            lambda e: e.activation(out=s["r_siga"][:, 0:T], in_=s["ab"][0:16, 0:T],
                                   func=AF.Sigmoid))
        p.op("scalar", reads=("r_siga",), writes=("r_logb",))(
            lambda e: e.activation(out=s["r_logb"][:, 0:T], in_=s["r_siga"][:, 0:T],
                                   func=AF.Ln, scale=0.98, bias=s["biasc"][0:16, 2:3]))
        p.op("scalar", reads=("ab",), writes=("r_beta",))(
            lambda e: e.activation(out=s["r_beta"][:, 0:T], in_=s["ab"][32:48, 0:T],
                                   func=AF.Sigmoid))
        for sc in range(cfg.NSC):
            p.op("vector", reads=("r_logb", "c_onesf_16x64"), writes=("r_L",))(
                lambda e, sc=sc: e.tensor_tensor_scan(
                    s["r_L"][:, sc * 64:(sc + 1) * 64], s["onesf_16x64"][:],
                    s["r_logb"][:, sc * 64:(sc + 1) * 64], 0.0, ALU.mult, ALU.add))
        p.op("vector", reads=("r_L",), writes=("r_negL",))(
            lambda e: e.tensor_scalar(s["r_negL"][:, 0:T], s["r_L"][:, 0:T], -1.0,
                                      None, ALU.mult))
        p.op("scalar", reads=("r_L",), writes=("r_eL",))(
            lambda e: e.activation(out=s["r_eL"][:, 0:T], in_=s["r_L"][:, 0:T],
                                   func=AF.Exp))
        p.op("vector", reads=("r_invq",), writes=("rb_qn",))(
            lambda e: e.tensor_copy(s["rb_qn"][:, 0:T], s["r_invq"][:, 0:T]))
        p.op("vector", reads=("r_invk",), writes=("rb_kn",))(
            lambda e: e.tensor_copy(s["rb_kn"][:, 0:T], s["r_invk"][:, 0:T]))
        p.op("vector", reads=("r_invq", "r_eL"), writes=("rb_qh",))(
            lambda e: e.tensor_mul(s["rb_qh"][:, 0:T], s["r_invq"][:, 0:T],
                                   s["r_eL"][:, 0:T]))
        p.op("vector", reads=("r_invk", "r_eL"), writes=("rb_khm",))(
            lambda e: e.scalar_tensor_tensor(s["rb_khm"][:, 0:T], s["r_invk"][:, 0:T],
                                             -1.0, s["r_eL"][:, 0:T],
                                             ALU.mult, ALU.mult))
        for (row, r0m, src, dst) in [("rb_qn", 0, "convq", "Qn"),
                                     ("rb_qh", 0, "convq", "Qh"),
                                     ("rb_kn", 0, "convk", "Kn"),
                                     ("rb_khm", 0, "convk", "Khm")]:
            for fc in range(8):
                def mmf(e, row=row, r0m=r0m, fc=fc):
                    return e.matmul(P[7][:, 0:T],
                                    s["mask2pack"][r0m:r0m + 16, fc * 128:(fc + 1) * 128],
                                    s[row][:, 0:T], start=True, stop=True)
                p.op("tensor", reads=(row, "c_mask2pack"), writes=("P7",), tag="bcmm")(mmf)
                p.op("vector", reads=("P7", src), writes=(dst,))(
                    lambda e, fc=fc, src=src, dst=dst: e.tensor_mul(
                        s[dst][:, fc * T:(fc + 1) * T],
                        s[src][:, fc * T:(fc + 1) * T], P[7][:, 0:T]))

    def _ln(self, xb, xr, dst, dstr, statcol, eps=1e-5):
        p, s, cfg = self.p, self.s, self.cfg
        for tt in range(cfg.NTT):
            sl = (tt * 1024, (tt + 1) * 1024)
            c0 = statcol + tt * 8
            p.op("vector", reads=(xr,), writes=("stats",))(
                lambda e, sl=sl, c0=c0: e.tensor_reduce(
                    s["stats"][:, c0:c0 + 1], xb[:, sl[0]:sl[1]], AX.X, ALU.add))
            p.op("scalar", reads=(xr,), writes=("sq128",))(
                lambda e, sl=sl: e.activation(out=s["sqf"][:, 0:1024],
                                              in_=xb[:, sl[0]:sl[1]], func=AF.Square))
            p.op("vector", reads=("sq128",), writes=("stats",))(
                lambda e, c0=c0: e.tensor_reduce(
                    s["stats"][:, c0 + 1:c0 + 2], s["sqf"][:, 0:1024], AX.X, ALU.add))
            p.op("scalar", reads=("stats",), writes=("stats",))(
                lambda e, c0=c0: e.activation(out=s["stats"][:, c0 + 6:c0 + 7],
                                              in_=s["stats"][:, c0:c0 + 1],
                                              func=AF.Copy, scale=1.0 / 1024))
            p.op("scalar", reads=("stats",), writes=("stats",))(
                lambda e, c0=c0: e.activation(out=s["stats"][:, c0 + 7:c0 + 8],
                                              in_=s["stats"][:, c0 + 1:c0 + 2],
                                              func=AF.Copy, scale=1.0 / 1024))
            p.op("vector", reads=("stats",), writes=("stats",))(
                lambda e, c0=c0: e.tensor_mul(s["stats"][:, c0 + 2:c0 + 3],
                                              s["stats"][:, c0 + 6:c0 + 7],
                                              s["stats"][:, c0 + 6:c0 + 7]))
            p.op("gpsimd", reads=("stats",), writes=("stats",))(
                lambda e, c0=c0: e.tensor_sub(s["stats"][:, c0 + 3:c0 + 4],
                                              s["stats"][:, c0 + 7:c0 + 8],
                                              s["stats"][:, c0 + 2:c0 + 3]))
            p.op("scalar", reads=("stats",), writes=("stats",))(
                lambda e, c0=c0: e.activation(out=s["stats"][:, c0 + 4:c0 + 5],
                                              in_=s["stats"][:, c0 + 3:c0 + 4],
                                              func=AF.Sqrt, bias=s["biasc"][:, 0:1] if abs(eps - 1e-5) < 1e-12 else s["biasc"][:, 1:2]))
            p.op("vector", reads=("stats",), writes=("stats",))(
                lambda e, c0=c0: e.reciprocal(s["stats"][:, c0 + 5:c0 + 6],
                                              s["stats"][:, c0 + 4:c0 + 5]))
            p.op("vector", reads=(xr, "stats"), writes=("sq128",))(
                lambda e, sl=sl, c0=c0: e.tensor_scalar(
                    s["sqf"][:, 0:1024], xb[:, sl[0]:sl[1]],
                    s["stats"][:, c0 + 6:c0 + 7], None, ALU.subtract))
            p.op("scalar", reads=("sq128", "stats"), writes=(dstr,))(
                lambda e, sl=sl, c0=c0: e.activation(
                    out=dst[:, sl[0]:sl[1]], in_=s["sqf"][:, 0:1024],
                    func=AF.Copy, scale=s["stats"][:, c0 + 5:c0 + 6]))
    def scan_front(self, ch, sc):
        p, s, P, cfg = self.p, self.s, self.P, self.cfg
        T = cfg.T
        ts = sc * 64
        BLn = "BL%d" % (sc % 2)

        def blk(t):
            return [(t[64 * r:64 * r + 64, 64 * j:64 * j + 64], r, j)
                    for j in range(8) for r in range(2)]

        def fmsl(t, r, j, w=64, off=0):
            return t[64 * r:64 * r + 64, j * T + ts + off: j * T + ts + off + w]

        negL_rep = ap_custom(s["r_negL"][:], ts, [[T, 16], [0, 2], [1, 64]])
        m2s3 = ap_custom(s["mask2s_f32"][:], 0, [[128, 16], [64, 2], [1, 64]])
        lhsE3 = ap_custom(s["lhsE"][:], 0, [[128, 16], [64, 2], [1, 64]])
        p.op("vector", reads=("r_negL", "c_mask2s_f32"), writes=("lhsE",), tag="lhsE")(
            lambda e: e.tensor_tensor(lhsE3, negL_rep, m2s3, ALU.mult))
        beta_rep = ap_custom(s["r_beta"][:], ts, [[T, 16], [0, 2], [1, 64]])
        m2sb3 = ap_custom(s["mask2s_bf"][:], 0, [[128, 16], [64, 2], [1, 64]])
        lhsB3 = ap_custom(s["lhsB"][:], 0, [[128, 16], [64, 2], [1, 64]])
        p.op("vector", reads=("r_beta", "c_mask2s_bf"), writes=("lhsB",), tag="lhsB")(
            lambda e: e.tensor_tensor(lhsB3, beta_rep, m2sb3, ALU.mult))
        L_rep = ap_custom(s["r_L"][:], ts, [[T, 16], [0, 8], [1, 64]])
        hm83 = ap_custom(s["headmask8_f32"][:], 0, [[512, 16], [64, 8], [1, 64]])
        Lm3 = ap_custom(s["Lmaskd"][:], 0, [[512, 16], [64, 8], [1, 64]])
        p.op("vector", reads=("r_L", "c_headmask8_f32"), writes=("Lmaskd",), tag="Lmaskd")(
            lambda e: e.tensor_tensor(Lm3, L_rep, hm83, ALU.mult))
        def mmE(e):
            e.matmul(P[1][:, 0:512], s["mask2s_f32"][:], s["Lmaskd"][:],
                     start=True, stop=False)
            return e.matmul(P[1][:, 0:512], s["lhsE"][:], s["headmask8_f32"][:],
                            start=False, stop=True)
        p.op("tensor", reads=("Lmaskd", "lhsE", "c_mask2s_f32", "c_headmask8_f32"),
             writes=("P1",), tag="Emm")(mmE)
        p.op("vector", reads=("P1",), writes=("t1",))(
            lambda e: e.tensor_scalar(s["t1"][:], P[1][:, 0:512], 0.0, None, ALU.min))
        p.op("scalar", reads=("t1",), writes=("E",))(
            lambda e: e.activation(out=s["E"][:], in_=s["t1"][:], func=AF.Exp))
        def mmG(e):
            ins = None
            for (b, r, j) in blk(P[1]):
                ins = e.matmul(b, fmsl(s["Kn"], r, j), fmsl(s["Kn"], r, j),
                               start=True, stop=True)
            return ins
        p.op("tensor", reads=("Kn",), writes=("P1",), tag="Gmm")(mmG)
        p.op("vector", reads=("E", "P1"), writes=("t1",))(
            lambda e: e.tensor_mul(s["t1"][:], s["E"][:], P[1][:, 0:512]))
        def mmP(e):
            ins = None
            for (b, r, j) in blk(P[2]):
                ins = e.matmul(b, fmsl(s["Kn"], r, j), fmsl(s["Qn"], r, j),
                               start=True, stop=True)
            return ins
        p.op("tensor", reads=("Kn", "Qn"), writes=("P2",), tag="Pmm")(mmP)
        p.op("vector", reads=("E", "P2"), writes=("t2",))(
            lambda e: e.tensor_mul(s["t2"][:], s["E"][:], P[2][:, 0:512]))
        p.op("tensor", reads=("lhsB", "c_headmask8_bf"), writes=("P2",), tag="bpmm")(
            lambda e: e.matmul(P[2][:, 0:512], s["lhsB"][:], s["headmask8_bf"][:],
                               start=True, stop=True))
        p.op("vector", reads=("P2",), writes=("bpartU",))(
            lambda e: e.tensor_copy(s["bpartU"][:], P[2][:, 0:512]))
        p.op("vector", reads=("bpartU", "c_negstriu"), writes=("bmask_t",))(
            lambda e: e.tensor_mul(s["bmask_t"][:], s["bpartU"][:], s["negstriu"][:]))
        p.op("vector", reads=("bpartU", "c_triuincl"), writes=("bmask_B",))(
            lambda e: e.tensor_mul(s["bmask_B"][:], s["bpartU"][:], s["triuincl"][:]))
        gcn = "gcall%d" % (sc % 2)
        daln = "dall%d" % (sc % 2)
        eL_last = ap_custom(s["r_eL"][:], ts + 63, [[T, 16], [0, 8]])
        p.op("vector", reads=("r_eL", "c_headmaskj"), writes=("gcrhs",))(
            lambda e: e.tensor_tensor(s["gcrhs"][:], eL_last, s["headmaskj"][:], ALU.mult))
        p.op("tensor", reads=("gcrhs", "c_mask2s_f32"), writes=("P2",), tag="gcmm")(
            lambda e: e.matmul(P[2][:, 0:8], s["mask2s_f32"][:], s["gcrhs"][:],
                               start=True, stop=True))
        p.op("scalar", reads=("P2",), writes=(gcn,))(
            lambda e, gcn=gcn: e.activation(out=s[gcn][:], in_=P[2][:, 0:8], func=AF.Copy))
        E_last = ap_custom(s["E"][:], 63, [[512, 128], [64, 8]])
        bp_first = ap_custom(s["bpartU"][:], 0, [[512, 128], [64, 8]])
        p.op("vector", reads=("E", "bpartU"), writes=(daln,))(
            lambda e, daln=daln: e.tensor_tensor(s[daln][:], E_last, bp_first, ALU.mult))
        p.op("vector", reads=("t1", "bmask_t"), writes=("NL0a", "NL0b"))(
            lambda e: e.tensor_mul(s["NL0"][:], s["t1"][:], s["bmask_t"][:]))
        p.op("vector", reads=("t2", "bmask_B"), writes=(BLn,))(
            lambda e, BLn=BLn: e.tensor_mul(s[BLn][:], s["t2"][:], s["bmask_B"][:]))
        def mmNN(e):
            ins = None
            for (b, r, j) in blk(P[1]):
                ins = e.matmul(b, s["NL0"][64 * r:64 * r + 64, 64 * j:64 * j + 64],
                               s["ident64"][64 * r:64 * r + 64, :], start=True, stop=True)
            return ins
        p.op("tensor", reads=("NL0a", "NL0b", "c_ident64"), writes=("P1",), tag="NNmm")(mmNN)
        p.op("vector", reads=("P1",), writes=("NN0a", "NN0b"))(
            lambda e: e.tensor_copy(s["NN0"][:], P[1][:, 0:512]))

    def scan_tail_a(self, ch, sc):
        p, s, P, cfg = self.p, self.s, self.P, self.cfg
        T = cfg.T
        ts = sc * 64
        def blk(t):
            return [(t[64 * r:64 * r + 64, 64 * j:64 * j + 64], r, j)
                    for j in range(8) for r in range(2)]

        def fmsl(t, r, j, w=64, off=0):
            return t[64 * r:64 * r + 64, j * T + ts + off: j * T + ts + off + w]
        def mmR0(e):
            ins = None
            for r in range(2):
                for j in range(8):
                    b = P[5][64 * r:64 * r + 64, 64 * j:64 * j + 64]
                    e.matmul(b, fmsl(s["Vfm"], r, j), s["ident64"][64 * r:64 * r + 64, :],
                             start=(j == 0), stop=False)
                    ins = e.matmul(b, fmsl(s["Khm"], r, j),
                                   s["Sall"][64 * r:64 * r + 64, 64 * j:64 * j + 64],
                                   start=False, stop=True)
            return ins
        p.op("tensor", reads=("Vfm", "Khm", "Sall", "c_ident64"), writes=("P5",),
             tag="R0mm")(mmR0)
        p.op("scalar", reads=("P5",), writes=("R",))(
            lambda e: e.activation(out=s["R"][:], in_=P[5][:, 0:512], func=AF.Copy))
        self._levels(ch, sc, 0, 2)

    def scan_tail_b(self, ch, sc):
        p, s, P, cfg = self.p, self.s, self.P, self.cfg
        T = cfg.T
        ts = sc * 64
        def blk(t):
            return [(t[64 * r:64 * r + 64, 64 * j:64 * j + 64], r, j)
                    for j in range(8) for r in range(2)]

        def fmsl(t, r, j, w=64, off=0):
            return t[64 * r:64 * r + 64, j * T + ts + off: j * T + ts + off + w]
        BLn = "BL%d" % (sc % 2)
        self._levels(ch, sc, 2, 6)
        def mmRead(e):
            ins = None
            for r in range(2):
                for j in range(8):
                    b = P[6][64 * r:64 * r + 64, 64 * j:64 * j + 64]
                    e.matmul(b, fmsl(s["Qh"], r, j),
                             s["Sall"][64 * r:64 * r + 64, 64 * j:64 * j + 64],
                             start=(j == 0), stop=False)
                    ins = e.matmul(b, s[BLn][64 * r:64 * r + 64, 64 * j:64 * j + 64],
                                   s["R"][64 * r:64 * r + 64, 64 * j:64 * j + 64],
                                   start=False, stop=True)
            return ins
        p.op("tensor", reads=("Qh", "Sall", BLn, "R"), writes=("P6",), tag="readmm")(mmRead)
        p.op("vector", reads=("P6",), writes=("readstm",))(
            lambda e: e.tensor_copy(s["readstm"][:], P[6][:, 0:512]))
        def mmKtm(e):
            ins = None
            for (b, r, j) in blk(P[4]):
                ins = e.matmul(b, fmsl(s["Kn"], r, j), s["ident64"][64 * r:64 * r + 64, :],
                               start=True, stop=True)
            return ins
        p.op("tensor", reads=("Kn", "c_ident64"), writes=("P4",), tag="Ktmmm")(mmKtm)
        daln = "dall%d" % (sc % 2)
        P4_3 = ap_custom(self.P[4][:], 0, [[512, 128], [64, 8], [1, 64]])
        dall_b = ap_custom(s[daln][:], 0, [[8, 128], [1, 8], [0, 64]])
        Ktmp3 = ap_custom(s["Ktmp"][:], 0, [[512, 128], [64, 8], [1, 64]])
        p.op("vector", reads=("P4", daln), writes=("Ktmp",))(
            lambda e: e.tensor_tensor(Ktmp3, P4_3, dall_b, ALU.mult))
        def mmState(e):
            ins = None
            for (b, r, j) in blk(P[6]):
                ins = e.matmul(b, s["Ktmp"][64 * r:64 * r + 64, 64 * j:64 * j + 64],
                               s["R"][64 * r:64 * r + 64, 64 * j:64 * j + 64],
                               start=True, stop=True)
            return ins
        p.op("tensor", reads=("Ktmp", "R"), writes=("P6",), tag="statemm")(mmState)
        gcn = "gcall%d" % (sc % 2)
        gc_b = ap_custom(s[gcn][:], 0, [[8, 128], [1, 8], [0, 64]])
        Sall3 = ap_custom(s["Sall"][:], 0, [[512, 128], [64, 8], [1, 64]])
        t23 = ap_custom(s["t2"][:], 0, [[512, 128], [64, 8], [1, 64]])
        p.op("vector", reads=("Sall", gcn), writes=("t2",))(
            lambda e: e.tensor_tensor(t23, Sall3, gc_b, ALU.mult))
        p.op("vector", reads=("P6", "t2"), writes=("Sall",))(
            lambda e: e.tensor_add(s["Sall"][:], P[6][:, 0:512], s["t2"][:]))
        def mmRFM(e):
            ins = None
            for (b, r, j) in blk(P[4]):
                ins = e.matmul(b, s["readstm"][64 * r:64 * r + 64, 64 * j:64 * j + 64],
                               s["ident64"][64 * r:64 * r + 64, :], start=True, stop=True)
            return ins
        p.op("tensor", reads=("readstm", "c_ident64"), writes=("P4",), tag="rfmm")(mmRFM)
        P4_3b = ap_custom(self.P[4][:], 0, [[512, 128], [64, 8], [1, 64]])
        rfm3 = ap_custom(s["readsFM"][:], ts, [[8 * T, 128], [T, 8], [1, 64]])
        p.op("vector", reads=("P4",), writes=("readsFM",))(
            lambda e: e.tensor_copy(rfm3, P4_3b))

    def _levels(self, ch, sc, lev0, lev1):
        p, s, P = self.p, self.s, self.P

        def blk(t):
            return [(t[64 * r:64 * r + 64, 64 * j:64 * j + 64], r, j)
                    for j in range(8) for r in range(2)]
        for lev in range(lev0, lev1):
            nl = "NL0" if lev == 0 else ("NLa" if lev % 2 == 1 else "NLb")
            nn = "NN0" if lev == 0 else ("NNa" if lev % 2 == 1 else "NNb")
            nl2 = "NLa" if lev % 2 == 0 else "NLb"
            nn2 = "NNa" if lev % 2 == 0 else "NNb"
            def mmAp(e, nl=nl):
                ins = None
                for (b, r, j) in blk(P[5]):
                    ins = e.matmul(b, s[nl][64 * r:64 * r + 64, 64 * j:64 * j + 64],
                                   s["R"][64 * r:64 * r + 64, 64 * j:64 * j + 64],
                                   start=False, stop=True)
                return ins
            p.op("tensor", reads=(nl, "R"), writes=("P5",), tag="apply%d" % lev)(mmAp)
            if lev < 5:
                def mmSqN(e, nl=nl, nn=nn):
                    ins = None
                    for (b, r, j) in blk(P[4]):
                        ins = e.matmul(b, s[nl][64 * r:64 * r + 64, 64 * j:64 * j + 64],
                                       s[nn][64 * r:64 * r + 64, 64 * j:64 * j + 64],
                                       start=True, stop=True)
                    return ins
                p.op("tensor", reads=(nl, nn), writes=("P4",), tag="sqN%d" % lev)(mmSqN)
                def mmSqL(e, nl=nl, nn=nn):
                    ins = None
                    for (b, r, j) in blk(P[7]):
                        ins = e.matmul(b, s[nn][64 * r:64 * r + 64, 64 * j:64 * j + 64],
                                       s[nl][64 * r:64 * r + 64, 64 * j:64 * j + 64],
                                       start=True, stop=True)
                    return ins
                p.op("tensor", reads=(nl, nn), writes=("P7",), tag="sqL%d" % lev)(mmSqL)
                p.op("vector", reads=("P4",), writes=(nn2,))(
                    lambda e, nn2=nn2: e.tensor_copy(s[nn2][:], P[4][:, 0:512]))
                p.op("scalar", reads=("P7",), writes=(nl2,))(
                    lambda e, nl2=nl2: e.activation(out=s[nl2][:], in_=P[7][:, 0:512],
                                                    func=AF.Copy))
            eng = "vector" if lev % 2 == 0 else "scalar"
            if eng == "vector":
                p.op("vector", reads=("P5",), writes=("R",))(
                    lambda e: e.tensor_copy(s["R"][:], P[5][:, 0:512]))
            else:
                p.op("scalar", reads=("P5",), writes=("R",))(
                    lambda e: e.activation(out=s["R"][:], in_=P[5][:, 0:512], func=AF.Copy))

    def epilogue(self, ch):
        p, s, P, cfg = self.p, self.s, self.P, self.cfg
        T, NTT = cfg.T, cfg.NTT
        xb = s["xbuf%d" % (ch % 2)]
        xr = "xbuf%d" % (ch % 2)
        for fc in range(8):
            p.op("scalar", reads=("readsFM",), writes=("sqbuf",))(
                lambda e, fc=fc: e.activation(out=s["sqbuf"][:, 0:T],
                                              in_=s["readsFM"][:, fc * T:(fc + 1) * T],
                                              func=AF.Square))
            def mmf(e, fc=fc):
                return e.matmul(P[7][0:16, 0:T], s["headmaskC"][:, fc * 16:(fc + 1) * 16],
                                s["sqbuf"][:, 0:T], start=(fc == 0), stop=(fc == 7))
            p.op("tensor", reads=("sqbuf", "c_headmaskC"), writes=("P7",), tag="rmsmm")(mmf)
        p.op("scalar", reads=("P7",), writes=("rmsrow",))(
            lambda e: e.activation(out=s["rmsrow"][:, 0:T], in_=P[7][0:16, 0:T],
                                   func=AF.Sqrt, scale=1.0 / 64, bias=s["biasc"][0:16, 1:2]))
        p.op("vector", reads=("rmsrow",), writes=("r_t",))(
            lambda e: e.reciprocal(s["r_t"][:, 0:T], s["rmsrow"][:, 0:T]))
        p.op("vector", reads=("r_t",), writes=("rmsbf",))(
            lambda e: e.tensor_copy(s["rmsbf"][:, 0:T], s["r_t"][:, 0:T]))
        for fc in range(8):
            p.op("tensor", reads=("rmsbf", "c_mask2w"), writes=("P0",), tag="nbc")(
                lambda e, fc=fc: e.matmul(P[0][:, 0:T], s["mask2w"][:, fc * 128:(fc + 1) * 128],
                                          s["rmsbf"][:, 0:T], start=True, stop=True))
            p.op("vector", reads=("P0", "readsFM"), writes=("normedFM",))(
                lambda e, fc=fc: e.tensor_mul(s["normedFM"][:, fc * T:(fc + 1) * T],
                                              s["readsFM"][:, fc * T:(fc + 1) * T],
                                              P[0][:, 0:T]))
        for kg in range(4):
            ob = "wbuf_out" + ("A" if kg % 2 == 0 else "B")
            p.op("sync", reads=("scr_outT",), writes=(ob,), dma=True, tag="wout")(
                lambda e, kg=kg, ob=ob: e.dma_start(
                    out=s[ob][:],
                    in_=ap_custom(self.scr["outT"], kg * 256 * D,
                                  [[D, 128], [128 * D, 2], [1, D]])))
            def mmf(e, kg=kg, ob=ob):
                ins = None
                for kk in range(2):
                    k = kg * 2 + kk
                    for tt in range(NTT):
                        for hf in range(2):
                            ins = e.matmul(P[3 + tt * 2 + hf][:, 0:512],
                                           s["normedFM"][:, k * T + tt * 128: k * T + (tt + 1) * 128],
                                           s[ob][:, kk * 1024 + hf * 512: kk * 1024 + (hf + 1) * 512],
                                           start=(k == 0), stop=False)
                return ins
            p.op("tensor", reads=(ob, "normedFM"),
                 writes=("P3", "P4", "P5", "P6"), tag="outmm")(mmf)
        def mmb(e):
            ins = None
            for tt in range(NTT):
                for hf in range(2):
                    ins = e.matmul(P[3 + tt * 2 + hf][:, 0:512],
                                   s["ones128row_bf"][0:1, :],
                                   s["out_b_bf"][0:1, hf * 512:(hf + 1) * 512],
                                   start=False, stop=True)
            return ins
        p.op("tensor", reads=("c_out_b", "c_ones128row_bf"),
             writes=("P3", "P4", "P5", "P6"), tag="outbias")(mmb)
        for tt in range(NTT):
            for hf in range(2):
                pi = 3 + tt * 2 + hf
                sl = (tt * 1024 + hf * 512, tt * 1024 + (hf + 1) * 512)
                p.op("vector", reads=("P%d" % pi, "outgate"), writes=("t1",))(
                    lambda e, pi=pi, sl=sl: e.tensor_mul(
                        s["t1"][:], P[pi][:, 0:512], s["outgate"][:, sl[0]:sl[1]]))
                p.op("vector", reads=("t1", xr), writes=(xr,))(
                    lambda e, sl=sl: e.tensor_add(xb[:, sl[0]:sl[1]],
                                                  xb[:, sl[0]:sl[1]], s["t1"][:, 0:512]))

    def ffn(self, ch):
        p, s, P, cfg = self.p, self.s, self.P, self.cfg
        T, NTT = cfg.T, cfg.NTT
        xb = s["xbuf%d" % (ch % 2)]
        xr = "xbuf%d" % (ch % 2)
        self._ln(xb, xr, s["h2"], "h2", statcol=16)
        for tt in range(NTT):
            sl = (tt * 1024, (tt + 1) * 1024)
            c0 = tt * 4
            p.op("vector", reads=("h2",), writes=("fstat",))(
                lambda e, sl=sl, c0=c0: e.tensor_reduce(
                    s["fstat"][:, c0:c0 + 1], s["h2"][:, sl[0]:sl[1]], AX.X, ALU.max,
                    apply_absolute_value=True))
            p.op("gpsimd", reads=("fstat",), writes=("fstat",))(
                lambda e, c0=c0: e.tensor_scalar(s["fstat"][:, c0 + 1:c0 + 2],
                                                 s["fstat"][:, c0:c0 + 1],
                                                 1e-5, None, ALU.max))
            p.op("vector", reads=("fstat",), writes=("fstat",))(
                lambda e, c0=c0: e.reciprocal(s["fstat"][:, c0 + 2:c0 + 3],
                                              s["fstat"][:, c0 + 1:c0 + 2]))
            p.op("scalar", reads=("fstat",), writes=("fstat",))(
                lambda e, c0=c0: e.activation(out=s["fstat"][:, c0 + 3:c0 + 4],
                                              in_=s["fstat"][:, c0 + 2:c0 + 3],
                                              func=AF.Copy, scale=127.0))
            p.op("vector", reads=("fstat", "wsc"), writes=("fstat",))(
                lambda e, c0=c0, tt=tt: e.scalar_tensor_tensor(
                    s["fstat"][:, 16 + tt:17 + tt], s["fstat"][:, c0 + 1:c0 + 2],
                    1.0 / 127, s["wsc"][:, 0:1], ALU.mult, ALU.mult))
            p.op("vector", reads=("fstat", "wsc"), writes=("fstat",))(
                lambda e, c0=c0, tt=tt: e.scalar_tensor_tensor(
                    s["fstat"][:, 18 + tt:19 + tt], s["fstat"][:, c0 + 1:c0 + 2],
                    1.0 / 127, s["wsc"][:, 1:2], ALU.mult, ALU.mult))
            p.op("scalar", reads=("h2", "fstat"), writes=("istage",))(
                lambda e, sl=sl, c0=c0: e.activation(
                    out=s["istage"][:, 0:1024], in_=s["h2"][:, sl[0]:sl[1]],
                    func=AF.Copy, scale=s["fstat"][:, c0 + 3:c0 + 4]))
            p.op("vector", reads=("istage",), writes=("istage",))(
                lambda e: e.tensor_scalar(s["istage"][:, 0:1024], s["istage"][:, 0:1024],
                                          127, -128, ALU.min, ALU.max))
            p.op("vector", reads=("istage",), writes=("h2q",))(
                lambda e, sl=sl: e.tensor_copy(s["h2q"][:, sl[0]:sl[1]],
                                               s["istage"][:, 0:1024]))
        for tt in range(NTT):
            def mmf(e, tt=tt):
                ins = None
                for g in range(2):
                    for b in range(4):
                        fc = g * 4 + b
                        ins = e.matmul(P[g][:, b * 128:(b + 1) * 128],
                                       s["h2q"][:, tt * 1024 + fc * 128: tt * 1024 + (fc + 1) * 128],
                                       s["ident128"][:], start=True, stop=True)
                return ins
            p.op("tensor", reads=("h2q", "c_ident128"), writes=("P0", "P1"), tag="h2qT")(mmf)
            for g in range(2):
                out_ap = ap_custom(s["h2qFM"][:], tt * 128 + g * 4 * T,
                                   [[8 * T, 128], [T, 4], [1, 128]])
                p.op("vector", reads=("P%d" % g,), writes=("h2qFM",))(
                    lambda e, g=g, o=out_ap: e.tensor_copy(o, P[g][:, 0:512]))
        if ch == getattr(self, "tap_chunk", 0):
            self.tap("h2q", lambda: s["h2q"][:], ["h2q"])
            self.tap("h2qFM", lambda: s["h2qFM"][:], ["h2qFM"])
            self.tap("fstat", lambda: s["fstat"][:], ["fstat"])
        for ic in range(8):
            for nm, b0, b1 in [("wgT", "wbuf_g0", "wbuf_g1"), ("wuT", "wbuf_u0", "wbuf_u1")]:
                for hh, bh in ((0, b0), (1, b1)):
                    p.op("sync", reads=("scr_" + nm,), writes=(bh,), dma=True,
                         tag="wld_" + nm)(
                        lambda e, ic=ic, nm=nm, bh=bh, hh=hh: e.dma_start(
                            out=s[bh][:],
                            in_=ap_custom(self.scr[nm], hh * 4 * 128 * INNER + ic * 512,
                                          [[INNER, 128], [128 * INNER, 4], [1, 512]])))
            pg = 0 if ic % 2 == 0 else 2
            pu = 1 if ic % 2 == 0 else 3
            for tt in range(NTT):
                for hh, bh in ((0, "wbuf_g0"), (1, "wbuf_g1")):
                    def mmg(e, tt=tt, hh=hh, bh=bh, pg=pg):
                        ins = None
                        for kk in range(4):
                            k = hh * 4 + kk
                            ins = e.matmul(P[pg][:, 0:512],
                                           s["h2qFM"][:, k * T + tt * 128: k * T + (tt + 1) * 128],
                                           s[bh][:, kk * 512:(kk + 1) * 512],
                                           start=(k == 0), stop=(k == 7))
                        return ins
                    p.op("tensor", reads=(bh, "h2qFM"), writes=("P%d" % pg,), tag="gmm")(mmg)
                for hh, bh in ((0, "wbuf_u0"), (1, "wbuf_u1")):
                    def mmu(e, tt=tt, hh=hh, bh=bh, pu=pu):
                        ins = None
                        for kk in range(4):
                            k = hh * 4 + kk
                            ins = e.matmul(P[pu][:, 0:512],
                                           s["h2qFM"][:, k * T + tt * 128: k * T + (tt + 1) * 128],
                                           s[bh][:, kk * 512:(kk + 1) * 512],
                                           start=(k == 0), stop=(k == 7))
                        return ins
                    p.op("tensor", reads=(bh, "h2qFM"), writes=("P%d" % pu,), tag="umm")(mmu)
                p.op("scalar", reads=("P%d" % pg, "fstat"), writes=("t1",))(
                    lambda e, tt=tt, pg=pg: e.activation(out=s["t1"][:], in_=P[pg][:, 0:512],
                                                  func=AF.Silu,
                                                  scale=s["fstat"][:, 16 + tt:17 + tt]))
                p.op("vector", reads=("t1", "P%d" % pu, "fstat"), writes=("y",))(
                    lambda e, ic=ic, tt=tt, pu=pu: e.scalar_tensor_tensor(
                        s["y"][:, tt * 4096 + ic * 512: tt * 4096 + (ic + 1) * 512],
                        P[pu][:, 0:512], s["fstat"][:, 18 + tt:19 + tt],
                        s["t1"][:, 0:512], ALU.mult, ALU.mult))
                p.op("vector", reads=("y",), writes=("ymax",))(
                    lambda e, ic=ic, tt=tt: e.tensor_reduce(
                        s["fstat"][:, 22 + tt:23 + tt],
                        s["y"][:, tt * 4096 + ic * 512: tt * 4096 + (ic + 1) * 512],
                        AX.X, ALU.max, apply_absolute_value=True))
                if ic > 0:
                    p.op("gpsimd", reads=("ymax", "fstat"), writes=("fstat",))(
                        lambda e, tt=tt, c0=8 + tt * 4: e.tensor_scalar(
                            s["fstat"][:, c0:c0 + 1], s["fstat"][:, 22 + tt:23 + tt],
                            s["fstat"][:, c0:c0 + 1], None, ALU.max))
                else:
                    p.op("gpsimd", reads=("ymax",), writes=("fstat",))(
                        lambda e, tt=tt, c0=8 + tt * 4: e.tensor_scalar(
                            s["fstat"][:, c0:c0 + 1], s["fstat"][:, 22 + tt:23 + tt],
                            0.0, None, ALU.max))
        if ch == getattr(self, "tap_chunk", 0):
            self.tap("ytap", lambda: s["y"][:, 0:4096], ["y"])
        for tt in range(NTT):
            sl = (tt * 4096, (tt + 1) * 4096)
            c0 = 8 + tt * 4
            p.op("gpsimd", reads=("fstat",), writes=("fstat",))(
                lambda e, c0=c0: e.tensor_scalar(s["fstat"][:, c0 + 1:c0 + 2],
                                                 s["fstat"][:, c0:c0 + 1],
                                                 1e-5, None, ALU.max))
            p.op("vector", reads=("fstat",), writes=("fstat",))(
                lambda e, c0=c0: e.reciprocal(s["fstat"][:, c0 + 2:c0 + 3],
                                              s["fstat"][:, c0 + 1:c0 + 2]))
            p.op("scalar", reads=("fstat",), writes=("fstat",))(
                lambda e, c0=c0: e.activation(out=s["fstat"][:, c0 + 3:c0 + 4],
                                              in_=s["fstat"][:, c0 + 2:c0 + 3],
                                              func=AF.Copy, scale=127.0))
            p.op("vector", reads=("fstat", "wsc"), writes=("fstat",))(
                lambda e, c0=c0, tt=tt: e.scalar_tensor_tensor(
                    s["fstat"][:, 20 + tt:21 + tt], s["fstat"][:, c0 + 1:c0 + 2],
                    1.0 / 127, s["wsc"][:, 2:3], ALU.mult, ALU.mult))
            for hh in range(2):
                p.op("scalar", reads=("y", "fstat"), writes=("istage",))(
                    lambda e, sl=sl, hh=hh, c0=c0: e.activation(
                        out=s["istage"][:, 0:2048],
                        in_=s["y"][:, sl[0] + hh * 2048: sl[0] + (hh + 1) * 2048],
                        func=AF.Copy, scale=s["fstat"][:, c0 + 3:c0 + 4]))
                p.op("vector", reads=("istage",), writes=("istage",))(
                    lambda e: e.tensor_scalar(s["istage"][:, 0:2048],
                                              s["istage"][:, 0:2048],
                                              127, -128, ALU.min, ALU.max))
                p.op("vector", reads=("istage",), writes=("yq",))(
                    lambda e, sl=sl, hh=hh: e.tensor_copy(
                        s["yq"][:, sl[0] + hh * 2048: sl[0] + (hh + 1) * 2048],
                        s["istage"][:, 0:2048]))
        if ch == getattr(self, "tap_chunk", 0):
            self.tap("yq", lambda: s["yq"][:], ["yq"])
        for tt in range(NTT):
            for icg in range(8):
                def mmf(e, tt=tt, icg=icg):
                    ins = None
                    for b in range(4):
                        ic = icg * 4 + b
                        ins = e.matmul(P[2][:, b * 128:(b + 1) * 128],
                                       s["yq"][:, tt * 4096 + ic * 128: tt * 4096 + (ic + 1) * 128],
                                       s["ident128"][:], start=True, stop=True)
                    return ins
                p.op("tensor", reads=("yq", "c_ident128"), writes=("P2",), tag="yqT")(mmf)
                out_ap = ap_custom(s["yqFM"][:], (icg * 4) * T + tt * 128,
                                   [[32 * T, 128], [T, 4], [1, 128]])
                p.op("vector", reads=("P2",), writes=("yqFM",))(
                    lambda e, o=out_ap: e.tensor_copy(o, P[2][:, 0:512]))
        for icg in range(16):
            db = "wbuf_d" + ("A" if icg % 2 == 0 else "B")
            p.op("sync", reads=("scr_wdT",), writes=(db,), dma=True, tag="wd")(
                lambda e, icg=icg, db=db: e.dma_start(
                    out=s[db][:],
                    in_=ap_custom(self.scr["wdT"], icg * 256 * D,
                                  [[D, 128], [128 * D, 2], [1, D]])))
            def mmf(e, icg=icg, db=db):
                ins = None
                for kk in range(2):
                    ic = icg * 2 + kk
                    for tt in range(NTT):
                        for hf in range(2):
                            ins = e.matmul(P[3 + tt * 2 + hf][:, 0:512],
                                           s["yqFM"][:, ic * T + tt * 128: ic * T + (tt + 1) * 128],
                                           s[db][:, kk * 1024 + hf * 512: kk * 1024 + (hf + 1) * 512],
                                           start=(ic == 0), stop=(ic == 31))
                return ins
            p.op("tensor", reads=(db, "yqFM"),
                 writes=("P3", "P4", "P5", "P6"), tag="dmm")(mmf)
        for tt in range(NTT):
            for hf in range(2):
                pi = 3 + tt * 2 + hf
                sl = (tt * 1024 + hf * 512, tt * 1024 + (hf + 1) * 512)
                p.op("vector", reads=("P%d" % pi, xr, "fstat"), writes=(xr,))(
                    lambda e, pi=pi, sl=sl, tt=tt: e.scalar_tensor_tensor(
                        xb[:, sl[0]:sl[1]], P[pi][:, 0:512],
                        s["fstat"][:, 20 + tt:21 + tt],
                        xb[:, sl[0]:sl[1]], ALU.mult, ALU.add))
        in_ap = ap_custom(xb[:], 0, [[2 * 1024, 128], [1024, NTT], [1, 1024]])
        out_ap = ap_custom(self.dram_out, ch * self.cfg.T * D,
                           [[D, 128], [128 * D, NTT], [1, D]])
        p.op("sync", reads=(xr,), writes=("xout",), dma=True, tag="xst")(
            lambda e, i=in_ap, o=out_ap: e.dma_start(out=o, in_=i))

    def build(self, with_prep=True, nch=None):
        p, s = self.p, self.s
        self.load_consts()
        if with_prep:
            self.prep_weights_attn()
        p.op("vector", writes=("Salla", "Sallb"))(lambda e: e.memset(s["Sall"][:], 0.0))
        nch = self.cfg.NCH if nch is None else nch
        for ch in range(nch):
            self.p.phase = "front"
            self.frontend(ch)
            self.run_taps_frontend(ch)
            self.p.phase = "scanF"
            self.scan_front(ch, 0)
            for sc in range(self.cfg.NSC):
                self.p.phase = "scanTA"
                self.scan_tail_a(ch, sc)
                if sc + 1 < self.cfg.NSC:
                    self.p.phase = "scanF"
                    self.scan_front(ch, sc + 1)
                self.p.phase = "scanTB"
                self.scan_tail_b(ch, sc)
            self.run_taps_scan(ch)
            if ch == 0 and with_prep:
                self.p.phase = "prepF"
                self.prep_weights_ffn()
            self.p.phase = "epi"
            self.epilogue(ch)
            self.run_taps_epi(ch)
            self.p.phase = "ffn"
            self.ffn(ch)
        p.op("sync", reads=("xout",), tag="fence")(lambda e: e.nop())
        with self.nc.Block() as block:
            p.finalize(self.nc, block)

    def run_taps_frontend(self, ch):
        if ch != getattr(self, "tap_chunk", 0):
            return
        s, cfg = self.s, self.cfg
        for scrn in ["scr_ctrlT", "scr_qkvT", "scr_wgT"]:
            if scrn in self.tap_outs:
                nm = scrn[4:] + "T" if False else scrn.replace("scr_", "")
                self.p.op("gpsimd", reads=(scrn,), dma=True, tag="tap" + scrn)(
                    lambda e, o=self.tap_outs[scrn], i=self.scr[nm]: e.dma_start(out=o, in_=i))

        self.tap("qkvFM", lambda: s["qkvFM"][:], ["qkvFM"])
        self.tap("convq", lambda: s["convq"][:], ["convq"])
        self.tap("Vfm", lambda: s["Vfm"][:], ["Vfm"])
        self.tap("Qn", lambda: s["Qn"][:], ["Qn"])
        self.tap("Kn", lambda: s["Kn"][:], ["Kn"])
        self.tap("Qh", lambda: s["Qh"][:], ["Qh"])
        self.tap("Khm", lambda: s["Khm"][:], ["Khm"])
        self.tap("ab", lambda: s["ab"][:], ["ab"])
        self.tap("r_L", lambda: s["r_L"][:], ["r_L"])
        self.tap("r_beta", lambda: s["r_beta"][:], ["r_beta"])
        self.tap("r_eL", lambda: s["r_eL"][:], ["r_eL"])
        self.tap("outgate", lambda: s["outgate"][:], ["outgate"])

    def run_taps_scan(self, ch):
        if ch != getattr(self, "tap_chunk", 0):
            return
        s = self.s
        self.tap("readsFM", lambda: s["readsFM"][:], ["readsFM"])
        self.tap("Sall", lambda: s["Sall"][:], ["Sall"])

    def run_taps_epi(self, ch):
        if ch != getattr(self, "tap_chunk", 0):
            return
        s = self.s
        self.tap("normedFM", lambda: s["normedFM"][:], ["normedFM"])
        self.tap("x1", lambda: s["xbuf%d" % (ch % 2)][:], ["xbuf%d" % (ch % 2)])


def kernel(L0_attn_hidden, L0_attn_qkv_w, L0_attn_ctrl_w, L0_attn_ctrl_b,
           L0_attn_convq_w, L0_attn_convk_w, L0_attn_convv_w, L0_attn_norm_w,
           L0_attn_out_w, L0_attn_out_b, L0_ffn_w_gate, L0_ffn_w_up,
           L0_ffn_w_down):
    B, S, Dm = L0_attn_hidden.shape
    assert (B, S, Dm) == (8, 4096, 1024)
    cfg = Cfg(S=S, T=256)
    nc = bass.Bass("TRN2", target_bir_lowering=False)
    b = Builder(nc, cfg, taps=[])
    b.build()
    norm_w_t = np.ascontiguousarray(np.tile(np.asarray(L0_attn_norm_w, np.float32), (16, 16)))
    shared = {
        "qkv_w": np.ascontiguousarray(L0_attn_qkv_w, dtype=np.float32),
        "ctrl_w": np.ascontiguousarray(L0_attn_ctrl_w, dtype=np.float32),
        "ctrl_b": np.ascontiguousarray(L0_attn_ctrl_b, dtype=np.float32),
        "convq_w": np.ascontiguousarray(L0_attn_convq_w, dtype=np.float32),
        "convk_w": np.ascontiguousarray(L0_attn_convk_w, dtype=np.float32),
        "convv_w": np.ascontiguousarray(L0_attn_convv_w, dtype=np.float32),
        "norm_w_t": norm_w_t,
        "out_w": np.ascontiguousarray(L0_attn_out_w, dtype=np.float32),
        "out_b": np.ascontiguousarray(L0_attn_out_b, dtype=np.float32),
        "w_gate": np.ascontiguousarray(L0_ffn_w_gate, dtype=np.float32),
        "w_up": np.ascontiguousarray(L0_ffn_w_up, dtype=np.float32),
        "w_down": np.ascontiguousarray(L0_ffn_w_down, dtype=np.float32),
    }
    xin = np.ascontiguousarray(L0_attn_hidden, dtype=np.float32)
    in_maps = [dict(shared, x=xin[i]) for i in range(8)]
    res = run_bass_kernel_spmd(nc, in_maps, core_ids=list(range(8)))
    out = np.stack([np.asarray(res.results[i]["out"], np.float32) for i in range(8)], axis=0)
    return out


# version 12
# speedup vs baseline: 1.3103x; 1.0508x over previous
import numpy as np
import ml_dtypes
import concourse.bass as bass
import concourse.mybir as mybir
from concourse.bass_utils import run_bass_kernel_spmd

BF = ml_dtypes.bfloat16

"""Support code for the mk_seq_preln_d1 deltanet megakernel (TRN2, raw bass).

Layout conventions:
- TM (token-major): [128 tokens on partitions x features on free]
- FM (feature-major): [128 features-within-chunk on partitions x (fc, t) on free]
  global feature index d = fc*128 + p ; head h = d // 64 = 2*fc + p//64
- scan all-heads tiles: [128 x 512] fp32/bf16 ; head h=2j+r at partition block
  r=h%2 (rows 64r..64r+63), free block j=h//2 (cols 64j..64j+63)
- weights in HBM scratch, bf16, transposed to [in_feat x out_feat]
"""
import numpy as np
import ml_dtypes
import concourse.bass as bass
import concourse.mybir as mybir

F32 = mybir.dt.float32
BF16 = mybir.dt.bfloat16
I32 = mybir.dt.int32
AF = mybir.ActivationFunctionType
ALU = mybir.AluOpType
AX = mybir.AxisListType

D = 1024
H = 16
HD = 64
INNER = 4096
CTRL = 2 * H + D
QKV = 3 * D
C = 64


class Cfg:
    def __init__(self, S=4096, T=256):
        assert S % T == 0 and T % 128 == 0
        self.S = S
        self.T = T
        self.NCH = S // T
        self.NTT = T // 128
        self.NSC = T // C


ENGINES = ("sync", "tensor", "vector", "scalar", "gpsimd")


class Op:
    __slots__ = ("idx", "eng", "fn", "reads", "writes", "dma", "tag",
                 "need_inc", "tick", "waits", "_deps", "dkey", "phase")

    def __init__(self, idx, eng, fn, reads, writes, dma, tag):
        self.idx = idx
        self.eng = eng
        self.fn = fn
        self.reads = reads
        self.writes = writes
        self.dma = dma
        self.tag = tag
        self.need_inc = False
        self.tick = 0
        self.waits = []
        self.dkey = None


class Prog:
    def __init__(self):
        self.ops = []
        self.phase = "init"
        self.last_writer = {}
        self.readers = {}

    def op(self, eng, reads=(), writes=(), dma=False, tag=""):
        def deco(fn):
            o = Op(len(self.ops), eng, fn, expand_couples(reads), expand_couples(writes), dma, tag)
            o.phase = self.phase
            if dma:
                o.dkey = "D_" + (o.writes[0] if o.writes else (o.reads[0] if o.reads else "misc"))
            deps = []
            for r in o.reads:
                w = self.last_writer.get(r)
                if w is not None:
                    deps.append(w)
            for w_ in o.writes:
                lw = self.last_writer.get(w_)
                if lw is not None:
                    deps.append(lw)
                deps.extend(self.readers.get(w_, ()))
            for d in deps:
                if d.eng != o.eng:
                    d.need_inc = True
            o._deps = [d for d in deps if d.eng != o.eng]
            for r in o.reads:
                self.readers.setdefault(r, []).append(o)
            for w_ in o.writes:
                self.last_writer[w_] = o
                self.readers[w_] = []
            self.ops.append(o)
            return o
        return deco

    def finalize(self, nc, block):
        for o in self.ops:
            if o.dma:
                o.need_inc = True
        tick = {}
        for o in self.ops:
            key = o.dkey if o.dkey else o.eng
            if o.need_inc:
                tick[key] = tick.get(key, 0) + (16 if o.dma else 1)
            o.tick = tick.get(key, 0)
        water = {}
        for o in self.ops:
            want = {}
            for d in o._deps:
                assert d.need_inc and d.tick > 0
                key = d.dkey if d.dkey else d.eng
                want[key] = max(want.get(key, 0), d.tick)
            for k2, v in want.items():
                wk = (o.eng, k2)
                if v > water.get(wk, 0):
                    o.waits.append((k2, v))
                    water[wk] = v
        n_inc = sum(1 for o in self.ops if o.need_inc)
        n_wait = sum(len(o.waits) for o in self.ops)
        by_eng = {e: sum(1 for o in self.ops if o.eng == e) for e in ENGINES}
        print(f"[prog] ops={len(self.ops)} incs={n_inc} waits={n_wait} by_eng={by_eng}")
        keys = set(e for e in ENGINES)
        for o in self.ops:
            if o.dkey:
                keys.add(o.dkey)
        sems = {k: nc.alloc_semaphore(f"sem_{k}"[:28]) for k in sorted(keys)}
        self.sems = sems

        def run_engine(ename):
            def body(eng):
                for o in self.ops:
                    if o.eng != ename:
                        continue
                    for (pe, v) in o.waits:
                        eng.wait_ge(sems[pe], v)
                    with nc.named_scope(o.phase):
                        ins = o.fn(eng)
                    if o.need_inc:
                        assert ins is not None, f"op {o.idx} {o.tag} returned no instruction"
                        key = o.dkey if o.dkey else ename
                        ins.then_inc(sems[key], 16 if o.dma else 1)
            return body

        for ename in ENGINES:
            getattr(block, ename)(run_engine(ename))


def make_consts():
    c = {}
    c["ident64"] = np.vstack([np.eye(64), np.eye(64)]).astype(ml_dtypes.bfloat16)
    c["ident128"] = np.eye(128, dtype=ml_dtypes.bfloat16)
    c["ones64"] = np.ones((1, 64), dtype=ml_dtypes.bfloat16)
    c["ones128row"] = np.ones((1, 128), dtype=ml_dtypes.bfloat16)
    c["onescol128"] = np.ones((128, 1), dtype=np.float32)
    i_idx = np.arange(64)
    negstriu = -(i_idx[:, None] < i_idx[None, :]).astype(np.float32)
    triuincl = (i_idx[:, None] <= i_idx[None, :]).astype(np.float32)
    def tile_all(m):
        out = np.zeros((128, 512), np.float32)
        for r in range(2):
            for j in range(8):
                out[64*r:64*r+64, 64*j:64*j+64] = m
        return out
    c["negstriu"] = tile_all(negstriu).astype(ml_dtypes.bfloat16)
    c["triuincl"] = tile_all(triuincl).astype(ml_dtypes.bfloat16)
    hm = np.zeros((128, 128), np.float32)
    for fc in range(8):
        for p in range(128):
            hm[p, fc*16 + 2*fc + p//64] = 1.0
    c["headmaskC"] = hm.astype(ml_dtypes.bfloat16)
    m2 = np.zeros((16, 1024), np.float32)
    for h in range(16):
        fc, r = h // 2, h % 2
        m2[h, fc*128 + 64*r: fc*128 + 64*r + 64] = 1.0
    mp = np.zeros((64, 1024), np.float32)
    mp[0:16] = m2
    mp[32:48] = -m2
    c["mask2pack"] = mp.astype(ml_dtypes.bfloat16)
    c["mask2_f32"] = m2.astype(np.float32)
    m2s = np.zeros((16, 128), np.float32)
    for h in range(16):
        m2s[h, 64*(h % 2): 64*(h % 2)+64] = 1.0
    c["mask2s_f32"] = m2s.astype(np.float32)
    c["mask2s_bf"] = m2s.astype(ml_dtypes.bfloat16)
    hm8 = np.zeros((16, 512), np.float32)
    for h in range(16):
        j = h // 2
        hm8[h, 64*j:64*j+64] = 1.0
    c["headmask8_f32"] = hm8
    c["headmask8_bf"] = hm8.astype(ml_dtypes.bfloat16)
    hmj = np.zeros((16, 8), np.float32)
    for h in range(16):
        hmj[h, h//2] = 1.0
    c["headmaskj"] = hmj
    return c


COUPLES = {
    "y": ("hid", "hidFM", "qkvFM", "convq", "convk", "Vfm"),
    "outgate": ("hid", "hidFM"),
    "istage": ("Qn", "Qh"),
    "yqFM": ("Qn", "Qh", "Kn", "Khm"),
    "h2": ("ab", "r_siga", "r_logb", "r_beta", "r_L", "r_negL", "r_eL", "r_invq"),
    "prep": ("convq",),
    "prepbf": ("convq",),
    "prepstage": ("convk",),
    "prepsums": ("convk",),
    "normw_f32": ("qkvFM",),
    "wbuf_outA": ("wbuf_qkvA", "wbuf_g0"),
    "wbuf_outB": ("wbuf_qkvB", "wbuf_g1"),
    "wbuf_g0": ("wbuf_qkvA",),
    "wbuf_g1": ("wbuf_qkvB",),
    "wbuf_u0": ("wbuf_ctrlA", "wbuf_ctrlB"),
    "wbuf_u1": ("readsFM",),
    "wbuf_dA": ("readsFM",),
    "wbuf_dB": ("normedFM",),
}


def expand_couples(res):
    out = list(res)
    for r in res:
        out.extend(COUPLES.get(r, ()))
    return tuple(dict.fromkeys(out))


"""Main megakernel builder. See kb.py for conventions.

Phases per token chunk (T tokens, NTT=T/128 token tiles, NSC=T/64 scan chunks):
  F: x load, LN1, hidden->FM, qkv mm, ctrl mm, conv mm, l2norm, scaled variants
  S: per scan chunk: WY delta rule (all 16 heads batched in [128x512] tiles)
  E: RMS head-norm, out proj (token-major out), gate, residual -> x1
  FF: LN2, act_quant, gate/up mm, silu*up, act_quant, down mm, residual -> x2, store
"""


def ap_custom(t_ap, offset, dims):
    return bass.AP(tensor=t_ap.tensor, offset=offset, ap=[list(d) for d in dims])


class Builder:
    def __init__(self, nc, cfg: Cfg, taps=None):
        self.nc = nc
        self.cfg = cfg
        self.p = Prog()
        self.taps = taps or []
        self.tap_outs = {}
        self.consts_np = make_consts()
        self._alloc()

    def _alloc(self):
        nc, cfg = self.nc, self.cfg
        T = cfg.T
        self.dram_in = {}
        for name, shape in [
            ("x", [cfg.S, D]), ("qkv_w", [QKV, D]), ("ctrl_w", [CTRL, D]),
            ("ctrl_b", [CTRL]), ("convq_w", [D, 4]), ("convk_w", [D, 4]),
            ("convv_w", [D, 4]), ("norm_w_t", [16, D]), ("out_w", [D, D]),
            ("out_b", [D]), ("w_gate", [INNER, D]), ("w_up", [INNER, D]),
            ("w_down", [D, INNER]),
        ]:
            self.dram_in[name] = nc.dram_tensor(name, shape, F32, kind="ExternalInput").ap()
        self.dram_out = nc.dram_tensor("out", [cfg.S, D], F32, kind="ExternalOutput").ap()
        self.scr = {}
        for name, shape in [("qkvT", [D, QKV]), ("ctrlT", [D, CTRL]),
                            ("outT", [D, D]), ("wgT", [D, INNER]),
                            ("wuT", [D, INNER]), ("wdT", [INNER, D])]:
            self.scr[name] = nc.dram_tensor("scr_" + name, shape, BF16, kind="Internal").ap()
        self.cst = {}
        for k, v in self.consts_np.items():
            self.cst[k] = nc.inline_tensor(np.ascontiguousarray(v), "c_" + k).ap()
        self.cst["onesT_bf"] = nc.inline_tensor(np.ones((1, T), BF), "c_onesT").ap()
        self.cst["ones64_f32"] = nc.inline_tensor(np.ones((1, 64), np.float32), "c_ones64f").ap()
        self.cst["ones128row_f32"] = nc.inline_tensor(np.ones((1, 128), np.float32), "c_o128rf").ap()
        self.cst["onescol128_f32"] = nc.inline_tensor(np.ones((128, 1), np.float32), "c_oc128f").ap()
        self.cst["onesf_16x64"] = nc.inline_tensor(np.ones((16, 64), np.float32), "c_of1664").ap()
        self.cst["biasc"] = nc.inline_tensor(
            np.tile(np.array([[1e-5, 1e-6, 0.01, 0.0]], np.float32), (128, 1)), "c_biasc").ap()

        A = nc.alloc_sbuf_tensor
        s = {}
        for nm, shape, dt in [
            ("ident64", [128, 64], BF16), ("ident128", [128, 128], BF16),
            ("negstriu", [128, 512], BF16), ("triuincl", [128, 512], BF16),
            ("headmaskC", [128, 128], BF16), ("mask2pack", [64, 1024], BF16),
            ("mask2s_f32", [16, 128], F32),
            ("mask2s_bf", [16, 128], BF16), ("headmask8_f32", [16, 512], F32),
            ("headmask8_bf", [16, 512], BF16), ("headmaskj", [16, 8], F32),
            ("onesT_bf", [1, T], BF16), ("ones64_bf", [1, 64], BF16),
            ("ones64_f32", [1, 64], F32), ("ones128row_bf", [1, 128], BF16),
            ("ones128row_f32", [1, 128], F32), ("onescol128_f32", [128, 1], F32),
            ("onesf_16x64", [16, 64], F32), ("biasc", [128, 4], F32),
            ("diags", [128, 96 * 128], BF16),
            ("ctrl_b_bf", [1, CTRL], BF16), ("out_b_bf", [1, D], BF16),
            ("mask2w", [16, 1024], BF16),
            ("cw", [128, 96], F32),
            ("xbuf0", [128, 2 * 1024], F32), ("xbuf1", [128, 2 * 1024], F32),
            ("stats", [128, 32], F32), ("stat1", [1, 64], F32),
            ("hid", [128, 2 * 1024], BF16), ("hidFM", [128, 8 * T], BF16),
            ("qkvFM", [128, 24 * (T + 3)], BF16),
            ("convq", [128, 8 * T], F32), ("convk", [128, 8 * T], F32),
            ("Vfm", [128, 8 * T], BF16), ("Qn", [128, 8 * T], BF16),
            ("Qh", [128, 8 * T], BF16), ("Kn", [128, 8 * T], BF16),
            ("Khm", [128, 8 * T], BF16),
            ("ab", [64, T], F32),
            ("r_siga", [16, T], F32), ("r_logb", [16, T], F32),
            ("r_beta", [16, T], F32), ("r_L", [16, T], F32),
            ("r_negL", [16, T], F32), ("r_eL", [16, T], F32),
            ("r_invq", [16, T], F32), ("r_invk", [16, T], F32),
            ("r_t", [16, T], F32),
            ("rb_qn", [16, T], BF16), ("rb_qh", [16, T], BF16),
            ("rb_kn", [16, T], BF16), ("rb_khm", [16, T], BF16),
            ("lhsE", [16, 128], F32), ("lhsB", [16, 128], BF16),
            ("E", [128, 512], F32), ("bpartU", [128, 512], F32),
            ("bmask_t", [128, 512], BF16), ("bmask_B", [128, 512], BF16),
            ("NL0", [128, 512], BF16), ("NLa", [128, 512], BF16),
            ("NLb", [128, 512], BF16),
            ("NN0", [128, 512], BF16), ("NNa", [128, 512], BF16),
            ("NNb", [128, 512], BF16),
            ("BL0", [128, 512], BF16), ("BL1", [128, 512], BF16),
            ("R", [128, 512], BF16),
            ("readstm", [128, 512], BF16), ("Sall", [128, 512], BF16),
            ("Ktmp", [128, 512], BF16), ("t1", [128, 512], F32),
            ("t2", [128, 512], F32),
            ("gcall0", [128, 8], F32), ("gcall1", [128, 8], F32),
            ("dall0", [128, 8], F32), ("dall1", [128, 8], F32),
            ("gcrhs", [16, 8], F32),
            ("readsFM", [128, 8 * T], BF16), ("normedFM", [128, 8 * T], BF16),
            ("Lmaskd", [16, 512], F32), ("sqf", [128, 1024], F32),
            ("sqbuf", [128, 2 * T], BF16), ("rmsrow", [16, T], F32),
            ("rmsbf", [16, T], BF16),
            ("h2q", [128, 2 * 1024], BF16), ("h2qFM", [128, 8 * T], BF16),
            ("yq", [128, 2 * 4096], BF16),
            ("fstat", [128, 24], F32), ("qhist", [128, 72], BF16),
            ("wsc", [128, 4], F32),
            ("wbuf_qkvA", [128, 8 * 256], BF16), ("wbuf_qkvB", [128, 8 * 256], BF16),
            ("wbuf_ctrlA", [128, CTRL], BF16), ("wbuf_ctrlB", [128, CTRL], BF16),

        ]:
            s[nm] = A(nm, shape, dt)
        AT = nc.alloc_sbuf_tensor_at
        adr = lambda nm: nc.lookup_mloc(s[nm]).addr
        s["y"] = AT("y", [128, 2 * 4096], F32, offset=adr("hid"))
        s["outgate"] = AT("outgate", [128, 2 * 1024], F32, offset=adr("hid"))
        s["istage"] = AT("istage", [128, 2048], I32, offset=adr("Qn"))
        s["yqFM"] = AT("yqFM", [128, 32 * T], BF16, offset=adr("Qn"))
        s["h2"] = AT("h2", [128, 2 * 1024], F32, offset=adr("ab"))
        s["prep"] = AT("prep", [128, 1024], F32, offset=adr("convq"))
        s["prepbf"] = AT("prepbf", [128, 1024], BF16, offset=adr("convq") + 4096)
        s["prepstage"] = AT("prepstage", [128, 512], BF16, offset=adr("convk"))
        s["prepsums"] = AT("prepsums", [128, 64], F32, offset=adr("convk") + 1024)
        s["normw_f32"] = AT("normw_f32", [16, 1024], F32, offset=adr("qkvFM"))
        s["wbuf_outA"] = AT("wbuf_outA", [128, 2048], BF16, offset=adr("wbuf_qkvA"))
        s["wbuf_outB"] = AT("wbuf_outB", [128, 2048], BF16, offset=adr("wbuf_qkvB"))
        s["wbuf_g0"] = AT("wbuf_g0", [128, 4 * 512], BF16, offset=adr("wbuf_qkvA"))
        s["wbuf_g1"] = AT("wbuf_g1", [128, 4 * 512], BF16, offset=adr("wbuf_qkvB"))
        s["wbuf_u0"] = AT("wbuf_u0", [128, 4 * 512], BF16, offset=adr("wbuf_ctrlA"))
        s["wbuf_u1"] = AT("wbuf_u1", [128, 4 * 512], BF16, offset=adr("readsFM"))
        s["wbuf_dA"] = AT("wbuf_dA", [128, 2048], BF16, offset=adr("readsFM"))
        s["wbuf_dB"] = AT("wbuf_dB", [128, 2048], BF16, offset=adr("normedFM"))
        self.s = s
        used = nc.sbuf_base - 16481
        print(f"[alloc] sbuf used ~{used/1024:.1f} KB of {(229344-16481)/1024:.1f}")
        self.P = [nc.alloc_psum_tensor(f"P{i}", [128, 512], F32) for i in range(8)]
        for name, shape, dt in self.taps:
            self.tap_outs[name] = nc.dram_tensor("tap_" + name, shape, dt,
                                                 kind="ExternalOutput").ap()

    def tap(self, name, ap_fn, reads):
        if name not in self.tap_outs:
            return
        out = self.tap_outs[name]
        self.p.op("gpsimd", reads=tuple(reads), dma=True, tag="tap_" + name)(
            lambda e, o=out, f=ap_fn: e.dma_start(out=o, in_=f()))

    def load_consts(self):
        p, s, cst = self.p, self.s, self.cst
        pairs = [(k, k) for k in
                 ["ident64", "ident128", "negstriu", "triuincl",
                  "headmaskC", "mask2pack", "mask2s_f32", "mask2s_bf",
                  "headmask8_f32", "headmask8_bf", "headmaskj", "onesT_bf",
                  "ones64_f32", "ones128row_f32", "onescol128_f32", "onesf_16x64"]]
        pairs += [("ones64_bf", "ones64"), ("ones128row_bf", "ones128row"),
                  ("biasc", "biasc")]
        for sb, ck in pairs:
            p.op("sync", writes=("c_" + sb,), dma=True, tag="ldc")(
                lambda e, a=s[sb], b=cst[ck]: e.dma_start(out=a[:], in_=b))
        p.op("gpsimd", writes=("c_ctrl_b",), dma=True)(
            lambda e: e.dma_start(out=s["ctrl_b_bf"][:], in_=self.dram_in["ctrl_b"][None, :]))
        p.op("gpsimd", writes=("c_out_b",), dma=True)(
            lambda e: e.dma_start(out=s["out_b_bf"][:], in_=self.dram_in["out_b"][None, :]))
        p.op("sync", writes=("c_normw",), dma=True)(
            lambda e: e.dma_start(out=s["normw_f32"][:], in_=self.dram_in["norm_w_t"]))
        p.op("vector", reads=("c_normw", "c_mask2pack"), writes=("c_mask2w",))(
            lambda e: e.tensor_mul(s["mask2w"][:], s["normw_f32"][:],
                                   s["mask2pack"][0:16, :]))
        for si, nm in enumerate(["convq_w", "convk_w", "convv_w"]):
            in_ap = ap_custom(self.dram_in[nm], 0, [[4, 128], [512, 8], [1, 4]])
            out_ap = ap_custom(s["cw"][:], si * 32, [[96, 128], [4, 8], [1, 4]])
            p.op("sync", writes=("c_cw",), dma=True)(
                lambda e, i=in_ap, o=out_ap: e.dma_start(out=o, in_=i))
        for si in range(3):
            for f in range(8):
                for j in range(4):
                    idx = si * 32 + f * 4 + j
                    p.op("scalar", reads=("c_cw", "c_ident128"), writes=("c_diag",),
                         tag="diag")(
                        lambda e, idx=idx, col=si * 32 + f * 4 + j: e.activation(
                            out=s["diags"][:, idx * 128:(idx + 1) * 128],
                            in_=s["ident128"][:], func=AF.Copy,
                            scale=s["cw"][:, col:col + 1]))

    def _transpose_store(self, dst_scr, rowlen, col0, row0, tag):
        p, s, P = self.p, self.s, self.P
        for g in range(2):
            def mmf(e, g=g):
                ins = None
                for b in range(4):
                    cb = g * 4 + b
                    ins = e.matmul(P[6][:, b * 128:(b + 1) * 128],
                                   s["prepbf"][:, cb * 128:(cb + 1) * 128],
                                   s["ident128"][:], start=True, stop=True)
                return ins
            p.op("tensor", reads=("prepbf", "c_ident128"), writes=("P6",), tag=tag)(mmf)
            p.op("vector", reads=("P6",), writes=("prepstage",))(
                lambda e: e.tensor_copy(s["prepstage"][:], P[6][:, 0:512]))
            out_ap = ap_custom(dst_scr, (row0 + g * 512) * rowlen + col0,
                               [[rowlen, 128], [128 * rowlen, 4], [1, 128]])
            in_ap = ap_custom(s["prepstage"][:], 0, [[512, 128], [128, 4], [1, 128]])
            p.op("sync", reads=("prepstage",), dma=True, tag=tag + "st")(
                lambda e, i=in_ap, o=out_ap: e.dma_start(out=o, in_=i))

    def prep_weights_attn(self):
        p, s, P = self.p, self.s, self.P
        for nm, Osz, dstn in [("qkv_w", QKV, "qkvT"), ("ctrl_w", CTRL, "ctrlT"),
                              ("out_w", D, "outT")]:
            dst = self.scr[dstn]
            for t in range((Osz + 127) // 128):
                nr = min(128, Osz - t * 128)
                p.op("sync", writes=("prep",), dma=True, tag="pwld")(
                    lambda e, t=t, nm=nm, nr=nr: e.dma_start(
                        out=s["prep"][0:nr, :],
                        in_=self.dram_in[nm][t * 128:t * 128 + nr, :]))
                if nr == 128:
                    p.op("vector", reads=("prep",), writes=("prepbf",))(
                        lambda e: e.tensor_copy(s["prepbf"][:], s["prep"][:]))
                else:
                    p.op("vector", writes=("prepbf",))(
                        lambda e: e.memset(s["prepbf"][:], 0.0))
                    p.op("vector", reads=("prep",), writes=("prepbf",))(
                        lambda e, nr=nr: e.tensor_copy(s["prepbf"][0:nr, :], s["prep"][0:nr, :]))
                self._transpose_store_n(dst, Osz, t * 128, nr, "tp_" + nm, scrres="scr_" + dstn)
        return

    def prep_weights_ffn(self):
        p, s, P = self.p, self.s, self.P
        for nm, Osz, Isz, dstn in [("w_gate", INNER, D, "wgT"),
                                   ("w_up", INNER, D, "wuT"),
                                   ("w_down", D, INNER, "wdT")]:
            ntile = Osz * Isz // (128 * 1024)
            for t in range(ntile):
                p.op("sync", writes=("prep",), dma=True, tag="fq1")(
                    lambda e, t=t, nm=nm: e.dma_start(
                        out=s["prep"][:],
                        in_=ap_custom(self.dram_in[nm], t * 131072,
                                      [[1024, 128], [1, 1024]])))
                p.op("vector", reads=("prep",), writes=("psums",))(
                    lambda e, t=t: e.tensor_reduce(
                        s["prepsums"][:, t:t + 1], s["prep"][:], AX.X, ALU.add,
                        apply_absolute_value=True))
            p.op("vector", reads=("prep",), writes=("spacer",))(
                lambda e: e.tensor_copy(s["sqf"][:, 0:1024], s["prep"][:]))
            p.op("vector", reads=("psums",), writes=("psumtot",))(
                lambda e, nt=ntile: e.tensor_reduce(
                    s["prepsums"][:, 63:64], s["prepsums"][:, 0:nt], AX.X, ALU.add))
            p.op("tensor", reads=("psumtot", "c_onescol128_f32"), writes=("P7",))(
                lambda e: e.matmul(P[7][0:1, 0:1], s["prepsums"][:, 63:64],
                                   s["onescol128_f32"][:], start=True, stop=True))
            p.op("vector", reads=("P7",), writes=("stat1",))(
                lambda e, n=Osz * Isz: e.tensor_scalar(
                    s["stat1"][0:1, 0:1], P[7][0:1, 0:1], 1.0 / n, 1e-5,
                    ALU.mult, ALU.max))
            p.op("tensor", reads=("stat1", "c_ones128row_f32"), writes=("P7",))(
                lambda e: e.matmul(P[7][:, 0:1], s["ones128row_f32"][:],
                                   s["stat1"][0:1, 0:1], start=True, stop=True))
            wi = {"w_gate": 0, "w_up": 1, "w_down": 2}[nm]
            p.op("scalar", reads=("P7",), writes=("wmean",))(
                lambda e: e.activation(out=s["stats"][:, 29:30], in_=P[7][:, 0:1],
                                       func=AF.Copy))
            p.op("scalar", reads=("P7",), writes=("wsc",))(
                lambda e, wi=wi: e.activation(out=s["wsc"][:, wi:wi + 1],
                                              in_=P[7][:, 0:1], func=AF.Copy))
            p.op("vector", reads=("wmean",), writes=("wscale",))(
                lambda e: e.reciprocal(s["stats"][:, 30:31], s["stats"][:, 29:30]))
            if nm == "w_gate" and "pstat" in self.tap_outs:
                self.p.op("gpsimd", reads=("wscale", "stat1", "psums"), dma=True, tag="tpstat")(
                    lambda e: e.dma_start(out=self.tap_outs["pstat"], in_=s["stat1"][:]))
                self.p.op("gpsimd", reads=("wscale",), dma=True, tag="tpsums")(
                    lambda e: e.dma_start(out=self.tap_outs["psums"], in_=s["prepsums"][:]))
            if nm in ("w_gate", "w_up"):
                tiles = [(t * 128, 0, Osz, t * 128, 0) for t in range(32)]
                tiles = [(t * 128, 0, Osz, t * 128, 0) for t in range(32)]
            else:
                tiles = [(t * 128, q * 1024, D, t * 128, q * 1024)
                         for t in range(8) for q in range(4)]
            for (r0, c0, rowlen, dcol0, drow0) in tiles:
                p.op("sync", writes=("prep",), dma=True, tag="fq2")(
                    lambda e, r0=r0, c0=c0, nm=nm: e.dma_start(
                        out=s["prep"][:],
                        in_=ap_custom(self.dram_in[nm], r0 * (D if nm != "w_down" else INNER) + c0,
                                      [[(D if nm != "w_down" else INNER), 128], [1, 1024]])))
                p.op("vector", reads=("prep", "wscale"), writes=("prep",))(
                    lambda e: e.tensor_scalar(s["prep"][:], s["prep"][:],
                                              s["stats"][:, 30:31], 1.0, ALU.mult, ALU.min))
                p.op("vector", reads=("prep",), writes=("prep",))(
                    lambda e: e.tensor_scalar(s["prep"][:], s["prep"][:], -1.0, None, ALU.max))
                p.op("vector", reads=("prep",), writes=("istage",))(
                    lambda e: e.tensor_copy(s["istage"][:, 0:1024], s["prep"][:]))
                p.op("vector", reads=("istage",), writes=("prepbf",))(
                    lambda e: e.tensor_copy(s["prepbf"][:], s["istage"][:, 0:1024]))
                self._transpose_store_n(self.scr[dstn], rowlen, dcol0, 128,
                                        "tq_" + nm, row0=drow0, scrres="scr_" + dstn)

    def _transpose_store_n(self, dst_scr, rowlen, col0, ncols, tag, row0=0, scrres="scr"):
        p, s, P = self.p, self.s, self.P
        for g in range(2):
            def mmf(e, g=g):
                ins = None
                for b in range(4):
                    cb = g * 4 + b
                    ins = e.matmul(P[6][:, b * 128:(b + 1) * 128],
                                   s["prepbf"][:, cb * 128:(cb + 1) * 128],
                                   s["ident128"][:], start=True, stop=True)
                return ins
            p.op("tensor", reads=("prepbf", "c_ident128"), writes=("P6",), tag=tag)(mmf)
            p.op("vector", reads=("P6",), writes=("prepstage",))(
                lambda e: e.tensor_copy(s["prepstage"][:], P[6][:, 0:512]))
            out_ap = ap_custom(dst_scr, (row0 + g * 512) * rowlen + col0,
                               [[rowlen, 128], [128 * rowlen, 4], [1, ncols]])
            in_ap = ap_custom(s["prepstage"][:], 0, [[512, 128], [128, 4], [1, ncols]])
            p.op("sync", reads=("prepstage",), writes=(scrres,), dma=True, tag=tag + "st")(
                lambda e, i=in_ap, o=out_ap: e.dma_start(out=o, in_=i))

    def frontend(self, ch):
        p, s, P, cfg = self.p, self.s, self.P, self.cfg
        T, NTT = cfg.T, cfg.NTT
        xb = s["xbuf%d" % (ch % 2)]
        xr = "xbuf%d" % (ch % 2)
        in_ap = ap_custom(self.dram_in["x"], ch * T * D, [[D, 128], [128 * D, NTT], [1, D]])
        out_ap = ap_custom(xb[:], 0, [[2 * 1024, 128], [1024, NTT], [1, 1024]])
        p.op("sync", writes=(xr,), dma=True, tag="xld")(
            lambda e, i=in_ap, o=out_ap: e.dma_start(out=o, in_=i))
        self._ln(xb, xr, s["hid"], "hid", statcol=0)
        for tt in range(NTT):
            def mmf(e, tt=tt):
                ins = None
                for g in range(2):
                    for b in range(4):
                        fc = g * 4 + b
                        ins = e.matmul(P[g][:, b * 128:(b + 1) * 128],
                                       s["hid"][:, tt * 1024 + fc * 128: tt * 1024 + (fc + 1) * 128],
                                       s["ident128"][:], start=True, stop=True)
                return ins
            p.op("tensor", reads=("hid", "c_ident128"), writes=("P0", "P1"), tag="hidT")(mmf)
            for g in range(2):
                out_ap = ap_custom(s["hidFM"][:], tt * 128 + g * 4 * T,
                                   [[8 * T, 128], [T, 4], [1, 128]])
                p.op("vector", reads=("P%d" % g,), writes=("hidFM",))(
                    lambda e, g=g, o=out_ap: e.tensor_copy(o, P[g][:, 0:512]))
        self.tap("hid", lambda: s["hid"][:], ["hid"]) if ch == getattr(self, "tap_chunk", 0) else None
        self.tap("stats", lambda: s["stats"][:], ["stats"]) if ch == getattr(self, "tap_chunk", 0) else None
        self.tap("hidFM", lambda: s["hidFM"][:], ["hidFM"]) if ch == getattr(self, "tap_chunk", 0) else None
        if ch == 0:
            p.op("vector", writes=("qkvFM_m",), tag="qmz")(
                lambda e: e.memset(ap_custom(s["qkvFM"][:], 0,
                                             [[24 * (T + 3), 128], [T + 3, 24], [1, 3]]), 0.0))
        else:
            mg_out = ap_custom(s["qkvFM"][:], 0, [[24 * (T + 3), 128], [T + 3, 24], [1, 3]])
            p.op("vector", reads=("qhist",), writes=("qkvFM_m",), tag="qmc")(
                lambda e, o=mg_out: e.tensor_copy(o, s["qhist"][:]))
        for pair in range(12):
            buf = "wbuf_qkv" + ("A" if pair % 2 == 0 else "B")
            p.op("sync", reads=("scr_qkvT",), writes=(buf,), dma=True, tag="wqkv")(
                lambda e, pair=pair, buf=buf: e.dma_start(
                    out=s[buf][:],
                    in_=ap_custom(self.scr["qkvT"], pair * 256,
                                  [[QKV, 128], [128 * QKV, 8], [1, 256]])))
            for ms in range(2):
                m = pair * 2 + ms
                bank = m % 2
                def mmf(e, ms=ms, bank=bank, buf=buf):
                    ins = None
                    for k in range(8):
                        ins = e.matmul(P[bank][:, 0:T],
                                       s[buf][:, k * 256 + ms * 128: k * 256 + (ms + 1) * 128],
                                       s["hidFM"][:, k * T:(k + 1) * T],
                                       start=(k == 0), stop=(k == 7))
                    return ins
                p.op("tensor", reads=(buf, "hidFM"), writes=("P%d" % bank,),
                     tag="qkvmm")(mmf)
                p.op("scalar", reads=("P%d" % bank,), writes=("qkvFM",))(
                    lambda e, m=m, bank=bank: e.activation(
                        out=s["qkvFM"][:, m * (T + 3) + 3: (m + 1) * (T + 3)],
                        in_=P[bank][:, 0:T], func=AF.Copy))
        for k in range(8):
            cb = "wbuf_ctrl" + ("A" if k % 2 == 0 else "B")
            p.op("sync", reads=("scr_ctrlT",), writes=(cb,), dma=True, tag="wctrl")(
                lambda e, k=k, cb=cb: e.dma_start(
                    out=s[cb][:],
                    in_=ap_custom(self.scr["ctrlT"], k * 128 * CTRL, [[CTRL, 128], [1, CTRL]])))
            def mmab(e, k=k, cb=cb):
                e.matmul(P[2][0:16, 0:T], s[cb][:, 0:16],
                         s["hidFM"][:, k * T:(k + 1) * T],
                         start=(k == 0), stop=False)
                return e.matmul(P[2][32:48, 0:T], s[cb][:, 16:32],
                                s["hidFM"][:, k * T:(k + 1) * T],
                                start=(k == 0), stop=False)
            p.op("tensor", reads=(cb, "hidFM"), writes=("P2",), tag="abmm")(mmab)
            def mmgate(e, k=k, cb=cb):
                ins = None
                for tt in range(NTT):
                    for hf in range(2):
                        ins = e.matmul(
                            P[3 + tt * 2 + hf][:, 0:512],
                            s["hidFM"][:, k * T + tt * 128: k * T + (tt + 1) * 128],
                            s[cb][:, 32 + hf * 512: 32 + (hf + 1) * 512],
                            start=(k == 0), stop=False)
                return ins
            p.op("tensor", reads=(cb, "hidFM"), writes=("P3", "P4", "P5", "P6"),
                 tag="gatemm")(mmgate)
        def mmab_b(e):
            e.matmul(P[2][0:16, 0:T], s["ctrl_b_bf"][0:1, 0:16],
                     s["onesT_bf"][0:1, 0:T], start=False, stop=True)
            return e.matmul(P[2][32:48, 0:T], s["ctrl_b_bf"][0:1, 16:32],
                            s["onesT_bf"][0:1, 0:T], start=False, stop=True)
        p.op("tensor", reads=("c_ctrl_b", "c_onesT_bf"), writes=("P2",), tag="abbias")(mmab_b)
        def mmgate_b(e):
            ins = None
            for tt in range(NTT):
                for hf in range(2):
                    ins = e.matmul(P[3 + tt * 2 + hf][:, 0:512],
                                   s["ones128row_bf"][0:1, :],
                                   s["ctrl_b_bf"][0:1, 32 + hf * 512: 32 + (hf + 1) * 512],
                                   start=False, stop=True)
            return ins
        p.op("tensor", reads=("c_ctrl_b", "c_ones128row_bf"),
             writes=("P3", "P4", "P5", "P6"), tag="gatebias")(mmgate_b)
        p.op("vector", reads=("P2",), writes=("ab",))(
            lambda e: e.tensor_copy(s["ab"][0:48, 0:T], P[2][0:48, 0:T]))
        for tt in range(NTT):
            for hf in range(2):
                p.op("scalar", reads=("P%d" % (3 + tt * 2 + hf),), writes=("outgate",))(
                    lambda e, tt=tt, hf=hf: e.activation(
                        out=s["outgate"][:, tt * 1024 + hf * 512: tt * 1024 + (hf + 1) * 512],
                        in_=P[3 + tt * 2 + hf][:, 0:512], func=AF.Sigmoid))
        for si, (dstn, dstr) in enumerate([("convq", "convq"), ("convk", "convk"),
                                           ("Vfm", "Vfm")]):
            for fc in range(8):
                m = si * 8 + fc
                bank = fc % 2
                def mmf(e, si=si, fc=fc, m=m, bank=bank):
                    ins = None
                    for j in range(4):
                        idx = si * 32 + fc * 4 + j
                        ins = e.matmul(P[bank][:, 0:T],
                                       s["diags"][:, idx * 128:(idx + 1) * 128],
                                       s["qkvFM"][:, m * (T + 3) + j: m * (T + 3) + j + T],
                                       start=(j == 0), stop=(j == 3))
                    return ins
                p.op("tensor", reads=("c_diag", "qkvFM", "qkvFM_m"),
                     writes=("P%d" % bank,), tag="convmm")(mmf)
                p.op("scalar", reads=("P%d" % bank,), writes=(dstr,))(
                    lambda e, fc=fc, bank=bank, dstn=dstn: e.activation(
                        out=s[dstn][:, fc * T:(fc + 1) * T],
                        in_=P[bank][:, 0:T], func=AF.Silu))
        mg_in = ap_custom(s["qkvFM"][:], T, [[24 * (T + 3), 128], [T + 3, 24], [1, 3]])
        p.op("vector", reads=("qkvFM",), writes=("qhist",), tag="qhist")(
            lambda e, i=mg_in: e.tensor_copy(s["qhist"][:], i))
        for src, dst in [("convq", "r_invq"), ("convk", "r_invk")]:
            for fc in range(8):
                h0 = (fc % 2) * T
                p.op("scalar", reads=(src,), writes=("sqbuf%d" % (fc % 2),))(
                    lambda e, fc=fc, src=src, h0=h0: e.activation(
                        out=s["sqbuf"][:, h0:h0 + T], in_=s[src][:, fc * T:(fc + 1) * T],
                        func=AF.Square))
                def mmf(e, fc=fc, h0=h0):
                    return e.matmul(P[7][0:16, 0:T],
                                    s["headmaskC"][:, fc * 16:(fc + 1) * 16],
                                    s["sqbuf"][:, h0:h0 + T], start=(fc == 0), stop=(fc == 7))
                p.op("tensor", reads=("sqbuf%d" % (fc % 2), "c_headmaskC"),
                     writes=("P7",), tag="l2mm")(mmf)
            p.op("scalar", reads=("P7",), writes=("r_t",))(
                lambda e: e.activation(out=s["r_t"][:, 0:T], in_=P[7][0:16, 0:T],
                                       func=AF.Sqrt))
            p.op("vector", reads=("r_t",), writes=("r_t",))(
                lambda e: e.tensor_scalar(s["r_t"][:, 0:T], s["r_t"][:, 0:T], 1e-6,
                                          None, ALU.max))
            p.op("vector", reads=("r_t",), writes=(dst,))(
                lambda e, dst=dst: e.reciprocal(s[dst][:, 0:T], s["r_t"][:, 0:T]))
        p.op("scalar", reads=("ab",), writes=("r_siga",))(
            lambda e: e.activation(out=s["r_siga"][:, 0:T], in_=s["ab"][0:16, 0:T],
                                   func=AF.Sigmoid))
        p.op("scalar", reads=("r_siga",), writes=("r_logb",))(
            lambda e: e.activation(out=s["r_logb"][:, 0:T], in_=s["r_siga"][:, 0:T],
                                   func=AF.Ln, scale=0.98, bias=s["biasc"][0:16, 2:3]))
        p.op("scalar", reads=("ab",), writes=("r_beta",))(
            lambda e: e.activation(out=s["r_beta"][:, 0:T], in_=s["ab"][32:48, 0:T],
                                   func=AF.Sigmoid))
        for sc in range(cfg.NSC):
            p.op("vector", reads=("r_logb", "c_onesf_16x64"), writes=("r_L",))(
                lambda e, sc=sc: e.tensor_tensor_scan(
                    s["r_L"][:, sc * 64:(sc + 1) * 64], s["onesf_16x64"][:],
                    s["r_logb"][:, sc * 64:(sc + 1) * 64], 0.0, ALU.mult, ALU.add))
        p.op("vector", reads=("r_L",), writes=("r_negL",))(
            lambda e: e.tensor_scalar(s["r_negL"][:, 0:T], s["r_L"][:, 0:T], -1.0,
                                      None, ALU.mult))
        p.op("scalar", reads=("r_L",), writes=("r_eL",))(
            lambda e: e.activation(out=s["r_eL"][:, 0:T], in_=s["r_L"][:, 0:T],
                                   func=AF.Exp))
        p.op("vector", reads=("r_invq",), writes=("rb_qn",))(
            lambda e: e.tensor_copy(s["rb_qn"][:, 0:T], s["r_invq"][:, 0:T]))
        p.op("vector", reads=("r_invk",), writes=("rb_kn",))(
            lambda e: e.tensor_copy(s["rb_kn"][:, 0:T], s["r_invk"][:, 0:T]))
        p.op("vector", reads=("r_invq", "r_eL"), writes=("rb_qh",))(
            lambda e: e.tensor_mul(s["rb_qh"][:, 0:T], s["r_invq"][:, 0:T],
                                   s["r_eL"][:, 0:T]))
        p.op("vector", reads=("r_invk", "r_eL"), writes=("rb_khm",))(
            lambda e: e.scalar_tensor_tensor(s["rb_khm"][:, 0:T], s["r_invk"][:, 0:T],
                                             -1.0, s["r_eL"][:, 0:T],
                                             ALU.mult, ALU.mult))
        for (row, r0m, src, dst) in [("rb_qn", 0, "convq", "Qn"),
                                     ("rb_qh", 0, "convq", "Qh"),
                                     ("rb_kn", 0, "convk", "Kn"),
                                     ("rb_khm", 0, "convk", "Khm")]:
            for fc in range(8):
                pb = 7 if fc % 2 == 0 else 0
                def mmf(e, row=row, r0m=r0m, fc=fc, pb=pb):
                    return e.matmul(P[pb][:, 0:T],
                                    s["mask2pack"][r0m:r0m + 16, fc * 128:(fc + 1) * 128],
                                    s[row][:, 0:T], start=True, stop=True)
                p.op("tensor", reads=(row, "c_mask2pack"), writes=("P%d" % pb,), tag="bcmm")(mmf)
                p.op("vector", reads=("P%d" % pb, src), writes=(dst,))(
                    lambda e, fc=fc, src=src, dst=dst, pb=pb: e.tensor_mul(
                        s[dst][:, fc * T:(fc + 1) * T],
                        s[src][:, fc * T:(fc + 1) * T], P[pb][:, 0:T]))

    def _ln(self, xb, xr, dst, dstr, statcol, eps=1e-5):
        p, s, cfg = self.p, self.s, self.cfg
        for tt in range(cfg.NTT):
            sl = (tt * 1024, (tt + 1) * 1024)
            c0 = statcol + tt * 8
            p.op("vector", reads=(xr,), writes=("stats",))(
                lambda e, sl=sl, c0=c0: e.tensor_reduce(
                    s["stats"][:, c0:c0 + 1], xb[:, sl[0]:sl[1]], AX.X, ALU.add))
            p.op("scalar", reads=(xr,), writes=("sq128",))(
                lambda e, sl=sl: e.activation(out=s["sqf"][:, 0:1024],
                                              in_=xb[:, sl[0]:sl[1]], func=AF.Square))
            p.op("vector", reads=("sq128",), writes=("stats",))(
                lambda e, c0=c0: e.tensor_reduce(
                    s["stats"][:, c0 + 1:c0 + 2], s["sqf"][:, 0:1024], AX.X, ALU.add))
            p.op("scalar", reads=("stats",), writes=("stats",))(
                lambda e, c0=c0: e.activation(out=s["stats"][:, c0 + 6:c0 + 7],
                                              in_=s["stats"][:, c0:c0 + 1],
                                              func=AF.Copy, scale=1.0 / 1024))
            p.op("scalar", reads=("stats",), writes=("stats",))(
                lambda e, c0=c0: e.activation(out=s["stats"][:, c0 + 7:c0 + 8],
                                              in_=s["stats"][:, c0 + 1:c0 + 2],
                                              func=AF.Copy, scale=1.0 / 1024))
            p.op("vector", reads=("stats",), writes=("stats",))(
                lambda e, c0=c0: e.tensor_mul(s["stats"][:, c0 + 2:c0 + 3],
                                              s["stats"][:, c0 + 6:c0 + 7],
                                              s["stats"][:, c0 + 6:c0 + 7]))
            p.op("gpsimd", reads=("stats",), writes=("stats",))(
                lambda e, c0=c0: e.tensor_sub(s["stats"][:, c0 + 3:c0 + 4],
                                              s["stats"][:, c0 + 7:c0 + 8],
                                              s["stats"][:, c0 + 2:c0 + 3]))
            p.op("scalar", reads=("stats",), writes=("stats",))(
                lambda e, c0=c0: e.activation(out=s["stats"][:, c0 + 4:c0 + 5],
                                              in_=s["stats"][:, c0 + 3:c0 + 4],
                                              func=AF.Sqrt, bias=s["biasc"][:, 0:1] if abs(eps - 1e-5) < 1e-12 else s["biasc"][:, 1:2]))
            p.op("vector", reads=("stats",), writes=("stats",))(
                lambda e, c0=c0: e.reciprocal(s["stats"][:, c0 + 5:c0 + 6],
                                              s["stats"][:, c0 + 4:c0 + 5]))
            p.op("vector", reads=(xr, "stats"), writes=("sq128",))(
                lambda e, sl=sl, c0=c0: e.tensor_scalar(
                    s["sqf"][:, 0:1024], xb[:, sl[0]:sl[1]],
                    s["stats"][:, c0 + 6:c0 + 7], None, ALU.subtract))
            p.op("scalar", reads=("sq128", "stats"), writes=(dstr,))(
                lambda e, sl=sl, c0=c0: e.activation(
                    out=dst[:, sl[0]:sl[1]], in_=s["sqf"][:, 0:1024],
                    func=AF.Copy, scale=s["stats"][:, c0 + 5:c0 + 6]))
    def scan_front(self, ch, sc):
        p, s, P, cfg = self.p, self.s, self.P, self.cfg
        T = cfg.T
        ts = sc * 64
        BLn = "BL%d" % (sc % 2)

        def blk(t):
            return [(t[64 * r:64 * r + 64, 64 * j:64 * j + 64], r, j)
                    for j in range(8) for r in range(2)]

        def fmsl(t, r, j, w=64, off=0):
            return t[64 * r:64 * r + 64, j * T + ts + off: j * T + ts + off + w]

        negL_rep = ap_custom(s["r_negL"][:], ts, [[T, 16], [0, 2], [1, 64]])
        m2s3 = ap_custom(s["mask2s_f32"][:], 0, [[128, 16], [64, 2], [1, 64]])
        lhsE3 = ap_custom(s["lhsE"][:], 0, [[128, 16], [64, 2], [1, 64]])
        p.op("vector", reads=("r_negL", "c_mask2s_f32"), writes=("lhsE",), tag="lhsE")(
            lambda e: e.tensor_tensor(lhsE3, negL_rep, m2s3, ALU.mult))
        beta_rep = ap_custom(s["r_beta"][:], ts, [[T, 16], [0, 2], [1, 64]])
        m2sb3 = ap_custom(s["mask2s_bf"][:], 0, [[128, 16], [64, 2], [1, 64]])
        lhsB3 = ap_custom(s["lhsB"][:], 0, [[128, 16], [64, 2], [1, 64]])
        p.op("vector", reads=("r_beta", "c_mask2s_bf"), writes=("lhsB",), tag="lhsB")(
            lambda e: e.tensor_tensor(lhsB3, beta_rep, m2sb3, ALU.mult))
        L_rep = ap_custom(s["r_L"][:], ts, [[T, 16], [0, 8], [1, 64]])
        hm83 = ap_custom(s["headmask8_f32"][:], 0, [[512, 16], [64, 8], [1, 64]])
        Lm3 = ap_custom(s["Lmaskd"][:], 0, [[512, 16], [64, 8], [1, 64]])
        p.op("vector", reads=("r_L", "c_headmask8_f32"), writes=("Lmaskd",), tag="Lmaskd")(
            lambda e: e.tensor_tensor(Lm3, L_rep, hm83, ALU.mult))
        def mmE(e):
            e.matmul(P[1][:, 0:512], s["mask2s_f32"][:], s["Lmaskd"][:],
                     start=True, stop=False)
            return e.matmul(P[1][:, 0:512], s["lhsE"][:], s["headmask8_f32"][:],
                            start=False, stop=True)
        p.op("tensor", reads=("Lmaskd", "lhsE", "c_mask2s_f32", "c_headmask8_f32"),
             writes=("P1",), tag="Emm")(mmE)
        p.op("vector", reads=("P1",), writes=("t1",))(
            lambda e: e.tensor_scalar(s["t1"][:], P[1][:, 0:512], 0.0, None, ALU.min))
        p.op("scalar", reads=("t1",), writes=("E",))(
            lambda e: e.activation(out=s["E"][:], in_=s["t1"][:], func=AF.Exp))
        def mmG(e):
            ins = None
            for (b, r, j) in blk(P[1]):
                ins = e.matmul(b, fmsl(s["Kn"], r, j), fmsl(s["Kn"], r, j),
                               start=True, stop=True)
            return ins
        p.op("tensor", reads=("Kn",), writes=("P1",), tag="Gmm")(mmG)
        p.op("vector", reads=("E", "P1"), writes=("t1",))(
            lambda e: e.tensor_mul(s["t1"][:], s["E"][:], P[1][:, 0:512]))
        def mmP(e):
            ins = None
            for (b, r, j) in blk(P[2]):
                ins = e.matmul(b, fmsl(s["Kn"], r, j), fmsl(s["Qn"], r, j),
                               start=True, stop=True)
            return ins
        p.op("tensor", reads=("Kn", "Qn"), writes=("P2",), tag="Pmm")(mmP)
        p.op("vector", reads=("E", "P2"), writes=("t2",))(
            lambda e: e.tensor_mul(s["t2"][:], s["E"][:], P[2][:, 0:512]))
        p.op("tensor", reads=("lhsB", "c_headmask8_bf"), writes=("P2",), tag="bpmm")(
            lambda e: e.matmul(P[2][:, 0:512], s["lhsB"][:], s["headmask8_bf"][:],
                               start=True, stop=True))
        p.op("vector", reads=("P2",), writes=("bpartU",))(
            lambda e: e.tensor_copy(s["bpartU"][:], P[2][:, 0:512]))
        p.op("vector", reads=("bpartU", "c_negstriu"), writes=("bmask_t",))(
            lambda e: e.tensor_mul(s["bmask_t"][:], s["bpartU"][:], s["negstriu"][:]))
        p.op("vector", reads=("bpartU", "c_triuincl"), writes=("bmask_B",))(
            lambda e: e.tensor_mul(s["bmask_B"][:], s["bpartU"][:], s["triuincl"][:]))
        gcn = "gcall%d" % (sc % 2)
        daln = "dall%d" % (sc % 2)
        eL_last = ap_custom(s["r_eL"][:], ts + 63, [[T, 16], [0, 8]])
        p.op("vector", reads=("r_eL", "c_headmaskj"), writes=("gcrhs",))(
            lambda e: e.tensor_tensor(s["gcrhs"][:], eL_last, s["headmaskj"][:], ALU.mult))
        p.op("tensor", reads=("gcrhs", "c_mask2s_f32"), writes=("P2",), tag="gcmm")(
            lambda e: e.matmul(P[2][:, 0:8], s["mask2s_f32"][:], s["gcrhs"][:],
                               start=True, stop=True))
        p.op("scalar", reads=("P2",), writes=(gcn,))(
            lambda e, gcn=gcn: e.activation(out=s[gcn][:], in_=P[2][:, 0:8], func=AF.Copy))
        E_last = ap_custom(s["E"][:], 63, [[512, 128], [64, 8]])
        bp_first = ap_custom(s["bpartU"][:], 0, [[512, 128], [64, 8]])
        p.op("vector", reads=("E", "bpartU"), writes=(daln,))(
            lambda e, daln=daln: e.tensor_tensor(s[daln][:], E_last, bp_first, ALU.mult))
        p.op("vector", reads=("t1", "bmask_t"), writes=("NL0a", "NL0b"))(
            lambda e: e.tensor_mul(s["NL0"][:], s["t1"][:], s["bmask_t"][:]))
        p.op("vector", reads=("t2", "bmask_B"), writes=(BLn,))(
            lambda e, BLn=BLn: e.tensor_mul(s[BLn][:], s["t2"][:], s["bmask_B"][:]))
        def mmNN(e):
            ins = None
            for (b, r, j) in blk(P[1]):
                ins = e.matmul(b, s["NL0"][64 * r:64 * r + 64, 64 * j:64 * j + 64],
                               s["ident64"][64 * r:64 * r + 64, :], start=True, stop=True)
            return ins
        p.op("tensor", reads=("NL0a", "NL0b", "c_ident64"), writes=("P1",), tag="NNmm")(mmNN)
        p.op("vector", reads=("P1",), writes=("NN0a", "NN0b"))(
            lambda e: e.tensor_copy(s["NN0"][:], P[1][:, 0:512]))

    def scan_tail_a(self, ch, sc):
        p, s, P, cfg = self.p, self.s, self.P, self.cfg
        T = cfg.T
        ts = sc * 64
        def blk(t):
            return [(t[64 * r:64 * r + 64, 64 * j:64 * j + 64], r, j)
                    for j in range(8) for r in range(2)]

        def fmsl(t, r, j, w=64, off=0):
            return t[64 * r:64 * r + 64, j * T + ts + off: j * T + ts + off + w]
        def mmR0(e):
            ins = None
            for r in range(2):
                for j in range(8):
                    b = P[5][64 * r:64 * r + 64, 64 * j:64 * j + 64]
                    e.matmul(b, fmsl(s["Vfm"], r, j), s["ident64"][64 * r:64 * r + 64, :],
                             start=(j == 0), stop=False)
                    ins = e.matmul(b, fmsl(s["Khm"], r, j),
                                   s["Sall"][64 * r:64 * r + 64, 64 * j:64 * j + 64],
                                   start=False, stop=True)
            return ins
        p.op("tensor", reads=("Vfm", "Khm", "Sall", "c_ident64"), writes=("P5",),
             tag="R0mm")(mmR0)
        p.op("scalar", reads=("P5",), writes=("R",))(
            lambda e: e.activation(out=s["R"][:], in_=P[5][:, 0:512], func=AF.Copy))
        self._levels(ch, sc, 0, 2)

    def scan_tail_b(self, ch, sc):
        p, s, P, cfg = self.p, self.s, self.P, self.cfg
        T = cfg.T
        ts = sc * 64
        def blk(t):
            return [(t[64 * r:64 * r + 64, 64 * j:64 * j + 64], r, j)
                    for j in range(8) for r in range(2)]

        def fmsl(t, r, j, w=64, off=0):
            return t[64 * r:64 * r + 64, j * T + ts + off: j * T + ts + off + w]
        BLn = "BL%d" % (sc % 2)
        self._levels(ch, sc, 2, 6)
        def mmRead(e):
            ins = None
            for r in range(2):
                for j in range(8):
                    b = P[6][64 * r:64 * r + 64, 64 * j:64 * j + 64]
                    e.matmul(b, fmsl(s["Qh"], r, j),
                             s["Sall"][64 * r:64 * r + 64, 64 * j:64 * j + 64],
                             start=(j == 0), stop=False)
                    ins = e.matmul(b, s[BLn][64 * r:64 * r + 64, 64 * j:64 * j + 64],
                                   s["R"][64 * r:64 * r + 64, 64 * j:64 * j + 64],
                                   start=False, stop=True)
            return ins
        p.op("tensor", reads=("Qh", "Sall", BLn, "R"), writes=("P6",), tag="readmm")(mmRead)
        p.op("vector", reads=("P6",), writes=("readstm",))(
            lambda e: e.tensor_copy(s["readstm"][:], P[6][:, 0:512]))
        def mmKtm(e):
            ins = None
            for (b, r, j) in blk(P[4]):
                ins = e.matmul(b, fmsl(s["Kn"], r, j), s["ident64"][64 * r:64 * r + 64, :],
                               start=True, stop=True)
            return ins
        p.op("tensor", reads=("Kn", "c_ident64"), writes=("P4",), tag="Ktmmm")(mmKtm)
        daln = "dall%d" % (sc % 2)
        P4_3 = ap_custom(self.P[4][:], 0, [[512, 128], [64, 8], [1, 64]])
        dall_b = ap_custom(s[daln][:], 0, [[8, 128], [1, 8], [0, 64]])
        Ktmp3 = ap_custom(s["Ktmp"][:], 0, [[512, 128], [64, 8], [1, 64]])
        p.op("vector", reads=("P4", daln), writes=("Ktmp",))(
            lambda e: e.tensor_tensor(Ktmp3, P4_3, dall_b, ALU.mult))
        def mmState(e):
            ins = None
            for (b, r, j) in blk(P[6]):
                ins = e.matmul(b, s["Ktmp"][64 * r:64 * r + 64, 64 * j:64 * j + 64],
                               s["R"][64 * r:64 * r + 64, 64 * j:64 * j + 64],
                               start=True, stop=True)
            return ins
        p.op("tensor", reads=("Ktmp", "R"), writes=("P6",), tag="statemm")(mmState)
        gcn = "gcall%d" % (sc % 2)
        gc_b = ap_custom(s[gcn][:], 0, [[8, 128], [1, 8], [0, 64]])
        Sall3 = ap_custom(s["Sall"][:], 0, [[512, 128], [64, 8], [1, 64]])
        t23 = ap_custom(s["t2"][:], 0, [[512, 128], [64, 8], [1, 64]])
        p.op("vector", reads=("Sall", gcn), writes=("t2",))(
            lambda e: e.tensor_tensor(t23, Sall3, gc_b, ALU.mult))
        p.op("vector", reads=("P6", "t2"), writes=("Sall",))(
            lambda e: e.tensor_add(s["Sall"][:], P[6][:, 0:512], s["t2"][:]))
        def mmRFM(e):
            ins = None
            for (b, r, j) in blk(P[4]):
                ins = e.matmul(b, s["readstm"][64 * r:64 * r + 64, 64 * j:64 * j + 64],
                               s["ident64"][64 * r:64 * r + 64, :], start=True, stop=True)
            return ins
        p.op("tensor", reads=("readstm", "c_ident64"), writes=("P4",), tag="rfmm")(mmRFM)
        P4_3b = ap_custom(self.P[4][:], 0, [[512, 128], [64, 8], [1, 64]])
        rfm3 = ap_custom(s["readsFM"][:], ts, [[8 * T, 128], [T, 8], [1, 64]])
        p.op("vector", reads=("P4",), writes=("readsFM",))(
            lambda e: e.tensor_copy(rfm3, P4_3b))

    def _levels(self, ch, sc, lev0, lev1):
        p, s, P = self.p, self.s, self.P

        def blk(t):
            return [(t[64 * r:64 * r + 64, 64 * j:64 * j + 64], r, j)
                    for j in range(8) for r in range(2)]
        for lev in range(lev0, lev1):
            nl = "NL0" if lev == 0 else ("NLa" if lev % 2 == 1 else "NLb")
            nn = "NN0" if lev == 0 else ("NNa" if lev % 2 == 1 else "NNb")
            nl2 = "NLa" if lev % 2 == 0 else "NLb"
            nn2 = "NNa" if lev % 2 == 0 else "NNb"
            def mmAp(e, nl=nl):
                ins = None
                for (b, r, j) in blk(P[5]):
                    ins = e.matmul(b, s[nl][64 * r:64 * r + 64, 64 * j:64 * j + 64],
                                   s["R"][64 * r:64 * r + 64, 64 * j:64 * j + 64],
                                   start=False, stop=True)
                return ins
            p.op("tensor", reads=(nl, "R"), writes=("P5",), tag="apply%d" % lev)(mmAp)
            if lev < 5:
                def mmSqN(e, nl=nl, nn=nn):
                    ins = None
                    for (b, r, j) in blk(P[4]):
                        ins = e.matmul(b, s[nl][64 * r:64 * r + 64, 64 * j:64 * j + 64],
                                       s[nn][64 * r:64 * r + 64, 64 * j:64 * j + 64],
                                       start=True, stop=True)
                    return ins
                p.op("tensor", reads=(nl, nn), writes=("P4",), tag="sqN%d" % lev)(mmSqN)
                def mmSqL(e, nl=nl, nn=nn):
                    ins = None
                    for (b, r, j) in blk(P[7]):
                        ins = e.matmul(b, s[nn][64 * r:64 * r + 64, 64 * j:64 * j + 64],
                                       s[nl][64 * r:64 * r + 64, 64 * j:64 * j + 64],
                                       start=True, stop=True)
                    return ins
                p.op("tensor", reads=(nl, nn), writes=("P7",), tag="sqL%d" % lev)(mmSqL)
                p.op("vector", reads=("P4",), writes=(nn2,))(
                    lambda e, nn2=nn2: e.tensor_copy(s[nn2][:], P[4][:, 0:512]))
                p.op("scalar", reads=("P7",), writes=(nl2,))(
                    lambda e, nl2=nl2: e.activation(out=s[nl2][:], in_=P[7][:, 0:512],
                                                    func=AF.Copy))
            eng = "vector" if lev % 2 == 0 else "scalar"
            if eng == "vector":
                p.op("vector", reads=("P5",), writes=("R",))(
                    lambda e: e.tensor_copy(s["R"][:], P[5][:, 0:512]))
            else:
                p.op("scalar", reads=("P5",), writes=("R",))(
                    lambda e: e.activation(out=s["R"][:], in_=P[5][:, 0:512], func=AF.Copy))

    def epilogue(self, ch):
        p, s, P, cfg = self.p, self.s, self.P, self.cfg
        T, NTT = cfg.T, cfg.NTT
        xb = s["xbuf%d" % (ch % 2)]
        xr = "xbuf%d" % (ch % 2)
        for fc in range(8):
            h0 = (fc % 2) * T
            p.op("scalar", reads=("readsFM",), writes=("sqbuf%d" % (fc % 2),))(
                lambda e, fc=fc, h0=h0: e.activation(out=s["sqbuf"][:, h0:h0 + T],
                                              in_=s["readsFM"][:, fc * T:(fc + 1) * T],
                                              func=AF.Square))
            def mmf(e, fc=fc, h0=h0):
                return e.matmul(P[7][0:16, 0:T], s["headmaskC"][:, fc * 16:(fc + 1) * 16],
                                s["sqbuf"][:, h0:h0 + T], start=(fc == 0), stop=(fc == 7))
            p.op("tensor", reads=("sqbuf%d" % (fc % 2), "c_headmaskC"),
                 writes=("P7",), tag="rmsmm")(mmf)
        p.op("scalar", reads=("P7",), writes=("rmsrow",))(
            lambda e: e.activation(out=s["rmsrow"][:, 0:T], in_=P[7][0:16, 0:T],
                                   func=AF.Sqrt, scale=1.0 / 64, bias=s["biasc"][0:16, 1:2]))
        p.op("vector", reads=("rmsrow",), writes=("r_t",))(
            lambda e: e.reciprocal(s["r_t"][:, 0:T], s["rmsrow"][:, 0:T]))
        p.op("vector", reads=("r_t",), writes=("rmsbf",))(
            lambda e: e.tensor_copy(s["rmsbf"][:, 0:T], s["r_t"][:, 0:T]))
        for fc in range(8):
            pb = fc % 2
            p.op("tensor", reads=("rmsbf", "c_mask2w"), writes=("P%d" % pb,), tag="nbc")(
                lambda e, fc=fc, pb=pb: e.matmul(P[pb][:, 0:T],
                                          s["mask2w"][:, fc * 128:(fc + 1) * 128],
                                          s["rmsbf"][:, 0:T], start=True, stop=True))
            p.op("vector", reads=("P%d" % pb, "readsFM"), writes=("normedFM",))(
                lambda e, fc=fc, pb=pb: e.tensor_mul(s["normedFM"][:, fc * T:(fc + 1) * T],
                                              s["readsFM"][:, fc * T:(fc + 1) * T],
                                              P[pb][:, 0:T]))
        for kg in range(4):
            ob = "wbuf_out" + ("A" if kg % 2 == 0 else "B")
            p.op("sync", reads=("scr_outT",), writes=(ob,), dma=True, tag="wout")(
                lambda e, kg=kg, ob=ob: e.dma_start(
                    out=s[ob][:],
                    in_=ap_custom(self.scr["outT"], kg * 256 * D,
                                  [[D, 128], [128 * D, 2], [1, D]])))
            def mmf(e, kg=kg, ob=ob):
                ins = None
                for kk in range(2):
                    k = kg * 2 + kk
                    for tt in range(NTT):
                        for hf in range(2):
                            ins = e.matmul(P[3 + tt * 2 + hf][:, 0:512],
                                           s["normedFM"][:, k * T + tt * 128: k * T + (tt + 1) * 128],
                                           s[ob][:, kk * 1024 + hf * 512: kk * 1024 + (hf + 1) * 512],
                                           start=(k == 0), stop=False)
                return ins
            p.op("tensor", reads=(ob, "normedFM"),
                 writes=("P3", "P4", "P5", "P6"), tag="outmm")(mmf)
        def mmb(e):
            ins = None
            for tt in range(NTT):
                for hf in range(2):
                    ins = e.matmul(P[3 + tt * 2 + hf][:, 0:512],
                                   s["ones128row_bf"][0:1, :],
                                   s["out_b_bf"][0:1, hf * 512:(hf + 1) * 512],
                                   start=False, stop=True)
            return ins
        p.op("tensor", reads=("c_out_b", "c_ones128row_bf"),
             writes=("P3", "P4", "P5", "P6"), tag="outbias")(mmb)
        for tt in range(NTT):
            for hf in range(2):
                pi = 3 + tt * 2 + hf
                sl = (tt * 1024 + hf * 512, tt * 1024 + (hf + 1) * 512)
                p.op("vector", reads=("P%d" % pi, "outgate"), writes=("t1",))(
                    lambda e, pi=pi, sl=sl: e.tensor_mul(
                        s["t1"][:], P[pi][:, 0:512], s["outgate"][:, sl[0]:sl[1]]))
                p.op("vector", reads=("t1", xr), writes=(xr,))(
                    lambda e, sl=sl: e.tensor_add(xb[:, sl[0]:sl[1]],
                                                  xb[:, sl[0]:sl[1]], s["t1"][:, 0:512]))

    def ffn(self, ch):
        p, s, P, cfg = self.p, self.s, self.P, self.cfg
        T, NTT = cfg.T, cfg.NTT
        xb = s["xbuf%d" % (ch % 2)]
        xr = "xbuf%d" % (ch % 2)
        self._ln(xb, xr, s["h2"], "h2", statcol=16)
        for tt in range(NTT):
            sl = (tt * 1024, (tt + 1) * 1024)
            c0 = tt * 4
            p.op("vector", reads=("h2",), writes=("fstat",))(
                lambda e, sl=sl, c0=c0: e.tensor_reduce(
                    s["fstat"][:, c0:c0 + 1], s["h2"][:, sl[0]:sl[1]], AX.X, ALU.max,
                    apply_absolute_value=True))
            p.op("gpsimd", reads=("fstat",), writes=("fstat",))(
                lambda e, c0=c0: e.tensor_scalar(s["fstat"][:, c0 + 1:c0 + 2],
                                                 s["fstat"][:, c0:c0 + 1],
                                                 1e-5, None, ALU.max))
            p.op("vector", reads=("fstat",), writes=("fstat",))(
                lambda e, c0=c0: e.reciprocal(s["fstat"][:, c0 + 2:c0 + 3],
                                              s["fstat"][:, c0 + 1:c0 + 2]))
            p.op("scalar", reads=("fstat",), writes=("fstat",))(
                lambda e, c0=c0: e.activation(out=s["fstat"][:, c0 + 3:c0 + 4],
                                              in_=s["fstat"][:, c0 + 2:c0 + 3],
                                              func=AF.Copy, scale=127.0))
            p.op("vector", reads=("fstat", "wsc"), writes=("fstat",))(
                lambda e, c0=c0, tt=tt: e.scalar_tensor_tensor(
                    s["fstat"][:, 16 + tt:17 + tt], s["fstat"][:, c0 + 1:c0 + 2],
                    1.0 / 127, s["wsc"][:, 0:1], ALU.mult, ALU.mult))
            p.op("vector", reads=("fstat", "wsc"), writes=("fstat",))(
                lambda e, c0=c0, tt=tt: e.scalar_tensor_tensor(
                    s["fstat"][:, 18 + tt:19 + tt], s["fstat"][:, c0 + 1:c0 + 2],
                    1.0 / 127, s["wsc"][:, 1:2], ALU.mult, ALU.mult))
            p.op("scalar", reads=("h2", "fstat"), writes=("istage",))(
                lambda e, sl=sl, c0=c0: e.activation(
                    out=s["istage"][:, 0:1024], in_=s["h2"][:, sl[0]:sl[1]],
                    func=AF.Copy, scale=s["fstat"][:, c0 + 3:c0 + 4]))
            p.op("vector", reads=("istage",), writes=("istage",))(
                lambda e: e.tensor_scalar(s["istage"][:, 0:1024], s["istage"][:, 0:1024],
                                          127, -128, ALU.min, ALU.max))
            p.op("vector", reads=("istage",), writes=("h2q",))(
                lambda e, sl=sl: e.tensor_copy(s["h2q"][:, sl[0]:sl[1]],
                                               s["istage"][:, 0:1024]))
        for tt in range(NTT):
            def mmf(e, tt=tt):
                ins = None
                for g in range(2):
                    for b in range(4):
                        fc = g * 4 + b
                        ins = e.matmul(P[g][:, b * 128:(b + 1) * 128],
                                       s["h2q"][:, tt * 1024 + fc * 128: tt * 1024 + (fc + 1) * 128],
                                       s["ident128"][:], start=True, stop=True)
                return ins
            p.op("tensor", reads=("h2q", "c_ident128"), writes=("P0", "P1"), tag="h2qT")(mmf)
            for g in range(2):
                out_ap = ap_custom(s["h2qFM"][:], tt * 128 + g * 4 * T,
                                   [[8 * T, 128], [T, 4], [1, 128]])
                p.op("vector", reads=("P%d" % g,), writes=("h2qFM",))(
                    lambda e, g=g, o=out_ap: e.tensor_copy(o, P[g][:, 0:512]))
        if ch == getattr(self, "tap_chunk", 0):
            self.tap("h2q", lambda: s["h2q"][:], ["h2q"])
            self.tap("h2qFM", lambda: s["h2qFM"][:], ["h2qFM"])
            self.tap("fstat", lambda: s["fstat"][:], ["fstat"])
        for ic in range(8):
            for nm, b0, b1 in [("wgT", "wbuf_g0", "wbuf_g1"), ("wuT", "wbuf_u0", "wbuf_u1")]:
                for hh, bh in ((0, b0), (1, b1)):
                    p.op("sync", reads=("scr_" + nm,), writes=(bh,), dma=True,
                         tag="wld_" + nm)(
                        lambda e, ic=ic, nm=nm, bh=bh, hh=hh: e.dma_start(
                            out=s[bh][:],
                            in_=ap_custom(self.scr[nm], hh * 4 * 128 * INNER + ic * 512,
                                          [[INNER, 128], [128 * INNER, 4], [1, 512]])))
            pg = 0 if ic % 2 == 0 else 2
            pu = 1 if ic % 2 == 0 else 3
            for tt in range(NTT):
                for hh, bh in ((0, "wbuf_g0"), (1, "wbuf_g1")):
                    def mmg(e, tt=tt, hh=hh, bh=bh, pg=pg):
                        ins = None
                        for kk in range(4):
                            k = hh * 4 + kk
                            ins = e.matmul(P[pg][:, 0:512],
                                           s["h2qFM"][:, k * T + tt * 128: k * T + (tt + 1) * 128],
                                           s[bh][:, kk * 512:(kk + 1) * 512],
                                           start=(k == 0), stop=(k == 7))
                        return ins
                    p.op("tensor", reads=(bh, "h2qFM"), writes=("P%d" % pg,), tag="gmm")(mmg)
                for hh, bh in ((0, "wbuf_u0"), (1, "wbuf_u1")):
                    def mmu(e, tt=tt, hh=hh, bh=bh, pu=pu):
                        ins = None
                        for kk in range(4):
                            k = hh * 4 + kk
                            ins = e.matmul(P[pu][:, 0:512],
                                           s["h2qFM"][:, k * T + tt * 128: k * T + (tt + 1) * 128],
                                           s[bh][:, kk * 512:(kk + 1) * 512],
                                           start=(k == 0), stop=(k == 7))
                        return ins
                    p.op("tensor", reads=(bh, "h2qFM"), writes=("P%d" % pu,), tag="umm")(mmu)
                p.op("scalar", reads=("P%d" % pg, "fstat"), writes=("t1",))(
                    lambda e, tt=tt, pg=pg: e.activation(out=s["t1"][:], in_=P[pg][:, 0:512],
                                                  func=AF.Silu,
                                                  scale=s["fstat"][:, 16 + tt:17 + tt]))
                p.op("vector", reads=("t1", "P%d" % pu, "fstat"), writes=("y",))(
                    lambda e, ic=ic, tt=tt, pu=pu: e.scalar_tensor_tensor(
                        s["y"][:, tt * 4096 + ic * 512: tt * 4096 + (ic + 1) * 512],
                        P[pu][:, 0:512], s["fstat"][:, 18 + tt:19 + tt],
                        s["t1"][:, 0:512], ALU.mult, ALU.mult))
                p.op("vector", reads=("y",), writes=("ymax",))(
                    lambda e, ic=ic, tt=tt: e.tensor_reduce(
                        s["fstat"][:, 22 + tt:23 + tt],
                        s["y"][:, tt * 4096 + ic * 512: tt * 4096 + (ic + 1) * 512],
                        AX.X, ALU.max, apply_absolute_value=True))
                if ic > 0:
                    p.op("gpsimd", reads=("ymax", "fstat"), writes=("fstat",))(
                        lambda e, tt=tt, c0=8 + tt * 4: e.tensor_scalar(
                            s["fstat"][:, c0:c0 + 1], s["fstat"][:, 22 + tt:23 + tt],
                            s["fstat"][:, c0:c0 + 1], None, ALU.max))
                else:
                    p.op("gpsimd", reads=("ymax",), writes=("fstat",))(
                        lambda e, tt=tt, c0=8 + tt * 4: e.tensor_scalar(
                            s["fstat"][:, c0:c0 + 1], s["fstat"][:, 22 + tt:23 + tt],
                            0.0, None, ALU.max))
        if ch == getattr(self, "tap_chunk", 0):
            self.tap("ytap", lambda: s["y"][:, 0:4096], ["y"])
        for tt in range(NTT):
            sl = (tt * 4096, (tt + 1) * 4096)
            c0 = 8 + tt * 4
            p.op("gpsimd", reads=("fstat",), writes=("fstat",))(
                lambda e, c0=c0: e.tensor_scalar(s["fstat"][:, c0 + 1:c0 + 2],
                                                 s["fstat"][:, c0:c0 + 1],
                                                 1e-5, None, ALU.max))
            p.op("vector", reads=("fstat",), writes=("fstat",))(
                lambda e, c0=c0: e.reciprocal(s["fstat"][:, c0 + 2:c0 + 3],
                                              s["fstat"][:, c0 + 1:c0 + 2]))
            p.op("scalar", reads=("fstat",), writes=("fstat",))(
                lambda e, c0=c0: e.activation(out=s["fstat"][:, c0 + 3:c0 + 4],
                                              in_=s["fstat"][:, c0 + 2:c0 + 3],
                                              func=AF.Copy, scale=127.0))
            p.op("vector", reads=("fstat", "wsc"), writes=("fstat",))(
                lambda e, c0=c0, tt=tt: e.scalar_tensor_tensor(
                    s["fstat"][:, 20 + tt:21 + tt], s["fstat"][:, c0 + 1:c0 + 2],
                    1.0 / 127, s["wsc"][:, 2:3], ALU.mult, ALU.mult))
            for hh in range(2):
                p.op("scalar", reads=("y", "fstat"), writes=("istage",))(
                    lambda e, sl=sl, hh=hh, c0=c0: e.activation(
                        out=s["istage"][:, 0:2048],
                        in_=s["y"][:, sl[0] + hh * 2048: sl[0] + (hh + 1) * 2048],
                        func=AF.Copy, scale=s["fstat"][:, c0 + 3:c0 + 4]))
                p.op("vector", reads=("istage",), writes=("istage",))(
                    lambda e: e.tensor_scalar(s["istage"][:, 0:2048],
                                              s["istage"][:, 0:2048],
                                              127, -128, ALU.min, ALU.max))
                p.op("vector", reads=("istage",), writes=("yq",))(
                    lambda e, sl=sl, hh=hh: e.tensor_copy(
                        s["yq"][:, sl[0] + hh * 2048: sl[0] + (hh + 1) * 2048],
                        s["istage"][:, 0:2048]))
        if ch == getattr(self, "tap_chunk", 0):
            self.tap("yq", lambda: s["yq"][:], ["yq"])
        for tt in range(NTT):
            for icg in range(8):
                def mmf(e, tt=tt, icg=icg):
                    ins = None
                    for b in range(4):
                        ic = icg * 4 + b
                        ins = e.matmul(P[2][:, b * 128:(b + 1) * 128],
                                       s["yq"][:, tt * 4096 + ic * 128: tt * 4096 + (ic + 1) * 128],
                                       s["ident128"][:], start=True, stop=True)
                    return ins
                p.op("tensor", reads=("yq", "c_ident128"), writes=("P2",), tag="yqT")(mmf)
                out_ap = ap_custom(s["yqFM"][:], (icg * 4) * T + tt * 128,
                                   [[32 * T, 128], [T, 4], [1, 128]])
                p.op("vector", reads=("P2",), writes=("yqFM",))(
                    lambda e, o=out_ap: e.tensor_copy(o, P[2][:, 0:512]))
        for icg in range(16):
            db = "wbuf_d" + ("A" if icg % 2 == 0 else "B")
            p.op("sync", reads=("scr_wdT",), writes=(db,), dma=True, tag="wd")(
                lambda e, icg=icg, db=db: e.dma_start(
                    out=s[db][:],
                    in_=ap_custom(self.scr["wdT"], icg * 256 * D,
                                  [[D, 128], [128 * D, 2], [1, D]])))
            def mmf(e, icg=icg, db=db):
                ins = None
                for kk in range(2):
                    ic = icg * 2 + kk
                    for tt in range(NTT):
                        for hf in range(2):
                            ins = e.matmul(P[3 + tt * 2 + hf][:, 0:512],
                                           s["yqFM"][:, ic * T + tt * 128: ic * T + (tt + 1) * 128],
                                           s[db][:, kk * 1024 + hf * 512: kk * 1024 + (hf + 1) * 512],
                                           start=(ic == 0), stop=(ic == 31))
                return ins
            p.op("tensor", reads=(db, "yqFM"),
                 writes=("P3", "P4", "P5", "P6"), tag="dmm")(mmf)
        for tt in range(NTT):
            for hf in range(2):
                pi = 3 + tt * 2 + hf
                sl = (tt * 1024 + hf * 512, tt * 1024 + (hf + 1) * 512)
                p.op("vector", reads=("P%d" % pi, xr, "fstat"), writes=(xr,))(
                    lambda e, pi=pi, sl=sl, tt=tt: e.scalar_tensor_tensor(
                        xb[:, sl[0]:sl[1]], P[pi][:, 0:512],
                        s["fstat"][:, 20 + tt:21 + tt],
                        xb[:, sl[0]:sl[1]], ALU.mult, ALU.add))
        in_ap = ap_custom(xb[:], 0, [[2 * 1024, 128], [1024, NTT], [1, 1024]])
        out_ap = ap_custom(self.dram_out, ch * self.cfg.T * D,
                           [[D, 128], [128 * D, NTT], [1, D]])
        p.op("sync", reads=(xr,), writes=("xout",), dma=True, tag="xst")(
            lambda e, i=in_ap, o=out_ap: e.dma_start(out=o, in_=i))

    def build(self, with_prep=True, nch=None):
        p, s = self.p, self.s
        self.load_consts()
        if with_prep:
            self.prep_weights_attn()
        p.op("vector", writes=("Salla", "Sallb"))(lambda e: e.memset(s["Sall"][:], 0.0))
        nch = self.cfg.NCH if nch is None else nch
        for ch in range(nch):
            self.p.phase = "front"
            self.frontend(ch)
            self.run_taps_frontend(ch)
            self.p.phase = "scanF"
            self.scan_front(ch, 0)
            for sc in range(self.cfg.NSC):
                self.p.phase = "scanTA"
                self.scan_tail_a(ch, sc)
                if sc + 1 < self.cfg.NSC:
                    self.p.phase = "scanF"
                    self.scan_front(ch, sc + 1)
                self.p.phase = "scanTB"
                self.scan_tail_b(ch, sc)
            self.run_taps_scan(ch)
            if ch == 0 and with_prep:
                self.p.phase = "prepF"
                self.prep_weights_ffn()
            self.p.phase = "epi"
            self.epilogue(ch)
            self.run_taps_epi(ch)
            self.p.phase = "ffn"
            self.ffn(ch)
        p.op("sync", reads=("xout",), tag="fence")(lambda e: e.nop())
        with self.nc.Block() as block:
            p.finalize(self.nc, block)

    def run_taps_frontend(self, ch):
        if ch != getattr(self, "tap_chunk", 0):
            return
        s, cfg = self.s, self.cfg
        for scrn in ["scr_ctrlT", "scr_qkvT", "scr_wgT"]:
            if scrn in self.tap_outs:
                nm = scrn[4:] + "T" if False else scrn.replace("scr_", "")
                self.p.op("gpsimd", reads=(scrn,), dma=True, tag="tap" + scrn)(
                    lambda e, o=self.tap_outs[scrn], i=self.scr[nm]: e.dma_start(out=o, in_=i))

        self.tap("qkvFM", lambda: s["qkvFM"][:], ["qkvFM"])
        self.tap("convq", lambda: s["convq"][:], ["convq"])
        self.tap("Vfm", lambda: s["Vfm"][:], ["Vfm"])
        self.tap("Qn", lambda: s["Qn"][:], ["Qn"])
        self.tap("Kn", lambda: s["Kn"][:], ["Kn"])
        self.tap("Qh", lambda: s["Qh"][:], ["Qh"])
        self.tap("Khm", lambda: s["Khm"][:], ["Khm"])
        self.tap("ab", lambda: s["ab"][:], ["ab"])
        self.tap("r_L", lambda: s["r_L"][:], ["r_L"])
        self.tap("r_beta", lambda: s["r_beta"][:], ["r_beta"])
        self.tap("r_eL", lambda: s["r_eL"][:], ["r_eL"])
        self.tap("outgate", lambda: s["outgate"][:], ["outgate"])

    def run_taps_scan(self, ch):
        if ch != getattr(self, "tap_chunk", 0):
            return
        s = self.s
        self.tap("readsFM", lambda: s["readsFM"][:], ["readsFM"])
        self.tap("Sall", lambda: s["Sall"][:], ["Sall"])

    def run_taps_epi(self, ch):
        if ch != getattr(self, "tap_chunk", 0):
            return
        s = self.s
        self.tap("normedFM", lambda: s["normedFM"][:], ["normedFM"])
        self.tap("x1", lambda: s["xbuf%d" % (ch % 2)][:], ["xbuf%d" % (ch % 2)])


def kernel(L0_attn_hidden, L0_attn_qkv_w, L0_attn_ctrl_w, L0_attn_ctrl_b,
           L0_attn_convq_w, L0_attn_convk_w, L0_attn_convv_w, L0_attn_norm_w,
           L0_attn_out_w, L0_attn_out_b, L0_ffn_w_gate, L0_ffn_w_up,
           L0_ffn_w_down):
    B, S, Dm = L0_attn_hidden.shape
    assert (B, S, Dm) == (8, 4096, 1024)
    cfg = Cfg(S=S, T=256)
    nc = bass.Bass("TRN2", target_bir_lowering=False)
    b = Builder(nc, cfg, taps=[])
    b.build()
    norm_w_t = np.ascontiguousarray(np.tile(np.asarray(L0_attn_norm_w, np.float32), (16, 16)))
    shared = {
        "qkv_w": np.ascontiguousarray(L0_attn_qkv_w, dtype=np.float32),
        "ctrl_w": np.ascontiguousarray(L0_attn_ctrl_w, dtype=np.float32),
        "ctrl_b": np.ascontiguousarray(L0_attn_ctrl_b, dtype=np.float32),
        "convq_w": np.ascontiguousarray(L0_attn_convq_w, dtype=np.float32),
        "convk_w": np.ascontiguousarray(L0_attn_convk_w, dtype=np.float32),
        "convv_w": np.ascontiguousarray(L0_attn_convv_w, dtype=np.float32),
        "norm_w_t": norm_w_t,
        "out_w": np.ascontiguousarray(L0_attn_out_w, dtype=np.float32),
        "out_b": np.ascontiguousarray(L0_attn_out_b, dtype=np.float32),
        "w_gate": np.ascontiguousarray(L0_ffn_w_gate, dtype=np.float32),
        "w_up": np.ascontiguousarray(L0_ffn_w_up, dtype=np.float32),
        "w_down": np.ascontiguousarray(L0_ffn_w_down, dtype=np.float32),
    }
    xin = np.ascontiguousarray(L0_attn_hidden, dtype=np.float32)
    in_maps = [dict(shared, x=xin[i]) for i in range(8)]
    res = run_bass_kernel_spmd(nc, in_maps, core_ids=list(range(8)))
    out = np.stack([np.asarray(res.results[i]["out"], np.float32) for i in range(8)], axis=0)
    return out


# version 13 (final)
# speedup vs baseline: 1.3125x; 1.0017x over previous
import numpy as np
import ml_dtypes
import concourse.bass as bass
import concourse.mybir as mybir
from concourse.bass_utils import run_bass_kernel_spmd

BF = ml_dtypes.bfloat16

"""Support code for the mk_seq_preln_d1 deltanet megakernel (TRN2, raw bass).

Layout conventions:
- TM (token-major): [128 tokens on partitions x features on free]
- FM (feature-major): [128 features-within-chunk on partitions x (fc, t) on free]
  global feature index d = fc*128 + p ; head h = d // 64 = 2*fc + p//64
- scan all-heads tiles: [128 x 512] fp32/bf16 ; head h=2j+r at partition block
  r=h%2 (rows 64r..64r+63), free block j=h//2 (cols 64j..64j+63)
- weights in HBM scratch, bf16, transposed to [in_feat x out_feat]
"""
import numpy as np
import ml_dtypes
import concourse.bass as bass
import concourse.mybir as mybir

F32 = mybir.dt.float32
BF16 = mybir.dt.bfloat16
I32 = mybir.dt.int32
AF = mybir.ActivationFunctionType
ALU = mybir.AluOpType
AX = mybir.AxisListType

D = 1024
H = 16
HD = 64
INNER = 4096
CTRL = 2 * H + D
QKV = 3 * D
C = 64


class Cfg:
    def __init__(self, S=4096, T=256):
        assert S % T == 0 and T % 128 == 0
        self.S = S
        self.T = T
        self.NCH = S // T
        self.NTT = T // 128
        self.NSC = T // C


ENGINES = ("sync", "tensor", "vector", "scalar", "gpsimd")


class Op:
    __slots__ = ("idx", "eng", "fn", "reads", "writes", "dma", "tag",
                 "need_inc", "tick", "waits", "_deps", "dkey", "phase")

    def __init__(self, idx, eng, fn, reads, writes, dma, tag):
        self.idx = idx
        self.eng = eng
        self.fn = fn
        self.reads = reads
        self.writes = writes
        self.dma = dma
        self.tag = tag
        self.need_inc = False
        self.tick = 0
        self.waits = []
        self.dkey = None


class Prog:
    def __init__(self):
        self.ops = []
        self.phase = "init"
        self.last_writer = {}
        self.readers = {}

    def op(self, eng, reads=(), writes=(), dma=False, tag=""):
        def deco(fn):
            o = Op(len(self.ops), eng, fn, expand_couples(reads), expand_couples(writes), dma, tag)
            o.phase = self.phase
            if dma:
                o.dkey = "D_" + (o.writes[0] if o.writes else (o.reads[0] if o.reads else "misc"))
            deps = []
            for r in o.reads:
                w = self.last_writer.get(r)
                if w is not None:
                    deps.append(w)
            for w_ in o.writes:
                lw = self.last_writer.get(w_)
                if lw is not None:
                    deps.append(lw)
                deps.extend(self.readers.get(w_, ()))
            for d in deps:
                if d.eng != o.eng:
                    d.need_inc = True
            o._deps = [d for d in deps if d.eng != o.eng]
            for r in o.reads:
                self.readers.setdefault(r, []).append(o)
            for w_ in o.writes:
                self.last_writer[w_] = o
                self.readers[w_] = []
            self.ops.append(o)
            return o
        return deco

    def finalize(self, nc, block):
        for o in self.ops:
            if o.dma:
                o.need_inc = True
        tick = {}
        for o in self.ops:
            key = o.dkey if o.dkey else o.eng
            if o.need_inc:
                tick[key] = tick.get(key, 0) + (16 if o.dma else 1)
            o.tick = tick.get(key, 0)
        water = {}
        for o in self.ops:
            want = {}
            for d in o._deps:
                assert d.need_inc and d.tick > 0
                key = d.dkey if d.dkey else d.eng
                want[key] = max(want.get(key, 0), d.tick)
            for k2, v in want.items():
                wk = (o.eng, k2)
                if v > water.get(wk, 0):
                    o.waits.append((k2, v))
                    water[wk] = v
        n_inc = sum(1 for o in self.ops if o.need_inc)
        n_wait = sum(len(o.waits) for o in self.ops)
        by_eng = {e: sum(1 for o in self.ops if o.eng == e) for e in ENGINES}
        print(f"[prog] ops={len(self.ops)} incs={n_inc} waits={n_wait} by_eng={by_eng}")
        keys = set(e for e in ENGINES)
        for o in self.ops:
            if o.dkey:
                keys.add(o.dkey)
        sems = {k: nc.alloc_semaphore(f"sem_{k}"[:28]) for k in sorted(keys)}
        self.sems = sems

        def run_engine(ename):
            def body(eng):
                for o in self.ops:
                    if o.eng != ename:
                        continue
                    for (pe, v) in o.waits:
                        eng.wait_ge(sems[pe], v)
                    with nc.named_scope(o.phase):
                        ins = o.fn(eng)
                    if o.need_inc:
                        assert ins is not None, f"op {o.idx} {o.tag} returned no instruction"
                        key = o.dkey if o.dkey else ename
                        ins.then_inc(sems[key], 16 if o.dma else 1)
            return body

        for ename in ENGINES:
            getattr(block, ename)(run_engine(ename))


def make_consts():
    c = {}
    c["ident64"] = np.vstack([np.eye(64), np.eye(64)]).astype(ml_dtypes.bfloat16)
    c["ident128"] = np.eye(128, dtype=ml_dtypes.bfloat16)
    c["ones64"] = np.ones((1, 64), dtype=ml_dtypes.bfloat16)
    c["ones128row"] = np.ones((1, 128), dtype=ml_dtypes.bfloat16)
    c["onescol128"] = np.ones((128, 1), dtype=np.float32)
    i_idx = np.arange(64)
    negstriu = -(i_idx[:, None] < i_idx[None, :]).astype(np.float32)
    triuincl = (i_idx[:, None] <= i_idx[None, :]).astype(np.float32)
    def tile_all(m):
        out = np.zeros((128, 512), np.float32)
        for r in range(2):
            for j in range(8):
                out[64*r:64*r+64, 64*j:64*j+64] = m
        return out
    c["negstriu"] = tile_all(negstriu).astype(ml_dtypes.bfloat16)
    c["triuincl"] = tile_all(triuincl).astype(ml_dtypes.bfloat16)
    hm = np.zeros((128, 128), np.float32)
    for fc in range(8):
        for p in range(128):
            hm[p, fc*16 + 2*fc + p//64] = 1.0
    c["headmaskC"] = hm.astype(ml_dtypes.bfloat16)
    m2 = np.zeros((16, 1024), np.float32)
    for h in range(16):
        fc, r = h // 2, h % 2
        m2[h, fc*128 + 64*r: fc*128 + 64*r + 64] = 1.0
    mp = np.zeros((64, 1024), np.float32)
    mp[0:16] = m2
    mp[32:48] = -m2
    c["mask2pack"] = mp.astype(ml_dtypes.bfloat16)
    c["mask2_f32"] = m2.astype(np.float32)
    m2s = np.zeros((16, 128), np.float32)
    for h in range(16):
        m2s[h, 64*(h % 2): 64*(h % 2)+64] = 1.0
    c["mask2s_f32"] = m2s.astype(np.float32)
    c["mask2s_bf"] = m2s.astype(ml_dtypes.bfloat16)
    hm8 = np.zeros((16, 512), np.float32)
    for h in range(16):
        j = h // 2
        hm8[h, 64*j:64*j+64] = 1.0
    c["headmask8_f32"] = hm8
    c["headmask8_bf"] = hm8.astype(ml_dtypes.bfloat16)
    hmj = np.zeros((16, 8), np.float32)
    for h in range(16):
        hmj[h, h//2] = 1.0
    c["headmaskj"] = hmj
    return c


COUPLES = {
    "y": ("hid", "hidFM", "qkvFM", "convq", "convk", "Vfm"),
    "outgate": ("hid", "hidFM"),
    "istage": ("Qn", "Qh"),
    "yqFM": ("Qn", "Qh", "Kn", "Khm"),
    "h2": ("ab", "r_siga", "r_logb", "r_beta", "r_L", "r_negL", "r_eL", "r_invq"),
    "prep": ("convq",),
    "prepbf": ("convq",),
    "prepstage": ("convk",),
    "prepsums": ("convk",),
    "normw_f32": ("qkvFM",),
    "wbuf_outA": ("wbuf_qkvA", "wbuf_g0"),
    "wbuf_outB": ("wbuf_qkvB", "wbuf_g1"),
    "wbuf_g0": ("wbuf_qkvA",),
    "wbuf_g1": ("wbuf_qkvB",),
    "wbuf_u0": ("wbuf_ctrlA", "wbuf_ctrlB"),
    "wbuf_u1": ("readsFM",),
    "wbuf_dA": ("readsFM",),
    "wbuf_dB": ("normedFM",),
}


def expand_couples(res):
    out = list(res)
    for r in res:
        out.extend(COUPLES.get(r, ()))
    return tuple(dict.fromkeys(out))


"""Main megakernel builder. See kb.py for conventions.

Phases per token chunk (T tokens, NTT=T/128 token tiles, NSC=T/64 scan chunks):
  F: x load, LN1, hidden->FM, qkv mm, ctrl mm, conv mm, l2norm, scaled variants
  S: per scan chunk: WY delta rule (all 16 heads batched in [128x512] tiles)
  E: RMS head-norm, out proj (token-major out), gate, residual -> x1
  FF: LN2, act_quant, gate/up mm, silu*up, act_quant, down mm, residual -> x2, store
"""


def ap_custom(t_ap, offset, dims):
    return bass.AP(tensor=t_ap.tensor, offset=offset, ap=[list(d) for d in dims])


class Builder:
    def __init__(self, nc, cfg: Cfg, taps=None):
        self.nc = nc
        self.cfg = cfg
        self.p = Prog()
        self.taps = taps or []
        self.tap_outs = {}
        self.consts_np = make_consts()
        self._alloc()

    def _alloc(self):
        nc, cfg = self.nc, self.cfg
        T = cfg.T
        self.dram_in = {}
        for name, shape in [
            ("x", [cfg.S, D]), ("qkv_w", [QKV, D]), ("ctrl_w", [CTRL, D]),
            ("ctrl_b", [CTRL]), ("convq_w", [D, 4]), ("convk_w", [D, 4]),
            ("convv_w", [D, 4]), ("norm_w_t", [16, D]), ("out_w", [D, D]),
            ("out_b", [D]), ("w_gate", [INNER, D]), ("w_up", [INNER, D]),
            ("w_down", [D, INNER]),
        ]:
            self.dram_in[name] = nc.dram_tensor(name, shape, F32, kind="ExternalInput").ap()
        self.dram_out = nc.dram_tensor("out", [cfg.S, D], F32, kind="ExternalOutput").ap()
        self.scr = {}
        for name, shape in [("qkvT", [D, QKV]), ("ctrlT", [D, CTRL]),
                            ("outT", [D, D]), ("wgT", [D, INNER]),
                            ("wuT", [D, INNER]), ("wdT", [INNER, D])]:
            self.scr[name] = nc.dram_tensor("scr_" + name, shape, BF16, kind="Internal").ap()
        self.cst = {}
        for k, v in self.consts_np.items():
            self.cst[k] = nc.inline_tensor(np.ascontiguousarray(v), "c_" + k).ap()
        self.cst["onesT_bf"] = nc.inline_tensor(np.ones((1, T), BF), "c_onesT").ap()
        self.cst["ones64_f32"] = nc.inline_tensor(np.ones((1, 64), np.float32), "c_ones64f").ap()
        self.cst["ones128row_f32"] = nc.inline_tensor(np.ones((1, 128), np.float32), "c_o128rf").ap()
        self.cst["onescol128_f32"] = nc.inline_tensor(np.ones((128, 1), np.float32), "c_oc128f").ap()
        self.cst["onesf_16x64"] = nc.inline_tensor(np.ones((16, 64), np.float32), "c_of1664").ap()
        self.cst["biasc"] = nc.inline_tensor(
            np.tile(np.array([[1e-5, 1e-6, 0.01, 0.0]], np.float32), (128, 1)), "c_biasc").ap()

        A = nc.alloc_sbuf_tensor
        s = {}
        for nm, shape, dt in [
            ("ident64", [128, 64], BF16), ("ident128", [128, 128], BF16),
            ("negstriu", [128, 512], BF16), ("triuincl", [128, 512], BF16),
            ("headmaskC", [128, 128], BF16), ("mask2pack", [64, 1024], BF16),
            ("mask2s_f32", [16, 128], F32),
            ("mask2s_bf", [16, 128], BF16), ("headmask8_f32", [16, 512], F32),
            ("headmask8_bf", [16, 512], BF16), ("headmaskj", [16, 8], F32),
            ("onesT_bf", [1, T], BF16), ("ones64_bf", [1, 64], BF16),
            ("ones64_f32", [1, 64], F32), ("ones128row_bf", [1, 128], BF16),
            ("ones128row_f32", [1, 128], F32), ("onescol128_f32", [128, 1], F32),
            ("onesf_16x64", [16, 64], F32), ("biasc", [128, 4], F32),
            ("diags", [128, 96 * 128], BF16),
            ("ctrl_b_bf", [1, CTRL], BF16), ("out_b_bf", [1, D], BF16),
            ("mask2w", [16, 1024], BF16),
            ("cw", [128, 96], F32),
            ("xbuf0", [128, 2 * 1024], F32), ("xbuf1", [128, 2 * 1024], F32),
            ("stats", [128, 32], F32), ("stat1", [1, 64], F32),
            ("hid", [128, 2 * 1024], BF16), ("hidFM", [128, 8 * T], BF16),
            ("qkvFM", [128, 24 * (T + 3)], BF16),
            ("convq", [128, 8 * T], F32), ("convk", [128, 8 * T], F32),
            ("Vfm", [128, 8 * T], BF16), ("Qn", [128, 8 * T], BF16),
            ("Qh", [128, 8 * T], BF16), ("Kn", [128, 8 * T], BF16),
            ("Khm", [128, 8 * T], BF16),
            ("ab", [64, T], F32),
            ("r_siga", [16, T], F32), ("r_logb", [16, T], F32),
            ("r_beta", [16, T], F32), ("r_L", [16, T], F32),
            ("r_negL", [16, T], F32), ("r_eL", [16, T], F32),
            ("r_invq", [16, T], F32), ("r_invk", [16, T], F32),
            ("r_t", [16, T], F32),
            ("rb_qn", [16, T], BF16), ("rb_qh", [16, T], BF16),
            ("rb_kn", [16, T], BF16), ("rb_khm", [16, T], BF16),
            ("lhsE", [16, 128], F32), ("lhsB", [16, 128], BF16),
            ("E", [128, 512], F32), ("bpartU", [128, 512], F32),
            ("bmask_t", [128, 512], BF16), ("bmask_B", [128, 512], BF16),
            ("NL0", [128, 512], BF16), ("NLa", [128, 512], BF16),
            ("NLb", [128, 512], BF16),
            ("NN0", [128, 512], BF16), ("NNa", [128, 512], BF16),
            ("NNb", [128, 512], BF16),
            ("BL0", [128, 512], BF16), ("BL1", [128, 512], BF16),
            ("R", [128, 512], BF16),
            ("readstm", [128, 512], BF16), ("Sall", [128, 512], BF16),
            ("Ktmp", [128, 512], BF16), ("t1", [128, 512], F32),
            ("t2", [128, 512], F32),
            ("gcall0", [128, 8], F32), ("gcall1", [128, 8], F32),
            ("dall0", [128, 8], F32), ("dall1", [128, 8], F32),
            ("gcrhs", [16, 8], F32),
            ("readsFM", [128, 8 * T], BF16), ("normedFM", [128, 8 * T], BF16),
            ("Lmaskd", [16, 512], F32), ("sqf", [128, 2048], F32),
            ("sqbuf", [128, 2 * T], BF16), ("rmsrow", [16, T], F32),
            ("rmsbf", [16, T], BF16),
            ("h2q", [128, 2 * 1024], BF16), ("h2qFM", [128, 8 * T], BF16),
            ("yq", [128, 2 * 4096], BF16),
            ("fstat", [128, 24], F32), ("qhist", [128, 72], BF16),
            ("wsc", [128, 4], F32),
            ("wbuf_qkvA", [128, 8 * 256], BF16), ("wbuf_qkvB", [128, 8 * 256], BF16),
            ("wbuf_ctrlA", [128, CTRL], BF16), ("wbuf_ctrlB", [128, CTRL], BF16),

        ]:
            s[nm] = A(nm, shape, dt)
        AT = nc.alloc_sbuf_tensor_at
        adr = lambda nm: nc.lookup_mloc(s[nm]).addr
        s["y"] = AT("y", [128, 2 * 4096], F32, offset=adr("hid"))
        s["outgate"] = AT("outgate", [128, 2 * 1024], F32, offset=adr("hid"))
        s["istage"] = AT("istage", [128, 2048], I32, offset=adr("Qn"))
        s["yqFM"] = AT("yqFM", [128, 32 * T], BF16, offset=adr("Qn"))
        s["h2"] = AT("h2", [128, 2 * 1024], F32, offset=adr("ab"))
        s["prep"] = AT("prep", [128, 1024], F32, offset=adr("convq"))
        s["prepbf"] = AT("prepbf", [128, 1024], BF16, offset=adr("convq") + 4096)
        s["prepstage"] = AT("prepstage", [128, 512], BF16, offset=adr("convk"))
        s["prepsums"] = AT("prepsums", [128, 64], F32, offset=adr("convk") + 1024)
        s["normw_f32"] = AT("normw_f32", [16, 1024], F32, offset=adr("qkvFM"))
        s["wbuf_outA"] = AT("wbuf_outA", [128, 2048], BF16, offset=adr("wbuf_qkvA"))
        s["wbuf_outB"] = AT("wbuf_outB", [128, 2048], BF16, offset=adr("wbuf_qkvB"))
        s["wbuf_g0"] = AT("wbuf_g0", [128, 4 * 512], BF16, offset=adr("wbuf_qkvA"))
        s["wbuf_g1"] = AT("wbuf_g1", [128, 4 * 512], BF16, offset=adr("wbuf_qkvB"))
        s["wbuf_u0"] = AT("wbuf_u0", [128, 4 * 512], BF16, offset=adr("wbuf_ctrlA"))
        s["wbuf_u1"] = AT("wbuf_u1", [128, 4 * 512], BF16, offset=adr("readsFM"))
        s["wbuf_dA"] = AT("wbuf_dA", [128, 2048], BF16, offset=adr("readsFM"))
        s["wbuf_dB"] = AT("wbuf_dB", [128, 2048], BF16, offset=adr("normedFM"))
        self.s = s
        used = nc.sbuf_base - 16481
        print(f"[alloc] sbuf used ~{used/1024:.1f} KB of {(229344-16481)/1024:.1f}")
        self.P = [nc.alloc_psum_tensor(f"P{i}", [128, 512], F32) for i in range(8)]
        for name, shape, dt in self.taps:
            self.tap_outs[name] = nc.dram_tensor("tap_" + name, shape, dt,
                                                 kind="ExternalOutput").ap()

    def tap(self, name, ap_fn, reads):
        if name not in self.tap_outs:
            return
        out = self.tap_outs[name]
        self.p.op("gpsimd", reads=tuple(reads), dma=True, tag="tap_" + name)(
            lambda e, o=out, f=ap_fn: e.dma_start(out=o, in_=f()))

    def load_consts(self):
        p, s, cst = self.p, self.s, self.cst
        pairs = [(k, k) for k in
                 ["ident64", "ident128", "negstriu", "triuincl",
                  "headmaskC", "mask2pack", "mask2s_f32", "mask2s_bf",
                  "headmask8_f32", "headmask8_bf", "headmaskj", "onesT_bf",
                  "ones64_f32", "ones128row_f32", "onescol128_f32", "onesf_16x64"]]
        pairs += [("ones64_bf", "ones64"), ("ones128row_bf", "ones128row"),
                  ("biasc", "biasc")]
        for sb, ck in pairs:
            p.op("sync", writes=("c_" + sb,), dma=True, tag="ldc")(
                lambda e, a=s[sb], b=cst[ck]: e.dma_start(out=a[:], in_=b))
        p.op("gpsimd", writes=("c_ctrl_b",), dma=True)(
            lambda e: e.dma_start(out=s["ctrl_b_bf"][:], in_=self.dram_in["ctrl_b"][None, :]))
        p.op("gpsimd", writes=("c_out_b",), dma=True)(
            lambda e: e.dma_start(out=s["out_b_bf"][:], in_=self.dram_in["out_b"][None, :]))
        p.op("sync", writes=("c_normw",), dma=True)(
            lambda e: e.dma_start(out=s["normw_f32"][:], in_=self.dram_in["norm_w_t"]))
        p.op("vector", reads=("c_normw", "c_mask2pack"), writes=("c_mask2w",))(
            lambda e: e.tensor_mul(s["mask2w"][:], s["normw_f32"][:],
                                   s["mask2pack"][0:16, :]))
        for si, nm in enumerate(["convq_w", "convk_w", "convv_w"]):
            in_ap = ap_custom(self.dram_in[nm], 0, [[4, 128], [512, 8], [1, 4]])
            out_ap = ap_custom(s["cw"][:], si * 32, [[96, 128], [4, 8], [1, 4]])
            p.op("sync", writes=("c_cw",), dma=True)(
                lambda e, i=in_ap, o=out_ap: e.dma_start(out=o, in_=i))
        for si in range(3):
            for f in range(8):
                for j in range(4):
                    idx = si * 32 + f * 4 + j
                    p.op("scalar", reads=("c_cw", "c_ident128"), writes=("c_diag",),
                         tag="diag")(
                        lambda e, idx=idx, col=si * 32 + f * 4 + j: e.activation(
                            out=s["diags"][:, idx * 128:(idx + 1) * 128],
                            in_=s["ident128"][:], func=AF.Copy,
                            scale=s["cw"][:, col:col + 1]))

    def _transpose_store(self, dst_scr, rowlen, col0, row0, tag):
        p, s, P = self.p, self.s, self.P
        for g in range(2):
            def mmf(e, g=g):
                ins = None
                for b in range(4):
                    cb = g * 4 + b
                    ins = e.matmul(P[6][:, b * 128:(b + 1) * 128],
                                   s["prepbf"][:, cb * 128:(cb + 1) * 128],
                                   s["ident128"][:], start=True, stop=True)
                return ins
            p.op("tensor", reads=("prepbf", "c_ident128"), writes=("P6",), tag=tag)(mmf)
            p.op("vector", reads=("P6",), writes=("prepstage",))(
                lambda e: e.tensor_copy(s["prepstage"][:], P[6][:, 0:512]))
            out_ap = ap_custom(dst_scr, (row0 + g * 512) * rowlen + col0,
                               [[rowlen, 128], [128 * rowlen, 4], [1, 128]])
            in_ap = ap_custom(s["prepstage"][:], 0, [[512, 128], [128, 4], [1, 128]])
            p.op("sync", reads=("prepstage",), dma=True, tag=tag + "st")(
                lambda e, i=in_ap, o=out_ap: e.dma_start(out=o, in_=i))

    def prep_weights_attn(self):
        p, s, P = self.p, self.s, self.P
        for nm, Osz, dstn in [("qkv_w", QKV, "qkvT"), ("ctrl_w", CTRL, "ctrlT"),
                              ("out_w", D, "outT")]:
            dst = self.scr[dstn]
            for t in range((Osz + 127) // 128):
                nr = min(128, Osz - t * 128)
                p.op("sync", writes=("prep",), dma=True, tag="pwld")(
                    lambda e, t=t, nm=nm, nr=nr: e.dma_start(
                        out=s["prep"][0:nr, :],
                        in_=self.dram_in[nm][t * 128:t * 128 + nr, :]))
                if nr == 128:
                    p.op("vector", reads=("prep",), writes=("prepbf",))(
                        lambda e: e.tensor_copy(s["prepbf"][:], s["prep"][:]))
                else:
                    p.op("vector", writes=("prepbf",))(
                        lambda e: e.memset(s["prepbf"][:], 0.0))
                    p.op("vector", reads=("prep",), writes=("prepbf",))(
                        lambda e, nr=nr: e.tensor_copy(s["prepbf"][0:nr, :], s["prep"][0:nr, :]))
                self._transpose_store_n(dst, Osz, t * 128, nr, "tp_" + nm, scrres="scr_" + dstn)
        return

    def prep_weights_ffn(self):
        p, s, P = self.p, self.s, self.P
        for nm, Osz, Isz, dstn in [("w_gate", INNER, D, "wgT"),
                                   ("w_up", INNER, D, "wuT"),
                                   ("w_down", D, INNER, "wdT")]:
            ntile = Osz * Isz // (128 * 1024)
            for t in range(ntile):
                p.op("sync", writes=("prep",), dma=True, tag="fq1")(
                    lambda e, t=t, nm=nm: e.dma_start(
                        out=s["prep"][:],
                        in_=ap_custom(self.dram_in[nm], t * 131072,
                                      [[1024, 128], [1, 1024]])))
                p.op("vector", reads=("prep",), writes=("psums",))(
                    lambda e, t=t: e.tensor_reduce(
                        s["prepsums"][:, t:t + 1], s["prep"][:], AX.X, ALU.add,
                        apply_absolute_value=True))
            p.op("vector", reads=("prep",), writes=("spacer",))(
                lambda e: e.tensor_copy(s["sqf"][:, 0:1024], s["prep"][:]))
            p.op("vector", reads=("psums",), writes=("psumtot",))(
                lambda e, nt=ntile: e.tensor_reduce(
                    s["prepsums"][:, 63:64], s["prepsums"][:, 0:nt], AX.X, ALU.add))
            p.op("tensor", reads=("psumtot", "c_onescol128_f32"), writes=("P7",))(
                lambda e: e.matmul(P[7][0:1, 0:1], s["prepsums"][:, 63:64],
                                   s["onescol128_f32"][:], start=True, stop=True))
            p.op("vector", reads=("P7",), writes=("stat1",))(
                lambda e, n=Osz * Isz: e.tensor_scalar(
                    s["stat1"][0:1, 0:1], P[7][0:1, 0:1], 1.0 / n, 1e-5,
                    ALU.mult, ALU.max))
            p.op("tensor", reads=("stat1", "c_ones128row_f32"), writes=("P7",))(
                lambda e: e.matmul(P[7][:, 0:1], s["ones128row_f32"][:],
                                   s["stat1"][0:1, 0:1], start=True, stop=True))
            wi = {"w_gate": 0, "w_up": 1, "w_down": 2}[nm]
            p.op("scalar", reads=("P7",), writes=("wmean",))(
                lambda e: e.activation(out=s["stats"][:, 29:30], in_=P[7][:, 0:1],
                                       func=AF.Copy))
            p.op("scalar", reads=("P7",), writes=("wsc",))(
                lambda e, wi=wi: e.activation(out=s["wsc"][:, wi:wi + 1],
                                              in_=P[7][:, 0:1], func=AF.Copy))
            p.op("vector", reads=("wmean",), writes=("wscale",))(
                lambda e: e.reciprocal(s["stats"][:, 30:31], s["stats"][:, 29:30]))
            if nm == "w_gate" and "pstat" in self.tap_outs:
                self.p.op("gpsimd", reads=("wscale", "stat1", "psums"), dma=True, tag="tpstat")(
                    lambda e: e.dma_start(out=self.tap_outs["pstat"], in_=s["stat1"][:]))
                self.p.op("gpsimd", reads=("wscale",), dma=True, tag="tpsums")(
                    lambda e: e.dma_start(out=self.tap_outs["psums"], in_=s["prepsums"][:]))
            if nm in ("w_gate", "w_up"):
                tiles = [(t * 128, 0, Osz, t * 128, 0) for t in range(32)]
                tiles = [(t * 128, 0, Osz, t * 128, 0) for t in range(32)]
            else:
                tiles = [(t * 128, q * 1024, D, t * 128, q * 1024)
                         for t in range(8) for q in range(4)]
            for (r0, c0, rowlen, dcol0, drow0) in tiles:
                p.op("sync", writes=("prep",), dma=True, tag="fq2")(
                    lambda e, r0=r0, c0=c0, nm=nm: e.dma_start(
                        out=s["prep"][:],
                        in_=ap_custom(self.dram_in[nm], r0 * (D if nm != "w_down" else INNER) + c0,
                                      [[(D if nm != "w_down" else INNER), 128], [1, 1024]])))
                p.op("vector", reads=("prep", "wscale"), writes=("prep",))(
                    lambda e: e.tensor_scalar(s["prep"][:], s["prep"][:],
                                              s["stats"][:, 30:31], 1.0, ALU.mult, ALU.min))
                p.op("vector", reads=("prep",), writes=("prep",))(
                    lambda e: e.tensor_scalar(s["prep"][:], s["prep"][:], -1.0, None, ALU.max))
                p.op("vector", reads=("prep",), writes=("istage",))(
                    lambda e: e.tensor_copy(s["istage"][:, 0:1024], s["prep"][:]))
                p.op("vector", reads=("istage",), writes=("prepbf",))(
                    lambda e: e.tensor_copy(s["prepbf"][:], s["istage"][:, 0:1024]))
                self._transpose_store_n(self.scr[dstn], rowlen, dcol0, 128,
                                        "tq_" + nm, row0=drow0, scrres="scr_" + dstn)

    def _transpose_store_n(self, dst_scr, rowlen, col0, ncols, tag, row0=0, scrres="scr"):
        p, s, P = self.p, self.s, self.P
        for g in range(2):
            def mmf(e, g=g):
                ins = None
                for b in range(4):
                    cb = g * 4 + b
                    ins = e.matmul(P[6][:, b * 128:(b + 1) * 128],
                                   s["prepbf"][:, cb * 128:(cb + 1) * 128],
                                   s["ident128"][:], start=True, stop=True)
                return ins
            p.op("tensor", reads=("prepbf", "c_ident128"), writes=("P6",), tag=tag)(mmf)
            p.op("vector", reads=("P6",), writes=("prepstage",))(
                lambda e: e.tensor_copy(s["prepstage"][:], P[6][:, 0:512]))
            out_ap = ap_custom(dst_scr, (row0 + g * 512) * rowlen + col0,
                               [[rowlen, 128], [128 * rowlen, 4], [1, ncols]])
            in_ap = ap_custom(s["prepstage"][:], 0, [[512, 128], [128, 4], [1, ncols]])
            p.op("sync", reads=("prepstage",), writes=(scrres,), dma=True, tag=tag + "st")(
                lambda e, i=in_ap, o=out_ap: e.dma_start(out=o, in_=i))

    def frontend(self, ch):
        p, s, P, cfg = self.p, self.s, self.P, self.cfg
        T, NTT = cfg.T, cfg.NTT
        xb = s["xbuf%d" % (ch % 2)]
        xr = "xbuf%d" % (ch % 2)
        in_ap = ap_custom(self.dram_in["x"], ch * T * D, [[D, 128], [128 * D, NTT], [1, D]])
        out_ap = ap_custom(xb[:], 0, [[2 * 1024, 128], [1024, NTT], [1, 1024]])
        p.op("sync", writes=(xr,), dma=True, tag="xld")(
            lambda e, i=in_ap, o=out_ap: e.dma_start(out=o, in_=i))
        self._ln(xb, xr, s["hid"], "hid", statcol=0)
        for tt in range(NTT):
            def mmf(e, tt=tt):
                ins = None
                for g in range(2):
                    for b in range(4):
                        fc = g * 4 + b
                        ins = e.matmul(P[g][:, b * 128:(b + 1) * 128],
                                       s["hid"][:, tt * 1024 + fc * 128: tt * 1024 + (fc + 1) * 128],
                                       s["ident128"][:], start=True, stop=True)
                return ins
            p.op("tensor", reads=("hid", "c_ident128"), writes=("P0", "P1"), tag="hidT")(mmf)
            for g in range(2):
                out_ap = ap_custom(s["hidFM"][:], tt * 128 + g * 4 * T,
                                   [[8 * T, 128], [T, 4], [1, 128]])
                p.op("vector", reads=("P%d" % g,), writes=("hidFM",))(
                    lambda e, g=g, o=out_ap: e.tensor_copy(o, P[g][:, 0:512]))
        self.tap("hid", lambda: s["hid"][:], ["hid"]) if ch == getattr(self, "tap_chunk", 0) else None
        self.tap("stats", lambda: s["stats"][:], ["stats"]) if ch == getattr(self, "tap_chunk", 0) else None
        self.tap("hidFM", lambda: s["hidFM"][:], ["hidFM"]) if ch == getattr(self, "tap_chunk", 0) else None
        if ch == 0:
            p.op("vector", writes=("qkvFM_m",), tag="qmz")(
                lambda e: e.memset(ap_custom(s["qkvFM"][:], 0,
                                             [[24 * (T + 3), 128], [T + 3, 24], [1, 3]]), 0.0))
        else:
            mg_out = ap_custom(s["qkvFM"][:], 0, [[24 * (T + 3), 128], [T + 3, 24], [1, 3]])
            p.op("vector", reads=("qhist",), writes=("qkvFM_m",), tag="qmc")(
                lambda e, o=mg_out: e.tensor_copy(o, s["qhist"][:]))
        for pair in range(12):
            buf = "wbuf_qkv" + ("A" if pair % 2 == 0 else "B")
            p.op("sync", reads=("scr_qkvT",), writes=(buf,), dma=True, tag="wqkv")(
                lambda e, pair=pair, buf=buf: e.dma_start(
                    out=s[buf][:],
                    in_=ap_custom(self.scr["qkvT"], pair * 256,
                                  [[QKV, 128], [128 * QKV, 8], [1, 256]])))
            for ms in range(2):
                m = pair * 2 + ms
                bank = m % 2
                def mmf(e, ms=ms, bank=bank, buf=buf):
                    ins = None
                    for k in range(8):
                        ins = e.matmul(P[bank][:, 0:T],
                                       s[buf][:, k * 256 + ms * 128: k * 256 + (ms + 1) * 128],
                                       s["hidFM"][:, k * T:(k + 1) * T],
                                       start=(k == 0), stop=(k == 7))
                    return ins
                p.op("tensor", reads=(buf, "hidFM"), writes=("P%d" % bank,),
                     tag="qkvmm")(mmf)
                p.op("scalar", reads=("P%d" % bank,), writes=("qkvFM",))(
                    lambda e, m=m, bank=bank: e.activation(
                        out=s["qkvFM"][:, m * (T + 3) + 3: (m + 1) * (T + 3)],
                        in_=P[bank][:, 0:T], func=AF.Copy))
        for k in range(8):
            cb = "wbuf_ctrl" + ("A" if k % 2 == 0 else "B")
            p.op("sync", reads=("scr_ctrlT",), writes=(cb,), dma=True, tag="wctrl")(
                lambda e, k=k, cb=cb: e.dma_start(
                    out=s[cb][:],
                    in_=ap_custom(self.scr["ctrlT"], k * 128 * CTRL, [[CTRL, 128], [1, CTRL]])))
            def mmab(e, k=k, cb=cb):
                e.matmul(P[2][0:16, 0:T], s[cb][:, 0:16],
                         s["hidFM"][:, k * T:(k + 1) * T],
                         start=(k == 0), stop=False)
                return e.matmul(P[2][32:48, 0:T], s[cb][:, 16:32],
                                s["hidFM"][:, k * T:(k + 1) * T],
                                start=(k == 0), stop=False)
            p.op("tensor", reads=(cb, "hidFM"), writes=("P2",), tag="abmm")(mmab)
            def mmgate(e, k=k, cb=cb):
                ins = None
                for tt in range(NTT):
                    for hf in range(2):
                        ins = e.matmul(
                            P[3 + tt * 2 + hf][:, 0:512],
                            s["hidFM"][:, k * T + tt * 128: k * T + (tt + 1) * 128],
                            s[cb][:, 32 + hf * 512: 32 + (hf + 1) * 512],
                            start=(k == 0), stop=False)
                return ins
            p.op("tensor", reads=(cb, "hidFM"), writes=("P3", "P4", "P5", "P6"),
                 tag="gatemm")(mmgate)
        def mmab_b(e):
            e.matmul(P[2][0:16, 0:T], s["ctrl_b_bf"][0:1, 0:16],
                     s["onesT_bf"][0:1, 0:T], start=False, stop=True)
            return e.matmul(P[2][32:48, 0:T], s["ctrl_b_bf"][0:1, 16:32],
                            s["onesT_bf"][0:1, 0:T], start=False, stop=True)
        p.op("tensor", reads=("c_ctrl_b", "c_onesT_bf"), writes=("P2",), tag="abbias")(mmab_b)
        def mmgate_b(e):
            ins = None
            for tt in range(NTT):
                for hf in range(2):
                    ins = e.matmul(P[3 + tt * 2 + hf][:, 0:512],
                                   s["ones128row_bf"][0:1, :],
                                   s["ctrl_b_bf"][0:1, 32 + hf * 512: 32 + (hf + 1) * 512],
                                   start=False, stop=True)
            return ins
        p.op("tensor", reads=("c_ctrl_b", "c_ones128row_bf"),
             writes=("P3", "P4", "P5", "P6"), tag="gatebias")(mmgate_b)
        p.op("vector", reads=("P2",), writes=("ab",))(
            lambda e: e.tensor_copy(s["ab"][0:48, 0:T], P[2][0:48, 0:T]))
        for tt in range(NTT):
            for hf in range(2):
                p.op("scalar", reads=("P%d" % (3 + tt * 2 + hf),), writes=("outgate",))(
                    lambda e, tt=tt, hf=hf: e.activation(
                        out=s["outgate"][:, tt * 1024 + hf * 512: tt * 1024 + (hf + 1) * 512],
                        in_=P[3 + tt * 2 + hf][:, 0:512], func=AF.Sigmoid))
        for si, (dstn, dstr) in enumerate([("convq", "convq"), ("convk", "convk"),
                                           ("Vfm", "Vfm")]):
            for fc in range(8):
                m = si * 8 + fc
                bank = fc % 2
                def mmf(e, si=si, fc=fc, m=m, bank=bank):
                    ins = None
                    for j in range(4):
                        idx = si * 32 + fc * 4 + j
                        ins = e.matmul(P[bank][:, 0:T],
                                       s["diags"][:, idx * 128:(idx + 1) * 128],
                                       s["qkvFM"][:, m * (T + 3) + j: m * (T + 3) + j + T],
                                       start=(j == 0), stop=(j == 3))
                    return ins
                p.op("tensor", reads=("c_diag", "qkvFM", "qkvFM_m"),
                     writes=("P%d" % bank,), tag="convmm")(mmf)
                p.op("scalar", reads=("P%d" % bank,), writes=(dstr,))(
                    lambda e, fc=fc, bank=bank, dstn=dstn: e.activation(
                        out=s[dstn][:, fc * T:(fc + 1) * T],
                        in_=P[bank][:, 0:T], func=AF.Silu))
        mg_in = ap_custom(s["qkvFM"][:], T, [[24 * (T + 3), 128], [T + 3, 24], [1, 3]])
        p.op("vector", reads=("qkvFM",), writes=("qhist",), tag="qhist")(
            lambda e, i=mg_in: e.tensor_copy(s["qhist"][:], i))
        for src, dst in [("convq", "r_invq"), ("convk", "r_invk")]:
            for fc in range(8):
                h0 = (fc % 2) * T
                p.op("scalar", reads=(src,), writes=("sqbuf%d" % (fc % 2),))(
                    lambda e, fc=fc, src=src, h0=h0: e.activation(
                        out=s["sqbuf"][:, h0:h0 + T], in_=s[src][:, fc * T:(fc + 1) * T],
                        func=AF.Square))
                def mmf(e, fc=fc, h0=h0):
                    return e.matmul(P[7][0:16, 0:T],
                                    s["headmaskC"][:, fc * 16:(fc + 1) * 16],
                                    s["sqbuf"][:, h0:h0 + T], start=(fc == 0), stop=(fc == 7))
                p.op("tensor", reads=("sqbuf%d" % (fc % 2), "c_headmaskC"),
                     writes=("P7",), tag="l2mm")(mmf)
            p.op("scalar", reads=("P7",), writes=("r_t",))(
                lambda e: e.activation(out=s["r_t"][:, 0:T], in_=P[7][0:16, 0:T],
                                       func=AF.Sqrt))
            p.op("vector", reads=("r_t",), writes=("r_t",))(
                lambda e: e.tensor_scalar(s["r_t"][:, 0:T], s["r_t"][:, 0:T], 1e-6,
                                          None, ALU.max))
            p.op("vector", reads=("r_t",), writes=(dst,))(
                lambda e, dst=dst: e.reciprocal(s[dst][:, 0:T], s["r_t"][:, 0:T]))
        p.op("scalar", reads=("ab",), writes=("r_siga",))(
            lambda e: e.activation(out=s["r_siga"][:, 0:T], in_=s["ab"][0:16, 0:T],
                                   func=AF.Sigmoid))
        p.op("scalar", reads=("r_siga",), writes=("r_logb",))(
            lambda e: e.activation(out=s["r_logb"][:, 0:T], in_=s["r_siga"][:, 0:T],
                                   func=AF.Ln, scale=0.98, bias=s["biasc"][0:16, 2:3]))
        p.op("scalar", reads=("ab",), writes=("r_beta",))(
            lambda e: e.activation(out=s["r_beta"][:, 0:T], in_=s["ab"][32:48, 0:T],
                                   func=AF.Sigmoid))
        for sc in range(cfg.NSC):
            p.op("vector", reads=("r_logb", "c_onesf_16x64"), writes=("r_L",))(
                lambda e, sc=sc: e.tensor_tensor_scan(
                    s["r_L"][:, sc * 64:(sc + 1) * 64], s["onesf_16x64"][:],
                    s["r_logb"][:, sc * 64:(sc + 1) * 64], 0.0, ALU.mult, ALU.add))
        p.op("vector", reads=("r_L",), writes=("r_negL",))(
            lambda e: e.tensor_scalar(s["r_negL"][:, 0:T], s["r_L"][:, 0:T], -1.0,
                                      None, ALU.mult))
        p.op("scalar", reads=("r_L",), writes=("r_eL",))(
            lambda e: e.activation(out=s["r_eL"][:, 0:T], in_=s["r_L"][:, 0:T],
                                   func=AF.Exp))
        p.op("vector", reads=("r_invq",), writes=("rb_qn",))(
            lambda e: e.tensor_copy(s["rb_qn"][:, 0:T], s["r_invq"][:, 0:T]))
        p.op("vector", reads=("r_invk",), writes=("rb_kn",))(
            lambda e: e.tensor_copy(s["rb_kn"][:, 0:T], s["r_invk"][:, 0:T]))
        p.op("vector", reads=("r_invq", "r_eL"), writes=("rb_qh",))(
            lambda e: e.tensor_mul(s["rb_qh"][:, 0:T], s["r_invq"][:, 0:T],
                                   s["r_eL"][:, 0:T]))
        p.op("vector", reads=("r_invk", "r_eL"), writes=("rb_khm",))(
            lambda e: e.scalar_tensor_tensor(s["rb_khm"][:, 0:T], s["r_invk"][:, 0:T],
                                             -1.0, s["r_eL"][:, 0:T],
                                             ALU.mult, ALU.mult))
        for (row, r0m, src, dst) in [("rb_qn", 0, "convq", "Qn"),
                                     ("rb_qh", 0, "convq", "Qh"),
                                     ("rb_kn", 0, "convk", "Kn"),
                                     ("rb_khm", 0, "convk", "Khm")]:
            for fc in range(8):
                pb = 7 if fc % 2 == 0 else 0
                def mmf(e, row=row, r0m=r0m, fc=fc, pb=pb):
                    return e.matmul(P[pb][:, 0:T],
                                    s["mask2pack"][r0m:r0m + 16, fc * 128:(fc + 1) * 128],
                                    s[row][:, 0:T], start=True, stop=True)
                p.op("tensor", reads=(row, "c_mask2pack"), writes=("P%d" % pb,), tag="bcmm")(mmf)
                p.op("vector", reads=("P%d" % pb, src), writes=(dst,))(
                    lambda e, fc=fc, src=src, dst=dst, pb=pb: e.tensor_mul(
                        s[dst][:, fc * T:(fc + 1) * T],
                        s[src][:, fc * T:(fc + 1) * T], P[pb][:, 0:T]))

    def _ln(self, xb, xr, dst, dstr, statcol, eps=1e-5):
        p, s, cfg = self.p, self.s, self.cfg
        for tt in range(cfg.NTT):
            sl = (tt * 1024, (tt + 1) * 1024)
            c0 = statcol + tt * 8
            p.op("vector", reads=(xr,), writes=("stats",))(
                lambda e, sl=sl, c0=c0: e.tensor_reduce(
                    s["stats"][:, c0:c0 + 1], xb[:, sl[0]:sl[1]], AX.X, ALU.add))
            sqr = "sq128" if tt == 0 else "sq128b"
            h0 = tt * 1024
            p.op("scalar", reads=(xr,), writes=(sqr,))(
                lambda e, sl=sl, h0=h0: e.activation(out=s["sqf"][:, h0:h0 + 1024],
                                              in_=xb[:, sl[0]:sl[1]], func=AF.Square))
            p.op("vector", reads=(sqr,), writes=("stats",))(
                lambda e, c0=c0, h0=h0: e.tensor_reduce(
                    s["stats"][:, c0 + 1:c0 + 2], s["sqf"][:, h0:h0 + 1024], AX.X, ALU.add))
            p.op("scalar", reads=("stats",), writes=("stats",))(
                lambda e, c0=c0: e.activation(out=s["stats"][:, c0 + 6:c0 + 7],
                                              in_=s["stats"][:, c0:c0 + 1],
                                              func=AF.Copy, scale=1.0 / 1024))
            p.op("scalar", reads=("stats",), writes=("stats",))(
                lambda e, c0=c0: e.activation(out=s["stats"][:, c0 + 7:c0 + 8],
                                              in_=s["stats"][:, c0 + 1:c0 + 2],
                                              func=AF.Copy, scale=1.0 / 1024))
            p.op("vector", reads=("stats",), writes=("stats",))(
                lambda e, c0=c0: e.tensor_mul(s["stats"][:, c0 + 2:c0 + 3],
                                              s["stats"][:, c0 + 6:c0 + 7],
                                              s["stats"][:, c0 + 6:c0 + 7]))
            p.op("gpsimd", reads=("stats",), writes=("stats",))(
                lambda e, c0=c0: e.tensor_sub(s["stats"][:, c0 + 3:c0 + 4],
                                              s["stats"][:, c0 + 7:c0 + 8],
                                              s["stats"][:, c0 + 2:c0 + 3]))
            p.op("scalar", reads=("stats",), writes=("stats",))(
                lambda e, c0=c0: e.activation(out=s["stats"][:, c0 + 4:c0 + 5],
                                              in_=s["stats"][:, c0 + 3:c0 + 4],
                                              func=AF.Sqrt, bias=s["biasc"][:, 0:1] if abs(eps - 1e-5) < 1e-12 else s["biasc"][:, 1:2]))
            p.op("vector", reads=("stats",), writes=("stats",))(
                lambda e, c0=c0: e.reciprocal(s["stats"][:, c0 + 5:c0 + 6],
                                              s["stats"][:, c0 + 4:c0 + 5]))
            p.op("vector", reads=(xr, "stats"), writes=(sqr,))(
                lambda e, sl=sl, c0=c0, h0=h0: e.tensor_scalar(
                    s["sqf"][:, h0:h0 + 1024], xb[:, sl[0]:sl[1]],
                    s["stats"][:, c0 + 6:c0 + 7], None, ALU.subtract))
            p.op("scalar", reads=(sqr, "stats"), writes=(dstr,))(
                lambda e, sl=sl, c0=c0, h0=h0: e.activation(
                    out=dst[:, sl[0]:sl[1]], in_=s["sqf"][:, h0:h0 + 1024],
                    func=AF.Copy, scale=s["stats"][:, c0 + 5:c0 + 6]))
    def scan_front(self, ch, sc):
        p, s, P, cfg = self.p, self.s, self.P, self.cfg
        T = cfg.T
        ts = sc * 64
        BLn = "BL%d" % (sc % 2)

        def blk(t):
            return [(t[64 * r:64 * r + 64, 64 * j:64 * j + 64], r, j)
                    for j in range(8) for r in range(2)]

        def fmsl(t, r, j, w=64, off=0):
            return t[64 * r:64 * r + 64, j * T + ts + off: j * T + ts + off + w]

        negL_rep = ap_custom(s["r_negL"][:], ts, [[T, 16], [0, 2], [1, 64]])
        m2s3 = ap_custom(s["mask2s_f32"][:], 0, [[128, 16], [64, 2], [1, 64]])
        lhsE3 = ap_custom(s["lhsE"][:], 0, [[128, 16], [64, 2], [1, 64]])
        p.op("vector", reads=("r_negL", "c_mask2s_f32"), writes=("lhsE",), tag="lhsE")(
            lambda e: e.tensor_tensor(lhsE3, negL_rep, m2s3, ALU.mult))
        beta_rep = ap_custom(s["r_beta"][:], ts, [[T, 16], [0, 2], [1, 64]])
        m2sb3 = ap_custom(s["mask2s_bf"][:], 0, [[128, 16], [64, 2], [1, 64]])
        lhsB3 = ap_custom(s["lhsB"][:], 0, [[128, 16], [64, 2], [1, 64]])
        p.op("vector", reads=("r_beta", "c_mask2s_bf"), writes=("lhsB",), tag="lhsB")(
            lambda e: e.tensor_tensor(lhsB3, beta_rep, m2sb3, ALU.mult))
        L_rep = ap_custom(s["r_L"][:], ts, [[T, 16], [0, 8], [1, 64]])
        hm83 = ap_custom(s["headmask8_f32"][:], 0, [[512, 16], [64, 8], [1, 64]])
        Lm3 = ap_custom(s["Lmaskd"][:], 0, [[512, 16], [64, 8], [1, 64]])
        p.op("vector", reads=("r_L", "c_headmask8_f32"), writes=("Lmaskd",), tag="Lmaskd")(
            lambda e: e.tensor_tensor(Lm3, L_rep, hm83, ALU.mult))
        def mmE(e):
            e.matmul(P[1][:, 0:512], s["mask2s_f32"][:], s["Lmaskd"][:],
                     start=True, stop=False)
            return e.matmul(P[1][:, 0:512], s["lhsE"][:], s["headmask8_f32"][:],
                            start=False, stop=True)
        p.op("tensor", reads=("Lmaskd", "lhsE", "c_mask2s_f32", "c_headmask8_f32"),
             writes=("P1",), tag="Emm")(mmE)
        p.op("vector", reads=("P1",), writes=("t1",))(
            lambda e: e.tensor_scalar(s["t1"][:], P[1][:, 0:512], 0.0, None, ALU.min))
        p.op("scalar", reads=("t1",), writes=("E",))(
            lambda e: e.activation(out=s["E"][:], in_=s["t1"][:], func=AF.Exp))
        def mmG(e):
            ins = None
            for (b, r, j) in blk(P[1]):
                ins = e.matmul(b, fmsl(s["Kn"], r, j), fmsl(s["Kn"], r, j),
                               start=True, stop=True)
            return ins
        p.op("tensor", reads=("Kn",), writes=("P1",), tag="Gmm")(mmG)
        p.op("vector", reads=("E", "P1"), writes=("t1",))(
            lambda e: e.tensor_mul(s["t1"][:], s["E"][:], P[1][:, 0:512]))
        def mmP(e):
            ins = None
            for (b, r, j) in blk(P[2]):
                ins = e.matmul(b, fmsl(s["Kn"], r, j), fmsl(s["Qn"], r, j),
                               start=True, stop=True)
            return ins
        p.op("tensor", reads=("Kn", "Qn"), writes=("P2",), tag="Pmm")(mmP)
        p.op("vector", reads=("E", "P2"), writes=("t2",))(
            lambda e: e.tensor_mul(s["t2"][:], s["E"][:], P[2][:, 0:512]))
        p.op("tensor", reads=("lhsB", "c_headmask8_bf"), writes=("P2",), tag="bpmm")(
            lambda e: e.matmul(P[2][:, 0:512], s["lhsB"][:], s["headmask8_bf"][:],
                               start=True, stop=True))
        p.op("vector", reads=("P2",), writes=("bpartU",))(
            lambda e: e.tensor_copy(s["bpartU"][:], P[2][:, 0:512]))
        p.op("vector", reads=("bpartU", "c_negstriu"), writes=("bmask_t",))(
            lambda e: e.tensor_mul(s["bmask_t"][:], s["bpartU"][:], s["negstriu"][:]))
        p.op("vector", reads=("bpartU", "c_triuincl"), writes=("bmask_B",))(
            lambda e: e.tensor_mul(s["bmask_B"][:], s["bpartU"][:], s["triuincl"][:]))
        gcn = "gcall%d" % (sc % 2)
        daln = "dall%d" % (sc % 2)
        eL_last = ap_custom(s["r_eL"][:], ts + 63, [[T, 16], [0, 8]])
        p.op("vector", reads=("r_eL", "c_headmaskj"), writes=("gcrhs",))(
            lambda e: e.tensor_tensor(s["gcrhs"][:], eL_last, s["headmaskj"][:], ALU.mult))
        p.op("tensor", reads=("gcrhs", "c_mask2s_f32"), writes=("P2",), tag="gcmm")(
            lambda e: e.matmul(P[2][:, 0:8], s["mask2s_f32"][:], s["gcrhs"][:],
                               start=True, stop=True))
        p.op("scalar", reads=("P2",), writes=(gcn,))(
            lambda e, gcn=gcn: e.activation(out=s[gcn][:], in_=P[2][:, 0:8], func=AF.Copy))
        E_last = ap_custom(s["E"][:], 63, [[512, 128], [64, 8]])
        bp_first = ap_custom(s["bpartU"][:], 0, [[512, 128], [64, 8]])
        p.op("vector", reads=("E", "bpartU"), writes=(daln,))(
            lambda e, daln=daln: e.tensor_tensor(s[daln][:], E_last, bp_first, ALU.mult))
        p.op("vector", reads=("t1", "bmask_t"), writes=("NL0a", "NL0b"))(
            lambda e: e.tensor_mul(s["NL0"][:], s["t1"][:], s["bmask_t"][:]))
        p.op("vector", reads=("t2", "bmask_B"), writes=(BLn,))(
            lambda e, BLn=BLn: e.tensor_mul(s[BLn][:], s["t2"][:], s["bmask_B"][:]))
        def mmNN(e):
            ins = None
            for (b, r, j) in blk(P[1]):
                ins = e.matmul(b, s["NL0"][64 * r:64 * r + 64, 64 * j:64 * j + 64],
                               s["ident64"][64 * r:64 * r + 64, :], start=True, stop=True)
            return ins
        p.op("tensor", reads=("NL0a", "NL0b", "c_ident64"), writes=("P1",), tag="NNmm")(mmNN)
        p.op("vector", reads=("P1",), writes=("NN0a", "NN0b"))(
            lambda e: e.tensor_copy(s["NN0"][:], P[1][:, 0:512]))

    def scan_tail_a(self, ch, sc):
        p, s, P, cfg = self.p, self.s, self.P, self.cfg
        T = cfg.T
        ts = sc * 64
        def blk(t):
            return [(t[64 * r:64 * r + 64, 64 * j:64 * j + 64], r, j)
                    for j in range(8) for r in range(2)]

        def fmsl(t, r, j, w=64, off=0):
            return t[64 * r:64 * r + 64, j * T + ts + off: j * T + ts + off + w]
        def mmR0(e):
            ins = None
            for r in range(2):
                for j in range(8):
                    b = P[5][64 * r:64 * r + 64, 64 * j:64 * j + 64]
                    e.matmul(b, fmsl(s["Vfm"], r, j), s["ident64"][64 * r:64 * r + 64, :],
                             start=(j == 0), stop=False)
                    ins = e.matmul(b, fmsl(s["Khm"], r, j),
                                   s["Sall"][64 * r:64 * r + 64, 64 * j:64 * j + 64],
                                   start=False, stop=True)
            return ins
        p.op("tensor", reads=("Vfm", "Khm", "Sall", "c_ident64"), writes=("P5",),
             tag="R0mm")(mmR0)
        p.op("scalar", reads=("P5",), writes=("R",))(
            lambda e: e.activation(out=s["R"][:], in_=P[5][:, 0:512], func=AF.Copy))
        self._levels(ch, sc, 0, 2)

    def scan_tail_b(self, ch, sc):
        p, s, P, cfg = self.p, self.s, self.P, self.cfg
        T = cfg.T
        ts = sc * 64
        def blk(t):
            return [(t[64 * r:64 * r + 64, 64 * j:64 * j + 64], r, j)
                    for j in range(8) for r in range(2)]

        def fmsl(t, r, j, w=64, off=0):
            return t[64 * r:64 * r + 64, j * T + ts + off: j * T + ts + off + w]
        BLn = "BL%d" % (sc % 2)
        self._levels(ch, sc, 2, 6)
        def mmRead(e):
            ins = None
            for r in range(2):
                for j in range(8):
                    b = P[6][64 * r:64 * r + 64, 64 * j:64 * j + 64]
                    e.matmul(b, fmsl(s["Qh"], r, j),
                             s["Sall"][64 * r:64 * r + 64, 64 * j:64 * j + 64],
                             start=(j == 0), stop=False)
                    ins = e.matmul(b, s[BLn][64 * r:64 * r + 64, 64 * j:64 * j + 64],
                                   s["R"][64 * r:64 * r + 64, 64 * j:64 * j + 64],
                                   start=False, stop=True)
            return ins
        p.op("tensor", reads=("Qh", "Sall", BLn, "R"), writes=("P6",), tag="readmm")(mmRead)
        p.op("vector", reads=("P6",), writes=("readstm",))(
            lambda e: e.tensor_copy(s["readstm"][:], P[6][:, 0:512]))
        def mmKtm(e):
            ins = None
            for (b, r, j) in blk(P[4]):
                ins = e.matmul(b, fmsl(s["Kn"], r, j), s["ident64"][64 * r:64 * r + 64, :],
                               start=True, stop=True)
            return ins
        p.op("tensor", reads=("Kn", "c_ident64"), writes=("P4",), tag="Ktmmm")(mmKtm)
        daln = "dall%d" % (sc % 2)
        P4_3 = ap_custom(self.P[4][:], 0, [[512, 128], [64, 8], [1, 64]])
        dall_b = ap_custom(s[daln][:], 0, [[8, 128], [1, 8], [0, 64]])
        Ktmp3 = ap_custom(s["Ktmp"][:], 0, [[512, 128], [64, 8], [1, 64]])
        p.op("vector", reads=("P4", daln), writes=("Ktmp",))(
            lambda e: e.tensor_tensor(Ktmp3, P4_3, dall_b, ALU.mult))
        def mmState(e):
            ins = None
            for (b, r, j) in blk(P[6]):
                ins = e.matmul(b, s["Ktmp"][64 * r:64 * r + 64, 64 * j:64 * j + 64],
                               s["R"][64 * r:64 * r + 64, 64 * j:64 * j + 64],
                               start=True, stop=True)
            return ins
        p.op("tensor", reads=("Ktmp", "R"), writes=("P6",), tag="statemm")(mmState)
        gcn = "gcall%d" % (sc % 2)
        gc_b = ap_custom(s[gcn][:], 0, [[8, 128], [1, 8], [0, 64]])
        Sall3 = ap_custom(s["Sall"][:], 0, [[512, 128], [64, 8], [1, 64]])
        t23 = ap_custom(s["t2"][:], 0, [[512, 128], [64, 8], [1, 64]])
        p.op("vector", reads=("Sall", gcn), writes=("t2",))(
            lambda e: e.tensor_tensor(t23, Sall3, gc_b, ALU.mult))
        p.op("vector", reads=("P6", "t2"), writes=("Sall",))(
            lambda e: e.tensor_add(s["Sall"][:], P[6][:, 0:512], s["t2"][:]))
        def mmRFM(e):
            ins = None
            for (b, r, j) in blk(P[4]):
                ins = e.matmul(b, s["readstm"][64 * r:64 * r + 64, 64 * j:64 * j + 64],
                               s["ident64"][64 * r:64 * r + 64, :], start=True, stop=True)
            return ins
        p.op("tensor", reads=("readstm", "c_ident64"), writes=("P4",), tag="rfmm")(mmRFM)
        P4_3b = ap_custom(self.P[4][:], 0, [[512, 128], [64, 8], [1, 64]])
        rfm3 = ap_custom(s["readsFM"][:], ts, [[8 * T, 128], [T, 8], [1, 64]])
        p.op("vector", reads=("P4",), writes=("readsFM",))(
            lambda e: e.tensor_copy(rfm3, P4_3b))

    def _levels(self, ch, sc, lev0, lev1):
        p, s, P = self.p, self.s, self.P

        def blk(t):
            return [(t[64 * r:64 * r + 64, 64 * j:64 * j + 64], r, j)
                    for j in range(8) for r in range(2)]
        for lev in range(lev0, lev1):
            nl = "NL0" if lev == 0 else ("NLa" if lev % 2 == 1 else "NLb")
            nn = "NN0" if lev == 0 else ("NNa" if lev % 2 == 1 else "NNb")
            nl2 = "NLa" if lev % 2 == 0 else "NLb"
            nn2 = "NNa" if lev % 2 == 0 else "NNb"
            def mmAp(e, nl=nl):
                ins = None
                for (b, r, j) in blk(P[5]):
                    ins = e.matmul(b, s[nl][64 * r:64 * r + 64, 64 * j:64 * j + 64],
                                   s["R"][64 * r:64 * r + 64, 64 * j:64 * j + 64],
                                   start=False, stop=True)
                return ins
            p.op("tensor", reads=(nl, "R"), writes=("P5",), tag="apply%d" % lev)(mmAp)
            if lev < 5:
                def mmSqN(e, nl=nl, nn=nn):
                    ins = None
                    for (b, r, j) in blk(P[4]):
                        ins = e.matmul(b, s[nl][64 * r:64 * r + 64, 64 * j:64 * j + 64],
                                       s[nn][64 * r:64 * r + 64, 64 * j:64 * j + 64],
                                       start=True, stop=True)
                    return ins
                p.op("tensor", reads=(nl, nn), writes=("P4",), tag="sqN%d" % lev)(mmSqN)
                def mmSqL(e, nl=nl, nn=nn):
                    ins = None
                    for (b, r, j) in blk(P[7]):
                        ins = e.matmul(b, s[nn][64 * r:64 * r + 64, 64 * j:64 * j + 64],
                                       s[nl][64 * r:64 * r + 64, 64 * j:64 * j + 64],
                                       start=True, stop=True)
                    return ins
                p.op("tensor", reads=(nl, nn), writes=("P7",), tag="sqL%d" % lev)(mmSqL)
                p.op("vector", reads=("P4",), writes=(nn2,))(
                    lambda e, nn2=nn2: e.tensor_copy(s[nn2][:], P[4][:, 0:512]))
                p.op("scalar", reads=("P7",), writes=(nl2,))(
                    lambda e, nl2=nl2: e.activation(out=s[nl2][:], in_=P[7][:, 0:512],
                                                    func=AF.Copy))
            eng = "vector" if lev % 2 == 0 else "scalar"
            if eng == "vector":
                p.op("vector", reads=("P5",), writes=("R",))(
                    lambda e: e.tensor_copy(s["R"][:], P[5][:, 0:512]))
            else:
                p.op("scalar", reads=("P5",), writes=("R",))(
                    lambda e: e.activation(out=s["R"][:], in_=P[5][:, 0:512], func=AF.Copy))

    def epilogue(self, ch):
        p, s, P, cfg = self.p, self.s, self.P, self.cfg
        T, NTT = cfg.T, cfg.NTT
        xb = s["xbuf%d" % (ch % 2)]
        xr = "xbuf%d" % (ch % 2)
        for fc in range(8):
            h0 = (fc % 2) * T
            p.op("scalar", reads=("readsFM",), writes=("sqbuf%d" % (fc % 2),))(
                lambda e, fc=fc, h0=h0: e.activation(out=s["sqbuf"][:, h0:h0 + T],
                                              in_=s["readsFM"][:, fc * T:(fc + 1) * T],
                                              func=AF.Square))
            def mmf(e, fc=fc, h0=h0):
                return e.matmul(P[7][0:16, 0:T], s["headmaskC"][:, fc * 16:(fc + 1) * 16],
                                s["sqbuf"][:, h0:h0 + T], start=(fc == 0), stop=(fc == 7))
            p.op("tensor", reads=("sqbuf%d" % (fc % 2), "c_headmaskC"),
                 writes=("P7",), tag="rmsmm")(mmf)
        p.op("scalar", reads=("P7",), writes=("rmsrow",))(
            lambda e: e.activation(out=s["rmsrow"][:, 0:T], in_=P[7][0:16, 0:T],
                                   func=AF.Sqrt, scale=1.0 / 64, bias=s["biasc"][0:16, 1:2]))
        p.op("vector", reads=("rmsrow",), writes=("r_t",))(
            lambda e: e.reciprocal(s["r_t"][:, 0:T], s["rmsrow"][:, 0:T]))
        p.op("vector", reads=("r_t",), writes=("rmsbf",))(
            lambda e: e.tensor_copy(s["rmsbf"][:, 0:T], s["r_t"][:, 0:T]))
        for fc in range(8):
            pb = fc % 2
            p.op("tensor", reads=("rmsbf", "c_mask2w"), writes=("P%d" % pb,), tag="nbc")(
                lambda e, fc=fc, pb=pb: e.matmul(P[pb][:, 0:T],
                                          s["mask2w"][:, fc * 128:(fc + 1) * 128],
                                          s["rmsbf"][:, 0:T], start=True, stop=True))
            p.op("vector", reads=("P%d" % pb, "readsFM"), writes=("normedFM",))(
                lambda e, fc=fc, pb=pb: e.tensor_mul(s["normedFM"][:, fc * T:(fc + 1) * T],
                                              s["readsFM"][:, fc * T:(fc + 1) * T],
                                              P[pb][:, 0:T]))
        for kg in range(4):
            ob = "wbuf_out" + ("A" if kg % 2 == 0 else "B")
            p.op("sync", reads=("scr_outT",), writes=(ob,), dma=True, tag="wout")(
                lambda e, kg=kg, ob=ob: e.dma_start(
                    out=s[ob][:],
                    in_=ap_custom(self.scr["outT"], kg * 256 * D,
                                  [[D, 128], [128 * D, 2], [1, D]])))
            def mmf(e, kg=kg, ob=ob):
                ins = None
                for kk in range(2):
                    k = kg * 2 + kk
                    for tt in range(NTT):
                        for hf in range(2):
                            ins = e.matmul(P[3 + tt * 2 + hf][:, 0:512],
                                           s["normedFM"][:, k * T + tt * 128: k * T + (tt + 1) * 128],
                                           s[ob][:, kk * 1024 + hf * 512: kk * 1024 + (hf + 1) * 512],
                                           start=(k == 0), stop=False)
                return ins
            p.op("tensor", reads=(ob, "normedFM"),
                 writes=("P3", "P4", "P5", "P6"), tag="outmm")(mmf)
        def mmb(e):
            ins = None
            for tt in range(NTT):
                for hf in range(2):
                    ins = e.matmul(P[3 + tt * 2 + hf][:, 0:512],
                                   s["ones128row_bf"][0:1, :],
                                   s["out_b_bf"][0:1, hf * 512:(hf + 1) * 512],
                                   start=False, stop=True)
            return ins
        p.op("tensor", reads=("c_out_b", "c_ones128row_bf"),
             writes=("P3", "P4", "P5", "P6"), tag="outbias")(mmb)
        for tt in range(NTT):
            for hf in range(2):
                pi = 3 + tt * 2 + hf
                sl = (tt * 1024 + hf * 512, tt * 1024 + (hf + 1) * 512)
                p.op("vector", reads=("P%d" % pi, "outgate"), writes=("t1",))(
                    lambda e, pi=pi, sl=sl: e.tensor_mul(
                        s["t1"][:], P[pi][:, 0:512], s["outgate"][:, sl[0]:sl[1]]))
                p.op("vector", reads=("t1", xr), writes=(xr,))(
                    lambda e, sl=sl: e.tensor_add(xb[:, sl[0]:sl[1]],
                                                  xb[:, sl[0]:sl[1]], s["t1"][:, 0:512]))

    def ffn(self, ch):
        p, s, P, cfg = self.p, self.s, self.P, self.cfg
        T, NTT = cfg.T, cfg.NTT
        xb = s["xbuf%d" % (ch % 2)]
        xr = "xbuf%d" % (ch % 2)
        self._ln(xb, xr, s["h2"], "h2", statcol=16)
        for tt in range(NTT):
            sl = (tt * 1024, (tt + 1) * 1024)
            c0 = tt * 4
            p.op("vector", reads=("h2",), writes=("fstat",))(
                lambda e, sl=sl, c0=c0: e.tensor_reduce(
                    s["fstat"][:, c0:c0 + 1], s["h2"][:, sl[0]:sl[1]], AX.X, ALU.max,
                    apply_absolute_value=True))
            p.op("gpsimd", reads=("fstat",), writes=("fstat",))(
                lambda e, c0=c0: e.tensor_scalar(s["fstat"][:, c0 + 1:c0 + 2],
                                                 s["fstat"][:, c0:c0 + 1],
                                                 1e-5, None, ALU.max))
            p.op("vector", reads=("fstat",), writes=("fstat",))(
                lambda e, c0=c0: e.reciprocal(s["fstat"][:, c0 + 2:c0 + 3],
                                              s["fstat"][:, c0 + 1:c0 + 2]))
            p.op("scalar", reads=("fstat",), writes=("fstat",))(
                lambda e, c0=c0: e.activation(out=s["fstat"][:, c0 + 3:c0 + 4],
                                              in_=s["fstat"][:, c0 + 2:c0 + 3],
                                              func=AF.Copy, scale=127.0))
            p.op("vector", reads=("fstat", "wsc"), writes=("fstat",))(
                lambda e, c0=c0, tt=tt: e.scalar_tensor_tensor(
                    s["fstat"][:, 16 + tt:17 + tt], s["fstat"][:, c0 + 1:c0 + 2],
                    1.0 / 127, s["wsc"][:, 0:1], ALU.mult, ALU.mult))
            p.op("vector", reads=("fstat", "wsc"), writes=("fstat",))(
                lambda e, c0=c0, tt=tt: e.scalar_tensor_tensor(
                    s["fstat"][:, 18 + tt:19 + tt], s["fstat"][:, c0 + 1:c0 + 2],
                    1.0 / 127, s["wsc"][:, 1:2], ALU.mult, ALU.mult))
            p.op("scalar", reads=("h2", "fstat"), writes=("istage",))(
                lambda e, sl=sl, c0=c0: e.activation(
                    out=s["istage"][:, 0:1024], in_=s["h2"][:, sl[0]:sl[1]],
                    func=AF.Copy, scale=s["fstat"][:, c0 + 3:c0 + 4]))
            p.op("vector", reads=("istage",), writes=("istage",))(
                lambda e: e.tensor_scalar(s["istage"][:, 0:1024], s["istage"][:, 0:1024],
                                          127, -128, ALU.min, ALU.max))
            p.op("vector", reads=("istage",), writes=("h2q",))(
                lambda e, sl=sl: e.tensor_copy(s["h2q"][:, sl[0]:sl[1]],
                                               s["istage"][:, 0:1024]))
        for tt in range(NTT):
            def mmf(e, tt=tt):
                ins = None
                for g in range(2):
                    for b in range(4):
                        fc = g * 4 + b
                        ins = e.matmul(P[g][:, b * 128:(b + 1) * 128],
                                       s["h2q"][:, tt * 1024 + fc * 128: tt * 1024 + (fc + 1) * 128],
                                       s["ident128"][:], start=True, stop=True)
                return ins
            p.op("tensor", reads=("h2q", "c_ident128"), writes=("P0", "P1"), tag="h2qT")(mmf)
            for g in range(2):
                out_ap = ap_custom(s["h2qFM"][:], tt * 128 + g * 4 * T,
                                   [[8 * T, 128], [T, 4], [1, 128]])
                p.op("vector", reads=("P%d" % g,), writes=("h2qFM",))(
                    lambda e, g=g, o=out_ap: e.tensor_copy(o, P[g][:, 0:512]))
        if ch == getattr(self, "tap_chunk", 0):
            self.tap("h2q", lambda: s["h2q"][:], ["h2q"])
            self.tap("h2qFM", lambda: s["h2qFM"][:], ["h2qFM"])
            self.tap("fstat", lambda: s["fstat"][:], ["fstat"])
        for ic in range(8):
            for nm, b0, b1 in [("wgT", "wbuf_g0", "wbuf_g1"), ("wuT", "wbuf_u0", "wbuf_u1")]:
                for hh, bh in ((0, b0), (1, b1)):
                    p.op("sync", reads=("scr_" + nm,), writes=(bh,), dma=True,
                         tag="wld_" + nm)(
                        lambda e, ic=ic, nm=nm, bh=bh, hh=hh: e.dma_start(
                            out=s[bh][:],
                            in_=ap_custom(self.scr[nm], hh * 4 * 128 * INNER + ic * 512,
                                          [[INNER, 128], [128 * INNER, 4], [1, 512]])))
            pg = 0 if ic % 2 == 0 else 2
            pu = 1 if ic % 2 == 0 else 3
            for tt in range(NTT):
                for hh, bh in ((0, "wbuf_g0"), (1, "wbuf_g1")):
                    def mmg(e, tt=tt, hh=hh, bh=bh, pg=pg):
                        ins = None
                        for kk in range(4):
                            k = hh * 4 + kk
                            ins = e.matmul(P[pg][:, 0:512],
                                           s["h2qFM"][:, k * T + tt * 128: k * T + (tt + 1) * 128],
                                           s[bh][:, kk * 512:(kk + 1) * 512],
                                           start=(k == 0), stop=(k == 7))
                        return ins
                    p.op("tensor", reads=(bh, "h2qFM"), writes=("P%d" % pg,), tag="gmm")(mmg)
                for hh, bh in ((0, "wbuf_u0"), (1, "wbuf_u1")):
                    def mmu(e, tt=tt, hh=hh, bh=bh, pu=pu):
                        ins = None
                        for kk in range(4):
                            k = hh * 4 + kk
                            ins = e.matmul(P[pu][:, 0:512],
                                           s["h2qFM"][:, k * T + tt * 128: k * T + (tt + 1) * 128],
                                           s[bh][:, kk * 512:(kk + 1) * 512],
                                           start=(k == 0), stop=(k == 7))
                        return ins
                    p.op("tensor", reads=(bh, "h2qFM"), writes=("P%d" % pu,), tag="umm")(mmu)
                p.op("scalar", reads=("P%d" % pg, "fstat"), writes=("t1",))(
                    lambda e, tt=tt, pg=pg: e.activation(out=s["t1"][:], in_=P[pg][:, 0:512],
                                                  func=AF.Silu,
                                                  scale=s["fstat"][:, 16 + tt:17 + tt]))
                p.op("vector", reads=("t1", "P%d" % pu, "fstat"), writes=("y",))(
                    lambda e, ic=ic, tt=tt, pu=pu: e.scalar_tensor_tensor(
                        s["y"][:, tt * 4096 + ic * 512: tt * 4096 + (ic + 1) * 512],
                        P[pu][:, 0:512], s["fstat"][:, 18 + tt:19 + tt],
                        s["t1"][:, 0:512], ALU.mult, ALU.mult))
                p.op("vector", reads=("y",), writes=("ymax",))(
                    lambda e, ic=ic, tt=tt: e.tensor_reduce(
                        s["fstat"][:, 22 + tt:23 + tt],
                        s["y"][:, tt * 4096 + ic * 512: tt * 4096 + (ic + 1) * 512],
                        AX.X, ALU.max, apply_absolute_value=True))
                if ic > 0:
                    p.op("gpsimd", reads=("ymax", "fstat"), writes=("fstat",))(
                        lambda e, tt=tt, c0=8 + tt * 4: e.tensor_scalar(
                            s["fstat"][:, c0:c0 + 1], s["fstat"][:, 22 + tt:23 + tt],
                            s["fstat"][:, c0:c0 + 1], None, ALU.max))
                else:
                    p.op("gpsimd", reads=("ymax",), writes=("fstat",))(
                        lambda e, tt=tt, c0=8 + tt * 4: e.tensor_scalar(
                            s["fstat"][:, c0:c0 + 1], s["fstat"][:, 22 + tt:23 + tt],
                            0.0, None, ALU.max))
        if ch == getattr(self, "tap_chunk", 0):
            self.tap("ytap", lambda: s["y"][:, 0:4096], ["y"])
        for tt in range(NTT):
            sl = (tt * 4096, (tt + 1) * 4096)
            c0 = 8 + tt * 4
            p.op("gpsimd", reads=("fstat",), writes=("fstat",))(
                lambda e, c0=c0: e.tensor_scalar(s["fstat"][:, c0 + 1:c0 + 2],
                                                 s["fstat"][:, c0:c0 + 1],
                                                 1e-5, None, ALU.max))
            p.op("vector", reads=("fstat",), writes=("fstat",))(
                lambda e, c0=c0: e.reciprocal(s["fstat"][:, c0 + 2:c0 + 3],
                                              s["fstat"][:, c0 + 1:c0 + 2]))
            p.op("scalar", reads=("fstat",), writes=("fstat",))(
                lambda e, c0=c0: e.activation(out=s["fstat"][:, c0 + 3:c0 + 4],
                                              in_=s["fstat"][:, c0 + 2:c0 + 3],
                                              func=AF.Copy, scale=127.0))
            p.op("vector", reads=("fstat", "wsc"), writes=("fstat",))(
                lambda e, c0=c0, tt=tt: e.scalar_tensor_tensor(
                    s["fstat"][:, 20 + tt:21 + tt], s["fstat"][:, c0 + 1:c0 + 2],
                    1.0 / 127, s["wsc"][:, 2:3], ALU.mult, ALU.mult))
            for hh in range(2):
                p.op("scalar", reads=("y", "fstat"), writes=("istage",))(
                    lambda e, sl=sl, hh=hh, c0=c0: e.activation(
                        out=s["istage"][:, 0:2048],
                        in_=s["y"][:, sl[0] + hh * 2048: sl[0] + (hh + 1) * 2048],
                        func=AF.Copy, scale=s["fstat"][:, c0 + 3:c0 + 4]))
                p.op("vector", reads=("istage",), writes=("istage",))(
                    lambda e: e.tensor_scalar(s["istage"][:, 0:2048],
                                              s["istage"][:, 0:2048],
                                              127, -128, ALU.min, ALU.max))
                p.op("vector", reads=("istage",), writes=("yq",))(
                    lambda e, sl=sl, hh=hh: e.tensor_copy(
                        s["yq"][:, sl[0] + hh * 2048: sl[0] + (hh + 1) * 2048],
                        s["istage"][:, 0:2048]))
        if ch == getattr(self, "tap_chunk", 0):
            self.tap("yq", lambda: s["yq"][:], ["yq"])
        for tt in range(NTT):
            for icg in range(8):
                def mmf(e, tt=tt, icg=icg):
                    ins = None
                    for b in range(4):
                        ic = icg * 4 + b
                        ins = e.matmul(P[2][:, b * 128:(b + 1) * 128],
                                       s["yq"][:, tt * 4096 + ic * 128: tt * 4096 + (ic + 1) * 128],
                                       s["ident128"][:], start=True, stop=True)
                    return ins
                p.op("tensor", reads=("yq", "c_ident128"), writes=("P2",), tag="yqT")(mmf)
                out_ap = ap_custom(s["yqFM"][:], (icg * 4) * T + tt * 128,
                                   [[32 * T, 128], [T, 4], [1, 128]])
                p.op("vector", reads=("P2",), writes=("yqFM",))(
                    lambda e, o=out_ap: e.tensor_copy(o, P[2][:, 0:512]))
        for icg in range(16):
            db = "wbuf_d" + ("A" if icg % 2 == 0 else "B")
            p.op("sync", reads=("scr_wdT",), writes=(db,), dma=True, tag="wd")(
                lambda e, icg=icg, db=db: e.dma_start(
                    out=s[db][:],
                    in_=ap_custom(self.scr["wdT"], icg * 256 * D,
                                  [[D, 128], [128 * D, 2], [1, D]])))
            def mmf(e, icg=icg, db=db):
                ins = None
                for kk in range(2):
                    ic = icg * 2 + kk
                    for tt in range(NTT):
                        for hf in range(2):
                            ins = e.matmul(P[3 + tt * 2 + hf][:, 0:512],
                                           s["yqFM"][:, ic * T + tt * 128: ic * T + (tt + 1) * 128],
                                           s[db][:, kk * 1024 + hf * 512: kk * 1024 + (hf + 1) * 512],
                                           start=(ic == 0), stop=(ic == 31))
                return ins
            p.op("tensor", reads=(db, "yqFM"),
                 writes=("P3", "P4", "P5", "P6"), tag="dmm")(mmf)
        for tt in range(NTT):
            for hf in range(2):
                pi = 3 + tt * 2 + hf
                sl = (tt * 1024 + hf * 512, tt * 1024 + (hf + 1) * 512)
                p.op("vector", reads=("P%d" % pi, xr, "fstat"), writes=(xr,))(
                    lambda e, pi=pi, sl=sl, tt=tt: e.scalar_tensor_tensor(
                        xb[:, sl[0]:sl[1]], P[pi][:, 0:512],
                        s["fstat"][:, 20 + tt:21 + tt],
                        xb[:, sl[0]:sl[1]], ALU.mult, ALU.add))
        in_ap = ap_custom(xb[:], 0, [[2 * 1024, 128], [1024, NTT], [1, 1024]])
        out_ap = ap_custom(self.dram_out, ch * self.cfg.T * D,
                           [[D, 128], [128 * D, NTT], [1, D]])
        p.op("sync", reads=(xr,), writes=("xout",), dma=True, tag="xst")(
            lambda e, i=in_ap, o=out_ap: e.dma_start(out=o, in_=i))

    def build(self, with_prep=True, nch=None):
        p, s = self.p, self.s
        self.load_consts()
        if with_prep:
            self.prep_weights_attn()
        p.op("vector", writes=("Salla", "Sallb"))(lambda e: e.memset(s["Sall"][:], 0.0))
        nch = self.cfg.NCH if nch is None else nch
        for ch in range(nch):
            self.p.phase = "front"
            self.frontend(ch)
            self.run_taps_frontend(ch)
            self.p.phase = "scanF"
            self.scan_front(ch, 0)
            for sc in range(self.cfg.NSC):
                self.p.phase = "scanTA"
                self.scan_tail_a(ch, sc)
                if sc + 1 < self.cfg.NSC:
                    self.p.phase = "scanF"
                    self.scan_front(ch, sc + 1)
                self.p.phase = "scanTB"
                self.scan_tail_b(ch, sc)
            self.run_taps_scan(ch)
            if ch == 0 and with_prep:
                self.p.phase = "prepF"
                self.prep_weights_ffn()
            self.p.phase = "epi"
            self.epilogue(ch)
            self.run_taps_epi(ch)
            self.p.phase = "ffn"
            self.ffn(ch)
        p.op("sync", reads=("xout",), tag="fence")(lambda e: e.nop())
        with self.nc.Block() as block:
            p.finalize(self.nc, block)

    def run_taps_frontend(self, ch):
        if ch != getattr(self, "tap_chunk", 0):
            return
        s, cfg = self.s, self.cfg
        for scrn in ["scr_ctrlT", "scr_qkvT", "scr_wgT"]:
            if scrn in self.tap_outs:
                nm = scrn[4:] + "T" if False else scrn.replace("scr_", "")
                self.p.op("gpsimd", reads=(scrn,), dma=True, tag="tap" + scrn)(
                    lambda e, o=self.tap_outs[scrn], i=self.scr[nm]: e.dma_start(out=o, in_=i))

        self.tap("qkvFM", lambda: s["qkvFM"][:], ["qkvFM"])
        self.tap("convq", lambda: s["convq"][:], ["convq"])
        self.tap("Vfm", lambda: s["Vfm"][:], ["Vfm"])
        self.tap("Qn", lambda: s["Qn"][:], ["Qn"])
        self.tap("Kn", lambda: s["Kn"][:], ["Kn"])
        self.tap("Qh", lambda: s["Qh"][:], ["Qh"])
        self.tap("Khm", lambda: s["Khm"][:], ["Khm"])
        self.tap("ab", lambda: s["ab"][:], ["ab"])
        self.tap("r_L", lambda: s["r_L"][:], ["r_L"])
        self.tap("r_beta", lambda: s["r_beta"][:], ["r_beta"])
        self.tap("r_eL", lambda: s["r_eL"][:], ["r_eL"])
        self.tap("outgate", lambda: s["outgate"][:], ["outgate"])

    def run_taps_scan(self, ch):
        if ch != getattr(self, "tap_chunk", 0):
            return
        s = self.s
        self.tap("readsFM", lambda: s["readsFM"][:], ["readsFM"])
        self.tap("Sall", lambda: s["Sall"][:], ["Sall"])

    def run_taps_epi(self, ch):
        if ch != getattr(self, "tap_chunk", 0):
            return
        s = self.s
        self.tap("normedFM", lambda: s["normedFM"][:], ["normedFM"])
        self.tap("x1", lambda: s["xbuf%d" % (ch % 2)][:], ["xbuf%d" % (ch % 2)])


def kernel(L0_attn_hidden, L0_attn_qkv_w, L0_attn_ctrl_w, L0_attn_ctrl_b,
           L0_attn_convq_w, L0_attn_convk_w, L0_attn_convv_w, L0_attn_norm_w,
           L0_attn_out_w, L0_attn_out_b, L0_ffn_w_gate, L0_ffn_w_up,
           L0_ffn_w_down):
    B, S, Dm = L0_attn_hidden.shape
    assert (B, S, Dm) == (8, 4096, 1024)
    cfg = Cfg(S=S, T=256)
    nc = bass.Bass("TRN2", target_bir_lowering=False)
    b = Builder(nc, cfg, taps=[])
    b.build()
    norm_w_t = np.ascontiguousarray(np.tile(np.asarray(L0_attn_norm_w, np.float32), (16, 16)))
    shared = {
        "qkv_w": np.ascontiguousarray(L0_attn_qkv_w, dtype=np.float32),
        "ctrl_w": np.ascontiguousarray(L0_attn_ctrl_w, dtype=np.float32),
        "ctrl_b": np.ascontiguousarray(L0_attn_ctrl_b, dtype=np.float32),
        "convq_w": np.ascontiguousarray(L0_attn_convq_w, dtype=np.float32),
        "convk_w": np.ascontiguousarray(L0_attn_convk_w, dtype=np.float32),
        "convv_w": np.ascontiguousarray(L0_attn_convv_w, dtype=np.float32),
        "norm_w_t": norm_w_t,
        "out_w": np.ascontiguousarray(L0_attn_out_w, dtype=np.float32),
        "out_b": np.ascontiguousarray(L0_attn_out_b, dtype=np.float32),
        "w_gate": np.ascontiguousarray(L0_ffn_w_gate, dtype=np.float32),
        "w_up": np.ascontiguousarray(L0_ffn_w_up, dtype=np.float32),
        "w_down": np.ascontiguousarray(L0_ffn_w_down, dtype=np.float32),
    }
    xin = np.ascontiguousarray(L0_attn_hidden, dtype=np.float32)
    in_maps = [dict(shared, x=xin[i]) for i in range(8)]
    res = run_bass_kernel_spmd(nc, in_maps, core_ids=list(range(8)))
    out = np.stack([np.asarray(res.results[i]["out"], np.float32) for i in range(8)], axis=0)
    return out
